# version 08
# baseline (speedup 1.0000x reference)
import contextlib
import numpy as np
import concourse.bass as bass
import concourse.mybir as mybir
from concourse.bass_utils import run_bass_kernel_spmd

F32 = mybir.dt.float32
BF16 = mybir.dt.bfloat16
AF = mybir.ActivationFunctionType
ALU = mybir.AluOpType
AX = mybir.AxisListType

NCORES = 8
BLOC = 2
S = 2048
TLOC = BLOC * S
D = 1024
DC = D // 128

ENGINES = ("tensor", "vector", "scalar", "gpsimd", "sync")
DMA_QUEUES = ("gpsimd", "sync")
NSEM = 4


def _ap_name(x):
    if isinstance(x, str):
        return x
    t = getattr(x, "tensor", None)
    if t is not None:
        return t.name
    n = getattr(x, "name", None)
    if n is not None:
        return n
    raise ValueError(f"cannot extract tensor from {x!r}")



def _ap(x):
    if isinstance(x, (float, int)) or x is None:
        return x
    if getattr(x, "tensor", None) is not None:
        return x
    return x[:]

class Prog:

    def __init__(self, name="prog"):
        self.name = name
        self.nc = bass.Bass("TRN2", target_bir_lowering=False)
        self.es = contextlib.ExitStack()
        self.phase_es = None
        self.sems = {}
        self.cnt = {e: 0 for e in ENGINES}
        self.opbuf = {e: [] for e in ENGINES}
        self.reg = {}
        self.aps = {}
        self._cvecs = {}
        self.scope = None
        for e in ENGINES:
            if e in DMA_QUEUES:
                for j in range(NSEM):
                    self.sems[(e, j)] = self.es.enter_context(
                        self.nc.semaphore(f"sem_{e}{j}"))
            else:
                self.sems[e] = self.es.enter_context(
                    self.nc.semaphore(f"sem_{e}"))
        self.known = {e: {} for e in ENGINES}

    def dram(self, name, shape, dtype, kind):
        ap = self.nc.dram_tensor(name, list(shape), dtype, kind=kind).ap()
        self.aps[name] = ap
        return ap

    def sbuf(self, name, shape, dtype):
        if self.phase_es is not None:
            return self.phase_es.enter_context(
                self.nc.sbuf_tensor(name, list(shape), dtype))
        return self.nc.alloc_sbuf_tensor(name, list(shape), dtype)

    def psum(self, name, shape, dtype=F32):
        if self.phase_es is not None:
            return self.phase_es.enter_context(
                self.nc.psum_tensor(name, list(shape), dtype))
        return self.nc.alloc_psum_tensor(name, list(shape), dtype)

    def cvec(self, val, dtype=F32, rows=128):
        key = (float(val), dtype, rows)
        if key not in self._cvecs:
            t = self.nc.alloc_sbuf_tensor(f"cv{len(self._cvecs)}", [rows, 1], dtype,
                                          side="right")
            self.memset("vector", t[:], float(val))
            self._cvecs[key] = t[:]
        return self._cvecs[key]

    @contextlib.contextmanager
    def phase(self, tag):
        prev = self.phase_es
        self.phase_es = contextlib.ExitStack()
        try:
            yield
        finally:
            self.barrier()
            self.phase_es.close()
            self.phase_es = prev

    def _covers(self, eng, idx):
        if eng in DMA_QUEUES:
            out = []
            for j in range(NSEM):
                cnt = (idx - 1 - j) // NSEM + 1 if idx - 1 >= j else 0
                if cnt > 0:
                    out.append(((eng, j), 16 * cnt))
            return out
        return [(eng, idx)]

    def emit(self, eng, thunk, reads=(), writes=(), register=True, war=True):
        waits = {}

        def need(src_eng, src_idx):
            if src_idx <= 0:
                return
            if src_eng == eng == "tensor":
                return
            for key, v in self._covers(src_eng, src_idx):
                if v <= self.known[eng].get(key, 0):
                    continue
                waits[key] = max(waits.get(key, 0), v)

        rnames = [_ap_name(r) for r in reads]
        wnames = [_ap_name(w) for w in writes]
        for n in rnames:
            e = self.reg.get(n)
            if e:
                for we, wi in e["w"].items():
                    need(we, wi)
        for n in wnames:
            e = self.reg.get(n)
            if e:
                for we, wi in e["w"].items():
                    need(we, wi)
                if war:
                    for re_, ri in e["r"].items():
                        need(re_, ri)

        self.cnt[eng] += 1
        idx = self.cnt[eng]
        if eng in DMA_QUEUES:
            semkey = (eng, (idx - 1) % NSEM)
            pv = 16 * ((idx - 1) // NSEM)
            if pv > 0 and pv > self.known[eng].get(semkey, 0):
                waits[semkey] = max(waits.get(semkey, 0), pv)
            inc = 16
        else:
            semkey, inc = eng, 1
        for s, v in waits.items():
            self.known[eng][s] = v
        self.opbuf[eng].append((sorted(waits.items(), key=str),
                                (semkey, inc), thunk, self.scope))
        for n in rnames:
            e = self.reg.setdefault(n, {"w": {}, "r": {}})
            e["r"][eng] = idx
        for n in wnames:
            e = self.reg.setdefault(n, {"w": {}, "r": {}})
            if register:
                e["r"] = {}
                if eng not in DMA_QUEUES:
                    e["w"] = {k: v for k, v in e["w"].items()
                              if k in DMA_QUEUES or k != eng}
                e["w"][eng] = idx
        return idx

    def barrier(self):
        snap = {e: self.cnt[e] for e in ENGINES}
        targets = []
        for s in ENGINES:
            if snap[s] == 0:
                continue
            if s in DMA_QUEUES:
                for j in range(NSEM):
                    u = (snap[s] - j + NSEM - 1) // NSEM
                    if u > 0:
                        targets.append(((s, j), 16 * u, s))
            else:
                targets.append((s, snap[s], s))
        for e in ENGINES:
            waits = []
            for key, v, src in targets:
                if src == e:
                    continue
                if v > self.known[e].get(key, 0):
                    waits.append((key, v))
                    self.known[e][key] = v
            if waits:
                self.opbuf[e].append((waits, None, None, self.scope))

    def finalize(self):
        with self.nc.Block() as blk:
            for e in ENGINES:
                ops = self.opbuf[e]
                if not ops:
                    continue
                sec = getattr(blk, e)

                def body(eng, ops=ops, e=e):
                    cur = [None]
                    st = contextlib.ExitStack()
                    for waits, inc_spec, thunk, scope in ops:
                        if scope != cur[0]:
                            st.close()
                            st = contextlib.ExitStack()
                            if scope is not None:
                                st.enter_context(self.nc.named_scope(scope))
                            cur[0] = scope
                        for (key, val) in waits:
                            eng.wait_ge(self.sems[key], val)
                        if thunk is not None:
                            ins = thunk(eng)
                            semkey, inc = inc_spec
                            ins.then_inc(self.sems[semkey], inc)
                    st.close()

                sec(body)
        self.es.close()
        return self.nc

    def dma(self, q, out, in_, **kw):
        assert q in DMA_QUEUES
        out, in_ = _ap(out), _ap(in_)
        self.emit(q, lambda e: e.dma_start(out=out, in_=in_, **kw),
                  reads=[in_], writes=[out])

    def mm(self, out, lhsT, rhs, start=True, stop=True, register=None,
           war=None):
        out, lhsT, rhs = _ap(out), _ap(lhsT), _ap(rhs)
        if register is None:
            register = stop
        if war is None:
            war = start
        self.emit("tensor",
                  lambda e: e.matmul(out, lhsT, rhs, start=start, stop=stop),
                  reads=[lhsT, rhs], writes=[out],
                  register=register, war=war)

    def transp(self, out, in_, ident, register=True, war=True):
        out, in_, ident = _ap(out), _ap(in_), _ap(ident)
        self.emit("tensor", lambda e: e.transpose(out, in_, ident),
                  reads=[in_, ident], writes=[out], register=register, war=war)

    def cp(self, eng, out, in_):
        out, in_ = _ap(out), _ap(in_)
        if eng == "scalar":
            self.emit(eng, lambda e: e.copy(out, in_),
                      reads=[in_], writes=[out])
        else:
            self.emit(eng, lambda e: e.tensor_copy(out, in_),
                      reads=[in_], writes=[out])

    def tt(self, eng, out, a, b, op):
        out, a, b = _ap(out), _ap(a), _ap(b)
        self.emit(eng, lambda e: e.tensor_tensor(out, a, b, op),
                  reads=[a, b], writes=[out])

    def ts(self, eng, out, a, s1, op, s2=None, op2=None, accum=None):
        out, a, s1, s2, accum = _ap(out), _ap(a), _ap(s1), _ap(s2), _ap(accum)
        extra = [s1] if not isinstance(s1, float) else []
        if s2 is not None and not isinstance(s2, float):
            extra.append(s2)
        kw = {}
        if op2 is not None:
            kw["op1"] = op2
        if accum is not None:
            kw["accum_out"] = accum
        wr = [out] + ([accum] if accum is not None else [])
        self.emit(eng, lambda e: e.tensor_scalar(out, a, s1, s2, op, **kw),
                  reads=[a] + extra, writes=wr)

    def stt(self, eng, out, in0, scalar, in1, op0, op1):
        out, in0, scalar, in1 = _ap(out), _ap(in0), _ap(scalar), _ap(in1)
        extra = [scalar] if not isinstance(scalar, float) else []
        self.emit(eng, lambda e: e.scalar_tensor_tensor(out, in0, scalar, in1, op0, op1),
                  reads=[in0, in1] + extra, writes=[out])

    def act(self, out, in_, func, bias=0.0, scale=1.0, accum=None):
        out, in_, bias, scale, accum = _ap(out), _ap(in_), _ap(bias), _ap(scale), _ap(accum)
        if isinstance(bias, float) and bias != 0.0:
            rows = _ap(in_).shape[0]
            bias = self.cvec(bias, rows=rows if rows in (1, 64, 128) else 128)
        extra = [x for x in (bias, scale) if not isinstance(x, float)]
        kw = {}
        if accum is not None:
            kw["accum_out"] = accum
        wr = [out] + ([accum] if accum is not None else [])
        self.emit("scalar",
                  lambda e: e.activation(out, in_, func, bias=bias, scale=scale, **kw),
                  reads=[in_] + extra, writes=wr)

    def red(self, eng, out, in_, op, absval=False, negate=False):
        out, in_ = _ap(out), _ap(in_)
        self.emit(eng,
                  lambda e: e.tensor_reduce(out, in_, axis=AX.X, op=op,
                                            apply_absolute_value=absval,
                                            negate=negate),
                  reads=[in_], writes=[out])

    def recip(self, out, in_):
        out, in_ = _ap(out), _ap(in_)
        self.emit("vector", lambda e: e.reciprocal(out, in_),
                  reads=[in_], writes=[out])

    def memset(self, eng, ap, v):
        ap = _ap(ap)
        self.emit(eng, lambda e: e.memset(ap, v), reads=[], writes=[ap])


INPUT_SPECS = {
    "L0_attn_hidden_states": ((16, 2048, 1024), F32),
    "L0_attn_Wq": ((1024, 1024), F32), "L0_attn_Wk": ((1024, 1024), F32),
    "L0_attn_Wv": ((1024, 1024), F32), "L0_attn_Wo": ((1024, 1024), F32),
    "L0_attn_norm_w": ((128,), F32),
    "L0_ffn_ln1_g": ((1024,), F32), "L0_ffn_ln1_b": ((1024,), F32),
    "L0_ffn_ln2_g": ((1024,), F32), "L0_ffn_ln2_b": ((1024,), F32),
    "L0_ffn_Wq": ((1024, 1024), F32), "L0_ffn_Wk": ((1024, 1024), F32),
    "L0_ffn_Wv": ((1024, 1024), F32), "L0_ffn_Wo": ((1024, 1024), F32),
    "L0_ffn_Wg": ((1024, 8), F32),
    "L0_ffn_W1": ((8, 1024, 4096), F32), "L0_ffn_b1": ((8, 4096), F32),
    "L0_ffn_W2": ((8, 4096, 1024), F32), "L0_ffn_b2": ((8, 1024), F32),
    "L1_attn_w_qkv": ((3072, 1024), F32), "L1_attn_b_qkv": ((3072,), F32),
    "L1_attn_w_out": ((1024, 1024), F32), "L1_attn_b_out": ((1024,), F32),
    "L1_ffn_gamma_x": ((1024,), F32), "L1_ffn_gamma_slot": ((1024,), F32),
    "L1_ffn_slot_embeds": ((8, 512, 1024), F32),
    "L1_ffn_w1": ((1024, 4096), F32), "L1_ffn_w2": ((4096, 1024), F32),
}
CONST_SPECS = {
    "c_eye_f32": ((128, 128), F32),
    "c_eye_bf": ((128, 128), BF16),
    "c_mask_ut": ((128, 128), F32),
    "c_ones_f32": ((128, 128), F32),
    "c_ones_bf": ((128, 128), BF16),
}
SCRATCH_SPECS = {
    "xn_tm": ((TLOC, D), F32),
    "xn_fm": ((DC, 128, TLOC), BF16),
    "att0_tm": ((TLOC, D), F32),
    "x2_tm": ((TLOC, D), F32),
    "m_fm": ((DC, 128, TLOC), BF16),
    "cw_tm": ((TLOC, 8), F32),
    "ffn0_tm": ((TLOC, D), F32),
    "w1m_bf": ((8, DC, 128, 4096), BF16),
    "w2m_bf": ((8, 32, 128, D), BF16),
    "x1_tm": ((TLOC, D), F32),
    "xn1_tm": ((TLOC, D), F32),
    "xn1_fm": ((DC, 128, TLOC), BF16),
    "att1_tm": ((TLOC, D), F32),
    "xnn_fm": ((BLOC, DC, 128, S), BF16),
    "se_fm": ((DC, 128, 4096), BF16),
    "proc_tm": ((BLOC, 4096, D), BF16),
    "ffn1_tm": ((TLOC, D), F32),
}
OUTPUT_SPECS = {"out": ((BLOC, S, D), F32)}

PRIV_SPECS = {
    "q0_fm": ((BLOC, DC, 128, S), BF16),
    "k0_fm": ((BLOC, DC, 128, S), BF16),
    "v0_fm": ((BLOC, DC, 128, S), BF16),
    "o0_fm": ((BLOC, DC, 128, S), BF16),
    "h0_fm": ((DC, 128, TLOC), BF16),
    "qf_fm": ((BLOC, DC, 128, S), BF16),
    "kf_fm": ((BLOC, DC, 128, S), BF16),
    "vf_tm": ((BLOC, 16, 128, D), BF16),
    "ctx0_fm": ((BLOC, DC, 128, S), BF16),
    "m_fm32": ((DC, 128, TLOC), F32),
    "qkv1_fm": ((BLOC, 24, 128, S), BF16),
    "v1_tm": ((BLOC, 16, 128, D), BF16),
    "ctx1_fm": ((BLOC, DC, 128, S), BF16),
    "xnn_tmb": ((BLOC, 16, 128, D), BF16),
    "w1l1_bf": ((32, 128, D), BF16),
    "w2l1_bf": ((32, 128, D), BF16),
}

PHASES = []


def phase(name, reads, writes, privs=()):
    def deco(fn):
        PHASES.append((name, fn, list(reads), list(writes), list(privs)))
        return fn
    return deco


def spec_of(name):
    for d in (INPUT_SPECS, CONST_SPECS, SCRATCH_SPECS, OUTPUT_SPECS, PRIV_SPECS):
        if name in d:
            sh, dt = d[name]
            if name == "L0_attn_hidden_states":
                sh = (BLOC, S, D)
            return sh, dt
    raise KeyError(name)



def ln_stats(P, xt, mu, rstd, xc, sq, acc, eps=1e-5):
    P.red("vector", mu, xt, ALU.add)
    P.ts("vector", mu, mu, 1.0 / D, ALU.mult)
    P.ts("vector", xc, xt, mu, ALU.subtract)
    P.act(sq, xc, AF.Square, accum=acc)
    P.act(acc, acc, AF.Ln, scale=1.0 / D, bias=eps)
    P.act(rstd, acc, AF.Exp, scale=-0.5)


def tm_tile_to_fm(P, tag, src_bf, eye_bf, dst_hbm_fm, ti, psum_bf, stage_bf,
                  n_chunks=DC, also_f32=None, psum32=None, eye32=None,
                  src32=None, stage32=None, dst32=None):
    for dc in range(n_chunks):
        P.transp(psum_bf[:, dc * 128:(dc + 1) * 128],
                 src_bf[:, dc * 128:(dc + 1) * 128], eye_bf,
                 register=(dc == n_chunks - 1), war=(dc == 0))
    P.cp("vector", stage_bf[:], psum_bf[:])
    P.dma("gpsimd",
          dst_hbm_fm[:, :, ti * 128:(ti + 1) * 128].rearrange("c p t -> p c t"),
          stage_bf[:].rearrange("p (c t) -> p c t", c=n_chunks))
    if also_f32:
        for dc in range(n_chunks):
            P.transp(psum32[:, dc * 128:(dc + 1) * 128],
                     src32[:, dc * 128:(dc + 1) * 128], eye32,
                     register=(dc == n_chunks - 1), war=(dc == 0))
        P.cp("vector", stage32[:], psum32[:])
        P.dma("gpsimd",
              dst32[:, :, ti * 128:(ti + 1) * 128].rearrange("c p t -> p c t"),
              stage32[:].rearrange("p (c t) -> p c t", c=n_chunks))



@phase("p0_ln0",
       reads=["L0_attn_hidden_states", "c_eye_bf"],
       writes=["xn_tm", "xn_fm"])
def p0_ln0(P, T):
    x = T["L0_attn_hidden_states"].rearrange("b s d -> (b s) d")
    eye_bf = T["c_eye_bf"]
    with P.phase("p0"):
        R2 = lambda nm, sh, dt: Rot([P.sbuf(f"{nm}{i}", sh, dt) for i in range(2)])
        xt_r = R2("p0_xt", [128, D], F32)
        xc_r = R2("p0_xc", [128, D], F32)
        xnb_r = R2("p0_xnb", [128, D], BF16)
        mu_r = R2("p0_mu", [128, 1], F32)
        rstd_r = R2("p0_rstd", [128, 1], F32)
        lsq = P.sbuf("p0_sq", [128, D], F32)
        lacc_r = R2("p0_acc", [128, 1], F32)
        ptr_r = Rot([P.psum(f"p0_ptr{i}", [128, D], BF16) for i in range(2)])
        stg_r = R2("p0_stg", [128, D], BF16)
        for ti in range(TLOC // 128):
            sl = slice(ti * 128, (ti + 1) * 128)
            xt, xc, xnb = xt_r.nxt(), xc_r.nxt(), xnb_r.nxt()
            mu, rstd, lacc = mu_r.nxt(), rstd_r.nxt(), lacc_r.nxt()
            ptr, stg = ptr_r.nxt(), stg_r.nxt()
            P.dma("sync", xt[:], x[sl, :])
            ln_stats(P, xt, mu, rstd, xc, lsq, lacc)
            P.ts("vector", xc[:], xc[:], rstd[:], ALU.mult)
            P.dma("gpsimd", T["xn_tm"][sl, :], xc[:])
            P.cp("scalar", xnb[:], xc[:])
            tm_tile_to_fm(P, "p0", xnb, eye_bf, T["xn_fm"], ti, ptr, stg)


class Rot:
    def __init__(self, items):
        self.items = list(items)
        self.i = 0

    def nxt(self):
        r = self.items[self.i % len(self.items)]
        self.i += 1
        return r


def load_wT(P, tag, w_dram, wt_sb, eye_bf, n_out=D, n_in=D):
    nic = n_in // 128
    f32t = Rot([P.sbuf(f"{tag}_wl32{i}", [128, n_in], F32) for i in range(2)])
    bft = Rot([P.sbuf(f"{tag}_wlbf{i}", [128, n_in], BF16) for i in range(2)])
    ptr = Rot([P.psum(f"{tag}_wlps{i}", [128, n_in], BF16) for i in range(2)])
    for oc in range(n_out // 128):
        f3, bf, pt = f32t.nxt(), bft.nxt(), ptr.nxt()
        P.dma("sync", f3[:], w_dram[oc * 128:(oc + 1) * 128, :])
        P.cp("vector", bf, f3)
        for dc in range(nic):
            P.transp(_ap(pt)[:, dc * 128:(dc + 1) * 128],
                     _ap(bf)[:, dc * 128:(dc + 1) * 128], eye_bf,
                     register=(dc == nic - 1), war=(dc == 0))
        P.cp("scalar",
             wt_sb[:, :, oc * 128:(oc + 1) * 128],
             _ap(pt).rearrange("p (c t) -> p c t", c=nic))


def load_w(P, tag, w_dram, w_sb, n_out):
    f32t = P.sbuf(f"{tag}_wl32", [128, n_out], F32)
    for dc in range(DC):
        P.dma("sync", f32t[:], w_dram[dc * 128:(dc + 1) * 128, :])
        P.cp("vector", w_sb[:, dc, :], f32t)


def row_of(P, tag, vec_dram, n):
    r = P.sbuf(f"{tag}_row", [1, n], F32)
    P.dma("sync", r[:], vec_dram.rearrange("(a d) -> a d", a=1))
    return r



@phase("p1_att0",
       reads=["xn_fm", "L0_attn_Wq", "L0_attn_Wk", "L0_attn_Wv", "L0_attn_Wo",
              "L0_attn_norm_w", "c_eye_bf", "c_mask_ut", "c_ones_bf",
              "c_ones_f32"],
       writes=["att0_tm"],
       privs=["q0_fm", "k0_fm", "v0_fm", "o0_fm"])
def p1_att0(P, T):
    eye = T["c_eye_bf"]
    ones_bf_col = _ap(T["c_ones_bf"])[:, 0:1]
    ones_f32_col = _ap(T["c_ones_f32"])[:, 0:1]
    mask = T["c_mask_ut"]

    P.scope = "p1_proj"
    with P.phase("p1proj"):
        wts = {}
        for wn, key in (("L0_attn_Wq", "q0_fm"), ("L0_attn_Wk", "k0_fm"),
                        ("L0_attn_Wv", "v0_fm")):
            wt = P.sbuf(f"p1_{wn}T", [128, DC, D], BF16)
            load_wT(P, f"p1{wn}", T[wn], wt, eye)
            wts[key] = wt
        xt = P.sbuf("p1_xt", [128, DC, 512], BF16)
        pos = Rot([P.psum(f"p1_po{i}", [128, 512], F32) for i in range(2)])
        obs = Rot([P.sbuf(f"p1_ob{i}", [128, 512], BF16) for i in range(2)])
        for b in range(BLOC):
            for tmac in range(S // 512):
                t0 = b * S + tmac * 512
                P.dma("sync", xt[:],
                      T["xn_fm"][:, :, t0:t0 + 512].rearrange("c p t -> p c t"))
                for key, wt in wts.items():
                    for ec in range(DC):
                        po, ob = pos.nxt(), obs.nxt()
                        for dc in range(DC):
                            P.mm(po, wt[:, dc, ec * 128:(ec + 1) * 128],
                                 xt[:, dc, :], start=(dc == 0), stop=(dc == DC - 1))
                        P.cp("vector", ob, po)
                        P.dma("gpsimd",
                              T[key][b, ec, :, tmac * 512:(tmac + 1) * 512], ob)

    NCH = S // 128
    P.scope = "p1_chunk"
    with P.phase("p1attn"):
        nw_row = row_of(P, "p1_nw", T["L0_attn_norm_w"], 128)
        qh = P.sbuf("p1_qh", [128, S], BF16)
        kh = P.sbuf("p1_kh", [128, S], BF16)
        vh = P.sbuf("p1_vh", [128, S], BF16)
        kv_s = P.sbuf("p1_kv", [128, 256], BF16)
        scst = P.sbuf("p1_scst", [128, 128], BF16)
        state_s = P.sbuf("p1_state", [128, 129], BF16)
        den_s = P.sbuf("p1_den", [1, 128], F32)
        d2_s = P.sbuf("p1_d2", [1, 128], F32)
        osq_s = P.sbuf("p1_osq", [128, 128], F32)
        t_s = P.sbuf("p1_t", [1, 128], F32)
        al_s = P.sbuf("p1_al", [1, 128], F32)
        scale_s = P.sbuf("p1_scale", [128, 128], F32)
        on_s = P.sbuf("p1_on", [128, 128], BF16)
        ps_kv = P.psum("p1_pskv", [128, 256], BF16)
        ps_sc = P.psum("p1_pssc", [128, 128], F32)
        ps_o = P.psum("p1_pso", [128, 128], F32)
        ps_state = P.psum("p1_psst", [128, 129], F32)
        ps_den = P.psum("p1_psden", [128, 128], F32)
        ps_ssq = P.psum("p1_psssq", [128, 128], F32)
        ps_scale = P.psum("p1_pssca", [128, 128], F32)
        for b in range(BLOC):
            for h in range(8):
                P.dma("sync", qh[:], T["q0_fm"][b, h, :, :])
                P.dma("sync", kh[:], T["k0_fm"][b, h, :, :])
                P.dma("sync", vh[:], T["v0_fm"][b, h, :, :])
                for c in range(NCH):
                    csl = slice(c * 128, (c + 1) * 128)
                    P.transp(ps_kv[:, 0:128], kh[:, csl], eye,
                             register=False, war=True)
                    P.transp(ps_kv[:, 128:256], vh[:, csl], eye,
                             register=True, war=False)
                    P.cp("vector", kv_s, ps_kv)
                    ktc = _ap(kv_s)[:, 0:128]
                    vtc = _ap(kv_s)[:, 128:256]
                    P.mm(ps_sc, kh[:, csl], qh[:, csl])
                    P.tt("vector", scst, ps_sc, mask, ALU.mult)
                    if c > 0:
                        P.cp("scalar", state_s, ps_state)
                        P.mm(ps_o, _ap(state_s)[:, 0:128], qh[:, csl],
                             start=True, stop=False)
                        P.mm(ps_o, vtc, scst, start=False, stop=True)
                        P.mm(ps_den[0:1, :], _ap(state_s)[:, 128:129],
                             qh[:, csl], start=True, stop=False)
                        P.mm(ps_den[0:1, :], ones_bf_col, scst,
                             start=False, stop=True)
                    else:
                        P.mm(ps_o, vtc, scst, start=True, stop=True)
                        P.mm(ps_den[0:1, :], ones_bf_col, scst,
                             start=True, stop=True)
                    P.mm(ps_state[:, 0:128], ktc, vtc,
                         start=(c == 0), stop=False, register=False)
                    P.mm(ps_state[:, 128:129], ktc, ones_bf_col,
                         start=False, stop=(c == NCH - 1), register=True)
                    P.act(den_s, ps_den[0:1, :], AF.Abs)
                    P.ts("vector", den_s, den_s, 1.0, ALU.max)
                    P.act(osq_s, ps_o, AF.Square)
                    P.mm(ps_ssq[0:1, :], ones_f32_col, osq_s)
                    P.tt("vector", d2_s, den_s, den_s, ALU.mult)
                    P.ts("vector", d2_s, d2_s, 1e-5, ALU.mult)
                    P.stt("vector", t_s, ps_ssq[0:1, :], 1.0 / 128.0, d2_s,
                          ALU.mult, ALU.add)
                    P.act(t_s, t_s, AF.Ln)
                    P.act(al_s, t_s, AF.Exp, scale=-0.5)
                    P.mm(ps_scale, nw_row, al_s)
                    P.cp("vector", scale_s, ps_scale)
                    P.tt("vector", on_s, ps_o, scale_s, ALU.mult)
                    P.dma("gpsimd", T["o0_fm"][b, h, :, csl], on_s)

    P.scope = "p1_wo"
    with P.phase("p1wo"):
        woT = P.sbuf("p1_woT", [128, DC, D], BF16)
        load_wT(P, "p1wo", T["L0_attn_Wo"], woT, eye)
        ots = Rot([P.sbuf(f"p1_ot{i}", [128, DC, 128], BF16) for i in range(2)])
        pws = Rot([P.psum(f"p1_pw{i}", [128, 512], F32) for i in range(4)])
        oss = Rot([P.sbuf(f"p1_os{i}", [128, D], F32) for i in range(2)])
        for b in range(BLOC):
            for tt_ in range(S // 128):
                ts_ = slice(tt_ * 128, (tt_ + 1) * 128)
                ot = ots.nxt()
                os_ = oss.nxt()
                P.dma("sync", ot[:],
                      T["o0_fm"][b, :, :, ts_].rearrange("c p t -> p c t"))
                for half in range(2):
                    pw = pws.nxt()
                    for ec in range(DC):
                        P.mm(pw, ot[:, ec, :],
                             woT[:, ec, half * 512:(half + 1) * 512],
                             start=(ec == 0), stop=(ec == DC - 1))
                    P.cp("vector", os_[:, half * 512:(half + 1) * 512], pw)
                P.dma("gpsimd", T["att0_tm"][b * S + tt_ * 128:
                                             b * S + (tt_ + 1) * 128, :], os_)


def bcast_row(P, tag, row_ap, n, ones_f32, scale=1.0):
    ps = P.psum(f"{tag}_bps", [128, 512], F32)
    out = P.sbuf(f"{tag}_bc", [128, n], F32)
    for j in range(0, n, 512):
        w = min(512, n - j)
        P.mm(ps[:, 0:w], _ap(ones_f32)[0:1, :], _ap(row_ap)[0:1, j:j + w])
        if scale == 1.0:
            P.cp("vector", out[:, j:j + w], ps[:, 0:w])
        else:
            P.ts("vector", out[:, j:j + w], ps[:, 0:w], float(scale), ALU.mult)
    return out


def softmax_TQ(P, bufs, kh, qh, v_s, head, negc_s, out_dst, scale, bv=None):
    (ps_sT, E_r, ps_ctx_r, ps_sig, sig_r, sg_r, ctx_r) = bufs
    NK2 = S // 256
    c0 = 65 * head

    def make_epilogue(pc, qsl):
        sig_s, sg_s, ctxb = sig_r.nxt(), sg_r.nxt(), ctx_r.nxt()

        def epi():
            P.act(sig_s, _ap(pc)[64:65, :], AF.Ln)
            P.act(sig_s, sig_s, AF.Exp, scale=-1.0)
            P.mm(_ap(ps_sig)[0:64, :], P.ones_f32_row64, sig_s)
            P.cp("vector", sg_s, _ap(ps_sig)[0:64, :])
            if bv is None:
                P.tt("vector", ctxb, _ap(pc)[0:64, :], sg_s, ALU.mult)
            else:
                P.tt("vector", sg_s, _ap(pc)[0:64, :], sg_s, ALU.mult)
                P.ts("vector", ctxb, sg_s, bv, ALU.add)
            P.dma("gpsimd", out_dst[:, qsl], ctxb)
        return epi

    pend = None
    for qm in range(S // 512):
        qsl = slice(qm * 512, (qm + 1) * 512)
        pc = ps_ctx_r.nxt()
        prev = None
        for kt2 in range(NK2):
            ka, kb = 2 * kt2, 2 * kt2 + 1
            pt = ps_sT.nxt()
            P.mm(_ap(pt)[:, 0:512], kh[:, ka * 128:(ka + 1) * 128],
                 qh[:, qsl], register=False)
            P.mm(_ap(pt)[:, 512:1024], kh[:, kb * 128:(kb + 1) * 128],
                 qh[:, qsl], war=False)
            E = E_r.nxt()
            P.act(E, pt, AF.Exp, bias=negc_s, scale=scale)
            if kt2 == 1 and pend is not None:
                pend()
                pend = None
            if prev is not None:
                pa, pE = prev
                P.mm(_ap(pc)[0:65, :], v_s[:, pa, c0:c0 + 65],
                     _ap(pE)[:, 0:512],
                     start=(pa == 0), stop=False, register=False)
                P.mm(_ap(pc)[0:65, :], v_s[:, pa + 1, c0:c0 + 65],
                     _ap(pE)[:, 512:1024], start=False,
                     stop=False)
            prev = (ka, E)
        pa, pE = prev
        P.mm(_ap(pc)[0:65, :], v_s[:, pa, c0:c0 + 65], _ap(pE)[:, 0:512],
             start=(pa == 0), stop=False, register=False)
        P.mm(_ap(pc)[0:65, :], v_s[:, pa + 1, c0:c0 + 65],
             _ap(pE)[:, 512:1024], start=False, stop=True)
        pend = make_epilogue(pc, qsl)
    pend()


def head_bound(P, bufs, qh, kh, negc_s, scale_mag):
    (sq_s, ps_n, nmax, tmp1) = bufs
    for which, src in ((0, qh), (1, kh)):
        P.tt("vector", sq_s, src, src, ALU.mult)
        for j in range(S // 512):
            P.mm(ps_n[0:1, :], P.ones_f32_col64, sq_s[:, j * 512:(j + 1) * 512])
            P.red("vector", tmp1, ps_n[0:1, :], ALU.max)
            if j == 0:
                P.cp("vector", nmax[0:1, which:which + 1], tmp1)
            else:
                P.tt("vector", nmax[0:1, which:which + 1],
                     nmax[0:1, which:which + 1], tmp1, ALU.max)
    P.tt("vector", tmp1, nmax[0:1, 0:1], nmax[0:1, 1:2], ALU.mult)
    P.act(tmp1, tmp1, AF.Ln, bias=1e-30)
    P.act(tmp1, tmp1, AF.Exp, scale=0.5)
    P.mm(ps_n[:, 0:1], _ap(P.ones_f32)[0:1, :], tmp1)
    P.act(negc_s, ps_n[:, 0:1], AF.Copy, scale=-0.75 * scale_mag)



@phase("p2_mha0",
       reads=["xn_tm", "xn_fm", "L0_ffn_ln1_g", "L0_ffn_ln1_b",
              "L0_ffn_ln2_g", "L0_ffn_ln2_b", "L0_ffn_Wq", "L0_ffn_Wk",
              "L0_ffn_Wv", "L0_ffn_Wo", "L0_ffn_Wg", "L0_ffn_W1", "L0_ffn_W2",
              "c_eye_bf", "c_eye_f32", "c_ones_f32", "c_ones_bf"],
       writes=["x2_tm", "m_fm", "cw_tm", "w1m_bf", "w2m_bf"],
       privs=["h0_fm", "qf_fm", "kf_fm", "vf_tm", "ctx0_fm", "m_fm32"])
def p2_mha0(P, T):
    eye = T["c_eye_bf"]
    eye32 = T["c_eye_f32"]
    P.ones_f32 = T["c_ones_f32"]
    P.ones_bf_col = _ap(T["c_ones_bf"])[:, 0:1]
    P.ones_f32_col64 = _ap(T["c_ones_f32"])[0:64, 0:1]
    P.ones_f32_row64 = _ap(T["c_ones_f32"])[0:1, 0:64]

    P.scope = "p2_a"
    with P.phase("p2a"):
        g1r = row_of(P, "p2g1", T["L0_ffn_ln1_g"], D)
        b1r = row_of(P, "p2b1", T["L0_ffn_ln1_b"], D)
        g1 = bcast_row(P, "p2g1", g1r, D, P.ones_f32)
        b1 = bcast_row(P, "p2b1", b1r, D, P.ones_f32)
        R2 = lambda nm, sh, dt: Rot([P.sbuf(f"{nm}{i}", sh, dt) for i in range(2)])
        xt_r = R2("p2_xt", [128, D], F32)
        xc_r = R2("p2_xc", [128, D], F32)
        hb_r = R2("p2_hb", [128, D], BF16)
        mu_r = R2("p2_mu", [128, 1], F32)
        rstd_r = R2("p2_rstd", [128, 1], F32)
        lsq = P.sbuf("p2_sq", [128, D], F32)
        lacc_r = R2("p2_acc", [128, 1], F32)
        ptr_r = Rot([P.psum(f"p2_ptr{i}", [128, D], BF16) for i in range(2)])
        stg_r = R2("p2_stg", [128, D], BF16)
        for ti in range(TLOC // 128):
            sl = slice(ti * 128, (ti + 1) * 128)
            xt, xc, hb = xt_r.nxt(), xc_r.nxt(), hb_r.nxt()
            mu, rstd, lacc = mu_r.nxt(), rstd_r.nxt(), lacc_r.nxt()
            ptr, stg = ptr_r.nxt(), stg_r.nxt()
            P.dma("sync", xt[:], T["xn_tm"][sl, :])
            ln_stats(P, xt, mu, rstd, xc, lsq, lacc)
            P.ts("vector", xc[:], xc[:], rstd[:], ALU.mult)
            P.tt("vector", xc, xc, g1, ALU.mult)
            P.tt("vector", xc, xc, b1, ALU.add)
            P.cp("scalar", hb, xc)
            tm_tile_to_fm(P, "p2", hb, eye, T["h0_fm"], ti, ptr, stg)

    P.scope = "p2_b"
    with P.phase("p2b"):
        wq = P.sbuf("p2_wq", [128, DC, D], BF16)
        wk = P.sbuf("p2_wk", [128, DC, D], BF16)
        wv = P.sbuf("p2_wv", [128, DC, D], BF16)
        load_w(P, "p2wq", T["L0_ffn_Wq"], wq, D)
        load_w(P, "p2wk", T["L0_ffn_Wk"], wk, D)
        load_w(P, "p2wv", T["L0_ffn_Wv"], wv, D)
        xt = P.sbuf("p2_xh", [128, DC, 512], BF16)
        pos = Rot([P.psum(f"p2_po{i}", [128, 512], F32) for i in range(4)])
        obs = Rot([P.sbuf(f"p2_ob{i}", [128, 512], BF16) for i in range(4)])
        for b in range(BLOC):
            for tmac in range(S // 512):
                t0 = b * S + tmac * 512
                P.dma("sync", xt[:],
                      T["h0_fm"][:, :, t0:t0 + 512].rearrange("c p t -> p c t"))
                for key, wt in (("qf_fm", wq), ("kf_fm", wk)):
                    for ec in range(DC):
                        po, ob = pos.nxt(), obs.nxt()
                        for dc in range(DC):
                            P.mm(po, wt[:, dc, ec * 128:(ec + 1) * 128],
                                 xt[:, dc, :], start=(dc == 0),
                                 stop=(dc == DC - 1))
                        P.cp("vector", ob, po)
                        P.dma("gpsimd",
                              T[key][b, ec, :, tmac * 512:(tmac + 1) * 512], ob)
                for st in range(4):
                    tc = tmac * 4 + st
                    for half in range(2):
                        po, ob = pos.nxt(), obs.nxt()
                        hsl = slice(half * 512, (half + 1) * 512)
                        for dc in range(DC):
                            P.mm(po, xt[:, dc, st * 128:(st + 1) * 128],
                                 wv[:, dc, hsl], start=(dc == 0),
                                 stop=(dc == DC - 1))
                        P.cp("vector", ob, po)
                        P.dma("gpsimd", T["vf_tm"][b, tc, :, hsl], ob)

    def precast_ops(wc32, wcb):
        for e in range(8):
            for dc in range(DC):
                c32, cb = wc32.nxt(), wcb.nxt()
                P.dma("gpsimd", c32[:],
                      T["L0_ffn_W1"][e, dc * 128:(dc + 1) * 128, :])
                P.cp("vector", cb, c32)
                P.dma("gpsimd", T["w1m_bf"][e, dc, :, :], cb)
                yield
        for e in range(8):
            for g in range(8):
                c32, cb = wc32.nxt(), wcb.nxt()
                P.dma("gpsimd",
                      c32[:].rearrange("p (a d) -> p a d", a=4),
                      T["L0_ffn_W2"][e, g * 512:(g + 1) * 512, :]
                      .rearrange("(a p) d -> p a d", p=128))
                P.cp("vector", cb, c32)
                P.dma("gpsimd",
                      T["w2m_bf"][e, 4 * g:4 * g + 4, :, :]
                      .rearrange("c p d -> p c d"),
                      cb[:].rearrange("p (c d) -> p c d", c=4))
                yield
        while True:
            yield

    P.scope = "p2_c"
    with P.phase("p2c"):
        NH = 16
        wc32 = Rot([P.sbuf(f"p2_wc32{i}", [128, 4096], F32) for i in range(2)])
        wcb = Rot([P.sbuf(f"p2_wcb{i}", [128, 4096], BF16) for i in range(2)])
        precast = precast_ops(wc32, wcb)
        v_r = Rot([P.sbuf(f"p2_vs{i}", [128, S // 128, NH * 65], BF16)
                   for i in range(2)])
        qc = Rot([P.sbuf(f"p2_qc{i}", [128, S], BF16) for i in range(2)])
        kc = Rot([P.sbuf(f"p2_kc{i}", [128, S], BF16) for i in range(2)])
        sq_s = P.sbuf("p2_sqs", [64, S], F32)
        nmax = P.sbuf("p2_nmax", [1, 2], F32)
        tmp1 = P.sbuf("p2_tmp1", [1, 1], F32)
        negc = Rot([P.sbuf(f"p2_negc{i}", [128, 1], F32) for i in range(2)])
        ps_n = P.psum("p2_psn", [128, 512], F32)
        ps_sT = Rot([P.psum(f"p2_psT{i}", [128, 1024], F32) for i in range(2)])
        E_r = Rot([P.sbuf(f"p2_E{i}", [128, 1024], BF16) for i in range(2)])
        ps_ctx = Rot([P.psum(f"p2_psctx{i}", [128, 512], F32) for i in range(2)])
        ps_sig = P.psum("p2_pssig", [128, 512], F32)
        sig_r = Rot([P.sbuf(f"p2_sig{i}", [1, 512], F32) for i in range(2)])
        sg_r = Rot([P.sbuf(f"p2_sg{i}", [64, 512], F32) for i in range(2)])
        ctx_r = Rot([P.sbuf(f"p2_ctxb{i}", [64, 512], BF16) for i in range(2)])
        hb_bufs = (sq_s, ps_n, nmax, tmp1)
        at_bufs = (ps_sT, E_r, ps_ctx, ps_sig, sig_r, sg_r, ctx_r)
        for vb in v_r.items:
            P.memset("vector",
                 _ap(vb).rearrange("p c (h w) -> p c h w", w=65)[:, :, :, 64:65],
                 1.0)
        for b in range(BLOC):
            v_s = v_r.nxt()
            for tc in range(S // 128):
                P.dma("gpsimd",
                      _ap(v_s)[:, tc, :].rearrange("p (h w) -> p h w", w=65)
                      [:, :, 0:64],
                      T["vf_tm"][b, tc, :, :].rearrange("p (h w) -> p h w", w=64))
            for hp in range(8):
                qcb, kcb = qc.nxt(), kc.nxt()
                P.dma("sync", qcb[:], T["qf_fm"][b, hp, :, :])
                P.dma("sync", kcb[:], T["kf_fm"][b, hp, :, :])
                for sub in range(2):
                    hbase = 64 * sub
                    h = 2 * hp + sub
                    qh = _ap(qcb)[hbase:hbase + 64, :]
                    kh = _ap(kcb)[hbase:hbase + 64, :]
                    nc_ = negc.nxt()
                    head_bound(P, hb_bufs, qh, kh, nc_, 0.125)
                    softmax_TQ(P, at_bufs, kh, qh, v_s, h, nc_,
                               T["ctx0_fm"][b, hp, hbase:hbase + 64, :], 0.125)
                    next(precast)
                    next(precast)
                    next(precast)
                    next(precast)

    P.scope = "p2_d"
    with P.phase("p2d"):
        wo = P.sbuf("p2_wo", [128, DC, D], BF16)
        load_w(P, "p2wo", T["L0_ffn_Wo"], wo, D)
        wg = P.sbuf("p2_wg", [128, DC, 8], F32)
        wgl = P.sbuf("p2_wgl", [128, 8], F32)
        for dc in range(DC):
            P.dma("sync", wgl[:], T["L0_ffn_Wg"][dc * 128:(dc + 1) * 128, :])
            P.cp("vector", wg[:, dc, :], wgl)
        g2r = row_of(P, "p2g2", T["L0_ffn_ln2_g"], D)
        b2r = row_of(P, "p2b2", T["L0_ffn_ln2_b"], D)
        g2 = bcast_row(P, "p2g2", g2r, D, P.ones_f32)
        b2 = bcast_row(P, "p2b2", b2r, D, P.ones_f32)
        ot = P.sbuf("p2_ot", [128, DC, 128], BF16)
        xnt = P.sbuf("p2_xnt", [128, D], F32)
        x2s = P.sbuf("p2_x2s", [128, D], F32)
        mb = P.sbuf("p2_mb", [128, D], BF16)
        m32 = P.sbuf("p2_m32", [128, D], F32)
        xc2 = P.sbuf("p2_xc2", [128, D], F32)
        mu = P.sbuf("p2_mu2", [128, 1], F32)
        rstd = P.sbuf("p2_rstd2", [128, 1], F32)
        lsq = P.sbuf("p2_sq2", [128, D], F32)
        lacc = P.sbuf("p2_acc2", [128, 1], F32)
        pw = Rot([P.psum(f"p2_pw{i}", [128, 512], F32) for i in range(2)])
        ptr = P.psum("p2_ptr2", [128, D], BF16)
        stg = P.sbuf("p2_stg2", [128, D], BF16)
        ptr32 = P.psum("p2_ptr32", [128, D], F32)
        stg32 = P.sbuf("p2_stg32", [128, D], F32)
        ps_r = P.psum("p2_psr", [128, 512], F32)
        E8 = P.sbuf("p2_E8", [128, 8], F32)
        mx = P.sbuf("p2_mx", [128, 1], F32)
        m1 = P.sbuf("p2_m1", [128, 1], F32)
        m2 = P.sbuf("p2_m2", [128, 1], F32)
        iseq = P.sbuf("p2_iseq", [128, 8], F32)
        e8b = P.sbuf("p2_e8b", [128, 8], F32)
        cw = P.sbuf("p2_cw", [128, 8], F32)
        for b in range(BLOC):
            for tt_ in range(S // 128):
                ti = b * (S // 128) + tt_
                sl = slice(ti * 128, (ti + 1) * 128)
                P.dma("sync", ot[:],
                      T["ctx0_fm"][b, :, :, tt_ * 128:(tt_ + 1) * 128]
                      .rearrange("c p t -> p c t"))
                P.dma("sync", xnt[:], T["xn_tm"][sl, :])
                for half in range(2):
                    p_ = pw.nxt()
                    hsl = slice(half * 512, (half + 1) * 512)
                    for ec in range(DC):
                        P.mm(p_, ot[:, ec, :], wo[:, ec, hsl],
                             start=(ec == 0), stop=(ec == DC - 1))
                    P.tt("vector", x2s[:, hsl], p_, _ap(xnt)[:, hsl], ALU.add)
                P.dma("gpsimd", T["x2_tm"][sl, :], x2s)
                ln_stats(P, x2s, mu, rstd, xc2, lsq, lacc)
                P.ts("vector", m32, xc2, rstd, ALU.mult)
                P.tt("vector", m32, m32, g2, ALU.mult)
                P.tt("vector", m32, m32, b2, ALU.add)
                P.cp("scalar", mb, m32)
                tm_tile_to_fm(P, "p2d", mb, eye, T["m_fm"], ti, ptr, stg,
                              also_f32=True, psum32=ptr32, eye32=eye32,
                              src32=m32, stage32=stg32, dst32=T["m_fm32"])
                for dc in range(DC):
                    P.mm(ps_r[:, 0:8], stg32[:, dc * 128:(dc + 1) * 128],
                         wg[:, dc, :], start=(dc == 0), stop=(dc == DC - 1))
                P.red("vector", mx, ps_r[:, 0:8], ALU.max)
                P.ts("vector", mx, mx, -1.0, ALU.mult)
                P.act(E8, ps_r[:, 0:8], AF.Exp, bias=mx)
                P.red("vector", m1, E8, ALU.max)
                P.ts("vector", iseq, E8, m1, ALU.is_equal)
                P.ts("vector", iseq, iseq, -1.0, ALU.mult, 1.0, ALU.add)
                P.tt("vector", e8b, E8, iseq, ALU.mult)
                P.red("vector", m2, e8b, ALU.max)
                P.ts("vector", e8b, E8, m2, ALU.is_ge)
                P.tt("vector", cw, E8, e8b, ALU.mult)
                P.tt("vector", m1, m1, m2, ALU.add)
                P.recip(m1, m1)
                P.ts("vector", cw, cw, m1, ALU.mult)
                P.dma("gpsimd", T["cw_tm"][sl, :], cw)



@phase("p3_moe0",
       reads=["m_fm", "cw_tm", "x2_tm", "w1m_bf", "L0_ffn_b1",
              "w2m_bf", "L0_ffn_b2", "c_ones_f32"],
       writes=["ffn0_tm"])
def p3_moe0(P, T):
    P.ones_f32 = T["c_ones_f32"]
    NE, F_ = 8, 4096
    FC = F_ // 128
    with P.phase("p3"):
        w1 = P.sbuf("p3_w1", [128, DC, F_], BF16)
        w2 = P.sbuf("p3_w2", [128, FC, D], BF16)
        b1 = P.sbuf("p3_b1", [128, FC], F32)

        b2r = P.sbuf("p3_b2r", [1, D], F32)
        b2bc = P.sbuf("p3_b2bc", [128, D], F32)
        ps_b = P.psum("p3_psb", [128, 512], F32)
        mt = Rot([P.sbuf(f"p3_mt{i}", [128, DC, 512], BF16) for i in range(2)])
        h_s = P.sbuf("p3_h", [128, FC, 512], BF16)
        cw_s = Rot([P.sbuf(f"p3_cw{i}", [128, 4, NE], F32) for i in range(2)])
        acc = Rot([P.sbuf(f"p3_acc{i}", [128, D], F32) for i in range(2)])
        ytmp = Rot([P.sbuf(f"p3_yt{i}", [128, 512], F32) for i in range(1)])
        ph = Rot([P.psum(f"p3_ph{i}", [128, 512], F32) for i in range(3)])
        py = Rot([P.psum(f"p3_py{i}", [128, 512], F32) for i in range(4)])

        def load_w1(e):
            for dc in range(DC):
                P.dma("gpsimd" if dc % 2 else "sync", w1[:, dc, :],
                      T["w1m_bf"][e, dc, :, :])

        def load_w2b(e):
            for g in range(4):
                P.dma("gpsimd" if g % 2 else "sync",
                      w2[:, 8 * g:8 * g + 8, :]
                      .rearrange("p c d -> p c d"),
                      T["w2m_bf"][e, 8 * g:8 * g + 8, :, :]
                      .rearrange("c p d -> p c d"))
            for fc in range(FC):
                P.dma("gpsimd", b1[:, fc:fc + 1],
                      T["L0_ffn_b1"][e, fc * 128:(fc + 1) * 128]
                      .rearrange("(p a) -> p a", a=1))
            P.dma("gpsimd", b2r[:], T["L0_ffn_b2"][e, :]
                  .rearrange("(a d) -> a d", a=1))
            for j in range(2):
                P.mm(ps_b[:, 0:512], _ap(P.ones_f32)[0:1, :],
                     _ap(b2r)[0:1, j * 512:(j + 1) * 512])
                P.cp("vector", b2bc[:, j * 512:(j + 1) * 512], ps_b[:, 0:512])

        load_w1(0)
        load_w2b(0)
        for e in range(NE):
            for tmac in range(TLOC // 512):
                t0 = tmac * 512
                m_, c_ = mt.nxt(), cw_s.nxt()
                P.dma("sync", m_[:],
                      T["m_fm"][:, :, t0:t0 + 512].rearrange("c p t -> p c t"))
                P.dma("sync", c_[:],
                      T["cw_tm"][t0:t0 + 512, :]
                      .rearrange("(s p) e -> p s e", p=128))
                for fc in range(FC):
                    p_ = ph.nxt()
                    for dc in range(DC):
                        P.mm(p_, w1[:, dc, fc * 128:(fc + 1) * 128],
                             m_[:, dc, :], start=(dc == 0), stop=(dc == DC - 1))
                    P.act(h_s[:, fc, :], p_, AF.Gelu_apprx_tanh,
                          bias=_ap(b1)[:, fc:fc + 1])
                if tmac == TLOC // 512 - 1 and e + 1 < NE:
                    load_w1(e + 1)
                for st in range(4):
                    ssl = slice(st * 128, (st + 1) * 128)
                    a_ = acc.nxt()
                    src_t = T["x2_tm"] if e == 0 else T["ffn0_tm"]
                    P.dma("sync", a_[:],
                          src_t[t0 + st * 128:t0 + (st + 1) * 128, :])
                    for half in range(2):
                        hsl = slice(half * 512, (half + 1) * 512)
                        p_ = py.nxt()
                        yt = ytmp.nxt()
                        for fc in range(FC):
                            P.mm(p_, h_s[:, fc, ssl], w2[:, fc, hsl],
                                 start=(fc == 0), stop=(fc == FC - 1))
                        P.tt("vector", yt, p_, _ap(b2bc)[:, hsl], ALU.add)
                        P.stt("vector", a_[:, hsl], yt,
                              _ap(c_)[:, st, e:e + 1], a_[:, hsl],
                              ALU.mult, ALU.add)
                    P.dma("gpsimd",
                          T["ffn0_tm"][t0 + st * 128:t0 + (st + 1) * 128, :],
                          a_)
            if e + 1 < NE:
                load_w2b(e + 1)



@phase("p4_comb0",
       reads=["L0_attn_hidden_states", "att0_tm", "ffn0_tm", "c_eye_bf"],
       writes=["x1_tm", "xn1_tm", "xn1_fm"])
def p4_comb0(P, T):
    x = T["L0_attn_hidden_states"].rearrange("b s d -> (b s) d")
    eye_bf = T["c_eye_bf"]
    with P.phase("p4"):
        R2 = lambda nm, sh, dt: Rot([P.sbuf(f"{nm}{i}", sh, dt) for i in range(2)])
        xt_r = R2("p4_xt", [128, D], F32)
        at_r = R2("p4_at", [128, D], F32)
        ft_r = R2("p4_ft", [128, D], F32)
        xc_r = R2("p4_xc", [128, D], F32)
        xnb_r = R2("p4_xnb", [128, D], BF16)
        mu_r = R2("p4_mu", [128, 1], F32)
        rstd_r = R2("p4_rstd", [128, 1], F32)
        lsq = P.sbuf("p4_sq", [128, D], F32)
        lacc_r = R2("p4_acc", [128, 1], F32)
        ptr_r = Rot([P.psum(f"p4_ptr{i}", [128, D], BF16) for i in range(2)])
        stg_r = R2("p4_stg", [128, D], BF16)
        for ti in range(TLOC // 128):
            sl = slice(ti * 128, (ti + 1) * 128)
            xt, at, ft, xc = xt_r.nxt(), at_r.nxt(), ft_r.nxt(), xc_r.nxt()
            xnb, mu = xnb_r.nxt(), mu_r.nxt()
            rstd, lacc = rstd_r.nxt(), lacc_r.nxt()
            ptr, stg = ptr_r.nxt(), stg_r.nxt()
            P.dma("sync", xt[:], x[sl, :])
            P.dma("sync", at[:], T["att0_tm"][sl, :])
            P.dma("sync", ft[:], T["ffn0_tm"][sl, :])
            P.tt("vector", xt, xt, at, ALU.add)
            P.tt("vector", xt, xt, ft, ALU.add)
            P.dma("gpsimd", T["x1_tm"][sl, :], xt)
            ln_stats(P, xt, mu, rstd, xc, lsq, lacc)
            P.ts("vector", xc[:], xc[:], rstd[:], ALU.mult)
            P.dma("gpsimd", T["xn1_tm"][sl, :], xc)
            P.cp("scalar", xnb, xc)
            tm_tile_to_fm(P, "p4", xnb, eye_bf, T["xn1_fm"], ti, ptr, stg)



@phase("p5_att1",
       reads=["xn1_fm", "L1_attn_w_qkv", "L1_attn_b_qkv", "L1_attn_w_out",
              "L1_attn_b_out", "c_eye_bf", "c_ones_f32", "c_ones_bf"],
       writes=["att1_tm"],
       privs=["qkv1_fm", "v1_tm", "ctx1_fm"])
def p5_att1(P, T):
    eye = T["c_eye_bf"]
    P.ones_f32 = T["c_ones_f32"]
    P.ones_bf_col = _ap(T["c_ones_bf"])[:, 0:1]
    P.ones_f32_col64 = _ap(T["c_ones_f32"])[0:64, 0:1]
    P.ones_f32_row64 = _ap(T["c_ones_f32"])[0:1, 0:64]
    NH = 16

    P.scope = "p5_b"
    with P.phase("p5b"):
        wqkvT = P.sbuf("p5_wqkvT", [128, DC, 3 * D], BF16)
        load_wT(P, "p5qkv", T["L1_attn_w_qkv"], wqkvT, eye, n_out=3 * D)
        bq = P.sbuf("p5_bq", [128, 24], F32)
        for oc in range(24):
            P.dma("sync", bq[:, oc:oc + 1],
                  T["L1_attn_b_qkv"][oc * 128:(oc + 1) * 128]
                  .rearrange("(p a) -> p a", a=1))
        xt = P.sbuf("p5_xt", [128, DC, 512], BF16)
        pos = Rot([P.psum(f"p5_po{i}", [128, 512], F32) for i in range(4)])
        obs = Rot([P.sbuf(f"p5_ob{i}", [128, 512], BF16) for i in range(4)])
        for b in range(BLOC):
            for tmac in range(S // 512):
                t0 = b * S + tmac * 512
                P.dma("sync", xt[:],
                      T["xn1_fm"][:, :, t0:t0 + 512]
                      .rearrange("c p t -> p c t"))
                for oc in range(24):
                    po, ob = pos.nxt(), obs.nxt()
                    for dc in range(DC):
                        P.mm(po, wqkvT[:, dc, oc * 128:(oc + 1) * 128],
                             xt[:, dc, :], start=(dc == 0), stop=(dc == DC - 1))
                    P.ts("vector", ob, po, _ap(bq)[:, oc:oc + 1], ALU.add)
                    P.dma("gpsimd",
                          T["qkv1_fm"][b, oc, :, tmac * 512:(tmac + 1) * 512],
                          ob)
                for st in range(4):
                    tc = tmac * 4 + st
                    for half in range(2):
                        po, ob = pos.nxt(), obs.nxt()
                        rhsv = _ap(wqkvT)[:, :, :].rearrange(
                            "p c (h x) -> p c h x", h=NH)[
                            :, :, half * 8:(half + 1) * 8, 128:192]
                        for dc in range(DC):
                            P.mm(po, xt[:, dc, st * 128:(st + 1) * 128],
                                 rhsv[:, dc], start=(dc == 0),
                                 stop=(dc == DC - 1))
                        P.cp("vector", ob, po)
                        P.dma("gpsimd",
                              T["v1_tm"][b, tc, :, half * 512:(half + 1) * 512],
                              ob)

    P.scope = "p5_c"
    with P.phase("p5c"):
        v_r = Rot([P.sbuf(f"p5_vs{i}", [128, S // 128, NH * 65], BF16)
                   for i in range(2)])
        qh_r = Rot([P.sbuf(f"p5_qh{i}", [64, S], BF16) for i in range(2)])
        kh_r = Rot([P.sbuf(f"p5_kh{i}", [64, S], BF16) for i in range(2)])
        sq_s = P.sbuf("p5_sqs", [64, S], F32)
        nmax = P.sbuf("p5_nmax", [1, 2], F32)
        tmp1 = P.sbuf("p5_tmp1", [1, 1], F32)
        negc = Rot([P.sbuf(f"p5_negc{i}", [128, 1], F32) for i in range(2)])
        ps_n = P.psum("p5_psn", [128, 512], F32)
        ps_sT = Rot([P.psum(f"p5_psT{i}", [128, 1024], F32) for i in range(2)])
        E_r = Rot([P.sbuf(f"p5_E{i}", [128, 1024], BF16) for i in range(2)])
        ps_ctx = Rot([P.psum(f"p5_psctx{i}", [128, 512], F32) for i in range(2)])
        ps_sig = P.psum("p5_pssig", [128, 512], F32)
        sig_r = Rot([P.sbuf(f"p5_sig{i}", [1, 512], F32) for i in range(2)])
        sg_r = Rot([P.sbuf(f"p5_sg{i}", [64, 512], F32) for i in range(2)])
        ctx_r = Rot([P.sbuf(f"p5_ctxb{i}", [64, 512], BF16) for i in range(2)])
        bq2 = P.sbuf("p5_bq2", [128, 24], F32)
        for oc in range(24):
            P.dma("sync", bq2[:, oc:oc + 1],
                  T["L1_attn_b_qkv"][oc * 128:(oc + 1) * 128]
                  .rearrange("(p a) -> p a", a=1))
        hb_bufs = (sq_s, ps_n, nmax, tmp1)
        at_bufs = (ps_sT, E_r, ps_ctx, ps_sig, sig_r, sg_r, ctx_r)
        for vb in v_r.items:
            P.memset("vector",
                 _ap(vb).rearrange("p c (h w) -> p c h w", w=65)[:, :, :, 64:65],
                 1.0)
        for b in range(BLOC):
            v_s = v_r.nxt()
            for tc in range(S // 128):
                P.dma("gpsimd",
                      _ap(v_s)[:, tc, :].rearrange("p (h w) -> p h w", w=65)
                      [:, :, 0:64],
                      T["v1_tm"][b, tc, :, :].rearrange("p (h w) -> p h w", w=64))
            for h in range(NH):
                fq, fk, fv = 192 * h, 192 * h + 64, 192 * h + 128
                qh_s, kh_s = qh_r.nxt(), kh_r.nxt()
                P.dma("sync", qh_s[:],
                      T["qkv1_fm"][b, fq // 128, fq % 128:fq % 128 + 64, :])
                P.dma("sync", kh_s[:],
                      T["qkv1_fm"][b, fk // 128, fk % 128:fk % 128 + 64, :])
                bv_col = _ap(bq2)[fv % 128:fv % 128 + 64,
                                  fv // 128:fv // 128 + 1]
                nc_ = negc.nxt()
                head_bound(P, hb_bufs, _ap(qh_s), _ap(kh_s), nc_, 0.125)
                softmax_TQ(P, at_bufs, _ap(kh_s), _ap(qh_s), v_s, h, nc_,
                           T["ctx1_fm"][b, h // 2,
                                        64 * (h % 2):64 * (h % 2) + 64, :],
                           0.125, bv=bv_col)

    P.scope = "p5_d"
    with P.phase("p5d"):
        woT = P.sbuf("p5_woT", [128, DC, D], BF16)
        load_wT(P, "p5wo", T["L1_attn_w_out"], woT, eye)
        bor = row_of(P, "p5bo", T["L1_attn_b_out"], D)
        bo = bcast_row(P, "p5bo", bor, D, P.ones_f32)
        ots = Rot([P.sbuf(f"p5_ot{i}", [128, DC, 128], BF16) for i in range(2)])
        pws = Rot([P.psum(f"p5_pw{i}", [128, 512], F32) for i in range(4)])
        oss = Rot([P.sbuf(f"p5_os{i}", [128, D], F32) for i in range(2)])
        for b in range(BLOC):
            for tt_ in range(S // 128):
                ot = ots.nxt()
                os_ = oss.nxt()
                P.dma("sync", ot[:],
                      T["ctx1_fm"][b, :, :, tt_ * 128:(tt_ + 1) * 128]
                      .rearrange("c p t -> p c t"))
                for half in range(2):
                    pw = pws.nxt()
                    hsl = slice(half * 512, (half + 1) * 512)
                    for ec in range(DC):
                        P.mm(pw, ot[:, ec, :], woT[:, ec, hsl],
                             start=(ec == 0), stop=(ec == DC - 1))
                    P.tt("vector", os_[:, hsl], pw, _ap(bo)[:, hsl], ALU.add)
                P.dma("gpsimd",
                      T["att1_tm"][b * S + tt_ * 128:b * S + (tt_ + 1) * 128, :],
                      os_)


def rms_tile(P, xt, gbc, out32, ssq, bufs):
    sq, r = bufs
    P.act(sq, xt, AF.Square, accum=ssq)
    P.act(ssq, ssq, AF.Ln, bias=1e-30)
    P.act(r, ssq, AF.Exp, scale=-0.5)
    P.ts("vector", out32, xt, r, ALU.mult)
    P.tt("vector", out32, out32, gbc, ALU.mult)



@phase("p6_sm_a",
       reads=["xn1_tm", "L1_ffn_gamma_x", "L1_ffn_gamma_slot",
              "L1_ffn_slot_embeds", "L1_ffn_w1", "L1_ffn_w2",
              "c_eye_bf", "c_ones_f32", "c_ones_bf"],
       writes=["xnn_fm", "se_fm", "proc_tm"],
       privs=["xnn_tmb", "w1l1_bf", "w2l1_bf"])
def p6_sm_a(P, T):
    eye = T["c_eye_bf"]
    P.ones_f32 = T["c_ones_f32"]
    NES = 4096
    FC = 32

    P.scope = "p6_prep"
    with P.phase("p6prep"):
        wt32 = P.sbuf("p6_wt32", [128, 4096], F32)
        wtb = P.sbuf("p6_wtb", [128, 4096], BF16)
        for dc in range(DC):
            P.dma("sync", wt32[:], T["L1_ffn_w1"][dc * 128:(dc + 1) * 128, :])
            P.cp("vector", wtb, wt32)
            P.dma("gpsimd",
                  T["w1l1_bf"][:, :, dc * 128:(dc + 1) * 128]
                  .rearrange("c p f -> p c f"),
                  wtb[:].rearrange("p (c f) -> p c f", c=32))
        for fc in range(FC):
            P.dma("sync", _ap(wt32)[:, 0:D],
                  T["L1_ffn_w2"][fc * 128:(fc + 1) * 128, :])
            P.cp("vector", _ap(wtb)[:, 0:D], _ap(wt32)[:, 0:D])
            P.dma("gpsimd", T["w2l1_bf"][fc, :, :], _ap(wtb)[:, 0:D])
        gxr = row_of(P, "p6gx", T["L1_ffn_gamma_x"], D)
        gsr = row_of(P, "p6gs", T["L1_ffn_gamma_slot"], D)
        gx = bcast_row(P, "p6gx", gxr, D, P.ones_f32, scale=32.0)
        gs = bcast_row(P, "p6gs", gsr, D, P.ones_f32, scale=32.0)
        R2 = lambda nm, sh, dt: Rot([P.sbuf(f"{nm}{i}", sh, dt) for i in range(2)])
        xt_r = R2("p6_xt", [128, D], F32)
        o32_r = R2("p6_o32", [128, D], F32)
        ob_r = R2("p6_ob", [128, D], BF16)
        sq = P.sbuf("p6_sq", [128, D], F32)
        ssq_r = R2("p6_ssq", [128, 1], F32)
        rr_r = R2("p6_rr", [128, 1], F32)
        ptr_r = Rot([P.psum(f"p6_ptr{i}", [128, D], BF16) for i in range(2)])
        stg_r = R2("p6_stg", [128, D], BF16)
        for ti in range(TLOC // 128):
            b, tt_ = ti // (S // 128), ti % (S // 128)
            xt, o32, ob = xt_r.nxt(), o32_r.nxt(), ob_r.nxt()
            ssq, rr, ptr, stg = ssq_r.nxt(), rr_r.nxt(), ptr_r.nxt(), stg_r.nxt()
            P.dma("sync", xt[:], T["xn1_tm"][ti * 128:(ti + 1) * 128, :])
            rms_tile(P, xt, gx, o32, ssq, (sq, rr))
            P.cp("scalar", ob, o32)
            P.dma("gpsimd", T["xnn_tmb"][b, tt_, :, :], ob)
            tm_tile_to_fm(P, "p6", ob, eye, T["xnn_fm"][b], tt_, ptr, stg)
        sed = T["L1_ffn_slot_embeds"].rearrange("e t d -> (e t) d")
        for si in range(NES // 128):
            xt, o32, ob = xt_r.nxt(), o32_r.nxt(), ob_r.nxt()
            ssq, rr, ptr, stg = ssq_r.nxt(), rr_r.nxt(), ptr_r.nxt(), stg_r.nxt()
            P.dma("sync", xt[:], sed[si * 128:(si + 1) * 128, :])
            rms_tile(P, xt, gs, o32, ssq, (sq, rr))
            P.cp("scalar", ob, o32)
            tm_tile_to_fm(P, "p6se", ob, eye, T["se_fm"], si, ptr, stg)

    P.scope = "p6_a"
    with P.phase("p6a"):
        xnn_fm_s = P.sbuf("p6_xf", [128, DC, S], BF16)
        xnn_tm_s = P.sbuf("p6_xt2", [128, S // 128, D], BF16)
        se_s = P.sbuf("p6_ses", [128, DC, 512], BF16)
        dT_s = P.sbuf("p6_dT", [128, S // 128, 512], BF16)
        disp_s = P.sbuf("p6_disp", [128, S], BF16)
        slots_s = P.sbuf("p6_slots", [128, DC, 512], BF16)
        h_s = P.sbuf("p6_h", [128, FC, 512], BF16)
        proc_s = P.sbuf("p6_proc", [128, 4, D], BF16)
        w1t = Rot([P.sbuf(f"p6_w1t{i}", [128, DC, 128], BF16) for i in range(2)])
        w2t = Rot([P.sbuf(f"p6_w2t{i}", [128, 512], BF16) for i in range(2)])
        mx = P.sbuf("p6_mx", [128, 1], F32)
        tmx = P.sbuf("p6_tmx", [128, 1], F32)
        as_ = P.sbuf("p6_as", [128, 4], F32)
        rsum = P.sbuf("p6_rsum", [128, 1], F32)
        pb = [P.psum(f"p6_pb{i}", [128, 512], F32) for i in range(6)]
        ptr_bf = P.psum("p6_ptrb", [128, 1024], BF16)
        for b in range(BLOC):
            P.dma("sync", xnn_fm_s[:],
                  T["xnn_fm"][b].rearrange("c p t -> p c t"))
            P.dma("sync", xnn_tm_s[:],
                  T["xnn_tmb"][b].rearrange("c p d -> p c d"))
            for em in range(8):
                e0 = em * 512
                P.dma("sync", se_s[:],
                      T["se_fm"][:, :, e0:e0 + 512].rearrange("c p t -> p c t"))
                for sc in range(4):
                    scl = slice(sc * 128, (sc + 1) * 128)
                    for tq in range(4):
                        for dc in range(DC):
                            P.mm(pb[tq], se_s[:, dc, scl],
                                 xnn_fm_s[:, dc, tq * 512:(tq + 1) * 512],
                                 start=(dc == 0), stop=(dc == DC - 1))
                    for tq in range(4):
                        if tq == 0:
                            P.red("vector", mx, pb[0], ALU.max)
                        else:
                            P.red("vector", tmx, pb[tq], ALU.max)
                            P.tt("vector", mx, mx, tmx, ALU.max)
                    P.ts("vector", mx, mx, -1.0, ALU.mult)
                    for tq in range(4):
                        P.act(disp_s[:, tq * 512:(tq + 1) * 512], pb[tq],
                              AF.Exp, bias=mx, accum=_ap(as_)[:, tq:tq + 1])
                    P.red("vector", rsum, as_, ALU.add)
                    P.recip(rsum, rsum)
                    P.ts("vector", disp_s, disp_s, rsum, ALU.mult)
                    for g in range(2):
                        for j in range(8):
                            tc = g * 8 + j
                            P.transp(ptr_bf[:, j * 128:(j + 1) * 128],
                                     disp_s[:, tc * 128:(tc + 1) * 128], eye,
                                     register=(j == 7), war=(j == 0))
                        P.cp("vector",
                             dT_s[:, g * 8:(g + 1) * 8, scl],
                             ptr_bf[:].rearrange("p (c t) -> p c t", c=8))
                for dc in range(DC):
                    p_ = pb[4]
                    for tc in range(S // 128):
                        P.mm(p_, xnn_tm_s[:, tc, dc * 128:(dc + 1) * 128],
                             dT_s[:, tc, :], start=(tc == 0),
                             stop=(tc == S // 128 - 1))
                    P.cp("vector", slots_s[:, dc, :], p_)
                for fc in range(FC):
                    w1c = w1t.nxt()
                    P.dma("sync", w1c[:].rearrange("p c f -> p (c f)"),
                          T["w1l1_bf"][fc, :, :])
                    p_ = pb[5]
                    for dc in range(DC):
                        P.mm(p_, w1c[:, dc, :], slots_s[:, dc, :],
                             start=(dc == 0), stop=(dc == DC - 1))
                    P.act(h_s[:, fc, :], p_, AF.Gelu)
                for half in range(2):
                    hsl = slice(half * 512, (half + 1) * 512)
                    for fc in range(FC):
                        w2c = w2t.nxt()
                        P.dma("sync", w2c[:], T["w2l1_bf"][fc, :, hsl])
                        for st in range(4):
                            P.mm(pb[st], h_s[:, fc, st * 128:(st + 1) * 128],
                                 w2c[:], start=(fc == 0), stop=(fc == FC - 1))
                    for st in range(4):
                        P.cp("vector", proc_s[:, st, hsl], pb[st])
                for st in range(4):
                    P.dma("gpsimd",
                          T["proc_tm"][b, e0 + st * 128:e0 + (st + 1) * 128, :],
                          proc_s[:, st, :])



@phase("p7_sm_b",
       reads=["xnn_fm", "se_fm", "proc_tm", "c_eye_bf", "c_ones_f32"],
       writes=["ffn1_tm"])
def p7_sm_b(P, T):
    eye = T["c_eye_bf"]
    NES = 4096
    with P.phase("p7"):
        se_fm_s = P.sbuf("p7_se", [128, DC, NES], BF16)
        xnn_fm_s = P.sbuf("p7_xf", [128, DC, S], BF16)
        combT_s = P.sbuf("p7_cT", [128, NES // 128, 512], BF16)
        comb_s = P.sbuf("p7_comb", [128, NES], BF16)
        lg_s = P.sbuf("p7_lg", [128, NES], F32)
        out_s = P.sbuf("p7_out", [128, 4, D], F32)
        proc_t = Rot([P.sbuf(f"p7_pt{i}", [128, 512], BF16) for i in range(2)])
        mx = P.sbuf("p7_mx", [128, 1], F32)
        rsum = P.sbuf("p7_rsum", [128, 1], F32)
        sq8 = P.sbuf("p7_sq8", [128, 8], F32)
        pb = [P.psum(f"p7_pb{i}", [128, 512], F32) for i in range(4)]
        ptr_bf = P.psum("p7_ptrb", [128, 1024], BF16)
        P.dma("sync", se_fm_s[:], T["se_fm"].rearrange("c p t -> p c t"))
        for b in range(BLOC):
            P.dma("sync", xnn_fm_s[:],
                  T["xnn_fm"][b].rearrange("c p t -> p c t"))
            for sm in range(4):
                s0 = sm * 512
                for tq in range(4):
                    tsl = slice(s0 + tq * 128, s0 + (tq + 1) * 128)
                    for eb in range(8):
                        p_ = pb[eb % 4]
                        for dc in range(DC):
                            P.mm(p_, xnn_fm_s[:, dc, tsl],
                                 se_fm_s[:, dc, eb * 512:(eb + 1) * 512],
                                 start=(dc == 0), stop=(dc == DC - 1))
                        P.cp("vector", lg_s[:, eb * 512:(eb + 1) * 512], p_)
                    P.red("vector", mx, lg_s, ALU.max)
                    P.ts("vector", mx, mx, -1.0, ALU.mult)
                    P.act(comb_s, lg_s, AF.Exp, bias=mx, accum=rsum)
                    P.recip(rsum, rsum)
                    P.ts("vector", comb_s, comb_s, rsum, ALU.mult)
                    for g in range(4):
                        for j in range(8):
                            etc = g * 8 + j
                            P.transp(ptr_bf[:, j * 128:(j + 1) * 128],
                                     comb_s[:, etc * 128:(etc + 1) * 128], eye,
                                     register=(j == 7), war=(j == 0))
                        P.cp("vector",
                             combT_s[:, g * 8:(g + 1) * 8,
                                     tq * 128:(tq + 1) * 128],
                             ptr_bf[:].rearrange("p (c t) -> p c t", c=8))
                for half in range(2):
                    hsl = slice(half * 512, (half + 1) * 512)
                    for etc in range(NES // 128):
                        pc = proc_t.nxt()
                        P.dma("sync", pc[:],
                              T["proc_tm"][b, etc * 128:(etc + 1) * 128, hsl])
                        for st in range(4):
                            P.mm(pb[st], combT_s[:, etc, st * 128:(st + 1) * 128],
                                 pc[:], start=(etc == 0),
                                 stop=(etc == NES // 128 - 1))
                    for st in range(4):
                        P.cp("vector", out_s[:, st, hsl], pb[st])
                for st in range(4):
                    P.dma("gpsimd",
                          T["ffn1_tm"][b * S + s0 + st * 128:
                                       b * S + s0 + (st + 1) * 128, :],
                          out_s[:, st, :])



@phase("p8_out",
       reads=["x1_tm", "att1_tm", "ffn1_tm"],
       writes=["out"])
def p8_out(P, T):
    o = T["out"].rearrange("b s d -> (b s) d")
    with P.phase("p8"):
        xts = Rot([P.sbuf(f"p8_xt{i}", [128, D], F32) for i in range(2)])
        ats = Rot([P.sbuf(f"p8_at{i}", [128, D], F32) for i in range(2)])
        fts = Rot([P.sbuf(f"p8_ft{i}", [128, D], F32) for i in range(2)])
        for ti in range(TLOC // 128):
            sl = slice(ti * 128, (ti + 1) * 128)
            xt, at, ft = xts.nxt(), ats.nxt(), fts.nxt()
            P.dma("sync", xt[:], T["x1_tm"][sl, :])
            P.dma("sync", at[:], T["att1_tm"][sl, :])
            P.dma("sync", ft[:], T["ffn1_tm"][sl, :])
            P.tt("vector", xt, xt, at, ALU.add)
            P.tt("vector", xt, xt, ft, ALU.add)
            P.dma("gpsimd", o[sl, :], xt)



def make_consts():
    return {
        "c_eye_f32": np.eye(128, dtype=np.float32),
        "c_eye_bf": np.eye(128, dtype=ml_bf16()),
        "c_mask_ut": np.triu(np.ones((128, 128), dtype=np.float32)),
        "c_ones_f32": np.ones((128, 128), dtype=np.float32),
        "c_ones_bf": np.ones((128, 128), dtype=ml_bf16()),
    }


def ml_bf16():
    import ml_dtypes
    return ml_dtypes.bfloat16


def _declare(P, names, kind):
    out = {}
    for n in names:
        sh, dt = spec_of(n)
        out[n] = P.dram(n, sh, dt, kind=kind)
    return out


def run_unfused(inputs, upto=None, only=None):
    consts = make_consts()
    x = np.ascontiguousarray(inputs["L0_attn_hidden_states"])
    pools = [dict() for _ in range(NCORES)]
    phases = PHASES[:upto] if upto else PHASES
    if only:
        phases = [p for p in phases if p[0] in only]
    for (name, fn, reads, writes, privs) in phases:
        P = Prog(name)
        T = {}
        T.update(_declare(P, reads, "ExternalInput"))
        T.update(_declare(P, writes, "ExternalOutput"))
        T.update(_declare(P, privs, "Internal"))
        for n in reads:
            if n in CONST_SPECS:
                sh, dt = CONST_SPECS[n]
                sb = P.sbuf("s_" + n, sh, dt)
                P.dma("sync", sb[:], T[n])
                T[n] = sb
        fn(P, T)
        P.barrier()
        nc = P.finalize()
        in_maps = []
        for c in range(NCORES):
            m = {}
            for n in reads:
                if n == "L0_attn_hidden_states":
                    m[n] = x[BLOC * c:BLOC * (c + 1)]
                elif n in INPUT_SPECS:
                    m[n] = inputs[n]
                elif n in CONST_SPECS:
                    m[n] = consts[n]
                else:
                    m[n] = pools[c][n]
            in_maps.append(m)
        res = run_bass_kernel_spmd(nc, in_maps, core_ids=list(range(NCORES)))
        for c in range(NCORES):
            pools[c].update(res.results[c])
    return pools


def run_fused(inputs):
    consts = make_consts()
    x = np.ascontiguousarray(inputs["L0_attn_hidden_states"])
    P = Prog("mega")
    T = {}
    ext_in, scratch = [], set()
    for (name, fn, reads, writes, privs) in PHASES:
        for n in reads:
            if n in INPUT_SPECS or n in CONST_SPECS:
                if n not in T:
                    T[n] = _declare(P, [n], "ExternalInput")[n]
                    ext_in.append(n)
            else:
                scratch.add(n)
        for n in writes:
            if n != "out":
                scratch.add(n)
        for n in privs:
            scratch.add(n)
    T.update(_declare(P, sorted(scratch), "Internal"))
    T.update(_declare(P, ["out"], "ExternalOutput"))
    for n in list(T):
        if n in CONST_SPECS:
            sh, dt = CONST_SPECS[n]
            sb = P.sbuf("s_" + n, sh, dt)
            P.dma("sync", sb[:], T[n])
            T[n] = sb
    for (name, fn, reads, writes, privs) in PHASES:
        P.scope = name
        fn(P, T)
    P.scope = None
    P.barrier()
    nc = P.finalize()
    in_maps = []
    for c in range(NCORES):
        m = {}
        for n in ext_in:
            if n == "L0_attn_hidden_states":
                m[n] = x[BLOC * c:BLOC * (c + 1)]
            elif n in CONST_SPECS:
                m[n] = consts[n]
            else:
                m[n] = inputs[n]
        in_maps.append(m)
    res = run_bass_kernel_spmd(nc, in_maps, core_ids=list(range(NCORES)))
    return np.concatenate([res.results[c]["out"] for c in range(NCORES)],
                          axis=0)


FUSED = True


def kernel(**inputs):
    if FUSED:
        return run_fused(inputs)
    pools = run_unfused(inputs)
    return np.concatenate([pools[c]["out"] for c in range(NCORES)], axis=0)


# version 09
# speedup vs baseline: 1.1784x; 1.1784x over previous
import contextlib
import numpy as np
import concourse.bass as bass
import concourse.mybir as mybir
from concourse.bass_utils import run_bass_kernel_spmd

F32 = mybir.dt.float32
BF16 = mybir.dt.bfloat16
AF = mybir.ActivationFunctionType
ALU = mybir.AluOpType
AX = mybir.AxisListType

NCORES = 8
BLOC = 2
S = 2048
TLOC = BLOC * S
D = 1024
DC = D // 128

ENGINES = ("tensor", "vector", "scalar", "gpsimd", "sync")
DMA_QUEUES = ("gpsimd", "sync")
NSEM = 4


def _ap_name(x):
    if isinstance(x, str):
        return x
    t = getattr(x, "tensor", None)
    if t is not None:
        return t.name
    n = getattr(x, "name", None)
    if n is not None:
        return n
    raise ValueError(f"cannot extract tensor from {x!r}")



def _ap(x):
    if isinstance(x, (float, int)) or x is None:
        return x
    if getattr(x, "tensor", None) is not None:
        return x
    return x[:]

class Prog:

    def __init__(self, name="prog"):
        self.name = name
        self.nc = bass.Bass("TRN2", target_bir_lowering=False)
        self.es = contextlib.ExitStack()
        self.phase_es = None
        self.sems = {}
        self.cnt = {e: 0 for e in ENGINES}
        self.opbuf = {e: [] for e in ENGINES}
        self.reg = {}
        self.aps = {}
        self._cvecs = {}
        self.scope = None
        for e in ENGINES:
            if e in DMA_QUEUES:
                for j in range(NSEM):
                    self.sems[(e, j)] = self.es.enter_context(
                        self.nc.semaphore(f"sem_{e}{j}"))
            else:
                self.sems[e] = self.es.enter_context(
                    self.nc.semaphore(f"sem_{e}"))
        self.known = {e: {} for e in ENGINES}

    def dram(self, name, shape, dtype, kind):
        ap = self.nc.dram_tensor(name, list(shape), dtype, kind=kind).ap()
        self.aps[name] = ap
        return ap

    def sbuf(self, name, shape, dtype):
        if self.phase_es is not None:
            return self.phase_es.enter_context(
                self.nc.sbuf_tensor(name, list(shape), dtype))
        return self.nc.alloc_sbuf_tensor(name, list(shape), dtype)

    def psum(self, name, shape, dtype=F32):
        if self.phase_es is not None:
            return self.phase_es.enter_context(
                self.nc.psum_tensor(name, list(shape), dtype))
        return self.nc.alloc_psum_tensor(name, list(shape), dtype)

    def cvec(self, val, dtype=F32, rows=128):
        key = (float(val), dtype, rows)
        if key not in self._cvecs:
            t = self.nc.alloc_sbuf_tensor(f"cv{len(self._cvecs)}", [rows, 1], dtype,
                                          side="right")
            self.memset("vector", t[:], float(val))
            self._cvecs[key] = t[:]
        return self._cvecs[key]

    @contextlib.contextmanager
    def phase(self, tag):
        prev = self.phase_es
        self.phase_es = contextlib.ExitStack()
        try:
            yield
        finally:
            self.barrier()
            self.phase_es.close()
            self.phase_es = prev

    def _covers(self, eng, idx):
        if eng in DMA_QUEUES:
            out = []
            for j in range(NSEM):
                cnt = (idx - 1 - j) // NSEM + 1 if idx - 1 >= j else 0
                if cnt > 0:
                    out.append(((eng, j), 16 * cnt))
            return out
        return [(eng, idx)]

    def emit(self, eng, thunk, reads=(), writes=(), register=True, war=True):
        waits = {}

        def need(src_eng, src_idx):
            if src_idx <= 0:
                return
            if src_eng == eng == "tensor":
                return
            for key, v in self._covers(src_eng, src_idx):
                if v <= self.known[eng].get(key, 0):
                    continue
                waits[key] = max(waits.get(key, 0), v)

        rnames = [_ap_name(r) for r in reads]
        wnames = [_ap_name(w) for w in writes]
        for n in rnames:
            e = self.reg.get(n)
            if e:
                for we, wi in e["w"].items():
                    need(we, wi)
        for n in wnames:
            e = self.reg.get(n)
            if e:
                for we, wi in e["w"].items():
                    need(we, wi)
                if war:
                    for re_, ri in e["r"].items():
                        need(re_, ri)

        self.cnt[eng] += 1
        idx = self.cnt[eng]
        if eng in DMA_QUEUES:
            semkey = (eng, (idx - 1) % NSEM)
            pv = 16 * ((idx - 1) // NSEM)
            if pv > 0 and pv > self.known[eng].get(semkey, 0):
                waits[semkey] = max(waits.get(semkey, 0), pv)
            inc = 16
        else:
            semkey, inc = eng, 1
        for s, v in waits.items():
            self.known[eng][s] = v
        self.opbuf[eng].append((sorted(waits.items(), key=str),
                                (semkey, inc), thunk, self.scope))
        for n in rnames:
            e = self.reg.setdefault(n, {"w": {}, "r": {}})
            e["r"][eng] = idx
        for n in wnames:
            e = self.reg.setdefault(n, {"w": {}, "r": {}})
            if register:
                e["r"] = {}
                if eng not in DMA_QUEUES:
                    e["w"] = {k: v for k, v in e["w"].items()
                              if k in DMA_QUEUES or k != eng}
                e["w"][eng] = idx
        return idx

    def barrier(self):
        snap = {e: self.cnt[e] for e in ENGINES}
        targets = []
        for s in ENGINES:
            if snap[s] == 0:
                continue
            if s in DMA_QUEUES:
                for j in range(NSEM):
                    u = (snap[s] - j + NSEM - 1) // NSEM
                    if u > 0:
                        targets.append(((s, j), 16 * u, s))
            else:
                targets.append((s, snap[s], s))
        for e in ENGINES:
            waits = []
            for key, v, src in targets:
                if src == e:
                    continue
                if v > self.known[e].get(key, 0):
                    waits.append((key, v))
                    self.known[e][key] = v
            if waits:
                self.opbuf[e].append((waits, None, None, self.scope))

    def finalize(self):
        with self.nc.Block() as blk:
            for e in ENGINES:
                ops = self.opbuf[e]
                if not ops:
                    continue
                sec = getattr(blk, e)

                def body(eng, ops=ops, e=e):
                    cur = [None]
                    st = contextlib.ExitStack()
                    for waits, inc_spec, thunk, scope in ops:
                        if scope != cur[0]:
                            st.close()
                            st = contextlib.ExitStack()
                            if scope is not None:
                                st.enter_context(self.nc.named_scope(scope))
                            cur[0] = scope
                        for (key, val) in waits:
                            eng.wait_ge(self.sems[key], val)
                        if thunk is not None:
                            ins = thunk(eng)
                            semkey, inc = inc_spec
                            ins.then_inc(self.sems[semkey], inc)
                    st.close()

                sec(body)
        self.es.close()
        return self.nc

    def dma(self, q, out, in_, **kw):
        assert q in DMA_QUEUES
        out, in_ = _ap(out), _ap(in_)
        self.emit(q, lambda e: e.dma_start(out=out, in_=in_, **kw),
                  reads=[in_], writes=[out])

    def mm(self, out, lhsT, rhs, start=True, stop=True, register=None,
           war=None):
        out, lhsT, rhs = _ap(out), _ap(lhsT), _ap(rhs)
        if register is None:
            register = stop
        if war is None:
            war = start
        self.emit("tensor",
                  lambda e: e.matmul(out, lhsT, rhs, start=start, stop=stop),
                  reads=[lhsT, rhs], writes=[out],
                  register=register, war=war)

    def transp(self, out, in_, ident, register=True, war=True):
        out, in_, ident = _ap(out), _ap(in_), _ap(ident)
        self.emit("tensor", lambda e: e.transpose(out, in_, ident),
                  reads=[in_, ident], writes=[out], register=register, war=war)

    def cp(self, eng, out, in_):
        out, in_ = _ap(out), _ap(in_)
        if eng == "scalar":
            self.emit(eng, lambda e: e.copy(out, in_),
                      reads=[in_], writes=[out])
        else:
            self.emit(eng, lambda e: e.tensor_copy(out, in_),
                      reads=[in_], writes=[out])

    def tt(self, eng, out, a, b, op):
        out, a, b = _ap(out), _ap(a), _ap(b)
        self.emit(eng, lambda e: e.tensor_tensor(out, a, b, op),
                  reads=[a, b], writes=[out])

    def ts(self, eng, out, a, s1, op, s2=None, op2=None, accum=None):
        out, a, s1, s2, accum = _ap(out), _ap(a), _ap(s1), _ap(s2), _ap(accum)
        extra = [s1] if not isinstance(s1, float) else []
        if s2 is not None and not isinstance(s2, float):
            extra.append(s2)
        kw = {}
        if op2 is not None:
            kw["op1"] = op2
        if accum is not None:
            kw["accum_out"] = accum
        wr = [out] + ([accum] if accum is not None else [])
        self.emit(eng, lambda e: e.tensor_scalar(out, a, s1, s2, op, **kw),
                  reads=[a] + extra, writes=wr)

    def stt(self, eng, out, in0, scalar, in1, op0, op1):
        out, in0, scalar, in1 = _ap(out), _ap(in0), _ap(scalar), _ap(in1)
        extra = [scalar] if not isinstance(scalar, float) else []
        self.emit(eng, lambda e: e.scalar_tensor_tensor(out, in0, scalar, in1, op0, op1),
                  reads=[in0, in1] + extra, writes=[out])

    def act(self, out, in_, func, bias=0.0, scale=1.0, accum=None):
        out, in_, bias, scale, accum = _ap(out), _ap(in_), _ap(bias), _ap(scale), _ap(accum)
        if isinstance(bias, float) and bias != 0.0:
            rows = _ap(in_).shape[0]
            bias = self.cvec(bias, rows=rows if rows in (1, 64, 128) else 128)
        extra = [x for x in (bias, scale) if not isinstance(x, float)]
        kw = {}
        if accum is not None:
            kw["accum_out"] = accum
        wr = [out] + ([accum] if accum is not None else [])
        self.emit("scalar",
                  lambda e: e.activation(out, in_, func, bias=bias, scale=scale, **kw),
                  reads=[in_] + extra, writes=wr)

    def red(self, eng, out, in_, op, absval=False, negate=False):
        out, in_ = _ap(out), _ap(in_)
        self.emit(eng,
                  lambda e: e.tensor_reduce(out, in_, axis=AX.X, op=op,
                                            apply_absolute_value=absval,
                                            negate=negate),
                  reads=[in_], writes=[out])

    def recip(self, out, in_):
        out, in_ = _ap(out), _ap(in_)
        self.emit("vector", lambda e: e.reciprocal(out, in_),
                  reads=[in_], writes=[out])

    def memset(self, eng, ap, v):
        ap = _ap(ap)
        self.emit(eng, lambda e: e.memset(ap, v), reads=[], writes=[ap])


INPUT_SPECS = {
    "L0_attn_hidden_states": ((16, 2048, 1024), F32),
    "L0_attn_Wq": ((1024, 1024), F32), "L0_attn_Wk": ((1024, 1024), F32),
    "L0_attn_Wv": ((1024, 1024), F32), "L0_attn_Wo": ((1024, 1024), F32),
    "L0_attn_norm_w": ((128,), F32),
    "L0_ffn_ln1_g": ((1024,), F32), "L0_ffn_ln1_b": ((1024,), F32),
    "L0_ffn_ln2_g": ((1024,), F32), "L0_ffn_ln2_b": ((1024,), F32),
    "L0_ffn_Wq": ((1024, 1024), F32), "L0_ffn_Wk": ((1024, 1024), F32),
    "L0_ffn_Wv": ((1024, 1024), F32), "L0_ffn_Wo": ((1024, 1024), F32),
    "L0_ffn_Wg": ((1024, 8), F32),
    "L0_ffn_W1": ((8, 1024, 4096), F32), "L0_ffn_b1": ((8, 4096), F32),
    "L0_ffn_W2": ((8, 4096, 1024), F32), "L0_ffn_b2": ((8, 1024), F32),
    "L1_attn_w_qkv": ((3072, 1024), F32), "L1_attn_b_qkv": ((3072,), F32),
    "L1_attn_w_out": ((1024, 1024), F32), "L1_attn_b_out": ((1024,), F32),
    "L1_ffn_gamma_x": ((1024,), F32), "L1_ffn_gamma_slot": ((1024,), F32),
    "L1_ffn_slot_embeds": ((8, 512, 1024), F32),
    "L1_ffn_w1": ((1024, 4096), F32), "L1_ffn_w2": ((4096, 1024), F32),
}
CONST_SPECS = {
    "c_eye_f32": ((128, 128), F32),
    "c_eye_bf": ((128, 128), BF16),
    "c_mask_ut": ((128, 128), F32),
    "c_ones_f32": ((128, 128), F32),
    "c_ones_bf": ((128, 128), BF16),
}
SCRATCH_SPECS = {
    "xn_tm": ((TLOC, D), F32),
    "xn_fm": ((DC, 128, TLOC), BF16),
    "att0_tm": ((TLOC, D), F32),
    "x2_tm": ((TLOC, D), F32),
    "m_fm": ((DC, 128, TLOC), BF16),
    "cw_tm": ((TLOC, 8), F32),
    "ffn0_tm": ((TLOC, D), F32),
    "w1m_bf": ((8, DC, 128, 4096), BF16),
    "w2m_bf": ((8, 32, 128, D), BF16),
    "x1_tm": ((TLOC, D), F32),
    "xn1_tm": ((TLOC, D), F32),
    "xn1_fm": ((DC, 128, TLOC), BF16),
    "att1_tm": ((TLOC, D), F32),
    "xnn_fm": ((BLOC, DC, 128, S), BF16),
    "se_fm": ((DC, 128, 4096), BF16),
    "proc_tm": ((BLOC, 4096, D), BF16),
    "ffn1_tm": ((TLOC, D), F32),
}
OUTPUT_SPECS = {"out": ((BLOC, S, D), F32)}

PRIV_SPECS = {
    "q0_fm": ((BLOC, DC, 128, S), BF16),
    "k0_fm": ((BLOC, DC, 128, S), BF16),
    "v0_fm": ((BLOC, DC, 128, S), BF16),
    "o0_fm": ((BLOC, DC, 128, S), BF16),
    "h0_fm": ((DC, 128, TLOC), BF16),
    "qf_fm": ((BLOC, DC, 128, S), BF16),
    "kf_fm": ((BLOC, DC, 128, S), BF16),
    "vf_tm": ((BLOC, 16, 128, D), BF16),
    "ctx0_fm": ((BLOC, DC, 128, S), BF16),
    "m_fm32": ((DC, 128, TLOC), F32),
    "qkv1_fm": ((BLOC, 24, 128, S), BF16),
    "v1_tm": ((BLOC, 16, 128, D), BF16),
    "ctx1_fm": ((BLOC, DC, 128, S), BF16),
    "xnn_tmb": ((BLOC, 16, 128, D), BF16),
    "w1l1_bf": ((32, 128, D), BF16),
    "w2l1_bf": ((32, 128, D), BF16),
}

PHASES = []


def phase(name, reads, writes, privs=()):
    def deco(fn):
        PHASES.append((name, fn, list(reads), list(writes), list(privs)))
        return fn
    return deco


def spec_of(name):
    for d in (INPUT_SPECS, CONST_SPECS, SCRATCH_SPECS, OUTPUT_SPECS, PRIV_SPECS):
        if name in d:
            sh, dt = d[name]
            if name == "L0_attn_hidden_states":
                sh = (BLOC, S, D)
            return sh, dt
    raise KeyError(name)



def ln_stats(P, xt, mu, rstd, xc, sq, acc, eps=1e-5):
    P.red("vector", mu, xt, ALU.add)
    P.ts("vector", mu, mu, 1.0 / D, ALU.mult)
    P.ts("vector", xc, xt, mu, ALU.subtract)
    P.act(sq, xc, AF.Square, accum=acc)
    P.act(acc, acc, AF.Ln, scale=1.0 / D, bias=eps)
    P.act(rstd, acc, AF.Exp, scale=-0.5)


def tm_tile_to_fm(P, tag, src_bf, eye_bf, dst_hbm_fm, ti, psum_bf, stage_bf,
                  n_chunks=DC, also_f32=None, psum32=None, eye32=None,
                  src32=None, stage32=None, dst32=None):
    for dc in range(n_chunks):
        P.transp(psum_bf[:, dc * 128:(dc + 1) * 128],
                 src_bf[:, dc * 128:(dc + 1) * 128], eye_bf,
                 register=(dc == n_chunks - 1), war=(dc == 0))
    P.cp("vector", stage_bf[:], psum_bf[:])
    P.dma("gpsimd",
          dst_hbm_fm[:, :, ti * 128:(ti + 1) * 128].rearrange("c p t -> p c t"),
          stage_bf[:].rearrange("p (c t) -> p c t", c=n_chunks))
    if also_f32:
        for dc in range(n_chunks):
            P.transp(psum32[:, dc * 128:(dc + 1) * 128],
                     src32[:, dc * 128:(dc + 1) * 128], eye32,
                     register=(dc == n_chunks - 1), war=(dc == 0))
        P.cp("vector", stage32[:], psum32[:])
        P.dma("gpsimd",
              dst32[:, :, ti * 128:(ti + 1) * 128].rearrange("c p t -> p c t"),
              stage32[:].rearrange("p (c t) -> p c t", c=n_chunks))



@phase("p0_ln0",
       reads=["L0_attn_hidden_states", "c_eye_bf"],
       writes=["xn_tm", "xn_fm"])
def p0_ln0(P, T):
    x = T["L0_attn_hidden_states"].rearrange("b s d -> (b s) d")
    eye_bf = T["c_eye_bf"]
    with P.phase("p0"):
        R2 = lambda nm, sh, dt: Rot([P.sbuf(f"{nm}{i}", sh, dt) for i in range(2)])
        xt_r = R2("p0_xt", [128, D], F32)
        xc_r = R2("p0_xc", [128, D], F32)
        xnb_r = R2("p0_xnb", [128, D], BF16)
        mu_r = R2("p0_mu", [128, 1], F32)
        rstd_r = R2("p0_rstd", [128, 1], F32)
        lsq = P.sbuf("p0_sq", [128, D], F32)
        lacc_r = R2("p0_acc", [128, 1], F32)
        ptr_r = Rot([P.psum(f"p0_ptr{i}", [128, D], BF16) for i in range(2)])
        stg_r = R2("p0_stg", [128, D], BF16)
        for ti in range(TLOC // 128):
            sl = slice(ti * 128, (ti + 1) * 128)
            xt, xc, xnb = xt_r.nxt(), xc_r.nxt(), xnb_r.nxt()
            mu, rstd, lacc = mu_r.nxt(), rstd_r.nxt(), lacc_r.nxt()
            ptr, stg = ptr_r.nxt(), stg_r.nxt()
            P.dma("sync", xt[:], x[sl, :])
            ln_stats(P, xt, mu, rstd, xc, lsq, lacc)
            P.ts("vector", xc[:], xc[:], rstd[:], ALU.mult)
            P.dma("gpsimd", T["xn_tm"][sl, :], xc[:])
            P.cp("scalar", xnb[:], xc[:])
            tm_tile_to_fm(P, "p0", xnb, eye_bf, T["xn_fm"], ti, ptr, stg)


class Rot:
    def __init__(self, items):
        self.items = list(items)
        self.i = 0

    def nxt(self):
        r = self.items[self.i % len(self.items)]
        self.i += 1
        return r


def load_wT(P, tag, w_dram, wt_sb, eye_bf, n_out=D, n_in=D):
    nic = n_in // 128
    f32t = Rot([P.sbuf(f"{tag}_wl32{i}", [128, n_in], F32) for i in range(2)])
    bft = Rot([P.sbuf(f"{tag}_wlbf{i}", [128, n_in], BF16) for i in range(2)])
    ptr = Rot([P.psum(f"{tag}_wlps{i}", [128, n_in], BF16) for i in range(2)])
    for oc in range(n_out // 128):
        f3, bf, pt = f32t.nxt(), bft.nxt(), ptr.nxt()
        P.dma("sync", f3[:], w_dram[oc * 128:(oc + 1) * 128, :])
        P.cp("vector", bf, f3)
        for dc in range(nic):
            P.transp(_ap(pt)[:, dc * 128:(dc + 1) * 128],
                     _ap(bf)[:, dc * 128:(dc + 1) * 128], eye_bf,
                     register=(dc == nic - 1), war=(dc == 0))
        P.cp("scalar",
             wt_sb[:, :, oc * 128:(oc + 1) * 128],
             _ap(pt).rearrange("p (c t) -> p c t", c=nic))


def load_w(P, tag, w_dram, w_sb, n_out):
    f32t = P.sbuf(f"{tag}_wl32", [128, n_out], F32)
    for dc in range(DC):
        P.dma("sync", f32t[:], w_dram[dc * 128:(dc + 1) * 128, :])
        P.cp("vector", w_sb[:, dc, :], f32t)


def row_of(P, tag, vec_dram, n):
    r = P.sbuf(f"{tag}_row", [1, n], F32)
    P.dma("sync", r[:], vec_dram.rearrange("(a d) -> a d", a=1))
    return r



@phase("p1_att0",
       reads=["xn_fm", "L0_attn_Wq", "L0_attn_Wk", "L0_attn_Wv", "L0_attn_Wo",
              "L0_attn_norm_w", "c_eye_bf", "c_mask_ut", "c_ones_bf",
              "c_ones_f32"],
       writes=["att0_tm"],
       privs=["q0_fm", "k0_fm", "v0_fm", "o0_fm"])
def p1_att0(P, T):
    eye = T["c_eye_bf"]
    ones_bf_col = _ap(T["c_ones_bf"])[:, 0:1]
    ones_f32_col = _ap(T["c_ones_f32"])[:, 0:1]
    mask = T["c_mask_ut"]

    P.scope = "p1_proj"
    with P.phase("p1proj"):
        wts = {}
        for wn, key in (("L0_attn_Wq", "q0_fm"), ("L0_attn_Wk", "k0_fm"),
                        ("L0_attn_Wv", "v0_fm")):
            wt = P.sbuf(f"p1_{wn}T", [128, DC, D], BF16)
            load_wT(P, f"p1{wn}", T[wn], wt, eye)
            wts[key] = wt
        xt = P.sbuf("p1_xt", [128, DC, 512], BF16)
        pos = Rot([P.psum(f"p1_po{i}", [128, 512], F32) for i in range(2)])
        obs = Rot([P.sbuf(f"p1_ob{i}", [128, 512], BF16) for i in range(2)])
        for b in range(BLOC):
            for tmac in range(S // 512):
                t0 = b * S + tmac * 512
                P.dma("sync", xt[:],
                      T["xn_fm"][:, :, t0:t0 + 512].rearrange("c p t -> p c t"))
                for key, wt in wts.items():
                    for ec in range(DC):
                        po, ob = pos.nxt(), obs.nxt()
                        for dc in range(DC):
                            P.mm(po, wt[:, dc, ec * 128:(ec + 1) * 128],
                                 xt[:, dc, :], start=(dc == 0), stop=(dc == DC - 1))
                        P.cp("vector", ob, po)
                        P.dma("gpsimd",
                              T[key][b, ec, :, tmac * 512:(tmac + 1) * 512], ob)

    NCH = S // 128
    P.scope = "p1_chunk"
    with P.phase("p1attn"):
        nw_row = row_of(P, "p1_nw", T["L0_attn_norm_w"], 128)
        qh = P.sbuf("p1_qh", [128, S], BF16)
        kh = P.sbuf("p1_kh", [128, S], BF16)
        vh = P.sbuf("p1_vh", [128, S], BF16)
        kv_s = P.sbuf("p1_kv", [128, 256], BF16)
        scst = P.sbuf("p1_scst", [128, 128], BF16)
        state_s = P.sbuf("p1_state", [128, 129], BF16)
        den_s = P.sbuf("p1_den", [1, 128], F32)
        d2_s = P.sbuf("p1_d2", [1, 128], F32)
        osq_s = P.sbuf("p1_osq", [128, 128], F32)
        t_s = P.sbuf("p1_t", [1, 128], F32)
        al_s = P.sbuf("p1_al", [1, 128], F32)
        scale_s = P.sbuf("p1_scale", [128, 128], F32)
        on_s = P.sbuf("p1_on", [128, 128], BF16)
        ps_kv = P.psum("p1_pskv", [128, 256], BF16)
        ps_sc = P.psum("p1_pssc", [128, 128], F32)
        ps_o = P.psum("p1_pso", [128, 128], F32)
        ps_state = P.psum("p1_psst", [128, 129], F32)
        ps_den = P.psum("p1_psden", [128, 128], F32)
        ps_ssq = P.psum("p1_psssq", [128, 128], F32)
        ps_scale = P.psum("p1_pssca", [128, 128], F32)
        for b in range(BLOC):
            for h in range(8):
                P.dma("sync", qh[:], T["q0_fm"][b, h, :, :])
                P.dma("sync", kh[:], T["k0_fm"][b, h, :, :])
                P.dma("sync", vh[:], T["v0_fm"][b, h, :, :])
                for c in range(NCH):
                    csl = slice(c * 128, (c + 1) * 128)
                    P.transp(ps_kv[:, 0:128], kh[:, csl], eye,
                             register=False, war=True)
                    P.transp(ps_kv[:, 128:256], vh[:, csl], eye,
                             register=True, war=False)
                    P.cp("vector", kv_s, ps_kv)
                    ktc = _ap(kv_s)[:, 0:128]
                    vtc = _ap(kv_s)[:, 128:256]
                    P.mm(ps_sc, kh[:, csl], qh[:, csl])
                    P.tt("vector", scst, ps_sc, mask, ALU.mult)
                    if c > 0:
                        P.cp("scalar", state_s, ps_state)
                        P.mm(ps_o, _ap(state_s)[:, 0:128], qh[:, csl],
                             start=True, stop=False)
                        P.mm(ps_o, vtc, scst, start=False, stop=True)
                        P.mm(ps_den[0:1, :], _ap(state_s)[:, 128:129],
                             qh[:, csl], start=True, stop=False)
                        P.mm(ps_den[0:1, :], ones_bf_col, scst,
                             start=False, stop=True)
                    else:
                        P.mm(ps_o, vtc, scst, start=True, stop=True)
                        P.mm(ps_den[0:1, :], ones_bf_col, scst,
                             start=True, stop=True)
                    P.mm(ps_state[:, 0:128], ktc, vtc,
                         start=(c == 0), stop=False, register=False)
                    P.mm(ps_state[:, 128:129], ktc, ones_bf_col,
                         start=False, stop=(c == NCH - 1), register=True)
                    P.act(den_s, ps_den[0:1, :], AF.Abs)
                    P.ts("vector", den_s, den_s, 1.0, ALU.max)
                    P.act(osq_s, ps_o, AF.Square)
                    P.mm(ps_ssq[0:1, :], ones_f32_col, osq_s)
                    P.tt("vector", d2_s, den_s, den_s, ALU.mult)
                    P.ts("vector", d2_s, d2_s, 1e-5, ALU.mult)
                    P.stt("vector", t_s, ps_ssq[0:1, :], 1.0 / 128.0, d2_s,
                          ALU.mult, ALU.add)
                    P.act(t_s, t_s, AF.Ln)
                    P.act(al_s, t_s, AF.Exp, scale=-0.5)
                    P.mm(ps_scale, nw_row, al_s)
                    P.cp("vector", scale_s, ps_scale)
                    P.tt("vector", on_s, ps_o, scale_s, ALU.mult)
                    P.dma("gpsimd", T["o0_fm"][b, h, :, csl], on_s)

    P.scope = "p1_wo"
    with P.phase("p1wo"):
        woT = P.sbuf("p1_woT", [128, DC, D], BF16)
        load_wT(P, "p1wo", T["L0_attn_Wo"], woT, eye)
        ots = Rot([P.sbuf(f"p1_ot{i}", [128, DC, 128], BF16) for i in range(2)])
        pws = Rot([P.psum(f"p1_pw{i}", [128, 512], F32) for i in range(4)])
        oss = Rot([P.sbuf(f"p1_os{i}", [128, D], F32) for i in range(2)])
        for b in range(BLOC):
            for tt_ in range(S // 128):
                ts_ = slice(tt_ * 128, (tt_ + 1) * 128)
                ot = ots.nxt()
                os_ = oss.nxt()
                P.dma("sync", ot[:],
                      T["o0_fm"][b, :, :, ts_].rearrange("c p t -> p c t"))
                for half in range(2):
                    pw = pws.nxt()
                    for ec in range(DC):
                        P.mm(pw, ot[:, ec, :],
                             woT[:, ec, half * 512:(half + 1) * 512],
                             start=(ec == 0), stop=(ec == DC - 1))
                    P.cp("vector", os_[:, half * 512:(half + 1) * 512], pw)
                P.dma("gpsimd", T["att0_tm"][b * S + tt_ * 128:
                                             b * S + (tt_ + 1) * 128, :], os_)


def bcast_row(P, tag, row_ap, n, ones_f32, scale=1.0):
    ps = P.psum(f"{tag}_bps", [128, 512], F32)
    out = P.sbuf(f"{tag}_bc", [128, n], F32)
    for j in range(0, n, 512):
        w = min(512, n - j)
        P.mm(ps[:, 0:w], _ap(ones_f32)[0:1, :], _ap(row_ap)[0:1, j:j + w])
        if scale == 1.0:
            P.cp("vector", out[:, j:j + w], ps[:, 0:w])
        else:
            P.ts("vector", out[:, j:j + w], ps[:, 0:w], float(scale), ALU.mult)
    return out


def softmax_TQ(P, bufs, kh, qh, v_s, head, negc_s, out_dst, scale, bv=None):
    (ps_sT, E_r, ps_ctx_r, ps_sig, sig_r, sg_r, ctx_r) = bufs
    NK2 = S // 256
    c0 = 65 * head

    def make_epilogue(pc, qsl):
        sig_s, sg_s, ctxb = sig_r.nxt(), sg_r.nxt(), ctx_r.nxt()

        def epi():
            P.act(sig_s, _ap(pc)[64:65, :], AF.Ln)
            P.act(sig_s, sig_s, AF.Exp, scale=-1.0)
            P.mm(_ap(ps_sig)[0:64, :], P.ones_f32_row64, sig_s)
            P.cp("vector", sg_s, _ap(ps_sig)[0:64, :])
            if bv is None:
                P.tt("vector", ctxb, _ap(pc)[0:64, :], sg_s, ALU.mult)
            else:
                P.tt("vector", sg_s, _ap(pc)[0:64, :], sg_s, ALU.mult)
                P.ts("vector", ctxb, sg_s, bv, ALU.add)
            P.dma("gpsimd", out_dst[:, qsl], ctxb)
        return epi

    pend = None
    for qm in range(S // 512):
        qsl = slice(qm * 512, (qm + 1) * 512)
        pc = ps_ctx_r.nxt()
        prev = None
        for kt2 in range(NK2):
            ka, kb = 2 * kt2, 2 * kt2 + 1
            pt = ps_sT.nxt()
            P.mm(_ap(pt)[:, 0:512], kh[:, ka * 128:(ka + 1) * 128],
                 qh[:, qsl], register=False)
            P.mm(_ap(pt)[:, 512:1024], kh[:, kb * 128:(kb + 1) * 128],
                 qh[:, qsl], war=False)
            E = E_r.nxt()
            P.act(E, pt, AF.Exp, bias=negc_s, scale=scale)
            if kt2 == 1 and pend is not None:
                pend()
                pend = None
            if prev is not None:
                pa, pE = prev
                P.mm(_ap(pc)[0:65, :], v_s[:, pa, c0:c0 + 65],
                     _ap(pE)[:, 0:512],
                     start=(pa == 0), stop=False, register=False)
                P.mm(_ap(pc)[0:65, :], v_s[:, pa + 1, c0:c0 + 65],
                     _ap(pE)[:, 512:1024], start=False,
                     stop=False)
            prev = (ka, E)
        pa, pE = prev
        P.mm(_ap(pc)[0:65, :], v_s[:, pa, c0:c0 + 65], _ap(pE)[:, 0:512],
             start=(pa == 0), stop=False, register=False)
        P.mm(_ap(pc)[0:65, :], v_s[:, pa + 1, c0:c0 + 65],
             _ap(pE)[:, 512:1024], start=False, stop=True)
        pend = make_epilogue(pc, qsl)
    pend()


def head_bound(P, bufs, qh, kh, negc_s, scale_mag):
    (sq_s, ps_n, nmax, tmp1) = bufs
    for which, src in ((0, qh), (1, kh)):
        P.tt("vector", sq_s, src, src, ALU.mult)
        for j in range(S // 512):
            P.mm(ps_n[0:1, :], P.ones_f32_col64, sq_s[:, j * 512:(j + 1) * 512])
            P.red("vector", tmp1, ps_n[0:1, :], ALU.max)
            if j == 0:
                P.cp("vector", nmax[0:1, which:which + 1], tmp1)
            else:
                P.tt("vector", nmax[0:1, which:which + 1],
                     nmax[0:1, which:which + 1], tmp1, ALU.max)
    P.tt("vector", tmp1, nmax[0:1, 0:1], nmax[0:1, 1:2], ALU.mult)
    P.act(tmp1, tmp1, AF.Ln, bias=1e-30)
    P.act(tmp1, tmp1, AF.Exp, scale=0.5)
    P.mm(ps_n[:, 0:1], _ap(P.ones_f32)[0:1, :], tmp1)
    P.act(negc_s, ps_n[:, 0:1], AF.Copy, scale=-0.75 * scale_mag)



@phase("p2_mha0",
       reads=["xn_tm", "xn_fm", "L0_ffn_ln1_g", "L0_ffn_ln1_b",
              "L0_ffn_ln2_g", "L0_ffn_ln2_b", "L0_ffn_Wq", "L0_ffn_Wk",
              "L0_ffn_Wv", "L0_ffn_Wo", "L0_ffn_Wg", "L0_ffn_W1", "L0_ffn_W2",
              "c_eye_bf", "c_eye_f32", "c_ones_f32", "c_ones_bf"],
       writes=["x2_tm", "m_fm", "cw_tm", "w1m_bf", "w2m_bf"],
       privs=["h0_fm", "qf_fm", "kf_fm", "vf_tm", "ctx0_fm", "m_fm32"])
def p2_mha0(P, T):
    eye = T["c_eye_bf"]
    eye32 = T["c_eye_f32"]
    P.ones_f32 = T["c_ones_f32"]
    P.ones_bf_col = _ap(T["c_ones_bf"])[:, 0:1]
    P.ones_f32_col64 = _ap(T["c_ones_f32"])[0:64, 0:1]
    P.ones_f32_row64 = _ap(T["c_ones_f32"])[0:1, 0:64]

    P.scope = "p2_a"
    with P.phase("p2a"):
        g1r = row_of(P, "p2g1", T["L0_ffn_ln1_g"], D)
        b1r = row_of(P, "p2b1", T["L0_ffn_ln1_b"], D)
        g1 = bcast_row(P, "p2g1", g1r, D, P.ones_f32)
        b1 = bcast_row(P, "p2b1", b1r, D, P.ones_f32)
        R2 = lambda nm, sh, dt: Rot([P.sbuf(f"{nm}{i}", sh, dt) for i in range(2)])
        xt_r = R2("p2_xt", [128, D], F32)
        xc_r = R2("p2_xc", [128, D], F32)
        hb_r = R2("p2_hb", [128, D], BF16)
        mu_r = R2("p2_mu", [128, 1], F32)
        rstd_r = R2("p2_rstd", [128, 1], F32)
        lsq = P.sbuf("p2_sq", [128, D], F32)
        lacc_r = R2("p2_acc", [128, 1], F32)
        ptr_r = Rot([P.psum(f"p2_ptr{i}", [128, D], BF16) for i in range(2)])
        stg_r = R2("p2_stg", [128, D], BF16)
        for ti in range(TLOC // 128):
            sl = slice(ti * 128, (ti + 1) * 128)
            xt, xc, hb = xt_r.nxt(), xc_r.nxt(), hb_r.nxt()
            mu, rstd, lacc = mu_r.nxt(), rstd_r.nxt(), lacc_r.nxt()
            ptr, stg = ptr_r.nxt(), stg_r.nxt()
            P.dma("sync", xt[:], T["xn_tm"][sl, :])
            ln_stats(P, xt, mu, rstd, xc, lsq, lacc)
            P.ts("vector", xc[:], xc[:], rstd[:], ALU.mult)
            P.tt("vector", xc, xc, g1, ALU.mult)
            P.tt("vector", xc, xc, b1, ALU.add)
            P.cp("scalar", hb, xc)
            tm_tile_to_fm(P, "p2", hb, eye, T["h0_fm"], ti, ptr, stg)

    P.scope = "p2_b"
    with P.phase("p2b"):
        wq = P.sbuf("p2_wq", [128, DC, D], BF16)
        wk = P.sbuf("p2_wk", [128, DC, D], BF16)
        wv = P.sbuf("p2_wv", [128, DC, D], BF16)
        load_w(P, "p2wq", T["L0_ffn_Wq"], wq, D)
        load_w(P, "p2wk", T["L0_ffn_Wk"], wk, D)
        load_w(P, "p2wv", T["L0_ffn_Wv"], wv, D)
        xt = P.sbuf("p2_xh", [128, DC, 512], BF16)
        pos = Rot([P.psum(f"p2_po{i}", [128, 512], F32) for i in range(4)])
        obs = Rot([P.sbuf(f"p2_ob{i}", [128, 512], BF16) for i in range(4)])
        for b in range(BLOC):
            for tmac in range(S // 512):
                t0 = b * S + tmac * 512
                P.dma("sync", xt[:],
                      T["h0_fm"][:, :, t0:t0 + 512].rearrange("c p t -> p c t"))
                for key, wt in (("qf_fm", wq), ("kf_fm", wk)):
                    for ec in range(DC):
                        po, ob = pos.nxt(), obs.nxt()
                        for dc in range(DC):
                            P.mm(po, wt[:, dc, ec * 128:(ec + 1) * 128],
                                 xt[:, dc, :], start=(dc == 0),
                                 stop=(dc == DC - 1))
                        P.cp("vector", ob, po)
                        P.dma("gpsimd",
                              T[key][b, ec, :, tmac * 512:(tmac + 1) * 512], ob)
                for st in range(4):
                    tc = tmac * 4 + st
                    for half in range(2):
                        po, ob = pos.nxt(), obs.nxt()
                        hsl = slice(half * 512, (half + 1) * 512)
                        for dc in range(DC):
                            P.mm(po, xt[:, dc, st * 128:(st + 1) * 128],
                                 wv[:, dc, hsl], start=(dc == 0),
                                 stop=(dc == DC - 1))
                        P.cp("vector", ob, po)
                        P.dma("gpsimd", T["vf_tm"][b, tc, :, hsl], ob)

    def precast_ops(wc32, wcb):
        for e in range(8):
            for dc in range(DC):
                c32, cb = wc32.nxt(), wcb.nxt()
                P.dma("sync", c32[:],
                      T["L0_ffn_W1"][e, dc * 128:(dc + 1) * 128, :])
                P.cp("vector", cb, c32)
                P.dma("gpsimd", T["w1m_bf"][e, dc, :, :], cb)
                yield
        for e in range(8):
            for g in range(8):
                c32, cb = wc32.nxt(), wcb.nxt()
                P.dma("sync",
                      c32[:].rearrange("p (a d) -> p a d", a=4),
                      T["L0_ffn_W2"][e, g * 512:(g + 1) * 512, :]
                      .rearrange("(a p) d -> p a d", p=128))
                P.cp("vector", cb, c32)
                P.dma("gpsimd",
                      T["w2m_bf"][e, 4 * g:4 * g + 4, :, :]
                      .rearrange("c p d -> p c d"),
                      cb[:].rearrange("p (c d) -> p c d", c=4))
                yield
        while True:
            yield

    P.scope = "p2_c"
    with P.phase("p2c"):
        NH = 16
        wc32 = Rot([P.sbuf(f"p2_wc32{i}", [128, 4096], F32) for i in range(2)])
        wcb = Rot([P.sbuf(f"p2_wcb{i}", [128, 4096], BF16) for i in range(2)])
        precast = precast_ops(wc32, wcb)
        v_r = Rot([P.sbuf(f"p2_vs{i}", [128, S // 128, NH * 65], BF16)
                   for i in range(2)])
        qc = Rot([P.sbuf(f"p2_qc{i}", [128, S], BF16) for i in range(2)])
        kc = Rot([P.sbuf(f"p2_kc{i}", [128, S], BF16) for i in range(2)])
        sq_s = P.sbuf("p2_sqs", [64, S], F32)
        nmax = P.sbuf("p2_nmax", [1, 2], F32)
        tmp1 = P.sbuf("p2_tmp1", [1, 1], F32)
        negc = Rot([P.sbuf(f"p2_negc{i}", [128, 1], F32) for i in range(2)])
        ps_n = P.psum("p2_psn", [128, 512], F32)
        ps_sT = Rot([P.psum(f"p2_psT{i}", [128, 1024], F32) for i in range(2)])
        E_r = Rot([P.sbuf(f"p2_E{i}", [128, 1024], BF16) for i in range(2)])
        ps_ctx = Rot([P.psum(f"p2_psctx{i}", [128, 512], F32) for i in range(2)])
        ps_sig = P.psum("p2_pssig", [128, 512], F32)
        sig_r = Rot([P.sbuf(f"p2_sig{i}", [1, 512], F32) for i in range(2)])
        sg_r = Rot([P.sbuf(f"p2_sg{i}", [64, 512], F32) for i in range(2)])
        ctx_r = Rot([P.sbuf(f"p2_ctxb{i}", [64, 512], BF16) for i in range(2)])
        hb_bufs = (sq_s, ps_n, nmax, tmp1)
        at_bufs = (ps_sT, E_r, ps_ctx, ps_sig, sig_r, sg_r, ctx_r)
        for vb in v_r.items:
            P.memset("vector",
                 _ap(vb).rearrange("p c (h w) -> p c h w", w=65)[:, :, :, 64:65],
                 1.0)
        for b in range(BLOC):
            v_s = v_r.nxt()
            for tc in range(S // 128):
                P.dma("gpsimd",
                      _ap(v_s)[:, tc, :].rearrange("p (h w) -> p h w", w=65)
                      [:, :, 0:64],
                      T["vf_tm"][b, tc, :, :].rearrange("p (h w) -> p h w", w=64))
            for hp in range(8):
                qcb, kcb = qc.nxt(), kc.nxt()
                P.dma("sync", qcb[:], T["qf_fm"][b, hp, :, :])
                P.dma("sync", kcb[:], T["kf_fm"][b, hp, :, :])
                for sub in range(2):
                    hbase = 64 * sub
                    h = 2 * hp + sub
                    qh = _ap(qcb)[hbase:hbase + 64, :]
                    kh = _ap(kcb)[hbase:hbase + 64, :]
                    nc_ = negc.nxt()
                    head_bound(P, hb_bufs, qh, kh, nc_, 0.125)
                    softmax_TQ(P, at_bufs, kh, qh, v_s, h, nc_,
                               T["ctx0_fm"][b, hp, hbase:hbase + 64, :], 0.125)
                    next(precast)
                    next(precast)
                    next(precast)
                    next(precast)

    P.scope = "p2_d"
    with P.phase("p2d"):
        wo = P.sbuf("p2_wo", [128, DC, D], BF16)
        load_w(P, "p2wo", T["L0_ffn_Wo"], wo, D)
        wg = P.sbuf("p2_wg", [128, DC, 8], F32)
        wgl = P.sbuf("p2_wgl", [128, 8], F32)
        for dc in range(DC):
            P.dma("sync", wgl[:], T["L0_ffn_Wg"][dc * 128:(dc + 1) * 128, :])
            P.cp("vector", wg[:, dc, :], wgl)
        g2r = row_of(P, "p2g2", T["L0_ffn_ln2_g"], D)
        b2r = row_of(P, "p2b2", T["L0_ffn_ln2_b"], D)
        g2 = bcast_row(P, "p2g2", g2r, D, P.ones_f32)
        b2 = bcast_row(P, "p2b2", b2r, D, P.ones_f32)
        ot = P.sbuf("p2_ot", [128, DC, 128], BF16)
        xnt = P.sbuf("p2_xnt", [128, D], F32)
        x2s = P.sbuf("p2_x2s", [128, D], F32)
        mb = P.sbuf("p2_mb", [128, D], BF16)
        m32 = P.sbuf("p2_m32", [128, D], F32)
        xc2 = P.sbuf("p2_xc2", [128, D], F32)
        mu = P.sbuf("p2_mu2", [128, 1], F32)
        rstd = P.sbuf("p2_rstd2", [128, 1], F32)
        lsq = P.sbuf("p2_sq2", [128, D], F32)
        lacc = P.sbuf("p2_acc2", [128, 1], F32)
        pw = Rot([P.psum(f"p2_pw{i}", [128, 512], F32) for i in range(2)])
        ptr = P.psum("p2_ptr2", [128, D], BF16)
        stg = P.sbuf("p2_stg2", [128, D], BF16)
        ptr32 = P.psum("p2_ptr32", [128, D], F32)
        stg32 = P.sbuf("p2_stg32", [128, D], F32)
        ps_r = P.psum("p2_psr", [128, 512], F32)
        E8 = P.sbuf("p2_E8", [128, 8], F32)
        mx = P.sbuf("p2_mx", [128, 1], F32)
        m1 = P.sbuf("p2_m1", [128, 1], F32)
        m2 = P.sbuf("p2_m2", [128, 1], F32)
        iseq = P.sbuf("p2_iseq", [128, 8], F32)
        e8b = P.sbuf("p2_e8b", [128, 8], F32)
        cw = P.sbuf("p2_cw", [128, 8], F32)
        for b in range(BLOC):
            for tt_ in range(S // 128):
                ti = b * (S // 128) + tt_
                sl = slice(ti * 128, (ti + 1) * 128)
                P.dma("sync", ot[:],
                      T["ctx0_fm"][b, :, :, tt_ * 128:(tt_ + 1) * 128]
                      .rearrange("c p t -> p c t"))
                P.dma("sync", xnt[:], T["xn_tm"][sl, :])
                for half in range(2):
                    p_ = pw.nxt()
                    hsl = slice(half * 512, (half + 1) * 512)
                    for ec in range(DC):
                        P.mm(p_, ot[:, ec, :], wo[:, ec, hsl],
                             start=(ec == 0), stop=(ec == DC - 1))
                    P.tt("vector", x2s[:, hsl], p_, _ap(xnt)[:, hsl], ALU.add)
                P.dma("gpsimd", T["x2_tm"][sl, :], x2s)
                ln_stats(P, x2s, mu, rstd, xc2, lsq, lacc)
                P.ts("vector", m32, xc2, rstd, ALU.mult)
                P.tt("vector", m32, m32, g2, ALU.mult)
                P.tt("vector", m32, m32, b2, ALU.add)
                P.cp("scalar", mb, m32)
                tm_tile_to_fm(P, "p2d", mb, eye, T["m_fm"], ti, ptr, stg,
                              also_f32=True, psum32=ptr32, eye32=eye32,
                              src32=m32, stage32=stg32, dst32=T["m_fm32"])
                for dc in range(DC):
                    P.mm(ps_r[:, 0:8], stg32[:, dc * 128:(dc + 1) * 128],
                         wg[:, dc, :], start=(dc == 0), stop=(dc == DC - 1))
                P.red("vector", mx, ps_r[:, 0:8], ALU.max)
                P.ts("vector", mx, mx, -1.0, ALU.mult)
                P.act(E8, ps_r[:, 0:8], AF.Exp, bias=mx)
                P.red("vector", m1, E8, ALU.max)
                P.ts("vector", iseq, E8, m1, ALU.is_equal)
                P.ts("vector", iseq, iseq, -1.0, ALU.mult, 1.0, ALU.add)
                P.tt("vector", e8b, E8, iseq, ALU.mult)
                P.red("vector", m2, e8b, ALU.max)
                P.ts("vector", e8b, E8, m2, ALU.is_ge)
                P.tt("vector", cw, E8, e8b, ALU.mult)
                P.tt("vector", m1, m1, m2, ALU.add)
                P.recip(m1, m1)
                P.ts("vector", cw, cw, m1, ALU.mult)
                P.dma("gpsimd", T["cw_tm"][sl, :], cw)



@phase("p3_moe0",
       reads=["m_fm", "cw_tm", "x2_tm", "w1m_bf", "L0_ffn_b1",
              "w2m_bf", "L0_ffn_b2", "c_ones_f32"],
       writes=["ffn0_tm"])
def p3_moe0(P, T):
    P.ones_f32 = T["c_ones_f32"]
    NE, F_ = 8, 4096
    FC = F_ // 128
    with P.phase("p3"):
        w1 = P.sbuf("p3_w1", [128, DC, F_], BF16)
        w2 = P.sbuf("p3_w2", [128, FC, D], BF16)
        b1 = P.sbuf("p3_b1", [128, FC], F32)

        b2r = P.sbuf("p3_b2r", [1, D], F32)
        b2bc = P.sbuf("p3_b2bc", [128, D], F32)
        ps_b = P.psum("p3_psb", [128, 512], F32)
        mt = Rot([P.sbuf(f"p3_mt{i}", [128, DC, 512], BF16) for i in range(2)])
        h_s = P.sbuf("p3_h", [128, FC, 512], BF16)
        cw_s = Rot([P.sbuf(f"p3_cw{i}", [128, 4, NE], F32) for i in range(2)])
        acc = Rot([P.sbuf(f"p3_acc{i}", [128, D], F32) for i in range(2)])
        ytmp = Rot([P.sbuf(f"p3_yt{i}", [128, 512], F32) for i in range(1)])
        ph = Rot([P.psum(f"p3_ph{i}", [128, 512], F32) for i in range(3)])
        py = Rot([P.psum(f"p3_py{i}", [128, 512], F32) for i in range(4)])

        def load_w1(e):
            for dc in range(DC):
                P.dma("gpsimd" if dc % 2 else "sync", w1[:, dc, :],
                      T["w1m_bf"][e, dc, :, :])

        def load_w2b(e):
            for g in range(4):
                P.dma("gpsimd" if g % 2 else "sync",
                      w2[:, 8 * g:8 * g + 8, :]
                      .rearrange("p c d -> p c d"),
                      T["w2m_bf"][e, 8 * g:8 * g + 8, :, :]
                      .rearrange("c p d -> p c d"))
            for fc in range(FC):
                P.dma("gpsimd", b1[:, fc:fc + 1],
                      T["L0_ffn_b1"][e, fc * 128:(fc + 1) * 128]
                      .rearrange("(p a) -> p a", a=1))
            P.dma("gpsimd", b2r[:], T["L0_ffn_b2"][e, :]
                  .rearrange("(a d) -> a d", a=1))
            for j in range(2):
                P.mm(ps_b[:, 0:512], _ap(P.ones_f32)[0:1, :],
                     _ap(b2r)[0:1, j * 512:(j + 1) * 512])
                P.cp("vector", b2bc[:, j * 512:(j + 1) * 512], ps_b[:, 0:512])

        load_w1(0)
        load_w2b(0)
        for e in range(NE):
            for tmac in range(TLOC // 512):
                t0 = tmac * 512
                m_, c_ = mt.nxt(), cw_s.nxt()
                P.dma("sync", m_[:],
                      T["m_fm"][:, :, t0:t0 + 512].rearrange("c p t -> p c t"))
                P.dma("sync", c_[:],
                      T["cw_tm"][t0:t0 + 512, :]
                      .rearrange("(s p) e -> p s e", p=128))
                for fc in range(FC):
                    p_ = ph.nxt()
                    for dc in range(DC):
                        P.mm(p_, w1[:, dc, fc * 128:(fc + 1) * 128],
                             m_[:, dc, :], start=(dc == 0), stop=(dc == DC - 1))
                    P.act(h_s[:, fc, :], p_, AF.Gelu_apprx_tanh,
                          bias=_ap(b1)[:, fc:fc + 1])
                if tmac == TLOC // 512 - 1 and e + 1 < NE:
                    load_w1(e + 1)
                for st in range(4):
                    ssl = slice(st * 128, (st + 1) * 128)
                    a_ = acc.nxt()
                    src_t = T["x2_tm"] if e == 0 else T["ffn0_tm"]
                    P.dma("sync", a_[:],
                          src_t[t0 + st * 128:t0 + (st + 1) * 128, :])
                    for half in range(2):
                        hsl = slice(half * 512, (half + 1) * 512)
                        p_ = py.nxt()
                        yt = ytmp.nxt()
                        for fc in range(FC):
                            P.mm(p_, h_s[:, fc, ssl], w2[:, fc, hsl],
                                 start=(fc == 0), stop=(fc == FC - 1))
                        P.tt("vector", yt, p_, _ap(b2bc)[:, hsl], ALU.add)
                        P.stt("vector", a_[:, hsl], yt,
                              _ap(c_)[:, st, e:e + 1], a_[:, hsl],
                              ALU.mult, ALU.add)
                    P.dma("gpsimd",
                          T["ffn0_tm"][t0 + st * 128:t0 + (st + 1) * 128, :],
                          a_)
            if e + 1 < NE:
                load_w2b(e + 1)



@phase("p4_comb0",
       reads=["L0_attn_hidden_states", "att0_tm", "ffn0_tm", "c_eye_bf"],
       writes=["x1_tm", "xn1_tm", "xn1_fm"])
def p4_comb0(P, T):
    x = T["L0_attn_hidden_states"].rearrange("b s d -> (b s) d")
    eye_bf = T["c_eye_bf"]
    with P.phase("p4"):
        R2 = lambda nm, sh, dt: Rot([P.sbuf(f"{nm}{i}", sh, dt) for i in range(2)])
        xt_r = R2("p4_xt", [128, D], F32)
        at_r = R2("p4_at", [128, D], F32)
        ft_r = R2("p4_ft", [128, D], F32)
        xc_r = R2("p4_xc", [128, D], F32)
        xnb_r = R2("p4_xnb", [128, D], BF16)
        mu_r = R2("p4_mu", [128, 1], F32)
        rstd_r = R2("p4_rstd", [128, 1], F32)
        lsq = P.sbuf("p4_sq", [128, D], F32)
        lacc_r = R2("p4_acc", [128, 1], F32)
        ptr_r = Rot([P.psum(f"p4_ptr{i}", [128, D], BF16) for i in range(2)])
        stg_r = R2("p4_stg", [128, D], BF16)
        for ti in range(TLOC // 128):
            sl = slice(ti * 128, (ti + 1) * 128)
            xt, at, ft, xc = xt_r.nxt(), at_r.nxt(), ft_r.nxt(), xc_r.nxt()
            xnb, mu = xnb_r.nxt(), mu_r.nxt()
            rstd, lacc = rstd_r.nxt(), lacc_r.nxt()
            ptr, stg = ptr_r.nxt(), stg_r.nxt()
            P.dma("sync", xt[:], x[sl, :])
            P.dma("sync", at[:], T["att0_tm"][sl, :])
            P.dma("sync", ft[:], T["ffn0_tm"][sl, :])
            P.tt("vector", xt, xt, at, ALU.add)
            P.tt("vector", xt, xt, ft, ALU.add)
            P.dma("gpsimd", T["x1_tm"][sl, :], xt)
            ln_stats(P, xt, mu, rstd, xc, lsq, lacc)
            P.ts("vector", xc[:], xc[:], rstd[:], ALU.mult)
            P.dma("gpsimd", T["xn1_tm"][sl, :], xc)
            P.cp("scalar", xnb, xc)
            tm_tile_to_fm(P, "p4", xnb, eye_bf, T["xn1_fm"], ti, ptr, stg)



@phase("p5_att1",
       reads=["xn1_fm", "L1_attn_w_qkv", "L1_attn_b_qkv", "L1_attn_w_out",
              "L1_attn_b_out", "c_eye_bf", "c_ones_f32", "c_ones_bf"],
       writes=["att1_tm"],
       privs=["qkv1_fm", "v1_tm", "ctx1_fm"])
def p5_att1(P, T):
    eye = T["c_eye_bf"]
    P.ones_f32 = T["c_ones_f32"]
    P.ones_bf_col = _ap(T["c_ones_bf"])[:, 0:1]
    P.ones_f32_col64 = _ap(T["c_ones_f32"])[0:64, 0:1]
    P.ones_f32_row64 = _ap(T["c_ones_f32"])[0:1, 0:64]
    NH = 16

    P.scope = "p5_b"
    with P.phase("p5b"):
        wqkvT = P.sbuf("p5_wqkvT", [128, DC, 3 * D], BF16)
        load_wT(P, "p5qkv", T["L1_attn_w_qkv"], wqkvT, eye, n_out=3 * D)
        bq = P.sbuf("p5_bq", [128, 24], F32)
        for oc in range(24):
            P.dma("sync", bq[:, oc:oc + 1],
                  T["L1_attn_b_qkv"][oc * 128:(oc + 1) * 128]
                  .rearrange("(p a) -> p a", a=1))
        xt = P.sbuf("p5_xt", [128, DC, 512], BF16)
        pos = Rot([P.psum(f"p5_po{i}", [128, 512], F32) for i in range(4)])
        obs = Rot([P.sbuf(f"p5_ob{i}", [128, 512], BF16) for i in range(4)])
        for b in range(BLOC):
            for tmac in range(S // 512):
                t0 = b * S + tmac * 512
                P.dma("sync", xt[:],
                      T["xn1_fm"][:, :, t0:t0 + 512]
                      .rearrange("c p t -> p c t"))
                for oc in range(24):
                    po, ob = pos.nxt(), obs.nxt()
                    for dc in range(DC):
                        P.mm(po, wqkvT[:, dc, oc * 128:(oc + 1) * 128],
                             xt[:, dc, :], start=(dc == 0), stop=(dc == DC - 1))
                    P.ts("vector", ob, po, _ap(bq)[:, oc:oc + 1], ALU.add)
                    P.dma("gpsimd",
                          T["qkv1_fm"][b, oc, :, tmac * 512:(tmac + 1) * 512],
                          ob)
                for st in range(4):
                    tc = tmac * 4 + st
                    for half in range(2):
                        po, ob = pos.nxt(), obs.nxt()
                        rhsv = _ap(wqkvT)[:, :, :].rearrange(
                            "p c (h x) -> p c h x", h=NH)[
                            :, :, half * 8:(half + 1) * 8, 128:192]
                        for dc in range(DC):
                            P.mm(po, xt[:, dc, st * 128:(st + 1) * 128],
                                 rhsv[:, dc], start=(dc == 0),
                                 stop=(dc == DC - 1))
                        P.cp("vector", ob, po)
                        P.dma("gpsimd",
                              T["v1_tm"][b, tc, :, half * 512:(half + 1) * 512],
                              ob)

    P.scope = "p5_c"
    with P.phase("p5c"):
        v_r = Rot([P.sbuf(f"p5_vs{i}", [128, S // 128, NH * 65], BF16)
                   for i in range(2)])
        qh_r = Rot([P.sbuf(f"p5_qh{i}", [64, S], BF16) for i in range(2)])
        kh_r = Rot([P.sbuf(f"p5_kh{i}", [64, S], BF16) for i in range(2)])
        sq_s = P.sbuf("p5_sqs", [64, S], F32)
        nmax = P.sbuf("p5_nmax", [1, 2], F32)
        tmp1 = P.sbuf("p5_tmp1", [1, 1], F32)
        negc = Rot([P.sbuf(f"p5_negc{i}", [128, 1], F32) for i in range(2)])
        ps_n = P.psum("p5_psn", [128, 512], F32)
        ps_sT = Rot([P.psum(f"p5_psT{i}", [128, 1024], F32) for i in range(2)])
        E_r = Rot([P.sbuf(f"p5_E{i}", [128, 1024], BF16) for i in range(2)])
        ps_ctx = Rot([P.psum(f"p5_psctx{i}", [128, 512], F32) for i in range(2)])
        ps_sig = P.psum("p5_pssig", [128, 512], F32)
        sig_r = Rot([P.sbuf(f"p5_sig{i}", [1, 512], F32) for i in range(2)])
        sg_r = Rot([P.sbuf(f"p5_sg{i}", [64, 512], F32) for i in range(2)])
        ctx_r = Rot([P.sbuf(f"p5_ctxb{i}", [64, 512], BF16) for i in range(2)])
        bq2 = P.sbuf("p5_bq2", [128, 24], F32)
        for oc in range(24):
            P.dma("sync", bq2[:, oc:oc + 1],
                  T["L1_attn_b_qkv"][oc * 128:(oc + 1) * 128]
                  .rearrange("(p a) -> p a", a=1))
        hb_bufs = (sq_s, ps_n, nmax, tmp1)
        at_bufs = (ps_sT, E_r, ps_ctx, ps_sig, sig_r, sg_r, ctx_r)
        for vb in v_r.items:
            P.memset("vector",
                 _ap(vb).rearrange("p c (h w) -> p c h w", w=65)[:, :, :, 64:65],
                 1.0)
        for b in range(BLOC):
            v_s = v_r.nxt()
            for tc in range(S // 128):
                P.dma("gpsimd",
                      _ap(v_s)[:, tc, :].rearrange("p (h w) -> p h w", w=65)
                      [:, :, 0:64],
                      T["v1_tm"][b, tc, :, :].rearrange("p (h w) -> p h w", w=64))
            for h in range(NH):
                fq, fk, fv = 192 * h, 192 * h + 64, 192 * h + 128
                qh_s, kh_s = qh_r.nxt(), kh_r.nxt()
                P.dma("sync", qh_s[:],
                      T["qkv1_fm"][b, fq // 128, fq % 128:fq % 128 + 64, :])
                P.dma("sync", kh_s[:],
                      T["qkv1_fm"][b, fk // 128, fk % 128:fk % 128 + 64, :])
                bv_col = _ap(bq2)[fv % 128:fv % 128 + 64,
                                  fv // 128:fv // 128 + 1]
                nc_ = negc.nxt()
                head_bound(P, hb_bufs, _ap(qh_s), _ap(kh_s), nc_, 0.125)
                softmax_TQ(P, at_bufs, _ap(kh_s), _ap(qh_s), v_s, h, nc_,
                           T["ctx1_fm"][b, h // 2,
                                        64 * (h % 2):64 * (h % 2) + 64, :],
                           0.125, bv=bv_col)

    P.scope = "p5_d"
    with P.phase("p5d"):
        woT = P.sbuf("p5_woT", [128, DC, D], BF16)
        load_wT(P, "p5wo", T["L1_attn_w_out"], woT, eye)
        bor = row_of(P, "p5bo", T["L1_attn_b_out"], D)
        bo = bcast_row(P, "p5bo", bor, D, P.ones_f32)
        ots = Rot([P.sbuf(f"p5_ot{i}", [128, DC, 128], BF16) for i in range(2)])
        pws = Rot([P.psum(f"p5_pw{i}", [128, 512], F32) for i in range(4)])
        oss = Rot([P.sbuf(f"p5_os{i}", [128, D], F32) for i in range(2)])
        for b in range(BLOC):
            for tt_ in range(S // 128):
                ot = ots.nxt()
                os_ = oss.nxt()
                P.dma("sync", ot[:],
                      T["ctx1_fm"][b, :, :, tt_ * 128:(tt_ + 1) * 128]
                      .rearrange("c p t -> p c t"))
                for half in range(2):
                    pw = pws.nxt()
                    hsl = slice(half * 512, (half + 1) * 512)
                    for ec in range(DC):
                        P.mm(pw, ot[:, ec, :], woT[:, ec, hsl],
                             start=(ec == 0), stop=(ec == DC - 1))
                    P.tt("vector", os_[:, hsl], pw, _ap(bo)[:, hsl], ALU.add)
                P.dma("gpsimd",
                      T["att1_tm"][b * S + tt_ * 128:b * S + (tt_ + 1) * 128, :],
                      os_)


def rms_tile(P, xt, gbc, out32, ssq, bufs):
    sq, r = bufs
    P.act(sq, xt, AF.Square, accum=ssq)
    P.act(ssq, ssq, AF.Ln, bias=1e-30)
    P.act(r, ssq, AF.Exp, scale=-0.5)
    P.ts("vector", out32, xt, r, ALU.mult)
    P.tt("vector", out32, out32, gbc, ALU.mult)



@phase("p6_sm_a",
       reads=["xn1_tm", "L1_ffn_gamma_x", "L1_ffn_gamma_slot",
              "L1_ffn_slot_embeds", "L1_ffn_w1", "L1_ffn_w2",
              "c_eye_bf", "c_ones_f32", "c_ones_bf"],
       writes=["xnn_fm", "se_fm", "proc_tm"],
       privs=["xnn_tmb", "w1l1_bf", "w2l1_bf"])
def p6_sm_a(P, T):
    eye = T["c_eye_bf"]
    P.ones_f32 = T["c_ones_f32"]
    NES = 4096
    FC = 32

    P.scope = "p6_prep"
    with P.phase("p6prep"):
        wt32 = P.sbuf("p6_wt32", [128, 4096], F32)
        wtb = P.sbuf("p6_wtb", [128, 4096], BF16)
        for dc in range(DC):
            P.dma("sync", wt32[:], T["L1_ffn_w1"][dc * 128:(dc + 1) * 128, :])
            P.cp("vector", wtb, wt32)
            P.dma("gpsimd",
                  T["w1l1_bf"][:, :, dc * 128:(dc + 1) * 128]
                  .rearrange("c p f -> p c f"),
                  wtb[:].rearrange("p (c f) -> p c f", c=32))
        for fc in range(FC):
            P.dma("sync", _ap(wt32)[:, 0:D],
                  T["L1_ffn_w2"][fc * 128:(fc + 1) * 128, :])
            P.cp("vector", _ap(wtb)[:, 0:D], _ap(wt32)[:, 0:D])
            P.dma("gpsimd", T["w2l1_bf"][fc, :, :], _ap(wtb)[:, 0:D])
        gxr = row_of(P, "p6gx", T["L1_ffn_gamma_x"], D)
        gsr = row_of(P, "p6gs", T["L1_ffn_gamma_slot"], D)
        gx = bcast_row(P, "p6gx", gxr, D, P.ones_f32, scale=32.0)
        gs = bcast_row(P, "p6gs", gsr, D, P.ones_f32, scale=32.0)
        R2 = lambda nm, sh, dt: Rot([P.sbuf(f"{nm}{i}", sh, dt) for i in range(2)])
        xt_r = R2("p6_xt", [128, D], F32)
        o32_r = R2("p6_o32", [128, D], F32)
        ob_r = R2("p6_ob", [128, D], BF16)
        sq = P.sbuf("p6_sq", [128, D], F32)
        ssq_r = R2("p6_ssq", [128, 1], F32)
        rr_r = R2("p6_rr", [128, 1], F32)
        ptr_r = Rot([P.psum(f"p6_ptr{i}", [128, D], BF16) for i in range(2)])
        stg_r = R2("p6_stg", [128, D], BF16)
        for ti in range(TLOC // 128):
            b, tt_ = ti // (S // 128), ti % (S // 128)
            xt, o32, ob = xt_r.nxt(), o32_r.nxt(), ob_r.nxt()
            ssq, rr, ptr, stg = ssq_r.nxt(), rr_r.nxt(), ptr_r.nxt(), stg_r.nxt()
            P.dma("sync", xt[:], T["xn1_tm"][ti * 128:(ti + 1) * 128, :])
            rms_tile(P, xt, gx, o32, ssq, (sq, rr))
            P.cp("scalar", ob, o32)
            P.dma("gpsimd", T["xnn_tmb"][b, tt_, :, :], ob)
            tm_tile_to_fm(P, "p6", ob, eye, T["xnn_fm"][b], tt_, ptr, stg)
        sed = T["L1_ffn_slot_embeds"].rearrange("e t d -> (e t) d")
        for si in range(NES // 128):
            xt, o32, ob = xt_r.nxt(), o32_r.nxt(), ob_r.nxt()
            ssq, rr, ptr, stg = ssq_r.nxt(), rr_r.nxt(), ptr_r.nxt(), stg_r.nxt()
            P.dma("sync", xt[:], sed[si * 128:(si + 1) * 128, :])
            rms_tile(P, xt, gs, o32, ssq, (sq, rr))
            P.cp("scalar", ob, o32)
            tm_tile_to_fm(P, "p6se", ob, eye, T["se_fm"], si, ptr, stg)

    P.scope = "p6_a"
    with P.phase("p6a"):
        xnn_fm_s = P.sbuf("p6_xf", [128, DC, S], BF16)
        xnn_tm_s = P.sbuf("p6_xt2", [128, S // 128, D], BF16)
        se_s = P.sbuf("p6_ses", [128, DC, 512], BF16)
        dT_s = P.sbuf("p6_dT", [128, S // 128, 512], BF16)
        disp_r = Rot([P.sbuf(f"p6_disp{i}", [128, S], BF16) for i in range(2)])
        slots_s = P.sbuf("p6_slots", [128, DC, 512], BF16)
        h_s = P.sbuf("p6_h", [128, FC, 512], BF16)
        proc_s = P.sbuf("p6_proc", [128, 4, D], BF16)
        w1t = Rot([P.sbuf(f"p6_w1t{i}", [128, DC, 128], BF16) for i in range(2)])
        w2t = Rot([P.sbuf(f"p6_w2t{i}", [128, 512], BF16) for i in range(2)])
        mx = P.sbuf("p6_mx", [128, 1], F32)
        tmx = P.sbuf("p6_tmx", [128, 1], F32)
        as_ = P.sbuf("p6_as", [128, 4], F32)
        rsum = P.sbuf("p6_rsum", [128, 1], F32)
        pb = [P.psum(f"p6_pb{i}", [128, 512], F32) for i in range(6)]
        ptr_bf = P.psum("p6_ptrb", [128, 1024], BF16)
        for b in range(BLOC):
            P.dma("sync", xnn_fm_s[:],
                  T["xnn_fm"][b].rearrange("c p t -> p c t"))
            P.dma("sync", xnn_tm_s[:],
                  T["xnn_tmb"][b].rearrange("c p d -> p c d"))
            for em in range(8):
                e0 = em * 512
                P.dma("sync", se_s[:],
                      T["se_fm"][:, :, e0:e0 + 512].rearrange("c p t -> p c t"))
                def disp_tail(sc, scl, dsp):
                    def exps():
                        for tq in range(4):
                            P.act(_ap(dsp)[:, tq * 512:(tq + 1) * 512], pb[tq],
                                  AF.Exp, bias=mx, accum=_ap(as_)[:, tq:tq + 1])
                            yield
                        P.red("vector", rsum, as_, ALU.add)
                        P.recip(rsum, rsum)
                        P.ts("vector", dsp, dsp, rsum, ALU.mult)
                        yield
                        for g in range(2):
                            for j in range(8):
                                tc = g * 8 + j
                                P.transp(ptr_bf[:, j * 128:(j + 1) * 128],
                                         _ap(dsp)[:, tc * 128:(tc + 1) * 128],
                                         eye, register=(j == 7), war=(j == 0))
                            P.cp("vector",
                                 dT_s[:, g * 8:(g + 1) * 8, scl],
                                 ptr_bf[:].rearrange("p (c t) -> p c t", c=8))
                            yield
                    return exps()

                pend = None
                for sc in range(4):
                    scl = slice(sc * 128, (sc + 1) * 128)
                    for tq in range(4):
                        if pend is not None:
                            next(pend, None)
                        for dc in range(DC):
                            P.mm(pb[tq], se_s[:, dc, scl],
                                 xnn_fm_s[:, dc, tq * 512:(tq + 1) * 512],
                                 start=(dc == 0), stop=(dc == DC - 1))
                    if pend is not None:
                        for _ in pend:
                            pass
                    for tq in range(4):
                        if tq == 0:
                            P.red("vector", mx, pb[0], ALU.max)
                        else:
                            P.red("vector", tmx, pb[tq], ALU.max)
                            P.tt("vector", mx, mx, tmx, ALU.max)
                    P.ts("vector", mx, mx, -1.0, ALU.mult)
                    pend = disp_tail(sc, scl, disp_r.nxt())
                for _ in pend:
                    pass
                for dc in range(DC):
                    p_ = pb[4]
                    for tc in range(S // 128):
                        P.mm(p_, xnn_tm_s[:, tc, dc * 128:(dc + 1) * 128],
                             dT_s[:, tc, :], start=(tc == 0),
                             stop=(tc == S // 128 - 1))
                    P.cp("vector", slots_s[:, dc, :], p_)
                for fc in range(FC):
                    w1c = w1t.nxt()
                    P.dma("sync", w1c[:].rearrange("p c f -> p (c f)"),
                          T["w1l1_bf"][fc, :, :])
                    p_ = pb[5]
                    for dc in range(DC):
                        P.mm(p_, w1c[:, dc, :], slots_s[:, dc, :],
                             start=(dc == 0), stop=(dc == DC - 1))
                    P.act(h_s[:, fc, :], p_, AF.Gelu)
                for half in range(2):
                    hsl = slice(half * 512, (half + 1) * 512)
                    for fc in range(FC):
                        w2c = w2t.nxt()
                        P.dma("sync", w2c[:], T["w2l1_bf"][fc, :, hsl])
                        for st in range(4):
                            P.mm(pb[st], h_s[:, fc, st * 128:(st + 1) * 128],
                                 w2c[:], start=(fc == 0), stop=(fc == FC - 1))
                    for st in range(4):
                        P.cp("vector", proc_s[:, st, hsl], pb[st])
                for st in range(4):
                    P.dma("gpsimd",
                          T["proc_tm"][b, e0 + st * 128:e0 + (st + 1) * 128, :],
                          proc_s[:, st, :])



@phase("p7_sm_b",
       reads=["xnn_fm", "se_fm", "proc_tm", "c_eye_bf", "c_ones_f32"],
       writes=["ffn1_tm"])
def p7_sm_b(P, T):
    eye = T["c_eye_bf"]
    NES = 4096
    with P.phase("p7"):
        se_fm_s = P.sbuf("p7_se", [128, DC, NES], BF16)
        xnn_fm_s = P.sbuf("p7_xf", [128, DC, S], BF16)
        combT_s = P.sbuf("p7_cT", [128, NES // 128, 512], BF16)
        comb_r = Rot([P.sbuf(f"p7_comb{i}", [128, NES], BF16) for i in range(2)])
        lg_r = Rot([P.sbuf(f"p7_lg{i}", [128, NES], F32) for i in range(2)])
        out_s = P.sbuf("p7_out", [128, 4, D], F32)
        proc_t = Rot([P.sbuf(f"p7_pt{i}", [128, 512], BF16) for i in range(2)])
        mx = P.sbuf("p7_mx", [128, 1], F32)
        rsum = P.sbuf("p7_rsum", [128, 1], F32)
        sq8 = P.sbuf("p7_sq8", [128, 8], F32)
        pb = [P.psum(f"p7_pb{i}", [128, 512], F32) for i in range(4)]
        ptr_bf = P.psum("p7_ptrb", [128, 1024], BF16)
        P.dma("sync", se_fm_s[:], T["se_fm"].rearrange("c p t -> p c t"))
        for b in range(BLOC):
            P.dma("sync", xnn_fm_s[:],
                  T["xnn_fm"][b].rearrange("c p t -> p c t"))
            for sm in range(4):
                s0 = sm * 512
                def comb_tail(tq, lg, cmb):
                    def gen():
                        P.red("vector", mx, lg, ALU.max)
                        P.ts("vector", mx, mx, -1.0, ALU.mult)
                        P.act(cmb, lg, AF.Exp, bias=mx, accum=rsum)
                        P.recip(rsum, rsum)
                        P.ts("vector", cmb, cmb, rsum, ALU.mult)
                        yield
                        for g in range(4):
                            for j in range(8):
                                etc = g * 8 + j
                                P.transp(ptr_bf[:, j * 128:(j + 1) * 128],
                                         _ap(cmb)[:, etc * 128:(etc + 1) * 128],
                                         eye, register=(j == 7), war=(j == 0))
                            P.cp("vector",
                                 combT_s[:, g * 8:(g + 1) * 8,
                                         tq * 128:(tq + 1) * 128],
                                 ptr_bf[:].rearrange("p (c t) -> p c t", c=8))
                            yield
                    return gen()

                pend = None
                for tq in range(4):
                    tsl = slice(s0 + tq * 128, s0 + (tq + 1) * 128)
                    lg = lg_r.nxt()
                    for eb in range(8):
                        if pend is not None:
                            next(pend, None)
                        p_ = pb[eb % 4]
                        for dc in range(DC):
                            P.mm(p_, xnn_fm_s[:, dc, tsl],
                                 se_fm_s[:, dc, eb * 512:(eb + 1) * 512],
                                 start=(dc == 0), stop=(dc == DC - 1))
                        P.cp("vector", _ap(lg)[:, eb * 512:(eb + 1) * 512], p_)
                    if pend is not None:
                        for _ in pend:
                            pass
                    pend = comb_tail(tq, lg, comb_r.nxt())
                for _ in pend:
                    pass
                for half in range(2):
                    hsl = slice(half * 512, (half + 1) * 512)
                    for etc in range(NES // 128):
                        pc = proc_t.nxt()
                        P.dma("sync", pc[:],
                              T["proc_tm"][b, etc * 128:(etc + 1) * 128, hsl])
                        for st in range(4):
                            P.mm(pb[st], combT_s[:, etc, st * 128:(st + 1) * 128],
                                 pc[:], start=(etc == 0),
                                 stop=(etc == NES // 128 - 1))
                    for st in range(4):
                        P.cp("vector", out_s[:, st, hsl], pb[st])
                for st in range(4):
                    P.dma("gpsimd",
                          T["ffn1_tm"][b * S + s0 + st * 128:
                                       b * S + s0 + (st + 1) * 128, :],
                          out_s[:, st, :])



@phase("p8_out",
       reads=["x1_tm", "att1_tm", "ffn1_tm"],
       writes=["out"])
def p8_out(P, T):
    o = T["out"].rearrange("b s d -> (b s) d")
    with P.phase("p8"):
        xts = Rot([P.sbuf(f"p8_xt{i}", [128, D], F32) for i in range(2)])
        ats = Rot([P.sbuf(f"p8_at{i}", [128, D], F32) for i in range(2)])
        fts = Rot([P.sbuf(f"p8_ft{i}", [128, D], F32) for i in range(2)])
        for ti in range(TLOC // 128):
            sl = slice(ti * 128, (ti + 1) * 128)
            xt, at, ft = xts.nxt(), ats.nxt(), fts.nxt()
            P.dma("sync", xt[:], T["x1_tm"][sl, :])
            P.dma("sync", at[:], T["att1_tm"][sl, :])
            P.dma("sync", ft[:], T["ffn1_tm"][sl, :])
            P.tt("vector", xt, xt, at, ALU.add)
            P.tt("vector", xt, xt, ft, ALU.add)
            P.dma("gpsimd", o[sl, :], xt)



def make_consts():
    return {
        "c_eye_f32": np.eye(128, dtype=np.float32),
        "c_eye_bf": np.eye(128, dtype=ml_bf16()),
        "c_mask_ut": np.triu(np.ones((128, 128), dtype=np.float32)),
        "c_ones_f32": np.ones((128, 128), dtype=np.float32),
        "c_ones_bf": np.ones((128, 128), dtype=ml_bf16()),
    }


def ml_bf16():
    import ml_dtypes
    return ml_dtypes.bfloat16


def _declare(P, names, kind):
    out = {}
    for n in names:
        sh, dt = spec_of(n)
        out[n] = P.dram(n, sh, dt, kind=kind)
    return out


def run_unfused(inputs, upto=None, only=None):
    consts = make_consts()
    x = np.ascontiguousarray(inputs["L0_attn_hidden_states"])
    pools = [dict() for _ in range(NCORES)]
    phases = PHASES[:upto] if upto else PHASES
    if only:
        phases = [p for p in phases if p[0] in only]
    for (name, fn, reads, writes, privs) in phases:
        P = Prog(name)
        T = {}
        T.update(_declare(P, reads, "ExternalInput"))
        T.update(_declare(P, writes, "ExternalOutput"))
        T.update(_declare(P, privs, "Internal"))
        for n in reads:
            if n in CONST_SPECS:
                sh, dt = CONST_SPECS[n]
                sb = P.sbuf("s_" + n, sh, dt)
                P.dma("sync", sb[:], T[n])
                T[n] = sb
        fn(P, T)
        P.barrier()
        nc = P.finalize()
        in_maps = []
        for c in range(NCORES):
            m = {}
            for n in reads:
                if n == "L0_attn_hidden_states":
                    m[n] = x[BLOC * c:BLOC * (c + 1)]
                elif n in INPUT_SPECS:
                    m[n] = inputs[n]
                elif n in CONST_SPECS:
                    m[n] = consts[n]
                else:
                    m[n] = pools[c][n]
            in_maps.append(m)
        res = run_bass_kernel_spmd(nc, in_maps, core_ids=list(range(NCORES)))
        for c in range(NCORES):
            pools[c].update(res.results[c])
    return pools


def run_fused(inputs):
    consts = make_consts()
    x = np.ascontiguousarray(inputs["L0_attn_hidden_states"])
    P = Prog("mega")
    T = {}
    ext_in, scratch = [], set()
    for (name, fn, reads, writes, privs) in PHASES:
        for n in reads:
            if n in INPUT_SPECS or n in CONST_SPECS:
                if n not in T:
                    T[n] = _declare(P, [n], "ExternalInput")[n]
                    ext_in.append(n)
            else:
                scratch.add(n)
        for n in writes:
            if n != "out":
                scratch.add(n)
        for n in privs:
            scratch.add(n)
    T.update(_declare(P, sorted(scratch), "Internal"))
    T.update(_declare(P, ["out"], "ExternalOutput"))
    for n in list(T):
        if n in CONST_SPECS:
            sh, dt = CONST_SPECS[n]
            sb = P.sbuf("s_" + n, sh, dt)
            P.dma("sync", sb[:], T[n])
            T[n] = sb
    for (name, fn, reads, writes, privs) in PHASES:
        P.scope = name
        fn(P, T)
    P.scope = None
    P.barrier()
    nc = P.finalize()
    in_maps = []
    for c in range(NCORES):
        m = {}
        for n in ext_in:
            if n == "L0_attn_hidden_states":
                m[n] = x[BLOC * c:BLOC * (c + 1)]
            elif n in CONST_SPECS:
                m[n] = consts[n]
            else:
                m[n] = inputs[n]
        in_maps.append(m)
    res = run_bass_kernel_spmd(nc, in_maps, core_ids=list(range(NCORES)))
    return np.concatenate([res.results[c]["out"] for c in range(NCORES)],
                          axis=0)


FUSED = True


def kernel(**inputs):
    if FUSED:
        return run_fused(inputs)
    pools = run_unfused(inputs)
    return np.concatenate([pools[c]["out"] for c in range(NCORES)], axis=0)


# version 10
# speedup vs baseline: 1.1992x; 1.0177x over previous
import contextlib
import numpy as np
import concourse.bass as bass
import concourse.mybir as mybir
from concourse.bass_utils import run_bass_kernel_spmd

F32 = mybir.dt.float32
BF16 = mybir.dt.bfloat16
AF = mybir.ActivationFunctionType
ALU = mybir.AluOpType
AX = mybir.AxisListType

NCORES = 8
BLOC = 2
S = 2048
TLOC = BLOC * S
D = 1024
DC = D // 128

ENGINES = ("tensor", "vector", "scalar", "gpsimd", "sync")
DMA_QUEUES = ("gpsimd", "sync")
NSEM = 4


def _ap_name(x):
    if isinstance(x, str):
        return x
    t = getattr(x, "tensor", None)
    if t is not None:
        return t.name
    n = getattr(x, "name", None)
    if n is not None:
        return n
    raise ValueError(f"cannot extract tensor from {x!r}")



def _ap(x):
    if isinstance(x, (float, int)) or x is None:
        return x
    if getattr(x, "tensor", None) is not None:
        return x
    return x[:]

class Prog:

    def __init__(self, name="prog"):
        self.name = name
        self.nc = bass.Bass("TRN2", target_bir_lowering=False)
        self.es = contextlib.ExitStack()
        self.phase_es = None
        self.sems = {}
        self.cnt = {e: 0 for e in ENGINES}
        self.opbuf = {e: [] for e in ENGINES}
        self.reg = {}
        self.aps = {}
        self._cvecs = {}
        self.scope = None
        for e in ENGINES:
            if e in DMA_QUEUES:
                for j in range(NSEM):
                    self.sems[(e, j)] = self.es.enter_context(
                        self.nc.semaphore(f"sem_{e}{j}"))
            else:
                self.sems[e] = self.es.enter_context(
                    self.nc.semaphore(f"sem_{e}"))
        self.known = {e: {} for e in ENGINES}

    def dram(self, name, shape, dtype, kind):
        ap = self.nc.dram_tensor(name, list(shape), dtype, kind=kind).ap()
        self.aps[name] = ap
        return ap

    def sbuf(self, name, shape, dtype):
        if self.phase_es is not None:
            return self.phase_es.enter_context(
                self.nc.sbuf_tensor(name, list(shape), dtype))
        return self.nc.alloc_sbuf_tensor(name, list(shape), dtype)

    def psum(self, name, shape, dtype=F32):
        if self.phase_es is not None:
            return self.phase_es.enter_context(
                self.nc.psum_tensor(name, list(shape), dtype))
        return self.nc.alloc_psum_tensor(name, list(shape), dtype)

    def cvec(self, val, dtype=F32, rows=128):
        key = (float(val), dtype, rows)
        if key not in self._cvecs:
            t = self.nc.alloc_sbuf_tensor(f"cv{len(self._cvecs)}", [rows, 1], dtype,
                                          side="right")
            self.memset("vector", t[:], float(val))
            self._cvecs[key] = t[:]
        return self._cvecs[key]

    PHASE_GROUPS = [("p0", "p1proj"), ("p2a", "p2b"), ("p4", "p5b"),
                    ("p7", "p8")]

    def _group_of(self, tag):
        for g in self.PHASE_GROUPS:
            if tag in g:
                return g
        return (tag,)

    @contextlib.contextmanager
    def phase(self, tag):
        g = self._group_of(tag)
        if getattr(self, "_open_group", None) == g:
            yield
            if tag == g[-1]:
                self._open_group = None
                self.barrier()
                self.phase_es.close()
                self.phase_es = self._prev_es
            return
        prev = self.phase_es
        self.phase_es = contextlib.ExitStack()
        if len(g) > 1 and tag == g[0]:
            self._open_group = g
            self._prev_es = prev
            yield
            return
        try:
            yield
        finally:
            self.barrier()
            self.phase_es.close()
            self.phase_es = prev

    def _covers(self, eng, idx):
        if eng in DMA_QUEUES:
            out = []
            for j in range(NSEM):
                cnt = (idx - 1 - j) // NSEM + 1 if idx - 1 >= j else 0
                if cnt > 0:
                    out.append(((eng, j), 16 * cnt))
            return out
        return [(eng, idx)]

    def emit(self, eng, thunk, reads=(), writes=(), register=True, war=True):
        waits = {}

        def need(src_eng, src_idx):
            if src_idx <= 0:
                return
            if src_eng == eng == "tensor":
                return
            for key, v in self._covers(src_eng, src_idx):
                if v <= self.known[eng].get(key, 0):
                    continue
                waits[key] = max(waits.get(key, 0), v)

        rnames = [_ap_name(r) for r in reads]
        wnames = [_ap_name(w) for w in writes]
        for n in rnames:
            e = self.reg.get(n)
            if e:
                for we, wi in e["w"].items():
                    need(we, wi)
        for n in wnames:
            e = self.reg.get(n)
            if e:
                for we, wi in e["w"].items():
                    need(we, wi)
                if war:
                    for re_, ri in e["r"].items():
                        need(re_, ri)

        self.cnt[eng] += 1
        idx = self.cnt[eng]
        if eng in DMA_QUEUES:
            semkey = (eng, (idx - 1) % NSEM)
            pv = 16 * ((idx - 1) // NSEM)
            if pv > 0 and pv > self.known[eng].get(semkey, 0):
                waits[semkey] = max(waits.get(semkey, 0), pv)
            inc = 16
        else:
            semkey, inc = eng, 1
        for s, v in waits.items():
            self.known[eng][s] = v
        self.opbuf[eng].append((sorted(waits.items(), key=str),
                                (semkey, inc), thunk, self.scope))
        for n in rnames:
            e = self.reg.setdefault(n, {"w": {}, "r": {}})
            e["r"][eng] = idx
        for n in wnames:
            e = self.reg.setdefault(n, {"w": {}, "r": {}})
            if register:
                e["r"] = {}
                if eng not in DMA_QUEUES:
                    e["w"] = {k: v for k, v in e["w"].items()
                              if k in DMA_QUEUES or k != eng}
                e["w"][eng] = idx
        return idx

    def barrier(self):
        snap = {e: self.cnt[e] for e in ENGINES}
        targets = []
        for s in ENGINES:
            if snap[s] == 0:
                continue
            if s in DMA_QUEUES:
                for j in range(NSEM):
                    u = (snap[s] - j + NSEM - 1) // NSEM
                    if u > 0:
                        targets.append(((s, j), 16 * u, s))
            else:
                targets.append((s, snap[s], s))
        for e in ENGINES:
            waits = []
            for key, v, src in targets:
                if src == e:
                    continue
                if v > self.known[e].get(key, 0):
                    waits.append((key, v))
                    self.known[e][key] = v
            if waits:
                self.opbuf[e].append((waits, None, None, self.scope))

    def finalize(self):
        with self.nc.Block() as blk:
            for e in ENGINES:
                ops = self.opbuf[e]
                if not ops:
                    continue
                sec = getattr(blk, e)

                def body(eng, ops=ops, e=e):
                    cur = [None]
                    st = contextlib.ExitStack()
                    for waits, inc_spec, thunk, scope in ops:
                        if scope != cur[0]:
                            st.close()
                            st = contextlib.ExitStack()
                            if scope is not None:
                                st.enter_context(self.nc.named_scope(scope))
                            cur[0] = scope
                        for (key, val) in waits:
                            eng.wait_ge(self.sems[key], val)
                        if thunk is not None:
                            ins = thunk(eng)
                            semkey, inc = inc_spec
                            ins.then_inc(self.sems[semkey], inc)
                    st.close()

                sec(body)
        self.es.close()
        return self.nc

    def dma(self, q, out, in_, **kw):
        assert q in DMA_QUEUES
        out, in_ = _ap(out), _ap(in_)
        self.emit(q, lambda e: e.dma_start(out=out, in_=in_, **kw),
                  reads=[in_], writes=[out])

    def mm(self, out, lhsT, rhs, start=True, stop=True, register=None,
           war=None):
        out, lhsT, rhs = _ap(out), _ap(lhsT), _ap(rhs)
        if register is None:
            register = stop
        if war is None:
            war = start
        self.emit("tensor",
                  lambda e: e.matmul(out, lhsT, rhs, start=start, stop=stop),
                  reads=[lhsT, rhs], writes=[out],
                  register=register, war=war)

    def transp(self, out, in_, ident, register=True, war=True):
        out, in_, ident = _ap(out), _ap(in_), _ap(ident)
        self.emit("tensor", lambda e: e.transpose(out, in_, ident),
                  reads=[in_, ident], writes=[out], register=register, war=war)

    def cp(self, eng, out, in_):
        out, in_ = _ap(out), _ap(in_)
        if eng == "scalar":
            self.emit(eng, lambda e: e.copy(out, in_),
                      reads=[in_], writes=[out])
        else:
            self.emit(eng, lambda e: e.tensor_copy(out, in_),
                      reads=[in_], writes=[out])

    def tt(self, eng, out, a, b, op):
        out, a, b = _ap(out), _ap(a), _ap(b)
        self.emit(eng, lambda e: e.tensor_tensor(out, a, b, op),
                  reads=[a, b], writes=[out])

    def ts(self, eng, out, a, s1, op, s2=None, op2=None, accum=None):
        out, a, s1, s2, accum = _ap(out), _ap(a), _ap(s1), _ap(s2), _ap(accum)
        extra = [s1] if not isinstance(s1, float) else []
        if s2 is not None and not isinstance(s2, float):
            extra.append(s2)
        kw = {}
        if op2 is not None:
            kw["op1"] = op2
        if accum is not None:
            kw["accum_out"] = accum
        wr = [out] + ([accum] if accum is not None else [])
        self.emit(eng, lambda e: e.tensor_scalar(out, a, s1, s2, op, **kw),
                  reads=[a] + extra, writes=wr)

    def stt(self, eng, out, in0, scalar, in1, op0, op1):
        out, in0, scalar, in1 = _ap(out), _ap(in0), _ap(scalar), _ap(in1)
        extra = [scalar] if not isinstance(scalar, float) else []
        self.emit(eng, lambda e: e.scalar_tensor_tensor(out, in0, scalar, in1, op0, op1),
                  reads=[in0, in1] + extra, writes=[out])

    def act(self, out, in_, func, bias=0.0, scale=1.0, accum=None):
        out, in_, bias, scale, accum = _ap(out), _ap(in_), _ap(bias), _ap(scale), _ap(accum)
        if isinstance(bias, float) and bias != 0.0:
            rows = _ap(in_).shape[0]
            bias = self.cvec(bias, rows=rows if rows in (1, 64, 128) else 128)
        extra = [x for x in (bias, scale) if not isinstance(x, float)]
        kw = {}
        if accum is not None:
            kw["accum_out"] = accum
        wr = [out] + ([accum] if accum is not None else [])
        self.emit("scalar",
                  lambda e: e.activation(out, in_, func, bias=bias, scale=scale, **kw),
                  reads=[in_] + extra, writes=wr)

    def red(self, eng, out, in_, op, absval=False, negate=False):
        out, in_ = _ap(out), _ap(in_)
        self.emit(eng,
                  lambda e: e.tensor_reduce(out, in_, axis=AX.X, op=op,
                                            apply_absolute_value=absval,
                                            negate=negate),
                  reads=[in_], writes=[out])

    def recip(self, out, in_):
        out, in_ = _ap(out), _ap(in_)
        self.emit("vector", lambda e: e.reciprocal(out, in_),
                  reads=[in_], writes=[out])

    def memset(self, eng, ap, v):
        ap = _ap(ap)
        self.emit(eng, lambda e: e.memset(ap, v), reads=[], writes=[ap])


INPUT_SPECS = {
    "L0_attn_hidden_states": ((16, 2048, 1024), F32),
    "L0_attn_Wq": ((1024, 1024), F32), "L0_attn_Wk": ((1024, 1024), F32),
    "L0_attn_Wv": ((1024, 1024), F32), "L0_attn_Wo": ((1024, 1024), F32),
    "L0_attn_norm_w": ((128,), F32),
    "L0_ffn_ln1_g": ((1024,), F32), "L0_ffn_ln1_b": ((1024,), F32),
    "L0_ffn_ln2_g": ((1024,), F32), "L0_ffn_ln2_b": ((1024,), F32),
    "L0_ffn_Wq": ((1024, 1024), F32), "L0_ffn_Wk": ((1024, 1024), F32),
    "L0_ffn_Wv": ((1024, 1024), F32), "L0_ffn_Wo": ((1024, 1024), F32),
    "L0_ffn_Wg": ((1024, 8), F32),
    "L0_ffn_W1": ((8, 1024, 4096), F32), "L0_ffn_b1": ((8, 4096), F32),
    "L0_ffn_W2": ((8, 4096, 1024), F32), "L0_ffn_b2": ((8, 1024), F32),
    "L1_attn_w_qkv": ((3072, 1024), F32), "L1_attn_b_qkv": ((3072,), F32),
    "L1_attn_w_out": ((1024, 1024), F32), "L1_attn_b_out": ((1024,), F32),
    "L1_ffn_gamma_x": ((1024,), F32), "L1_ffn_gamma_slot": ((1024,), F32),
    "L1_ffn_slot_embeds": ((8, 512, 1024), F32),
    "L1_ffn_w1": ((1024, 4096), F32), "L1_ffn_w2": ((4096, 1024), F32),
}
CONST_SPECS = {
    "c_eye_f32": ((128, 128), F32),
    "c_eye_bf": ((128, 128), BF16),
    "c_mask_ut": ((128, 128), F32),
    "c_ones_f32": ((128, 128), F32),
    "c_ones_bf": ((128, 128), BF16),
}
SCRATCH_SPECS = {
    "xn_tm": ((TLOC, D), F32),
    "xn_fm": ((DC, 128, TLOC), BF16),
    "att0_tm": ((TLOC, D), F32),
    "x2_tm": ((TLOC, D), F32),
    "m_fm": ((DC, 128, TLOC), BF16),
    "cw_tm": ((TLOC, 8), F32),
    "ffn0_tm": ((TLOC, D), F32),
    "w1m_bf": ((8, DC, 128, 4096), BF16),
    "w2m_bf": ((8, 32, 128, D), BF16),
    "x1_tm": ((TLOC, D), F32),
    "xn1_tm": ((TLOC, D), F32),
    "xn1_fm": ((DC, 128, TLOC), BF16),
    "att1_tm": ((TLOC, D), F32),
    "xnn_fm": ((BLOC, DC, 128, S), BF16),
    "se_fm": ((DC, 128, 4096), BF16),
    "proc_tm": ((BLOC, 4096, D), BF16),
    "ffn1_tm": ((TLOC, D), F32),
}
OUTPUT_SPECS = {"out": ((BLOC, S, D), F32)}

PRIV_SPECS = {
    "q0_fm": ((BLOC, DC, 128, S), BF16),
    "k0_fm": ((BLOC, DC, 128, S), BF16),
    "v0_fm": ((BLOC, DC, 128, S), BF16),
    "o0_fm": ((BLOC, DC, 128, S), BF16),
    "h0_fm": ((DC, 128, TLOC), BF16),
    "qf_fm": ((BLOC, DC, 128, S), BF16),
    "kf_fm": ((BLOC, DC, 128, S), BF16),
    "vf_tm": ((BLOC, 16, 128, D), BF16),
    "ctx0_fm": ((BLOC, DC, 128, S), BF16),
    "m_fm32": ((DC, 128, TLOC), F32),
    "qkv1_fm": ((BLOC, 24, 128, S), BF16),
    "v1_tm": ((BLOC, 16, 128, D), BF16),
    "ctx1_fm": ((BLOC, DC, 128, S), BF16),
    "xnn_tmb": ((BLOC, 16, 128, D), BF16),
    "w1l1_bf": ((32, 128, D), BF16),
    "w2l1_bf": ((32, 128, D), BF16),
}

PHASES = []


def phase(name, reads, writes, privs=()):
    def deco(fn):
        PHASES.append((name, fn, list(reads), list(writes), list(privs)))
        return fn
    return deco


def spec_of(name):
    for d in (INPUT_SPECS, CONST_SPECS, SCRATCH_SPECS, OUTPUT_SPECS, PRIV_SPECS):
        if name in d:
            sh, dt = d[name]
            if name == "L0_attn_hidden_states":
                sh = (BLOC, S, D)
            return sh, dt
    raise KeyError(name)



def ln_stats(P, xt, mu, rstd, xc, sq, acc, eps=1e-5):
    P.red("vector", mu, xt, ALU.add)
    P.ts("vector", mu, mu, 1.0 / D, ALU.mult)
    P.ts("vector", xc, xt, mu, ALU.subtract)
    P.act(sq, xc, AF.Square, accum=acc)
    P.act(acc, acc, AF.Ln, scale=1.0 / D, bias=eps)
    P.act(rstd, acc, AF.Exp, scale=-0.5)


def tm_tile_to_fm(P, tag, src_bf, eye_bf, dst_hbm_fm, ti, psum_bf, stage_bf,
                  n_chunks=DC, also_f32=None, psum32=None, eye32=None,
                  src32=None, stage32=None, dst32=None):
    for dc in range(n_chunks):
        P.transp(psum_bf[:, dc * 128:(dc + 1) * 128],
                 src_bf[:, dc * 128:(dc + 1) * 128], eye_bf,
                 register=(dc == n_chunks - 1), war=(dc == 0))
    P.cp("vector", stage_bf[:], psum_bf[:])
    P.dma("gpsimd",
          dst_hbm_fm[:, :, ti * 128:(ti + 1) * 128].rearrange("c p t -> p c t"),
          stage_bf[:].rearrange("p (c t) -> p c t", c=n_chunks))
    if also_f32:
        for dc in range(n_chunks):
            P.transp(psum32[:, dc * 128:(dc + 1) * 128],
                     src32[:, dc * 128:(dc + 1) * 128], eye32,
                     register=(dc == n_chunks - 1), war=(dc == 0))
        P.cp("vector", stage32[:], psum32[:])
        P.dma("gpsimd",
              dst32[:, :, ti * 128:(ti + 1) * 128].rearrange("c p t -> p c t"),
              stage32[:].rearrange("p (c t) -> p c t", c=n_chunks))



@phase("p0_ln0",
       reads=["L0_attn_hidden_states", "c_eye_bf"],
       writes=["xn_tm", "xn_fm"])
def p0_ln0(P, T):
    x = T["L0_attn_hidden_states"].rearrange("b s d -> (b s) d")
    eye_bf = T["c_eye_bf"]
    with P.phase("p0"):
        R2 = lambda nm, sh, dt: Rot([P.sbuf(f"{nm}{i}", sh, dt) for i in range(2)])
        xt_r = R2("p0_xt", [128, D], F32)
        xc_r = R2("p0_xc", [128, D], F32)
        xnb_r = R2("p0_xnb", [128, D], BF16)
        mu_r = R2("p0_mu", [128, 1], F32)
        rstd_r = R2("p0_rstd", [128, 1], F32)
        lsq = P.sbuf("p0_sq", [128, D], F32)
        lacc_r = R2("p0_acc", [128, 1], F32)
        ptr_r = Rot([P.psum(f"p0_ptr{i}", [128, D], BF16) for i in range(2)])
        stg_r = R2("p0_stg", [128, D], BF16)
        for ti in range(TLOC // 128):
            sl = slice(ti * 128, (ti + 1) * 128)
            xt, xc, xnb = xt_r.nxt(), xc_r.nxt(), xnb_r.nxt()
            mu, rstd, lacc = mu_r.nxt(), rstd_r.nxt(), lacc_r.nxt()
            ptr, stg = ptr_r.nxt(), stg_r.nxt()
            P.dma("sync", xt[:], x[sl, :])
            ln_stats(P, xt, mu, rstd, xc, lsq, lacc)
            P.ts("vector", xc[:], xc[:], rstd[:], ALU.mult)
            P.dma("gpsimd", T["xn_tm"][sl, :], xc[:])
            P.cp("scalar", xnb[:], xc[:])
            tm_tile_to_fm(P, "p0", xnb, eye_bf, T["xn_fm"], ti, ptr, stg)


class Rot:
    def __init__(self, items):
        self.items = list(items)
        self.i = 0

    def nxt(self):
        r = self.items[self.i % len(self.items)]
        self.i += 1
        return r


def load_wT(P, tag, w_dram, wt_sb, eye_bf, n_out=D, n_in=D, bufs=None):
    nic = n_in // 128
    if bufs is None:
        bufs = wT_bufs(P, tag, n_in)
    f32t, bft, ptr = bufs
    for oc in range(n_out // 128):
        f3, bf, pt = f32t.nxt(), bft.nxt(), ptr.nxt()
        P.dma("sync", f3[:], w_dram[oc * 128:(oc + 1) * 128, :])
        P.cp("vector", bf, f3)
        for dc in range(nic):
            P.transp(_ap(pt)[:, dc * 128:(dc + 1) * 128],
                     _ap(bf)[:, dc * 128:(dc + 1) * 128], eye_bf,
                     register=(dc == nic - 1), war=(dc == 0))
        P.cp("scalar",
             wt_sb[:, :, oc * 128:(oc + 1) * 128],
             _ap(pt).rearrange("p (c t) -> p c t", c=nic))


def wT_bufs(P, tag, n_in=D):
    return (
        Rot([P.sbuf(f"{tag}_wl32{i}", [128, n_in], F32) for i in range(2)]),
        Rot([P.sbuf(f"{tag}_wlbf{i}", [128, n_in], BF16) for i in range(2)]),
        Rot([P.psum(f"{tag}_wlps{i}", [128, n_in], BF16) for i in range(2)]),
    )


def load_w(P, tag, w_dram, w_sb, n_out):
    f32t = P.sbuf(f"{tag}_wl32", [128, n_out], F32)
    for dc in range(DC):
        P.dma("sync", f32t[:], w_dram[dc * 128:(dc + 1) * 128, :])
        P.cp("vector", w_sb[:, dc, :], f32t)


def row_of(P, tag, vec_dram, n):
    r = P.sbuf(f"{tag}_row", [1, n], F32)
    P.dma("sync", r[:], vec_dram.rearrange("(a d) -> a d", a=1))
    return r



@phase("p1_att0",
       reads=["xn_fm", "L0_attn_Wq", "L0_attn_Wk", "L0_attn_Wv", "L0_attn_Wo",
              "L0_attn_norm_w", "c_eye_bf", "c_mask_ut", "c_ones_bf",
              "c_ones_f32"],
       writes=["att0_tm"],
       privs=["q0_fm", "k0_fm", "v0_fm", "o0_fm"])
def p1_att0(P, T):
    eye = T["c_eye_bf"]
    ones_bf_col = _ap(T["c_ones_bf"])[:, 0:1]
    ones_f32_col = _ap(T["c_ones_f32"])[:, 0:1]
    mask = T["c_mask_ut"]

    P.scope = "p1_proj"
    with P.phase("p1proj"):
        wts = {}
        wb = wT_bufs(P, "p1w")
        for wn, key in (("L0_attn_Wq", "q0_fm"), ("L0_attn_Wk", "k0_fm"),
                        ("L0_attn_Wv", "v0_fm")):
            wt = P.sbuf(f"p1_{wn}T", [128, DC, D], BF16)
            load_wT(P, f"p1{wn}", T[wn], wt, eye, bufs=wb)
            wts[key] = wt
        xt = P.sbuf("p1_xt", [128, DC, 512], BF16)
        pos = Rot([P.psum(f"p1_po{i}", [128, 512], F32) for i in range(2)])
        obs = Rot([P.sbuf(f"p1_ob{i}", [128, 512], BF16) for i in range(2)])
        for b in range(BLOC):
            for tmac in range(S // 512):
                t0 = b * S + tmac * 512
                P.dma("sync", xt[:],
                      T["xn_fm"][:, :, t0:t0 + 512].rearrange("c p t -> p c t"))
                for key, wt in wts.items():
                    for ec in range(DC):
                        po, ob = pos.nxt(), obs.nxt()
                        for dc in range(DC):
                            P.mm(po, wt[:, dc, ec * 128:(ec + 1) * 128],
                                 xt[:, dc, :], start=(dc == 0), stop=(dc == DC - 1))
                        P.cp("vector", ob, po)
                        P.dma("gpsimd",
                              T[key][b, ec, :, tmac * 512:(tmac + 1) * 512], ob)

    NCH = S // 128
    P.scope = "p1_chunk"
    with P.phase("p1attn"):
        nw_row = row_of(P, "p1_nw", T["L0_attn_norm_w"], 128)
        qh = P.sbuf("p1_qh", [128, S], BF16)
        kh = P.sbuf("p1_kh", [128, S], BF16)
        vh = P.sbuf("p1_vh", [128, S], BF16)
        kv_s = P.sbuf("p1_kv", [128, 256], BF16)
        scst = P.sbuf("p1_scst", [128, 128], BF16)
        state_s = P.sbuf("p1_state", [128, 129], BF16)
        den_s = P.sbuf("p1_den", [1, 128], F32)
        d2_s = P.sbuf("p1_d2", [1, 128], F32)
        osq_s = P.sbuf("p1_osq", [128, 128], F32)
        t_s = P.sbuf("p1_t", [1, 128], F32)
        al_s = P.sbuf("p1_al", [1, 128], F32)
        scale_s = P.sbuf("p1_scale", [128, 128], F32)
        on_s = P.sbuf("p1_on", [128, 128], BF16)
        ps_kv = P.psum("p1_pskv", [128, 256], BF16)
        ps_sc = P.psum("p1_pssc", [128, 128], F32)
        ps_o = P.psum("p1_pso", [128, 128], F32)
        ps_state = P.psum("p1_psst", [128, 129], F32)
        ps_den = P.psum("p1_psden", [128, 128], F32)
        ps_ssq = P.psum("p1_psssq", [128, 128], F32)
        ps_scale = P.psum("p1_pssca", [128, 128], F32)
        for b in range(BLOC):
            for h in range(8):
                P.dma("sync", qh[:], T["q0_fm"][b, h, :, :])
                P.dma("sync", kh[:], T["k0_fm"][b, h, :, :])
                P.dma("sync", vh[:], T["v0_fm"][b, h, :, :])
                for c in range(NCH):
                    csl = slice(c * 128, (c + 1) * 128)
                    P.transp(ps_kv[:, 0:128], kh[:, csl], eye,
                             register=False, war=True)
                    P.transp(ps_kv[:, 128:256], vh[:, csl], eye,
                             register=True, war=False)
                    P.cp("vector", kv_s, ps_kv)
                    ktc = _ap(kv_s)[:, 0:128]
                    vtc = _ap(kv_s)[:, 128:256]
                    P.mm(ps_sc, kh[:, csl], qh[:, csl])
                    P.tt("vector", scst, ps_sc, mask, ALU.mult)
                    if c > 0:
                        P.cp("scalar", state_s, ps_state)
                        P.mm(ps_o, _ap(state_s)[:, 0:128], qh[:, csl],
                             start=True, stop=False)
                        P.mm(ps_o, vtc, scst, start=False, stop=True)
                        P.mm(ps_den[0:1, :], _ap(state_s)[:, 128:129],
                             qh[:, csl], start=True, stop=False)
                        P.mm(ps_den[0:1, :], ones_bf_col, scst,
                             start=False, stop=True)
                    else:
                        P.mm(ps_o, vtc, scst, start=True, stop=True)
                        P.mm(ps_den[0:1, :], ones_bf_col, scst,
                             start=True, stop=True)
                    P.mm(ps_state[:, 0:128], ktc, vtc,
                         start=(c == 0), stop=False, register=False)
                    P.mm(ps_state[:, 128:129], ktc, ones_bf_col,
                         start=False, stop=(c == NCH - 1), register=True)
                    P.act(den_s, ps_den[0:1, :], AF.Abs)
                    P.ts("vector", den_s, den_s, 1.0, ALU.max)
                    P.act(osq_s, ps_o, AF.Square)
                    P.mm(ps_ssq[0:1, :], ones_f32_col, osq_s)
                    P.tt("vector", d2_s, den_s, den_s, ALU.mult)
                    P.ts("vector", d2_s, d2_s, 1e-5, ALU.mult)
                    P.stt("vector", t_s, ps_ssq[0:1, :], 1.0 / 128.0, d2_s,
                          ALU.mult, ALU.add)
                    P.act(t_s, t_s, AF.Ln)
                    P.act(al_s, t_s, AF.Exp, scale=-0.5)
                    P.mm(ps_scale, nw_row, al_s)
                    P.cp("vector", scale_s, ps_scale)
                    P.tt("vector", on_s, ps_o, scale_s, ALU.mult)
                    P.dma("gpsimd", T["o0_fm"][b, h, :, csl], on_s)

    P.scope = "p1_wo"
    with P.phase("p1wo"):
        woT = P.sbuf("p1_woT", [128, DC, D], BF16)
        load_wT(P, "p1wo", T["L0_attn_Wo"], woT, eye)
        ots = Rot([P.sbuf(f"p1_ot{i}", [128, DC, 128], BF16) for i in range(2)])
        pws = Rot([P.psum(f"p1_pw{i}", [128, 512], F32) for i in range(4)])
        oss = Rot([P.sbuf(f"p1_os{i}", [128, D], F32) for i in range(2)])
        for b in range(BLOC):
            for tt_ in range(S // 128):
                ts_ = slice(tt_ * 128, (tt_ + 1) * 128)
                ot = ots.nxt()
                os_ = oss.nxt()
                P.dma("sync", ot[:],
                      T["o0_fm"][b, :, :, ts_].rearrange("c p t -> p c t"))
                for half in range(2):
                    pw = pws.nxt()
                    for ec in range(DC):
                        P.mm(pw, ot[:, ec, :],
                             woT[:, ec, half * 512:(half + 1) * 512],
                             start=(ec == 0), stop=(ec == DC - 1))
                    P.cp("vector", os_[:, half * 512:(half + 1) * 512], pw)
                P.dma("gpsimd", T["att0_tm"][b * S + tt_ * 128:
                                             b * S + (tt_ + 1) * 128, :], os_)


def bcast_row(P, tag, row_ap, n, ones_f32, scale=1.0):
    ps = P.psum(f"{tag}_bps", [128, 512], F32)
    out = P.sbuf(f"{tag}_bc", [128, n], F32)
    for j in range(0, n, 512):
        w = min(512, n - j)
        P.mm(ps[:, 0:w], _ap(ones_f32)[0:1, :], _ap(row_ap)[0:1, j:j + w])
        if scale == 1.0:
            P.cp("vector", out[:, j:j + w], ps[:, 0:w])
        else:
            P.ts("vector", out[:, j:j + w], ps[:, 0:w], float(scale), ALU.mult)
    return out


def softmax_TQ(P, bufs, kh, qh, v_s, head, negc_s, out_dst, scale, bv=None):
    (ps_sT, E_r, ps_ctx_r, ps_sig, sig_r, sg_r, ctx_r) = bufs
    NK2 = S // 256
    c0 = 65 * head

    def make_epilogue(pc, qsl):
        sig_s, sg_s, ctxb = sig_r.nxt(), sg_r.nxt(), ctx_r.nxt()

        def epi():
            P.act(sig_s, _ap(pc)[64:65, :], AF.Ln)
            P.act(sig_s, sig_s, AF.Exp, scale=-1.0)
            P.mm(_ap(ps_sig)[0:64, :], P.ones_f32_row64, sig_s)
            P.cp("vector", sg_s, _ap(ps_sig)[0:64, :])
            if bv is None:
                P.tt("vector", ctxb, _ap(pc)[0:64, :], sg_s, ALU.mult)
            else:
                P.tt("vector", sg_s, _ap(pc)[0:64, :], sg_s, ALU.mult)
                P.ts("vector", ctxb, sg_s, bv, ALU.add)
            P.dma("gpsimd", out_dst[:, qsl], ctxb)
        return epi

    pend = None
    for qm in range(S // 512):
        qsl = slice(qm * 512, (qm + 1) * 512)
        pc = ps_ctx_r.nxt()
        prev = None
        for kt2 in range(NK2):
            ka, kb = 2 * kt2, 2 * kt2 + 1
            pt = ps_sT.nxt()
            P.mm(_ap(pt)[:, 0:512], kh[:, ka * 128:(ka + 1) * 128],
                 qh[:, qsl], register=False)
            P.mm(_ap(pt)[:, 512:1024], kh[:, kb * 128:(kb + 1) * 128],
                 qh[:, qsl], war=False)
            E = E_r.nxt()
            P.act(E, pt, AF.Exp, bias=negc_s, scale=scale)
            if kt2 == 1 and pend is not None:
                pend()
                pend = None
            if prev is not None:
                pa, pE = prev
                P.mm(_ap(pc)[0:65, :], v_s[:, pa, c0:c0 + 65],
                     _ap(pE)[:, 0:512],
                     start=(pa == 0), stop=False, register=False)
                P.mm(_ap(pc)[0:65, :], v_s[:, pa + 1, c0:c0 + 65],
                     _ap(pE)[:, 512:1024], start=False,
                     stop=False)
            prev = (ka, E)
        pa, pE = prev
        P.mm(_ap(pc)[0:65, :], v_s[:, pa, c0:c0 + 65], _ap(pE)[:, 0:512],
             start=(pa == 0), stop=False, register=False)
        P.mm(_ap(pc)[0:65, :], v_s[:, pa + 1, c0:c0 + 65],
             _ap(pE)[:, 512:1024], start=False, stop=True)
        pend = make_epilogue(pc, qsl)
    pend()


def head_bound(P, bufs, qh, kh, negc_s, scale_mag):
    (sq_s, ps_n, nmax, tmp1) = bufs
    for which, src in ((0, qh), (1, kh)):
        P.tt("vector", sq_s, src, src, ALU.mult)
        for j in range(S // 512):
            P.mm(ps_n[0:1, :], P.ones_f32_col64, sq_s[:, j * 512:(j + 1) * 512])
            P.red("vector", tmp1, ps_n[0:1, :], ALU.max)
            if j == 0:
                P.cp("vector", nmax[0:1, which:which + 1], tmp1)
            else:
                P.tt("vector", nmax[0:1, which:which + 1],
                     nmax[0:1, which:which + 1], tmp1, ALU.max)
    P.tt("vector", tmp1, nmax[0:1, 0:1], nmax[0:1, 1:2], ALU.mult)
    P.act(tmp1, tmp1, AF.Ln, bias=1e-30)
    P.act(tmp1, tmp1, AF.Exp, scale=0.5)
    P.mm(ps_n[:, 0:1], _ap(P.ones_f32)[0:1, :], tmp1)
    P.act(negc_s, ps_n[:, 0:1], AF.Copy, scale=-0.75 * scale_mag)



@phase("p2_mha0",
       reads=["xn_tm", "xn_fm", "L0_ffn_ln1_g", "L0_ffn_ln1_b",
              "L0_ffn_ln2_g", "L0_ffn_ln2_b", "L0_ffn_Wq", "L0_ffn_Wk",
              "L0_ffn_Wv", "L0_ffn_Wo", "L0_ffn_Wg", "L0_ffn_W1", "L0_ffn_W2",
              "c_eye_bf", "c_eye_f32", "c_ones_f32", "c_ones_bf"],
       writes=["x2_tm", "m_fm", "cw_tm", "w1m_bf", "w2m_bf"],
       privs=["h0_fm", "qf_fm", "kf_fm", "vf_tm", "ctx0_fm", "m_fm32"])
def p2_mha0(P, T):
    eye = T["c_eye_bf"]
    eye32 = T["c_eye_f32"]
    P.ones_f32 = T["c_ones_f32"]
    P.ones_bf_col = _ap(T["c_ones_bf"])[:, 0:1]
    P.ones_f32_col64 = _ap(T["c_ones_f32"])[0:64, 0:1]
    P.ones_f32_row64 = _ap(T["c_ones_f32"])[0:1, 0:64]

    P.scope = "p2_a"
    with P.phase("p2a"):
        g1r = row_of(P, "p2g1", T["L0_ffn_ln1_g"], D)
        b1r = row_of(P, "p2b1", T["L0_ffn_ln1_b"], D)
        g1 = bcast_row(P, "p2g1", g1r, D, P.ones_f32)
        b1 = bcast_row(P, "p2b1", b1r, D, P.ones_f32)
        R2 = lambda nm, sh, dt: Rot([P.sbuf(f"{nm}{i}", sh, dt) for i in range(2)])
        xt_r = R2("p2_xt", [128, D], F32)
        xc_r = R2("p2_xc", [128, D], F32)
        hb_r = R2("p2_hb", [128, D], BF16)
        mu_r = R2("p2_mu", [128, 1], F32)
        rstd_r = R2("p2_rstd", [128, 1], F32)
        lsq = P.sbuf("p2_sq", [128, D], F32)
        lacc_r = R2("p2_acc", [128, 1], F32)
        ptr_r = Rot([P.psum(f"p2_ptr{i}", [128, D], BF16) for i in range(2)])
        stg_r = R2("p2_stg", [128, D], BF16)
        for ti in range(TLOC // 128):
            sl = slice(ti * 128, (ti + 1) * 128)
            xt, xc, hb = xt_r.nxt(), xc_r.nxt(), hb_r.nxt()
            mu, rstd, lacc = mu_r.nxt(), rstd_r.nxt(), lacc_r.nxt()
            ptr, stg = ptr_r.nxt(), stg_r.nxt()
            P.dma("sync", xt[:], T["xn_tm"][sl, :])
            ln_stats(P, xt, mu, rstd, xc, lsq, lacc)
            P.ts("vector", xc[:], xc[:], rstd[:], ALU.mult)
            P.tt("vector", xc, xc, g1, ALU.mult)
            P.tt("vector", xc, xc, b1, ALU.add)
            P.cp("scalar", hb, xc)
            tm_tile_to_fm(P, "p2", hb, eye, T["h0_fm"], ti, ptr, stg)

    P.scope = "p2_b"
    with P.phase("p2b"):
        wq = P.sbuf("p2_wq", [128, DC, D], BF16)
        wk = P.sbuf("p2_wk", [128, DC, D], BF16)
        wv = P.sbuf("p2_wv", [128, DC, D], BF16)
        load_w(P, "p2wq", T["L0_ffn_Wq"], wq, D)
        load_w(P, "p2wk", T["L0_ffn_Wk"], wk, D)
        load_w(P, "p2wv", T["L0_ffn_Wv"], wv, D)
        xt = P.sbuf("p2_xh", [128, DC, 512], BF16)
        pos = Rot([P.psum(f"p2_po{i}", [128, 512], F32) for i in range(4)])
        obs = Rot([P.sbuf(f"p2_ob{i}", [128, 512], BF16) for i in range(4)])
        for b in range(BLOC):
            for tmac in range(S // 512):
                t0 = b * S + tmac * 512
                P.dma("sync", xt[:],
                      T["h0_fm"][:, :, t0:t0 + 512].rearrange("c p t -> p c t"))
                for key, wt in (("qf_fm", wq), ("kf_fm", wk)):
                    for ec in range(DC):
                        po, ob = pos.nxt(), obs.nxt()
                        for dc in range(DC):
                            P.mm(po, wt[:, dc, ec * 128:(ec + 1) * 128],
                                 xt[:, dc, :], start=(dc == 0),
                                 stop=(dc == DC - 1))
                        P.cp("vector", ob, po)
                        P.dma("gpsimd",
                              T[key][b, ec, :, tmac * 512:(tmac + 1) * 512], ob)
                for st in range(4):
                    tc = tmac * 4 + st
                    for half in range(2):
                        po, ob = pos.nxt(), obs.nxt()
                        hsl = slice(half * 512, (half + 1) * 512)
                        for dc in range(DC):
                            P.mm(po, xt[:, dc, st * 128:(st + 1) * 128],
                                 wv[:, dc, hsl], start=(dc == 0),
                                 stop=(dc == DC - 1))
                        P.cp("vector", ob, po)
                        P.dma("gpsimd", T["vf_tm"][b, tc, :, hsl], ob)

    def precast_ops(wc32, wcb):
        for e in range(8):
            for dc in range(DC):
                c32, cb = wc32.nxt(), wcb.nxt()
                P.dma("sync", c32[:],
                      T["L0_ffn_W1"][e, dc * 128:(dc + 1) * 128, :])
                P.cp("vector", cb, c32)
                P.dma("gpsimd", T["w1m_bf"][e, dc, :, :], cb)
                yield
        for e in range(8):
            for g in range(8):
                c32, cb = wc32.nxt(), wcb.nxt()
                P.dma("sync",
                      c32[:].rearrange("p (a d) -> p a d", a=4),
                      T["L0_ffn_W2"][e, g * 512:(g + 1) * 512, :]
                      .rearrange("(a p) d -> p a d", p=128))
                P.cp("vector", cb, c32)
                P.dma("gpsimd",
                      T["w2m_bf"][e, 4 * g:4 * g + 4, :, :]
                      .rearrange("c p d -> p c d"),
                      cb[:].rearrange("p (c d) -> p c d", c=4))
                yield
        while True:
            yield

    P.scope = "p2_c"
    with P.phase("p2c"):
        NH = 16
        wc32 = Rot([P.sbuf(f"p2_wc32{i}", [128, 4096], F32) for i in range(2)])
        wcb = Rot([P.sbuf(f"p2_wcb{i}", [128, 4096], BF16) for i in range(2)])
        precast = precast_ops(wc32, wcb)
        v_r = Rot([P.sbuf(f"p2_vs{i}", [128, S // 128, NH * 65], BF16)
                   for i in range(2)])
        qc = Rot([P.sbuf(f"p2_qc{i}", [128, S], BF16) for i in range(2)])
        kc = Rot([P.sbuf(f"p2_kc{i}", [128, S], BF16) for i in range(2)])
        sq_s = P.sbuf("p2_sqs", [64, S], F32)
        nmax = P.sbuf("p2_nmax", [1, 2], F32)
        tmp1 = P.sbuf("p2_tmp1", [1, 1], F32)
        negc = Rot([P.sbuf(f"p2_negc{i}", [128, 1], F32) for i in range(2)])
        ps_n = P.psum("p2_psn", [128, 512], F32)
        ps_sT = Rot([P.psum(f"p2_psT{i}", [128, 1024], F32) for i in range(2)])
        E_r = Rot([P.sbuf(f"p2_E{i}", [128, 1024], BF16) for i in range(2)])
        ps_ctx = Rot([P.psum(f"p2_psctx{i}", [128, 512], F32) for i in range(2)])
        ps_sig = P.psum("p2_pssig", [128, 512], F32)
        sig_r = Rot([P.sbuf(f"p2_sig{i}", [1, 512], F32) for i in range(2)])
        sg_r = Rot([P.sbuf(f"p2_sg{i}", [64, 512], F32) for i in range(2)])
        ctx_r = Rot([P.sbuf(f"p2_ctxb{i}", [64, 512], BF16) for i in range(2)])
        hb_bufs = (sq_s, ps_n, nmax, tmp1)
        at_bufs = (ps_sT, E_r, ps_ctx, ps_sig, sig_r, sg_r, ctx_r)
        for vb in v_r.items:
            P.memset("vector",
                 _ap(vb).rearrange("p c (h w) -> p c h w", w=65)[:, :, :, 64:65],
                 1.0)
        for b in range(BLOC):
            v_s = v_r.nxt()
            for tc in range(S // 128):
                P.dma("gpsimd",
                      _ap(v_s)[:, tc, :].rearrange("p (h w) -> p h w", w=65)
                      [:, :, 0:64],
                      T["vf_tm"][b, tc, :, :].rearrange("p (h w) -> p h w", w=64))
            for hp in range(8):
                qcb, kcb = qc.nxt(), kc.nxt()
                P.dma("sync", qcb[:], T["qf_fm"][b, hp, :, :])
                P.dma("sync", kcb[:], T["kf_fm"][b, hp, :, :])
                for sub in range(2):
                    hbase = 64 * sub
                    h = 2 * hp + sub
                    qh = _ap(qcb)[hbase:hbase + 64, :]
                    kh = _ap(kcb)[hbase:hbase + 64, :]
                    nc_ = negc.nxt()
                    head_bound(P, hb_bufs, qh, kh, nc_, 0.125)
                    softmax_TQ(P, at_bufs, kh, qh, v_s, h, nc_,
                               T["ctx0_fm"][b, hp, hbase:hbase + 64, :], 0.125)
                    next(precast)
                    next(precast)
                    next(precast)
                    next(precast)

    P.scope = "p2_d"
    with P.phase("p2d"):
        wo = P.sbuf("p2_wo", [128, DC, D], BF16)
        load_w(P, "p2wo", T["L0_ffn_Wo"], wo, D)
        wg = P.sbuf("p2_wg", [128, DC, 8], F32)
        wgl = P.sbuf("p2_wgl", [128, 8], F32)
        for dc in range(DC):
            P.dma("sync", wgl[:], T["L0_ffn_Wg"][dc * 128:(dc + 1) * 128, :])
            P.cp("vector", wg[:, dc, :], wgl)
        g2r = row_of(P, "p2g2", T["L0_ffn_ln2_g"], D)
        b2r = row_of(P, "p2b2", T["L0_ffn_ln2_b"], D)
        g2 = bcast_row(P, "p2g2", g2r, D, P.ones_f32)
        b2 = bcast_row(P, "p2b2", b2r, D, P.ones_f32)
        ot = P.sbuf("p2_ot", [128, DC, 128], BF16)
        xnt = P.sbuf("p2_xnt", [128, D], F32)
        x2s = P.sbuf("p2_x2s", [128, D], F32)
        mb = P.sbuf("p2_mb", [128, D], BF16)
        m32 = P.sbuf("p2_m32", [128, D], F32)
        xc2 = P.sbuf("p2_xc2", [128, D], F32)
        mu = P.sbuf("p2_mu2", [128, 1], F32)
        rstd = P.sbuf("p2_rstd2", [128, 1], F32)
        lsq = P.sbuf("p2_sq2", [128, D], F32)
        lacc = P.sbuf("p2_acc2", [128, 1], F32)
        pw = Rot([P.psum(f"p2_pw{i}", [128, 512], F32) for i in range(2)])
        ptr = P.psum("p2_ptr2", [128, D], BF16)
        stg = P.sbuf("p2_stg2", [128, D], BF16)
        ptr32 = P.psum("p2_ptr32", [128, D], F32)
        stg32 = P.sbuf("p2_stg32", [128, D], F32)
        ps_r = P.psum("p2_psr", [128, 512], F32)
        E8 = P.sbuf("p2_E8", [128, 8], F32)
        mx = P.sbuf("p2_mx", [128, 1], F32)
        m1 = P.sbuf("p2_m1", [128, 1], F32)
        m2 = P.sbuf("p2_m2", [128, 1], F32)
        iseq = P.sbuf("p2_iseq", [128, 8], F32)
        e8b = P.sbuf("p2_e8b", [128, 8], F32)
        cw = P.sbuf("p2_cw", [128, 8], F32)
        for b in range(BLOC):
            for tt_ in range(S // 128):
                ti = b * (S // 128) + tt_
                sl = slice(ti * 128, (ti + 1) * 128)
                P.dma("sync", ot[:],
                      T["ctx0_fm"][b, :, :, tt_ * 128:(tt_ + 1) * 128]
                      .rearrange("c p t -> p c t"))
                P.dma("sync", xnt[:], T["xn_tm"][sl, :])
                for half in range(2):
                    p_ = pw.nxt()
                    hsl = slice(half * 512, (half + 1) * 512)
                    for ec in range(DC):
                        P.mm(p_, ot[:, ec, :], wo[:, ec, hsl],
                             start=(ec == 0), stop=(ec == DC - 1))
                    P.tt("vector", x2s[:, hsl], p_, _ap(xnt)[:, hsl], ALU.add)
                P.dma("gpsimd", T["x2_tm"][sl, :], x2s)
                ln_stats(P, x2s, mu, rstd, xc2, lsq, lacc)
                P.ts("vector", m32, xc2, rstd, ALU.mult)
                P.tt("vector", m32, m32, g2, ALU.mult)
                P.tt("vector", m32, m32, b2, ALU.add)
                P.cp("scalar", mb, m32)
                tm_tile_to_fm(P, "p2d", mb, eye, T["m_fm"], ti, ptr, stg,
                              also_f32=True, psum32=ptr32, eye32=eye32,
                              src32=m32, stage32=stg32, dst32=T["m_fm32"])
                for dc in range(DC):
                    P.mm(ps_r[:, 0:8], stg32[:, dc * 128:(dc + 1) * 128],
                         wg[:, dc, :], start=(dc == 0), stop=(dc == DC - 1))
                P.red("vector", mx, ps_r[:, 0:8], ALU.max)
                P.ts("vector", mx, mx, -1.0, ALU.mult)
                P.act(E8, ps_r[:, 0:8], AF.Exp, bias=mx)
                P.red("vector", m1, E8, ALU.max)
                P.ts("vector", iseq, E8, m1, ALU.is_equal)
                P.ts("vector", iseq, iseq, -1.0, ALU.mult, 1.0, ALU.add)
                P.tt("vector", e8b, E8, iseq, ALU.mult)
                P.red("vector", m2, e8b, ALU.max)
                P.ts("vector", e8b, E8, m2, ALU.is_ge)
                P.tt("vector", cw, E8, e8b, ALU.mult)
                P.tt("vector", m1, m1, m2, ALU.add)
                P.recip(m1, m1)
                P.ts("vector", cw, cw, m1, ALU.mult)
                P.dma("gpsimd", T["cw_tm"][sl, :], cw)



@phase("p3_moe0",
       reads=["m_fm", "cw_tm", "x2_tm", "w1m_bf", "L0_ffn_b1",
              "w2m_bf", "L0_ffn_b2", "c_ones_f32"],
       writes=["ffn0_tm"])
def p3_moe0(P, T):
    P.ones_f32 = T["c_ones_f32"]
    NE, F_ = 8, 4096
    FC = F_ // 128
    with P.phase("p3"):
        w1 = P.sbuf("p3_w1", [128, DC, F_], BF16)
        w2 = P.sbuf("p3_w2", [128, FC, D], BF16)
        b1 = P.sbuf("p3_b1", [128, FC], F32)

        b2r = P.sbuf("p3_b2r", [1, D], F32)
        b2bc = P.sbuf("p3_b2bc", [128, D], F32)
        ps_b = P.psum("p3_psb", [128, 512], F32)
        mt = Rot([P.sbuf(f"p3_mt{i}", [128, DC, 512], BF16) for i in range(2)])
        h_s = P.sbuf("p3_h", [128, FC, 512], BF16)
        cw_s = Rot([P.sbuf(f"p3_cw{i}", [128, 4, NE], F32) for i in range(2)])
        acc = Rot([P.sbuf(f"p3_acc{i}", [128, D], F32) for i in range(2)])
        ytmp = Rot([P.sbuf(f"p3_yt{i}", [128, 512], F32) for i in range(1)])
        ph = Rot([P.psum(f"p3_ph{i}", [128, 512], F32) for i in range(3)])
        py = Rot([P.psum(f"p3_py{i}", [128, 512], F32) for i in range(4)])

        def load_w1(e):
            for dc in range(DC):
                P.dma("gpsimd" if dc % 2 else "sync", w1[:, dc, :],
                      T["w1m_bf"][e, dc, :, :])

        def load_w2b(e):
            for g in range(4):
                P.dma("gpsimd" if g % 2 else "sync",
                      w2[:, 8 * g:8 * g + 8, :]
                      .rearrange("p c d -> p c d"),
                      T["w2m_bf"][e, 8 * g:8 * g + 8, :, :]
                      .rearrange("c p d -> p c d"))
            for fc in range(FC):
                P.dma("gpsimd", b1[:, fc:fc + 1],
                      T["L0_ffn_b1"][e, fc * 128:(fc + 1) * 128]
                      .rearrange("(p a) -> p a", a=1))
            P.dma("gpsimd", b2r[:], T["L0_ffn_b2"][e, :]
                  .rearrange("(a d) -> a d", a=1))
            for j in range(2):
                P.mm(ps_b[:, 0:512], _ap(P.ones_f32)[0:1, :],
                     _ap(b2r)[0:1, j * 512:(j + 1) * 512])
                P.cp("vector", b2bc[:, j * 512:(j + 1) * 512], ps_b[:, 0:512])

        load_w1(0)
        load_w2b(0)
        for e in range(NE):
            for tmac in range(TLOC // 512):
                t0 = tmac * 512
                m_, c_ = mt.nxt(), cw_s.nxt()
                P.dma("sync", m_[:],
                      T["m_fm"][:, :, t0:t0 + 512].rearrange("c p t -> p c t"))
                P.dma("sync", c_[:],
                      T["cw_tm"][t0:t0 + 512, :]
                      .rearrange("(s p) e -> p s e", p=128))
                for fc in range(FC):
                    p_ = ph.nxt()
                    for dc in range(DC):
                        P.mm(p_, w1[:, dc, fc * 128:(fc + 1) * 128],
                             m_[:, dc, :], start=(dc == 0), stop=(dc == DC - 1))
                    P.act(h_s[:, fc, :], p_, AF.Gelu_apprx_tanh,
                          bias=_ap(b1)[:, fc:fc + 1])
                if tmac == TLOC // 512 - 1 and e + 1 < NE:
                    load_w1(e + 1)
                for st in range(4):
                    ssl = slice(st * 128, (st + 1) * 128)
                    a_ = acc.nxt()
                    src_t = T["x2_tm"] if e == 0 else T["ffn0_tm"]
                    P.dma("sync", a_[:],
                          src_t[t0 + st * 128:t0 + (st + 1) * 128, :])
                    for half in range(2):
                        hsl = slice(half * 512, (half + 1) * 512)
                        p_ = py.nxt()
                        yt = ytmp.nxt()
                        for fc in range(FC):
                            P.mm(p_, h_s[:, fc, ssl], w2[:, fc, hsl],
                                 start=(fc == 0), stop=(fc == FC - 1))
                        P.tt("vector", yt, p_, _ap(b2bc)[:, hsl], ALU.add)
                        P.stt("vector", a_[:, hsl], yt,
                              _ap(c_)[:, st, e:e + 1], a_[:, hsl],
                              ALU.mult, ALU.add)
                    P.dma("gpsimd",
                          T["ffn0_tm"][t0 + st * 128:t0 + (st + 1) * 128, :],
                          a_)
            if e + 1 < NE:
                load_w2b(e + 1)



@phase("p4_comb0",
       reads=["L0_attn_hidden_states", "att0_tm", "ffn0_tm", "c_eye_bf"],
       writes=["x1_tm", "xn1_tm", "xn1_fm"])
def p4_comb0(P, T):
    x = T["L0_attn_hidden_states"].rearrange("b s d -> (b s) d")
    eye_bf = T["c_eye_bf"]
    with P.phase("p4"):
        R2 = lambda nm, sh, dt: Rot([P.sbuf(f"{nm}{i}", sh, dt) for i in range(2)])
        xt_r = R2("p4_xt", [128, D], F32)
        at_r = R2("p4_at", [128, D], F32)
        ft_r = R2("p4_ft", [128, D], F32)
        xc_r = R2("p4_xc", [128, D], F32)
        xnb_r = R2("p4_xnb", [128, D], BF16)
        mu_r = R2("p4_mu", [128, 1], F32)
        rstd_r = R2("p4_rstd", [128, 1], F32)
        lsq = P.sbuf("p4_sq", [128, D], F32)
        lacc_r = R2("p4_acc", [128, 1], F32)
        ptr_r = Rot([P.psum(f"p4_ptr{i}", [128, D], BF16) for i in range(2)])
        stg_r = R2("p4_stg", [128, D], BF16)
        for ti in range(TLOC // 128):
            sl = slice(ti * 128, (ti + 1) * 128)
            xt, at, ft, xc = xt_r.nxt(), at_r.nxt(), ft_r.nxt(), xc_r.nxt()
            xnb, mu = xnb_r.nxt(), mu_r.nxt()
            rstd, lacc = rstd_r.nxt(), lacc_r.nxt()
            ptr, stg = ptr_r.nxt(), stg_r.nxt()
            P.dma("sync", xt[:], x[sl, :])
            P.dma("sync", at[:], T["att0_tm"][sl, :])
            P.dma("sync", ft[:], T["ffn0_tm"][sl, :])
            P.tt("vector", xt, xt, at, ALU.add)
            P.tt("vector", xt, xt, ft, ALU.add)
            P.dma("gpsimd", T["x1_tm"][sl, :], xt)
            ln_stats(P, xt, mu, rstd, xc, lsq, lacc)
            P.ts("vector", xc[:], xc[:], rstd[:], ALU.mult)
            P.dma("gpsimd", T["xn1_tm"][sl, :], xc)
            P.cp("scalar", xnb, xc)
            tm_tile_to_fm(P, "p4", xnb, eye_bf, T["xn1_fm"], ti, ptr, stg)



@phase("p5_att1",
       reads=["xn1_fm", "L1_attn_w_qkv", "L1_attn_b_qkv", "L1_attn_w_out",
              "L1_attn_b_out", "c_eye_bf", "c_ones_f32", "c_ones_bf"],
       writes=["att1_tm"],
       privs=["qkv1_fm", "v1_tm", "ctx1_fm"])
def p5_att1(P, T):
    eye = T["c_eye_bf"]
    P.ones_f32 = T["c_ones_f32"]
    P.ones_bf_col = _ap(T["c_ones_bf"])[:, 0:1]
    P.ones_f32_col64 = _ap(T["c_ones_f32"])[0:64, 0:1]
    P.ones_f32_row64 = _ap(T["c_ones_f32"])[0:1, 0:64]
    NH = 16

    P.scope = "p5_b"
    with P.phase("p5b"):
        wqkvT = P.sbuf("p5_wqkvT", [128, DC, 3 * D], BF16)
        load_wT(P, "p5qkv", T["L1_attn_w_qkv"], wqkvT, eye, n_out=3 * D)
        bq = P.sbuf("p5_bq", [128, 24], F32)
        for oc in range(24):
            P.dma("sync", bq[:, oc:oc + 1],
                  T["L1_attn_b_qkv"][oc * 128:(oc + 1) * 128]
                  .rearrange("(p a) -> p a", a=1))
        xt = P.sbuf("p5_xt", [128, DC, 512], BF16)
        pos = Rot([P.psum(f"p5_po{i}", [128, 512], F32) for i in range(4)])
        obs = Rot([P.sbuf(f"p5_ob{i}", [128, 512], BF16) for i in range(4)])
        for b in range(BLOC):
            for tmac in range(S // 512):
                t0 = b * S + tmac * 512
                P.dma("sync", xt[:],
                      T["xn1_fm"][:, :, t0:t0 + 512]
                      .rearrange("c p t -> p c t"))
                for oc in range(24):
                    po, ob = pos.nxt(), obs.nxt()
                    for dc in range(DC):
                        P.mm(po, wqkvT[:, dc, oc * 128:(oc + 1) * 128],
                             xt[:, dc, :], start=(dc == 0), stop=(dc == DC - 1))
                    P.ts("vector", ob, po, _ap(bq)[:, oc:oc + 1], ALU.add)
                    P.dma("gpsimd",
                          T["qkv1_fm"][b, oc, :, tmac * 512:(tmac + 1) * 512],
                          ob)
                for st in range(4):
                    tc = tmac * 4 + st
                    for half in range(2):
                        po, ob = pos.nxt(), obs.nxt()
                        rhsv = _ap(wqkvT)[:, :, :].rearrange(
                            "p c (h x) -> p c h x", h=NH)[
                            :, :, half * 8:(half + 1) * 8, 128:192]
                        for dc in range(DC):
                            P.mm(po, xt[:, dc, st * 128:(st + 1) * 128],
                                 rhsv[:, dc], start=(dc == 0),
                                 stop=(dc == DC - 1))
                        P.cp("vector", ob, po)
                        P.dma("gpsimd",
                              T["v1_tm"][b, tc, :, half * 512:(half + 1) * 512],
                              ob)

    P.scope = "p5_c"
    with P.phase("p5c"):
        v_r = Rot([P.sbuf(f"p5_vs{i}", [128, S // 128, NH * 65], BF16)
                   for i in range(2)])
        qh_r = Rot([P.sbuf(f"p5_qh{i}", [64, S], BF16) for i in range(2)])
        kh_r = Rot([P.sbuf(f"p5_kh{i}", [64, S], BF16) for i in range(2)])
        sq_s = P.sbuf("p5_sqs", [64, S], F32)
        nmax = P.sbuf("p5_nmax", [1, 2], F32)
        tmp1 = P.sbuf("p5_tmp1", [1, 1], F32)
        negc = Rot([P.sbuf(f"p5_negc{i}", [128, 1], F32) for i in range(2)])
        ps_n = P.psum("p5_psn", [128, 512], F32)
        ps_sT = Rot([P.psum(f"p5_psT{i}", [128, 1024], F32) for i in range(2)])
        E_r = Rot([P.sbuf(f"p5_E{i}", [128, 1024], BF16) for i in range(2)])
        ps_ctx = Rot([P.psum(f"p5_psctx{i}", [128, 512], F32) for i in range(2)])
        ps_sig = P.psum("p5_pssig", [128, 512], F32)
        sig_r = Rot([P.sbuf(f"p5_sig{i}", [1, 512], F32) for i in range(2)])
        sg_r = Rot([P.sbuf(f"p5_sg{i}", [64, 512], F32) for i in range(2)])
        ctx_r = Rot([P.sbuf(f"p5_ctxb{i}", [64, 512], BF16) for i in range(2)])
        bq2 = P.sbuf("p5_bq2", [128, 24], F32)
        for oc in range(24):
            P.dma("sync", bq2[:, oc:oc + 1],
                  T["L1_attn_b_qkv"][oc * 128:(oc + 1) * 128]
                  .rearrange("(p a) -> p a", a=1))
        hb_bufs = (sq_s, ps_n, nmax, tmp1)
        at_bufs = (ps_sT, E_r, ps_ctx, ps_sig, sig_r, sg_r, ctx_r)
        for vb in v_r.items:
            P.memset("vector",
                 _ap(vb).rearrange("p c (h w) -> p c h w", w=65)[:, :, :, 64:65],
                 1.0)
        for b in range(BLOC):
            v_s = v_r.nxt()
            for tc in range(S // 128):
                P.dma("gpsimd",
                      _ap(v_s)[:, tc, :].rearrange("p (h w) -> p h w", w=65)
                      [:, :, 0:64],
                      T["v1_tm"][b, tc, :, :].rearrange("p (h w) -> p h w", w=64))
            for h in range(NH):
                fq, fk, fv = 192 * h, 192 * h + 64, 192 * h + 128
                qh_s, kh_s = qh_r.nxt(), kh_r.nxt()
                P.dma("sync", qh_s[:],
                      T["qkv1_fm"][b, fq // 128, fq % 128:fq % 128 + 64, :])
                P.dma("sync", kh_s[:],
                      T["qkv1_fm"][b, fk // 128, fk % 128:fk % 128 + 64, :])
                bv_col = _ap(bq2)[fv % 128:fv % 128 + 64,
                                  fv // 128:fv // 128 + 1]
                nc_ = negc.nxt()
                head_bound(P, hb_bufs, _ap(qh_s), _ap(kh_s), nc_, 0.125)
                softmax_TQ(P, at_bufs, _ap(kh_s), _ap(qh_s), v_s, h, nc_,
                           T["ctx1_fm"][b, h // 2,
                                        64 * (h % 2):64 * (h % 2) + 64, :],
                           0.125, bv=bv_col)

    P.scope = "p5_d"
    with P.phase("p5d"):
        woT = P.sbuf("p5_woT", [128, DC, D], BF16)
        load_wT(P, "p5wo", T["L1_attn_w_out"], woT, eye)
        bor = row_of(P, "p5bo", T["L1_attn_b_out"], D)
        bo = bcast_row(P, "p5bo", bor, D, P.ones_f32)
        ots = Rot([P.sbuf(f"p5_ot{i}", [128, DC, 128], BF16) for i in range(2)])
        pws = Rot([P.psum(f"p5_pw{i}", [128, 512], F32) for i in range(4)])
        oss = Rot([P.sbuf(f"p5_os{i}", [128, D], F32) for i in range(2)])
        for b in range(BLOC):
            for tt_ in range(S // 128):
                ot = ots.nxt()
                os_ = oss.nxt()
                P.dma("sync", ot[:],
                      T["ctx1_fm"][b, :, :, tt_ * 128:(tt_ + 1) * 128]
                      .rearrange("c p t -> p c t"))
                for half in range(2):
                    pw = pws.nxt()
                    hsl = slice(half * 512, (half + 1) * 512)
                    for ec in range(DC):
                        P.mm(pw, ot[:, ec, :], woT[:, ec, hsl],
                             start=(ec == 0), stop=(ec == DC - 1))
                    P.tt("vector", os_[:, hsl], pw, _ap(bo)[:, hsl], ALU.add)
                P.dma("gpsimd",
                      T["att1_tm"][b * S + tt_ * 128:b * S + (tt_ + 1) * 128, :],
                      os_)


def rms_tile(P, xt, gbc, out32, ssq, bufs):
    sq, r = bufs
    P.act(sq, xt, AF.Square, accum=ssq)
    P.act(ssq, ssq, AF.Ln, bias=1e-30)
    P.act(r, ssq, AF.Exp, scale=-0.5)
    P.ts("vector", out32, xt, r, ALU.mult)
    P.tt("vector", out32, out32, gbc, ALU.mult)



@phase("p6_sm_a",
       reads=["xn1_tm", "L1_ffn_gamma_x", "L1_ffn_gamma_slot",
              "L1_ffn_slot_embeds", "L1_ffn_w1", "L1_ffn_w2",
              "c_eye_bf", "c_ones_f32", "c_ones_bf"],
       writes=["xnn_fm", "se_fm", "proc_tm"],
       privs=["xnn_tmb", "w1l1_bf", "w2l1_bf"])
def p6_sm_a(P, T):
    eye = T["c_eye_bf"]
    P.ones_f32 = T["c_ones_f32"]
    NES = 4096
    FC = 32

    P.scope = "p6_prep"
    with P.phase("p6prep"):
        wt32 = P.sbuf("p6_wt32", [128, 4096], F32)
        wtb = P.sbuf("p6_wtb", [128, 4096], BF16)
        for dc in range(DC):
            P.dma("sync", wt32[:], T["L1_ffn_w1"][dc * 128:(dc + 1) * 128, :])
            P.cp("vector", wtb, wt32)
            P.dma("gpsimd",
                  T["w1l1_bf"][:, :, dc * 128:(dc + 1) * 128]
                  .rearrange("c p f -> p c f"),
                  wtb[:].rearrange("p (c f) -> p c f", c=32))
        for fc in range(FC):
            P.dma("sync", _ap(wt32)[:, 0:D],
                  T["L1_ffn_w2"][fc * 128:(fc + 1) * 128, :])
            P.cp("vector", _ap(wtb)[:, 0:D], _ap(wt32)[:, 0:D])
            P.dma("gpsimd", T["w2l1_bf"][fc, :, :], _ap(wtb)[:, 0:D])
        gxr = row_of(P, "p6gx", T["L1_ffn_gamma_x"], D)
        gsr = row_of(P, "p6gs", T["L1_ffn_gamma_slot"], D)
        gx = bcast_row(P, "p6gx", gxr, D, P.ones_f32, scale=32.0)
        gs = bcast_row(P, "p6gs", gsr, D, P.ones_f32, scale=32.0)
        R2 = lambda nm, sh, dt: Rot([P.sbuf(f"{nm}{i}", sh, dt) for i in range(2)])
        xt_r = R2("p6_xt", [128, D], F32)
        o32_r = R2("p6_o32", [128, D], F32)
        ob_r = R2("p6_ob", [128, D], BF16)
        sq = P.sbuf("p6_sq", [128, D], F32)
        ssq_r = R2("p6_ssq", [128, 1], F32)
        rr_r = R2("p6_rr", [128, 1], F32)
        ptr_r = Rot([P.psum(f"p6_ptr{i}", [128, D], BF16) for i in range(2)])
        stg_r = R2("p6_stg", [128, D], BF16)
        for ti in range(TLOC // 128):
            b, tt_ = ti // (S // 128), ti % (S // 128)
            xt, o32, ob = xt_r.nxt(), o32_r.nxt(), ob_r.nxt()
            ssq, rr, ptr, stg = ssq_r.nxt(), rr_r.nxt(), ptr_r.nxt(), stg_r.nxt()
            P.dma("sync", xt[:], T["xn1_tm"][ti * 128:(ti + 1) * 128, :])
            rms_tile(P, xt, gx, o32, ssq, (sq, rr))
            P.cp("scalar", ob, o32)
            P.dma("gpsimd", T["xnn_tmb"][b, tt_, :, :], ob)
            tm_tile_to_fm(P, "p6", ob, eye, T["xnn_fm"][b], tt_, ptr, stg)
        sed = T["L1_ffn_slot_embeds"].rearrange("e t d -> (e t) d")
        for si in range(NES // 128):
            xt, o32, ob = xt_r.nxt(), o32_r.nxt(), ob_r.nxt()
            ssq, rr, ptr, stg = ssq_r.nxt(), rr_r.nxt(), ptr_r.nxt(), stg_r.nxt()
            P.dma("sync", xt[:], sed[si * 128:(si + 1) * 128, :])
            rms_tile(P, xt, gs, o32, ssq, (sq, rr))
            P.cp("scalar", ob, o32)
            tm_tile_to_fm(P, "p6se", ob, eye, T["se_fm"], si, ptr, stg)

    P.scope = "p6_a"
    with P.phase("p6a"):
        xnn_fm_s = P.sbuf("p6_xf", [128, DC, S], BF16)
        xnn_tm_s = P.sbuf("p6_xt2", [128, S // 128, D], BF16)
        se_s = P.sbuf("p6_ses", [128, DC, 512], BF16)
        dT_s = P.sbuf("p6_dT", [128, S // 128, 512], BF16)
        disp_r = Rot([P.sbuf(f"p6_disp{i}", [128, S], BF16) for i in range(2)])
        slots_s = P.sbuf("p6_slots", [128, DC, 512], BF16)
        h_s = P.sbuf("p6_h", [128, FC, 512], BF16)
        proc_s = P.sbuf("p6_proc", [128, 4, D], BF16)
        w1t = Rot([P.sbuf(f"p6_w1t{i}", [128, DC, 128], BF16) for i in range(2)])
        w2t = Rot([P.sbuf(f"p6_w2t{i}", [128, 512], BF16) for i in range(2)])
        mx = P.sbuf("p6_mx", [128, 1], F32)
        tmx = P.sbuf("p6_tmx", [128, 1], F32)
        as_ = P.sbuf("p6_as", [128, 4], F32)
        rsum = P.sbuf("p6_rsum", [128, 1], F32)
        pb = [P.psum(f"p6_pb{i}", [128, 512], F32) for i in range(6)]
        ptr_bf = P.psum("p6_ptrb", [128, 1024], BF16)
        for b in range(BLOC):
            P.dma("sync", xnn_fm_s[:],
                  T["xnn_fm"][b].rearrange("c p t -> p c t"))
            P.dma("sync", xnn_tm_s[:],
                  T["xnn_tmb"][b].rearrange("c p d -> p c d"))
            for em in range(8):
                e0 = em * 512
                P.dma("sync", se_s[:],
                      T["se_fm"][:, :, e0:e0 + 512].rearrange("c p t -> p c t"))
                def disp_tail(sc, scl, dsp):
                    def exps():
                        for tq in range(4):
                            P.act(_ap(dsp)[:, tq * 512:(tq + 1) * 512], pb[tq],
                                  AF.Exp, bias=mx, accum=_ap(as_)[:, tq:tq + 1])
                            yield
                        P.red("vector", rsum, as_, ALU.add)
                        P.recip(rsum, rsum)
                        P.ts("vector", dsp, dsp, rsum, ALU.mult)
                        yield
                        for g in range(2):
                            for j in range(8):
                                tc = g * 8 + j
                                P.transp(ptr_bf[:, j * 128:(j + 1) * 128],
                                         _ap(dsp)[:, tc * 128:(tc + 1) * 128],
                                         eye, register=(j == 7), war=(j == 0))
                            P.cp("vector",
                                 dT_s[:, g * 8:(g + 1) * 8, scl],
                                 ptr_bf[:].rearrange("p (c t) -> p c t", c=8))
                            yield
                    return exps()

                pend = None
                for sc in range(4):
                    scl = slice(sc * 128, (sc + 1) * 128)
                    for tq in range(4):
                        if pend is not None:
                            next(pend, None)
                        for dc in range(DC):
                            P.mm(pb[tq], se_s[:, dc, scl],
                                 xnn_fm_s[:, dc, tq * 512:(tq + 1) * 512],
                                 start=(dc == 0), stop=(dc == DC - 1))
                    if pend is not None:
                        for _ in pend:
                            pass
                    for tq in range(4):
                        if tq == 0:
                            P.red("vector", mx, pb[0], ALU.max)
                        else:
                            P.red("vector", tmx, pb[tq], ALU.max)
                            P.tt("vector", mx, mx, tmx, ALU.max)
                    P.ts("vector", mx, mx, -1.0, ALU.mult)
                    pend = disp_tail(sc, scl, disp_r.nxt())
                for _ in pend:
                    pass
                for dc in range(DC):
                    p_ = pb[4]
                    for tc in range(S // 128):
                        P.mm(p_, xnn_tm_s[:, tc, dc * 128:(dc + 1) * 128],
                             dT_s[:, tc, :], start=(tc == 0),
                             stop=(tc == S // 128 - 1))
                    P.cp("vector", slots_s[:, dc, :], p_)
                for fc in range(FC):
                    w1c = w1t.nxt()
                    P.dma("sync", w1c[:].rearrange("p c f -> p (c f)"),
                          T["w1l1_bf"][fc, :, :])
                    p_ = pb[5]
                    for dc in range(DC):
                        P.mm(p_, w1c[:, dc, :], slots_s[:, dc, :],
                             start=(dc == 0), stop=(dc == DC - 1))
                    P.act(h_s[:, fc, :], p_, AF.Gelu)
                for half in range(2):
                    hsl = slice(half * 512, (half + 1) * 512)
                    for fc in range(FC):
                        w2c = w2t.nxt()
                        P.dma("sync", w2c[:], T["w2l1_bf"][fc, :, hsl])
                        for st in range(4):
                            P.mm(pb[st], h_s[:, fc, st * 128:(st + 1) * 128],
                                 w2c[:], start=(fc == 0), stop=(fc == FC - 1))
                    for st in range(4):
                        P.cp("vector", proc_s[:, st, hsl], pb[st])
                for st in range(4):
                    P.dma("gpsimd",
                          T["proc_tm"][b, e0 + st * 128:e0 + (st + 1) * 128, :],
                          proc_s[:, st, :])



@phase("p7_sm_b",
       reads=["xnn_fm", "se_fm", "proc_tm", "c_eye_bf", "c_ones_f32"],
       writes=["ffn1_tm"])
def p7_sm_b(P, T):
    eye = T["c_eye_bf"]
    NES = 4096
    with P.phase("p7"):
        se_fm_s = P.sbuf("p7_se", [128, DC, NES], BF16)
        xnn_fm_s = P.sbuf("p7_xf", [128, DC, S], BF16)
        combT_s = P.sbuf("p7_cT", [128, NES // 128, 512], BF16)
        comb_r = Rot([P.sbuf(f"p7_comb{i}", [128, NES], BF16) for i in range(2)])
        lg_r = Rot([P.sbuf(f"p7_lg{i}", [128, NES], F32) for i in range(1)])
        out_s = P.sbuf("p7_out", [128, 4, D], F32)
        proc_t = Rot([P.sbuf(f"p7_pt{i}", [128, 512], BF16) for i in range(2)])
        mx = P.sbuf("p7_mx", [128, 1], F32)
        rsum = P.sbuf("p7_rsum", [128, 1], F32)
        sq8 = P.sbuf("p7_sq8", [128, 8], F32)
        pb = [P.psum(f"p7_pb{i}", [128, 512], F32) for i in range(4)]
        ptr_bf = P.psum("p7_ptrb", [128, 1024], BF16)
        P.dma("sync", se_fm_s[:], T["se_fm"].rearrange("c p t -> p c t"))
        for b in range(BLOC):
            P.dma("sync", xnn_fm_s[:],
                  T["xnn_fm"][b].rearrange("c p t -> p c t"))
            for sm in range(4):
                s0 = sm * 512
                def comb_tail(tq, lg, cmb):
                    def gen():
                        P.red("vector", mx, lg, ALU.max)
                        P.ts("vector", mx, mx, -1.0, ALU.mult)
                        P.act(cmb, lg, AF.Exp, bias=mx, accum=rsum)
                        P.recip(rsum, rsum)
                        P.ts("vector", cmb, cmb, rsum, ALU.mult)
                        yield
                        for g in range(4):
                            for j in range(8):
                                etc = g * 8 + j
                                P.transp(ptr_bf[:, j * 128:(j + 1) * 128],
                                         _ap(cmb)[:, etc * 128:(etc + 1) * 128],
                                         eye, register=(j == 7), war=(j == 0))
                            P.cp("vector",
                                 combT_s[:, g * 8:(g + 1) * 8,
                                         tq * 128:(tq + 1) * 128],
                                 ptr_bf[:].rearrange("p (c t) -> p c t", c=8))
                            yield
                    return gen()

                pend = None
                for tq in range(4):
                    tsl = slice(s0 + tq * 128, s0 + (tq + 1) * 128)
                    lg = lg_r.nxt()
                    for eb in range(8):
                        if pend is not None:
                            next(pend, None)
                        p_ = pb[eb % 4]
                        for dc in range(DC):
                            P.mm(p_, xnn_fm_s[:, dc, tsl],
                                 se_fm_s[:, dc, eb * 512:(eb + 1) * 512],
                                 start=(dc == 0), stop=(dc == DC - 1))
                        P.cp("vector", _ap(lg)[:, eb * 512:(eb + 1) * 512], p_)
                    if pend is not None:
                        for _ in pend:
                            pass
                    pend = comb_tail(tq, lg, comb_r.nxt())
                for _ in pend:
                    pass
                for half in range(2):
                    hsl = slice(half * 512, (half + 1) * 512)
                    for etc in range(NES // 128):
                        pc = proc_t.nxt()
                        P.dma("sync", pc[:],
                              T["proc_tm"][b, etc * 128:(etc + 1) * 128, hsl])
                        for st in range(4):
                            P.mm(pb[st], combT_s[:, etc, st * 128:(st + 1) * 128],
                                 pc[:], start=(etc == 0),
                                 stop=(etc == NES // 128 - 1))
                    for st in range(4):
                        P.cp("vector", out_s[:, st, hsl], pb[st])
                for st in range(4):
                    P.dma("gpsimd",
                          T["ffn1_tm"][b * S + s0 + st * 128:
                                       b * S + s0 + (st + 1) * 128, :],
                          out_s[:, st, :])



@phase("p8_out",
       reads=["x1_tm", "att1_tm", "ffn1_tm"],
       writes=["out"])
def p8_out(P, T):
    o = T["out"].rearrange("b s d -> (b s) d")
    with P.phase("p8"):
        xts = Rot([P.sbuf(f"p8_xt{i}", [128, D], F32) for i in range(2)])
        ats = Rot([P.sbuf(f"p8_at{i}", [128, D], F32) for i in range(2)])
        fts = Rot([P.sbuf(f"p8_ft{i}", [128, D], F32) for i in range(2)])
        for ti in range(TLOC // 128):
            sl = slice(ti * 128, (ti + 1) * 128)
            xt, at, ft = xts.nxt(), ats.nxt(), fts.nxt()
            P.dma("sync", xt[:], T["x1_tm"][sl, :])
            P.dma("sync", at[:], T["att1_tm"][sl, :])
            P.dma("sync", ft[:], T["ffn1_tm"][sl, :])
            P.tt("vector", xt, xt, at, ALU.add)
            P.tt("vector", xt, xt, ft, ALU.add)
            P.dma("gpsimd", o[sl, :], xt)



def make_consts():
    return {
        "c_eye_f32": np.eye(128, dtype=np.float32),
        "c_eye_bf": np.eye(128, dtype=ml_bf16()),
        "c_mask_ut": np.triu(np.ones((128, 128), dtype=np.float32)),
        "c_ones_f32": np.ones((128, 128), dtype=np.float32),
        "c_ones_bf": np.ones((128, 128), dtype=ml_bf16()),
    }


def ml_bf16():
    import ml_dtypes
    return ml_dtypes.bfloat16


def _declare(P, names, kind):
    out = {}
    for n in names:
        sh, dt = spec_of(n)
        out[n] = P.dram(n, sh, dt, kind=kind)
    return out


def run_unfused(inputs, upto=None, only=None):
    consts = make_consts()
    x = np.ascontiguousarray(inputs["L0_attn_hidden_states"])
    pools = [dict() for _ in range(NCORES)]
    phases = PHASES[:upto] if upto else PHASES
    if only:
        phases = [p for p in phases if p[0] in only]
    for (name, fn, reads, writes, privs) in phases:
        P = Prog(name)
        T = {}
        T.update(_declare(P, reads, "ExternalInput"))
        T.update(_declare(P, writes, "ExternalOutput"))
        T.update(_declare(P, privs, "Internal"))
        for n in reads:
            if n in CONST_SPECS:
                sh, dt = CONST_SPECS[n]
                sb = P.sbuf("s_" + n, sh, dt)
                P.dma("sync", sb[:], T[n])
                T[n] = sb
        fn(P, T)
        P.barrier()
        nc = P.finalize()
        in_maps = []
        for c in range(NCORES):
            m = {}
            for n in reads:
                if n == "L0_attn_hidden_states":
                    m[n] = x[BLOC * c:BLOC * (c + 1)]
                elif n in INPUT_SPECS:
                    m[n] = inputs[n]
                elif n in CONST_SPECS:
                    m[n] = consts[n]
                else:
                    m[n] = pools[c][n]
            in_maps.append(m)
        res = run_bass_kernel_spmd(nc, in_maps, core_ids=list(range(NCORES)))
        for c in range(NCORES):
            pools[c].update(res.results[c])
    return pools


def run_fused(inputs):
    consts = make_consts()
    x = np.ascontiguousarray(inputs["L0_attn_hidden_states"])
    P = Prog("mega")
    T = {}
    ext_in, scratch = [], set()
    for (name, fn, reads, writes, privs) in PHASES:
        for n in reads:
            if n in INPUT_SPECS or n in CONST_SPECS:
                if n not in T:
                    T[n] = _declare(P, [n], "ExternalInput")[n]
                    ext_in.append(n)
            else:
                scratch.add(n)
        for n in writes:
            if n != "out":
                scratch.add(n)
        for n in privs:
            scratch.add(n)
    T.update(_declare(P, sorted(scratch), "Internal"))
    T.update(_declare(P, ["out"], "ExternalOutput"))
    for n in list(T):
        if n in CONST_SPECS:
            sh, dt = CONST_SPECS[n]
            sb = P.sbuf("s_" + n, sh, dt)
            P.dma("sync", sb[:], T[n])
            T[n] = sb
    for (name, fn, reads, writes, privs) in PHASES:
        P.scope = name
        fn(P, T)
    P.scope = None
    P.barrier()
    nc = P.finalize()
    in_maps = []
    for c in range(NCORES):
        m = {}
        for n in ext_in:
            if n == "L0_attn_hidden_states":
                m[n] = x[BLOC * c:BLOC * (c + 1)]
            elif n in CONST_SPECS:
                m[n] = consts[n]
            else:
                m[n] = inputs[n]
        in_maps.append(m)
    res = run_bass_kernel_spmd(nc, in_maps, core_ids=list(range(NCORES)))
    return np.concatenate([res.results[c]["out"] for c in range(NCORES)],
                          axis=0)


FUSED = True


def kernel(**inputs):
    if FUSED:
        return run_fused(inputs)
    pools = run_unfused(inputs)
    return np.concatenate([pools[c]["out"] for c in range(NCORES)], axis=0)


# version 13
# speedup vs baseline: 1.2255x; 1.0219x over previous
import contextlib
import numpy as np
import concourse.bass as bass
import concourse.mybir as mybir
from concourse.bass_utils import run_bass_kernel_spmd

F32 = mybir.dt.float32
BF16 = mybir.dt.bfloat16
AF = mybir.ActivationFunctionType
ALU = mybir.AluOpType
AX = mybir.AxisListType

NCORES = 8
BLOC = 2
S = 2048
TLOC = BLOC * S
D = 1024
DC = D // 128

ENGINES = ("tensor", "vector", "scalar", "gpsimd", "sync")
DMA_QUEUES = ("gpsimd", "sync")
NSEM = 4


def _ap_name(x):
    if isinstance(x, str):
        return x
    t = getattr(x, "tensor", None)
    if t is not None:
        return t.name
    n = getattr(x, "name", None)
    if n is not None:
        return n
    raise ValueError(f"cannot extract tensor from {x!r}")



def _ap(x):
    if isinstance(x, (float, int)) or x is None:
        return x
    if getattr(x, "tensor", None) is not None:
        return x
    return x[:]

class Prog:

    def __init__(self, name="prog"):
        self.name = name
        self.nc = bass.Bass("TRN2", target_bir_lowering=False)
        self.es = contextlib.ExitStack()
        self.phase_es = None
        self.sems = {}
        self.cnt = {e: 0 for e in ENGINES}
        self.opbuf = {e: [] for e in ENGINES}
        self.reg = {}
        self.aps = {}
        self._cvecs = {}
        self.scope = None
        for e in ENGINES:
            if e in DMA_QUEUES:
                for j in range(NSEM):
                    self.sems[(e, j)] = self.es.enter_context(
                        self.nc.semaphore(f"sem_{e}{j}"))
            else:
                self.sems[e] = self.es.enter_context(
                    self.nc.semaphore(f"sem_{e}"))
        self.known = {e: {} for e in ENGINES}

    def dram(self, name, shape, dtype, kind):
        ap = self.nc.dram_tensor(name, list(shape), dtype, kind=kind).ap()
        self.aps[name] = ap
        return ap

    def sbuf(self, name, shape, dtype):
        if self.phase_es is not None:
            return self.phase_es.enter_context(
                self.nc.sbuf_tensor(name, list(shape), dtype))
        return self.nc.alloc_sbuf_tensor(name, list(shape), dtype)

    def psum(self, name, shape, dtype=F32):
        if self.phase_es is not None:
            return self.phase_es.enter_context(
                self.nc.psum_tensor(name, list(shape), dtype))
        return self.nc.alloc_psum_tensor(name, list(shape), dtype)

    def sbuf_persist(self, name, shape, dtype):
        return self.nc.alloc_sbuf_tensor(name, list(shape), dtype, side="right")

    def cvec(self, val, dtype=F32, rows=128):
        key = (float(val), dtype, rows)
        if key not in self._cvecs:
            t = self.nc.alloc_sbuf_tensor(f"cv{len(self._cvecs)}", [rows, 1], dtype,
                                          side="right")
            self.memset("vector", t[:], float(val))
            self._cvecs[key] = t[:]
        return self._cvecs[key]

    PHASE_GROUPS = [("p0", "p1proj"), ("p2a", "p2b"), ("p4", "p5b"),
                    ("p7", "p8")]

    def _group_of(self, tag):
        for g in self.PHASE_GROUPS:
            if tag in g:
                return g
        return (tag,)

    @contextlib.contextmanager
    def phase(self, tag):
        g = self._group_of(tag)
        if getattr(self, "_open_group", None) == g:
            yield
            if tag == g[-1]:
                self._open_group = None
                self.barrier()
                self.phase_es.close()
                self.phase_es = self._prev_es
            return
        prev = self.phase_es
        self.phase_es = contextlib.ExitStack()
        if len(g) > 1 and tag == g[0]:
            self._open_group = g
            self._prev_es = prev
            yield
            return
        try:
            yield
        finally:
            self.barrier()
            self.phase_es.close()
            self.phase_es = prev

    def _covers(self, eng, idx):
        if eng in DMA_QUEUES:
            out = []
            for j in range(NSEM):
                cnt = (idx - 1 - j) // NSEM + 1 if idx - 1 >= j else 0
                if cnt > 0:
                    out.append(((eng, j), 16 * cnt))
            return out
        return [(eng, idx)]

    def emit(self, eng, thunk, reads=(), writes=(), register=True, war=True):
        waits = {}

        def need(src_eng, src_idx):
            if src_idx <= 0:
                return
            if src_eng == eng == "tensor":
                return
            for key, v in self._covers(src_eng, src_idx):
                if v <= self.known[eng].get(key, 0):
                    continue
                waits[key] = max(waits.get(key, 0), v)

        rnames = [_ap_name(r) for r in reads]
        wnames = [_ap_name(w) for w in writes]
        for n in rnames:
            e = self.reg.get(n)
            if e:
                for we, wi in e["w"].items():
                    need(we, wi)
        for n in wnames:
            e = self.reg.get(n)
            if e:
                for we, wi in e["w"].items():
                    need(we, wi)
                if war:
                    for re_, ri in e["r"].items():
                        need(re_, ri)

        self.cnt[eng] += 1
        idx = self.cnt[eng]
        if eng in DMA_QUEUES:
            semkey = (eng, (idx - 1) % NSEM)
            pv = 16 * ((idx - 1) // NSEM)
            if pv > 0 and pv > self.known[eng].get(semkey, 0):
                waits[semkey] = max(waits.get(semkey, 0), pv)
            inc = 16
        else:
            semkey, inc = eng, 1
        for s, v in waits.items():
            self.known[eng][s] = v
        self.opbuf[eng].append((sorted(waits.items(), key=str),
                                (semkey, inc), thunk, self.scope))
        for n in rnames:
            e = self.reg.setdefault(n, {"w": {}, "r": {}})
            e["r"][eng] = idx
        for n in wnames:
            e = self.reg.setdefault(n, {"w": {}, "r": {}})
            if register:
                e["r"] = {}
                if eng not in DMA_QUEUES:
                    e["w"] = {k: v for k, v in e["w"].items()
                              if k in DMA_QUEUES or k != eng}
                e["w"][eng] = idx
        return idx

    def barrier(self):
        snap = {e: self.cnt[e] for e in ENGINES}
        targets = []
        for s in ENGINES:
            if snap[s] == 0:
                continue
            if s in DMA_QUEUES:
                for j in range(NSEM):
                    u = (snap[s] - j + NSEM - 1) // NSEM
                    if u > 0:
                        targets.append(((s, j), 16 * u, s))
            else:
                targets.append((s, snap[s], s))
        for e in ENGINES:
            waits = []
            for key, v, src in targets:
                if src == e:
                    continue
                if v > self.known[e].get(key, 0):
                    waits.append((key, v))
                    self.known[e][key] = v
            if waits:
                self.opbuf[e].append((waits, None, None, self.scope))

    def finalize(self):
        with self.nc.Block() as blk:
            for e in ENGINES:
                ops = self.opbuf[e]
                if not ops:
                    continue
                sec = getattr(blk, e)

                def body(eng, ops=ops, e=e):
                    cur = [None]
                    st = contextlib.ExitStack()
                    for waits, inc_spec, thunk, scope in ops:
                        if scope != cur[0]:
                            st.close()
                            st = contextlib.ExitStack()
                            if scope is not None:
                                st.enter_context(self.nc.named_scope(scope))
                            cur[0] = scope
                        for (key, val) in waits:
                            eng.wait_ge(self.sems[key], val)
                        if thunk is not None:
                            ins = thunk(eng)
                            semkey, inc = inc_spec
                            ins.then_inc(self.sems[semkey], inc)
                    st.close()

                sec(body)
        self.es.close()
        return self.nc

    def dma_t(self, q, out, in_, rnames, wnames):
        assert q in DMA_QUEUES
        out, in_ = _ap(out), _ap(in_)
        self.emit(q, lambda e: e.dma_start_transpose(out, in_),
                  reads=rnames, writes=wnames)

    def dma(self, q, out, in_, **kw):
        assert q in DMA_QUEUES
        out, in_ = _ap(out), _ap(in_)
        self.emit(q, lambda e: e.dma_start(out=out, in_=in_, **kw),
                  reads=[in_], writes=[out])

    def mm(self, out, lhsT, rhs, start=True, stop=True, register=None,
           war=None, extra_reads=()):
        out, lhsT, rhs = _ap(out), _ap(lhsT), _ap(rhs)
        if register is None:
            register = stop
        if war is None:
            war = start
        self.emit("tensor",
                  lambda e: e.matmul(out, lhsT, rhs, start=start, stop=stop),
                  reads=[lhsT, rhs] + list(extra_reads), writes=[out],
                  register=register, war=war)

    def transp(self, out, in_, ident, register=True, war=True):
        out, in_, ident = _ap(out), _ap(in_), _ap(ident)
        self.emit("tensor", lambda e: e.transpose(out, in_, ident),
                  reads=[in_, ident], writes=[out], register=register, war=war)

    def cp(self, eng, out, in_):
        out, in_ = _ap(out), _ap(in_)
        if eng == "scalar":
            self.emit(eng, lambda e: e.copy(out, in_),
                      reads=[in_], writes=[out])
        else:
            self.emit(eng, lambda e: e.tensor_copy(out, in_),
                      reads=[in_], writes=[out])

    def tt(self, eng, out, a, b, op):
        out, a, b = _ap(out), _ap(a), _ap(b)
        self.emit(eng, lambda e: e.tensor_tensor(out, a, b, op),
                  reads=[a, b], writes=[out])

    def ts(self, eng, out, a, s1, op, s2=None, op2=None, accum=None):
        out, a, s1, s2, accum = _ap(out), _ap(a), _ap(s1), _ap(s2), _ap(accum)
        extra = [s1] if not isinstance(s1, float) else []
        if s2 is not None and not isinstance(s2, float):
            extra.append(s2)
        kw = {}
        if op2 is not None:
            kw["op1"] = op2
        if accum is not None:
            kw["accum_out"] = accum
        wr = [out] + ([accum] if accum is not None else [])
        self.emit(eng, lambda e: e.tensor_scalar(out, a, s1, s2, op, **kw),
                  reads=[a] + extra, writes=wr)

    def stt(self, eng, out, in0, scalar, in1, op0, op1):
        out, in0, scalar, in1 = _ap(out), _ap(in0), _ap(scalar), _ap(in1)
        extra = [scalar] if not isinstance(scalar, float) else []
        self.emit(eng, lambda e: e.scalar_tensor_tensor(out, in0, scalar, in1, op0, op1),
                  reads=[in0, in1] + extra, writes=[out])

    def act(self, out, in_, func, bias=0.0, scale=1.0, accum=None):
        out, in_, bias, scale, accum = _ap(out), _ap(in_), _ap(bias), _ap(scale), _ap(accum)
        if isinstance(bias, float) and bias != 0.0:
            rows = _ap(in_).shape[0]
            bias = self.cvec(bias, rows=rows if rows in (1, 64, 128) else 128)
        extra = [x for x in (bias, scale) if not isinstance(x, float)]
        kw = {}
        if accum is not None:
            kw["accum_out"] = accum
        wr = [out] + ([accum] if accum is not None else [])
        self.emit("scalar",
                  lambda e: e.activation(out, in_, func, bias=bias, scale=scale, **kw),
                  reads=[in_] + extra, writes=wr)

    def red(self, eng, out, in_, op, absval=False, negate=False):
        out, in_ = _ap(out), _ap(in_)
        self.emit(eng,
                  lambda e: e.tensor_reduce(out, in_, axis=AX.X, op=op,
                                            apply_absolute_value=absval,
                                            negate=negate),
                  reads=[in_], writes=[out])

    def recip(self, out, in_):
        out, in_ = _ap(out), _ap(in_)
        self.emit("vector", lambda e: e.reciprocal(out, in_),
                  reads=[in_], writes=[out])

    def memset(self, eng, ap, v):
        ap = _ap(ap)
        self.emit(eng, lambda e: e.memset(ap, v), reads=[], writes=[ap])


INPUT_SPECS = {
    "L0_attn_hidden_states": ((16, 2048, 1024), F32),
    "L0_attn_Wq": ((1024, 1024), F32), "L0_attn_Wk": ((1024, 1024), F32),
    "L0_attn_Wv": ((1024, 1024), F32), "L0_attn_Wo": ((1024, 1024), F32),
    "L0_attn_norm_w": ((128,), F32),
    "L0_ffn_ln1_g": ((1024,), F32), "L0_ffn_ln1_b": ((1024,), F32),
    "L0_ffn_ln2_g": ((1024,), F32), "L0_ffn_ln2_b": ((1024,), F32),
    "L0_ffn_Wq": ((1024, 1024), F32), "L0_ffn_Wk": ((1024, 1024), F32),
    "L0_ffn_Wv": ((1024, 1024), F32), "L0_ffn_Wo": ((1024, 1024), F32),
    "L0_ffn_Wg": ((1024, 8), F32),
    "L0_ffn_W1": ((8, 1024, 4096), F32), "L0_ffn_b1": ((8, 4096), F32),
    "L0_ffn_W2": ((8, 4096, 1024), F32), "L0_ffn_b2": ((8, 1024), F32),
    "L1_attn_w_qkv": ((3072, 1024), F32), "L1_attn_b_qkv": ((3072,), F32),
    "L1_attn_w_out": ((1024, 1024), F32), "L1_attn_b_out": ((1024,), F32),
    "L1_ffn_gamma_x": ((1024,), F32), "L1_ffn_gamma_slot": ((1024,), F32),
    "L1_ffn_slot_embeds": ((8, 512, 1024), F32),
    "L1_ffn_w1": ((1024, 4096), F32), "L1_ffn_w2": ((4096, 1024), F32),
}
CONST_SPECS = {
    "c_eye_f32": ((128, 128), F32),
    "c_eye_bf": ((128, 128), BF16),
    "c_mask_ut": ((128, 128), F32),
    "c_ones_f32": ((128, 128), F32),
    "c_ones_bf": ((128, 128), BF16),
}
SCRATCH_SPECS = {
    "xn_tm": ((TLOC, D), F32),
    "xn_fm": ((DC, 128, TLOC), BF16),
    "att0_tm": ((TLOC, D), F32),
    "x2_tm": ((TLOC, D), F32),
    "m_fm": ((DC, 128, TLOC), BF16),
    "cw_tm": ((TLOC, 8), F32),
    "ffn0_tm": ((TLOC, D), F32),
    "w1m_bf": ((8, DC, 128, 4096), BF16),
    "w2m_bf": ((8, 32, 128, D), BF16),
    "x1_tm": ((TLOC, D), F32),
    "xn1_tm": ((TLOC, D), F32),
    "xn1_fm": ((DC, 128, TLOC), BF16),
    "att1_tm": ((TLOC, D), F32),
    "xnn_fm": ((BLOC, DC, 128, S), BF16),
    "se_fm": ((DC, 128, 4096), BF16),
    "proc_tm": ((BLOC, 4096, D), BF16),
    "ffn1_tm": ((TLOC, D), F32),
}
OUTPUT_SPECS = {"out": ((BLOC, S, D), F32)}

PRIV_SPECS = {
    "q0_fm": ((BLOC, DC, 128, S), BF16),
    "k0_fm": ((BLOC, DC, 128, S), BF16),
    "v0_fm": ((BLOC, DC, 128, S), BF16),
    "o0_fm": ((BLOC, DC, 128, S), BF16),
    "h0_fm": ((DC, 128, TLOC), BF16),
    "qf_fm": ((BLOC, DC, 128, S), BF16),
    "kf_fm": ((BLOC, DC, 128, S), BF16),
    "vf_tm": ((BLOC, 16, 128, D), BF16),
    "ctx0_fm": ((BLOC, DC, 128, S), BF16),
    "m_fm32": ((DC, 128, TLOC), F32),
    "qkv1_fm": ((BLOC, 24, 128, S), BF16),
    "v1_tm": ((BLOC, 16, 128, D), BF16),
    "ctx1_fm": ((BLOC, DC, 128, S), BF16),
    "xnn_tmb": ((BLOC, 16, 128, D), BF16),
    "w1l1_bf": ((32, 128, D), BF16),
    "w2l1_bf": ((32, 128, D), BF16),
}

PHASES = []


def phase(name, reads, writes, privs=()):
    def deco(fn):
        PHASES.append((name, fn, list(reads), list(writes), list(privs)))
        return fn
    return deco


def spec_of(name):
    for d in (INPUT_SPECS, CONST_SPECS, SCRATCH_SPECS, OUTPUT_SPECS, PRIV_SPECS):
        if name in d:
            sh, dt = d[name]
            if name == "L0_attn_hidden_states":
                sh = (BLOC, S, D)
            return sh, dt
    raise KeyError(name)



def ln_stats(P, xt, mu, rstd, xc, sq, acc, eps=1e-5):
    P.red("vector", mu, xt, ALU.add)
    P.ts("vector", mu, mu, 1.0 / D, ALU.mult)
    P.ts("vector", xc, xt, mu, ALU.subtract)
    P.act(sq, xc, AF.Square, accum=acc)
    P.act(acc, acc, AF.Ln, scale=1.0 / D, bias=eps)
    P.act(rstd, acc, AF.Exp, scale=-0.5)


def tm_tile_to_fm(P, tag, src_bf, eye_bf, dst_hbm_fm, ti, psum_bf, stage_bf,
                  n_chunks=DC, also_f32=None, psum32=None, eye32=None,
                  src32=None, stage32=None, dst32=None):
    for dc in range(n_chunks):
        P.transp(psum_bf[:, dc * 128:(dc + 1) * 128],
                 src_bf[:, dc * 128:(dc + 1) * 128], eye_bf,
                 register=(dc == n_chunks - 1), war=(dc == 0))
    P.cp("vector", stage_bf[:], psum_bf[:])
    P.dma("gpsimd",
          dst_hbm_fm[:, :, ti * 128:(ti + 1) * 128].rearrange("c p t -> p c t"),
          stage_bf[:].rearrange("p (c t) -> p c t", c=n_chunks))
    if also_f32:
        for dc in range(n_chunks):
            P.transp(psum32[:, dc * 128:(dc + 1) * 128],
                     src32[:, dc * 128:(dc + 1) * 128], eye32,
                     register=(dc == n_chunks - 1), war=(dc == 0))
        P.cp("vector", stage32[:], psum32[:])
        P.dma("gpsimd",
              dst32[:, :, ti * 128:(ti + 1) * 128].rearrange("c p t -> p c t"),
              stage32[:].rearrange("p (c t) -> p c t", c=n_chunks))



@phase("p0_ln0",
       reads=["L0_attn_hidden_states", "c_eye_bf"],
       writes=["xn_tm", "xn_fm"])
def p0_ln0(P, T):
    x = T["L0_attn_hidden_states"].rearrange("b s d -> (b s) d")
    eye_bf = T["c_eye_bf"]
    with P.phase("p0"):
        R2 = lambda nm, sh, dt: Rot([P.sbuf(f"{nm}{i}", sh, dt) for i in range(2)])
        xt_r = R2("p0_xt", [128, D], F32)
        xc_r = R2("p0_xc", [128, D], F32)
        xnb_r = R2("p0_xnb", [128, D], BF16)
        mu_r = R2("p0_mu", [128, 1], F32)
        rstd_r = R2("p0_rstd", [128, 1], F32)
        lsq = P.sbuf("p0_sq", [128, D], F32)
        lacc_r = R2("p0_acc", [128, 1], F32)
        ptr_r = Rot([P.psum(f"p0_ptr{i}", [128, D], BF16) for i in range(2)])
        stg_r = R2("p0_stg", [128, D], BF16)
        for ti in range(TLOC // 128):
            sl = slice(ti * 128, (ti + 1) * 128)
            xt, xc, xnb = xt_r.nxt(), xc_r.nxt(), xnb_r.nxt()
            mu, rstd, lacc = mu_r.nxt(), rstd_r.nxt(), lacc_r.nxt()
            ptr, stg = ptr_r.nxt(), stg_r.nxt()
            P.dma("sync", xt[:], x[sl, :])
            ln_stats(P, xt, mu, rstd, xc, lsq, lacc)
            P.ts("vector", xc[:], xc[:], rstd[:], ALU.mult)
            P.dma("gpsimd", T["xn_tm"][sl, :], xc[:])
            P.cp("scalar", xnb[:], xc[:])
            tm_tile_to_fm(P, "p0", xnb, eye_bf, T["xn_fm"], ti, ptr, stg)


class Rot:
    def __init__(self, items):
        self.items = list(items)
        self.i = 0

    def nxt(self):
        r = self.items[self.i % len(self.items)]
        self.i += 1
        return r


def load_wT(P, tag, w_dram, wt_sb, eye_bf, n_out=D, n_in=D, bufs=None):
    nic = n_in // 128
    if bufs is None:
        bufs = wT_bufs(P, tag, n_in)
    f32t, bft, ptr = bufs
    for oc in range(n_out // 128):
        f3, bf, pt = f32t.nxt(), bft.nxt(), ptr.nxt()
        P.dma("sync", f3[:], w_dram[oc * 128:(oc + 1) * 128, :])
        P.cp("vector", bf, f3)
        for dc in range(nic):
            P.transp(_ap(pt)[:, dc * 128:(dc + 1) * 128],
                     _ap(bf)[:, dc * 128:(dc + 1) * 128], eye_bf,
                     register=(dc == nic - 1), war=(dc == 0))
        P.cp("scalar",
             wt_sb[:, :, oc * 128:(oc + 1) * 128],
             _ap(pt).rearrange("p (c t) -> p c t", c=nic))


def wT_bufs(P, tag, n_in=D):
    return (
        Rot([P.sbuf(f"{tag}_wl32{i}", [128, n_in], F32) for i in range(2)]),
        Rot([P.sbuf(f"{tag}_wlbf{i}", [128, n_in], BF16) for i in range(2)]),
        Rot([P.psum(f"{tag}_wlps{i}", [128, n_in], BF16) for i in range(2)]),
    )


def load_w(P, tag, w_dram, w_sb, n_out):
    f32t = P.sbuf(f"{tag}_wl32", [128, n_out], F32)
    for dc in range(DC):
        P.dma("sync", f32t[:], w_dram[dc * 128:(dc + 1) * 128, :])
        P.cp("vector", w_sb[:, dc, :], f32t)


def row_of(P, tag, vec_dram, n):
    r = P.sbuf(f"{tag}_row", [1, n], F32)
    P.dma("sync", r[:], vec_dram.rearrange("(a d) -> a d", a=1))
    return r



@phase("p1_att0",
       reads=["xn_fm", "L0_attn_Wq", "L0_attn_Wk", "L0_attn_Wv", "L0_attn_Wo",
              "L0_attn_norm_w", "c_eye_bf", "c_mask_ut", "c_ones_bf",
              "c_ones_f32"],
       writes=["att0_tm"],
       privs=["q0_fm", "k0_fm", "v0_fm", "o0_fm"])
def p1_att0(P, T):
    eye = T["c_eye_bf"]
    ones_bf_col = _ap(T["c_ones_bf"])[:, 0:1]
    ones_f32_col = _ap(T["c_ones_f32"])[:, 0:1]
    mask = T["c_mask_ut"]

    P.scope = "p1_proj"
    with P.phase("p1proj"):
        wts = {}
        wb = wT_bufs(P, "p1w")
        for wn, key in (("L0_attn_Wq", "q0_fm"), ("L0_attn_Wk", "k0_fm"),
                        ("L0_attn_Wv", "v0_fm")):
            wt = P.sbuf(f"p1_{wn}T", [128, DC, D], BF16)
            load_wT(P, f"p1{wn}", T[wn], wt, eye, bufs=wb)
            wts[key] = wt
        xt = P.sbuf("p1_xt", [128, DC, 512], BF16)
        pos = Rot([P.psum(f"p1_po{i}", [128, 512], F32) for i in range(2)])
        obs = Rot([P.sbuf(f"p1_ob{i}", [128, 512], BF16) for i in range(2)])
        for b in range(BLOC):
            for tmac in range(S // 512):
                t0 = b * S + tmac * 512
                P.dma("sync", xt[:],
                      T["xn_fm"][:, :, t0:t0 + 512].rearrange("c p t -> p c t"))
                for key, wt in wts.items():
                    for ec in range(DC):
                        po, ob = pos.nxt(), obs.nxt()
                        for dc in range(DC):
                            P.mm(po, wt[:, dc, ec * 128:(ec + 1) * 128],
                                 xt[:, dc, :], start=(dc == 0), stop=(dc == DC - 1))
                        P.cp("vector", ob, po)
                        P.dma("gpsimd",
                              T[key][b, ec, :, tmac * 512:(tmac + 1) * 512], ob)

    NCH = S // 128
    P.scope = "p1_chunk"
    with P.phase("p1attn"):
        nw_row = row_of(P, "p1_nw", T["L0_attn_norm_w"], 128)
        qh = P.sbuf("p1_qh", [128, S], BF16)
        kh = P.sbuf("p1_kh", [128, S], BF16)
        vh = P.sbuf("p1_vh", [128, S], BF16)
        kv_s = P.sbuf("p1_kv", [128, NCH, 256], BF16)
        scst_r = Rot([P.sbuf(f"p1_scst{i}", [128, 128], BF16) for i in range(2)])
        state_s = P.sbuf("p1_state", [128, 129], BF16)
        den_r = Rot([P.sbuf(f"p1_den{i}", [1, 128], F32) for i in range(2)])
        d2_s = P.sbuf("p1_d2", [1, 128], F32)
        osq_s = P.sbuf("p1_osq", [128, 128], F32)
        t_s = P.sbuf("p1_t", [1, 128], F32)
        al_s = P.sbuf("p1_al", [1, 128], F32)
        scale_s = P.sbuf("p1_scale", [128, 128], F32)
        on_r = Rot([P.sbuf(f"p1_on{i}", [128, 128], BF16) for i in range(2)])
        ps_sc = Rot([P.psum(f"p1_pssc{i}", [128, 128], F32) for i in range(1)])
        ps_kv = P.psum("p1_pskv", [128, 256], BF16)
        ps_o = Rot([P.psum(f"p1_pso{i}", [128, 128], F32) for i in range(2)])
        ps_ds = Rot([P.psum(f"p1_psds{i}", [128, 512], F32) for i in range(2)])
        ps_state = P.psum("p1_psst", [128, 129], F32)
        ps_scale = P.psum("p1_pssca", [128, 128], F32)

        def epi(b, h, c, csl, po, pd):
            def run():
                den_s = den_r.nxt()
                on_s = on_r.nxt()
                P.act(den_s, _ap(pd)[0:1, 0:128], AF.Abs)
                P.ts("vector", den_s, den_s, 1.0, ALU.max)
                P.act(osq_s, po, AF.Square)
                P.mm(_ap(pd)[0:1, 128:256], ones_f32_col, osq_s)
                P.tt("vector", d2_s, den_s, den_s, ALU.mult)
                P.ts("vector", d2_s, d2_s, 1e-5, ALU.mult)
                P.stt("vector", t_s, _ap(pd)[0:1, 128:256], 1.0 / 128.0, d2_s,
                      ALU.mult, ALU.add)
                P.act(t_s, t_s, AF.Ln)
                P.act(al_s, t_s, AF.Exp, scale=-0.5)
                P.mm(ps_scale, nw_row, al_s)
                P.cp("vector", scale_s, ps_scale)
                P.tt("vector", on_s, po, scale_s, ALU.mult)
                P.dma("gpsimd", T["o0_fm"][b, h, :, csl], on_s)
            return run

        pend = None
        for b in range(BLOC):
            for h in range(8):
                P.dma("sync", qh[:], T["q0_fm"][b, h, :, :])
                P.dma("sync", kh[:], T["k0_fm"][b, h, :, :])
                P.dma("sync", vh[:], T["v0_fm"][b, h, :, :])
                for c in range(NCH):
                    csl = slice(c * 128, (c + 1) * 128)
                    P.transp(ps_kv[:, 0:128], _ap(kh)[:, csl], eye,
                             register=False, war=True)
                    P.transp(ps_kv[:, 128:256], _ap(vh)[:, csl], eye,
                             register=True, war=False)
                    P.emit("vector",
                           lambda e, o=_ap(kv_s)[:, c, :], i=_ap(ps_kv):
                           e.tensor_copy(o, i),
                           reads=[ps_kv], writes=[f"p1kv@{c}"])
                    ktc = _ap(kv_s)[:, c, 0:128]
                    vtc = _ap(kv_s)[:, c, 128:256]
                    psc = ps_sc.nxt()
                    po = ps_o.nxt()
                    pd = ps_ds.nxt()
                    scst = scst_r.nxt()
                    P.mm(psc, kh[:, csl], qh[:, csl])
                    P.emit("vector",
                           lambda e, o=_ap(scst), a=_ap(psc), m=_ap(mask):
                           e.tensor_tensor(o, a, m, ALU.mult),
                           reads=[psc, mask, f"p1kv@{c}"], writes=[scst])
                    if c > 0:
                        P.cp("scalar", state_s, ps_state)
                        P.mm(po, _ap(state_s)[:, 0:128], qh[:, csl],
                             start=True, stop=False)
                        P.mm(po, vtc, scst, start=False, stop=True,
                             extra_reads=[f"p1kv@{c}"])
                        P.mm(_ap(pd)[0:1, 0:128], _ap(state_s)[:, 128:129],
                             qh[:, csl], start=True, stop=False)
                        P.mm(_ap(pd)[0:1, 0:128], ones_bf_col, scst,
                             start=False, stop=True)
                    else:
                        P.mm(po, vtc, scst, start=True, stop=True,
                             extra_reads=[f"p1kv@{c}"])
                        P.mm(_ap(pd)[0:1, 0:128], ones_bf_col, scst,
                             start=True, stop=True)
                    P.mm(ps_state[:, 0:128], ktc, vtc,
                         start=(c == 0), stop=False, register=False,
                         extra_reads=[f"p1kv@{c}"])
                    P.mm(ps_state[:, 128:129], ktc, ones_bf_col,
                         start=False, stop=(c == NCH - 1), register=True,
                         extra_reads=[f"p1kv@{c}"])
                    if pend is not None:
                        pend()
                    pend = epi(b, h, c, csl, po, pd)
        pend()

    P.scope = "p1_wo"
    with P.phase("p1wo"):
        woT = P.sbuf("p1_woT", [128, DC, D], BF16)
        load_wT(P, "p1wo", T["L0_attn_Wo"], woT, eye)
        ots = Rot([P.sbuf(f"p1_ot{i}", [128, DC, 128], BF16) for i in range(2)])
        pws = Rot([P.psum(f"p1_pw{i}", [128, 512], F32) for i in range(4)])
        oss = Rot([P.sbuf(f"p1_os{i}", [128, D], F32) for i in range(2)])
        for b in range(BLOC):
            for tt_ in range(S // 128):
                ts_ = slice(tt_ * 128, (tt_ + 1) * 128)
                ot = ots.nxt()
                os_ = oss.nxt()
                P.dma("sync", ot[:],
                      T["o0_fm"][b, :, :, ts_].rearrange("c p t -> p c t"))
                for half in range(2):
                    pw = pws.nxt()
                    for ec in range(DC):
                        P.mm(pw, ot[:, ec, :],
                             woT[:, ec, half * 512:(half + 1) * 512],
                             start=(ec == 0), stop=(ec == DC - 1))
                    P.cp("vector", os_[:, half * 512:(half + 1) * 512], pw)
                P.dma("gpsimd", T["att0_tm"][b * S + tt_ * 128:
                                             b * S + (tt_ + 1) * 128, :], os_)


def bcast_row(P, tag, row_ap, n, ones_f32, scale=1.0):
    ps = P.psum(f"{tag}_bps", [128, 512], F32)
    out = P.sbuf(f"{tag}_bc", [128, n], F32)
    for j in range(0, n, 512):
        w = min(512, n - j)
        P.mm(ps[:, 0:w], _ap(ones_f32)[0:1, :], _ap(row_ap)[0:1, j:j + w])
        if scale == 1.0:
            P.cp("vector", out[:, j:j + w], ps[:, 0:w])
        else:
            P.ts("vector", out[:, j:j + w], ps[:, 0:w], float(scale), ALU.mult)
    return out


def softmax_TQ(P, bufs, kh, qh, v_s, head, negc_s, out_dst, scale, bv=None):
    (ps_sT, E_r, ps_ctx_r, ps_sig, sig_r, sg_r, ctx_r) = bufs
    NK2 = S // 256
    c0 = 65 * head

    def make_epilogue(pc, qsl):
        sig_s, sg_s, ctxb = sig_r.nxt(), sg_r.nxt(), ctx_r.nxt()

        def epi():
            P.act(sig_s, _ap(pc)[64:65, :], AF.Ln)
            P.act(sig_s, sig_s, AF.Exp, scale=-1.0)
            P.mm(_ap(ps_sig)[0:64, :], P.ones_f32_row64, sig_s)
            P.cp("vector", sg_s, _ap(ps_sig)[0:64, :])
            if bv is None:
                P.tt("vector", ctxb, _ap(pc)[0:64, :], sg_s, ALU.mult)
            else:
                P.tt("vector", sg_s, _ap(pc)[0:64, :], sg_s, ALU.mult)
                P.ts("vector", ctxb, sg_s, bv, ALU.add)
            P.dma("gpsimd", out_dst[:, qsl], ctxb)
        return epi

    pend = None
    for qm in range(S // 512):
        qsl = slice(qm * 512, (qm + 1) * 512)
        pc = ps_ctx_r.nxt()
        prev = None
        for kt2 in range(NK2):
            ka, kb = 2 * kt2, 2 * kt2 + 1
            pt = ps_sT.nxt()
            P.mm(_ap(pt)[:, 0:512], kh[:, ka * 128:(ka + 1) * 128],
                 qh[:, qsl], register=False)
            P.mm(_ap(pt)[:, 512:1024], kh[:, kb * 128:(kb + 1) * 128],
                 qh[:, qsl], war=False)
            E = E_r.nxt()
            P.act(E, pt, AF.Exp, bias=negc_s, scale=scale)
            if kt2 == 1 and pend is not None:
                pend()
                pend = None
            if prev is not None:
                pa, pE = prev
                P.mm(_ap(pc)[0:65, :], v_s[:, pa, c0:c0 + 65],
                     _ap(pE)[:, 0:512],
                     start=(pa == 0), stop=False, register=False)
                P.mm(_ap(pc)[0:65, :], v_s[:, pa + 1, c0:c0 + 65],
                     _ap(pE)[:, 512:1024], start=False,
                     stop=False)
            prev = (ka, E)
        pa, pE = prev
        P.mm(_ap(pc)[0:65, :], v_s[:, pa, c0:c0 + 65], _ap(pE)[:, 0:512],
             start=(pa == 0), stop=False, register=False)
        P.mm(_ap(pc)[0:65, :], v_s[:, pa + 1, c0:c0 + 65],
             _ap(pE)[:, 512:1024], start=False, stop=True)
        pend = make_epilogue(pc, qsl)
    pend()


def head_bound(P, bufs, qh, kh, negc_s, scale_mag):
    (sq_s, ps_n, nmax, tmp1) = bufs
    for which, src in ((0, qh), (1, kh)):
        P.tt("vector", sq_s, src, src, ALU.mult)
        for j in range(S // 512):
            P.mm(ps_n[0:1, :], P.ones_f32_col64, sq_s[:, j * 512:(j + 1) * 512])
            P.red("vector", tmp1, ps_n[0:1, :], ALU.max)
            if j == 0:
                P.cp("vector", nmax[0:1, which:which + 1], tmp1)
            else:
                P.tt("vector", nmax[0:1, which:which + 1],
                     nmax[0:1, which:which + 1], tmp1, ALU.max)
    P.tt("vector", tmp1, nmax[0:1, 0:1], nmax[0:1, 1:2], ALU.mult)
    P.act(tmp1, tmp1, AF.Ln, bias=1e-30)
    P.act(tmp1, tmp1, AF.Exp, scale=0.5)
    P.mm(ps_n[:, 0:1], _ap(P.ones_f32)[0:1, :], tmp1)
    P.act(negc_s, ps_n[:, 0:1], AF.Copy, scale=-0.75 * scale_mag)



@phase("p2_mha0",
       reads=["xn_tm", "xn_fm", "L0_ffn_ln1_g", "L0_ffn_ln1_b",
              "L0_ffn_ln2_g", "L0_ffn_ln2_b", "L0_ffn_Wq", "L0_ffn_Wk",
              "L0_ffn_Wv", "L0_ffn_Wo", "L0_ffn_Wg", "L0_ffn_W1", "L0_ffn_W2",
              "c_eye_bf", "c_eye_f32", "c_ones_f32", "c_ones_bf"],
       writes=["x2_tm", "m_fm", "cw_tm", "w1m_bf", "w2m_bf"],
       privs=["h0_fm", "qf_fm", "kf_fm", "vf_tm", "ctx0_fm", "m_fm32"])
def p2_mha0(P, T):
    eye = T["c_eye_bf"]
    eye32 = T["c_eye_f32"]
    P.ones_f32 = T["c_ones_f32"]
    P.ones_bf_col = _ap(T["c_ones_bf"])[:, 0:1]
    P.ones_f32_col64 = _ap(T["c_ones_f32"])[0:64, 0:1]
    P.ones_f32_row64 = _ap(T["c_ones_f32"])[0:1, 0:64]
    bnd_q = P.sbuf_persist("p2_bndq", [1, 16], F32)
    bnd_k = P.sbuf_persist("p2_bndk", [1, 16], F32)
    bt1 = P.sbuf_persist("p2_bt1", [1, 1], F32)
    P.memset("vector", bnd_q[:], 0.0)
    P.memset("vector", bnd_k[:], 0.0)

    P.scope = "p2_a"
    with P.phase("p2a"):
        g1r = row_of(P, "p2g1", T["L0_ffn_ln1_g"], D)
        b1r = row_of(P, "p2b1", T["L0_ffn_ln1_b"], D)
        g1 = bcast_row(P, "p2g1", g1r, D, P.ones_f32)
        b1 = bcast_row(P, "p2b1", b1r, D, P.ones_f32)
        R2 = lambda nm, sh, dt: Rot([P.sbuf(f"{nm}{i}", sh, dt) for i in range(2)])
        xt_r = R2("p2_xt", [128, D], F32)
        xc_r = R2("p2_xc", [128, D], F32)
        hb_r = R2("p2_hb", [128, D], BF16)
        mu_r = R2("p2_mu", [128, 1], F32)
        rstd_r = R2("p2_rstd", [128, 1], F32)
        lsq = P.sbuf("p2_sq", [128, D], F32)
        lacc_r = R2("p2_acc", [128, 1], F32)
        ptr_r = Rot([P.psum(f"p2_ptr{i}", [128, D], BF16) for i in range(2)])
        stg_r = R2("p2_stg", [128, D], BF16)
        for ti in range(TLOC // 128):
            sl = slice(ti * 128, (ti + 1) * 128)
            xt, xc, hb = xt_r.nxt(), xc_r.nxt(), hb_r.nxt()
            mu, rstd, lacc = mu_r.nxt(), rstd_r.nxt(), lacc_r.nxt()
            ptr, stg = ptr_r.nxt(), stg_r.nxt()
            P.dma("sync", xt[:], T["xn_tm"][sl, :])
            ln_stats(P, xt, mu, rstd, xc, lsq, lacc)
            P.ts("vector", xc[:], xc[:], rstd[:], ALU.mult)
            P.tt("vector", xc, xc, g1, ALU.mult)
            P.tt("vector", xc, xc, b1, ALU.add)
            P.cp("scalar", hb, xc)
            tm_tile_to_fm(P, "p2", hb, eye, T["h0_fm"], ti, ptr, stg)

    P.scope = "p2_b"
    with P.phase("p2b"):
        wq = P.sbuf("p2_wq", [128, DC, D], BF16)
        wk = P.sbuf("p2_wk", [128, DC, D], BF16)
        wv = P.sbuf("p2_wv", [128, DC, D], BF16)
        load_w(P, "p2wq", T["L0_ffn_Wq"], wq, D)
        load_w(P, "p2wk", T["L0_ffn_Wk"], wk, D)
        load_w(P, "p2wv", T["L0_ffn_Wv"], wv, D)
        xt = P.sbuf("p2_xh", [128, DC, 512], BF16)
        pos = Rot([P.psum(f"p2_po{i}", [128, 512], F32) for i in range(3)])
        obs = Rot([P.sbuf(f"p2_ob{i}", [128, 512], BF16) for i in range(4)])
        ps_bnd = P.psum("p2_psbnd", [128, 512], F32)
        sqb = P.sbuf("p2_sqb", [128, 512], BF16)
        ones_bf = _ap(T["c_ones_bf"])
        for b in range(BLOC):
            for tmac in range(S // 512):
                t0 = b * S + tmac * 512
                P.dma("sync", xt[:],
                      T["h0_fm"][:, :, t0:t0 + 512].rearrange("c p t -> p c t"))
                for key, wt, bnd in (("qf_fm", wq, bnd_q), ("kf_fm", wk, bnd_k)):
                    for ec in range(DC):
                        po, ob = pos.nxt(), obs.nxt()
                        for dc in range(DC):
                            P.mm(po, wt[:, dc, ec * 128:(ec + 1) * 128],
                                 xt[:, dc, :], start=(dc == 0),
                                 stop=(dc == DC - 1))
                        P.cp("vector", ob, po)
                        P.dma("gpsimd",
                              T[key][b, ec, :, tmac * 512:(tmac + 1) * 512], ob)
                        P.tt("vector", sqb, ob, ob, ALU.mult)
                        for sub in (0, 64):
                            hh = 2 * ec + sub // 64
                            P.mm(_ap(ps_bnd)[0:1, :],
                                 ones_bf[sub:sub + 64, 0:1],
                                 _ap(sqb)[sub:sub + 64, :])
                            P.red("vector", bt1, _ap(ps_bnd)[0:1, :], ALU.max)
                            P.tt("vector", bnd[0:1, hh:hh + 1],
                                 bnd[0:1, hh:hh + 1], bt1, ALU.max)
                for st in range(4):
                    tc = tmac * 4 + st
                    for half in range(2):
                        po, ob = pos.nxt(), obs.nxt()
                        hsl = slice(half * 512, (half + 1) * 512)
                        for dc in range(DC):
                            P.mm(po, xt[:, dc, st * 128:(st + 1) * 128],
                                 wv[:, dc, hsl], start=(dc == 0),
                                 stop=(dc == DC - 1))
                        P.cp("vector", ob, po)
                        P.dma("gpsimd", T["vf_tm"][b, tc, :, hsl], ob)

    def precast_ops(wc32, wcb):
        for e in range(8):
            for dc in range(DC):
                c32, cb = wc32.nxt(), wcb.nxt()
                P.dma("sync", c32[:],
                      T["L0_ffn_W1"][e, dc * 128:(dc + 1) * 128, :])
                P.cp("vector", cb, c32)
                P.dma("gpsimd", T["w1m_bf"][e, dc, :, :], cb)
                yield
        for e in range(8):
            for g in range(8):
                c32, cb = wc32.nxt(), wcb.nxt()
                P.dma("sync",
                      c32[:].rearrange("p (a d) -> p a d", a=4),
                      T["L0_ffn_W2"][e, g * 512:(g + 1) * 512, :]
                      .rearrange("(a p) d -> p a d", p=128))
                P.cp("vector", cb, c32)
                P.dma("gpsimd",
                      T["w2m_bf"][e, 4 * g:4 * g + 4, :, :]
                      .rearrange("c p d -> p c d"),
                      cb[:].rearrange("p (c d) -> p c d", c=4))
                yield
        while True:
            yield

    P.scope = "p2_c"
    with P.phase("p2c"):
        NH = 16
        wc32 = Rot([P.sbuf(f"p2_wc32{i}", [128, 4096], F32) for i in range(2)])
        wcb = Rot([P.sbuf(f"p2_wcb{i}", [128, 4096], BF16) for i in range(2)])
        precast = precast_ops(wc32, wcb)
        v_r = Rot([P.sbuf(f"p2_vs{i}", [128, S // 128, NH * 65], BF16)
                   for i in range(2)])
        qc = Rot([P.sbuf(f"p2_qc{i}", [128, S], BF16) for i in range(2)])
        kc = Rot([P.sbuf(f"p2_kc{i}", [128, S], BF16) for i in range(2)])
        tmp1 = P.sbuf("p2_tmp1", [1, 1], F32)
        negc = Rot([P.sbuf(f"p2_negc{i}", [128, 1], F32) for i in range(2)])
        ps_n = P.psum("p2_psn", [128, 512], F32)
        ps_sT = Rot([P.psum(f"p2_psT{i}", [128, 1024], F32) for i in range(2)])
        E_r = Rot([P.sbuf(f"p2_E{i}", [128, 1024], BF16) for i in range(2)])
        ps_ctx = Rot([P.psum(f"p2_psctx{i}", [128, 512], F32) for i in range(2)])
        ps_sig = P.psum("p2_pssig", [128, 512], F32)
        sig_r = Rot([P.sbuf(f"p2_sig{i}", [1, 512], F32) for i in range(2)])
        sg_r = Rot([P.sbuf(f"p2_sg{i}", [64, 512], F32) for i in range(2)])
        ctx_r = Rot([P.sbuf(f"p2_ctxb{i}", [64, 512], BF16) for i in range(2)])
        at_bufs = (ps_sT, E_r, ps_ctx, ps_sig, sig_r, sg_r, ctx_r)
        for vb in v_r.items:
            P.memset("vector",
                 _ap(vb).rearrange("p c (h w) -> p c h w", w=65)[:, :, :, 64:65],
                 1.0)
        for b in range(BLOC):
            v_s = v_r.nxt()
            for tc in range(S // 128):
                P.dma("gpsimd",
                      _ap(v_s)[:, tc, :].rearrange("p (h w) -> p h w", w=65)
                      [:, :, 0:64],
                      T["vf_tm"][b, tc, :, :].rearrange("p (h w) -> p h w", w=64))
            for hp in range(8):
                qcb, kcb = qc.nxt(), kc.nxt()
                P.dma("sync", qcb[:], T["qf_fm"][b, hp, :, :])
                P.dma("sync", kcb[:], T["kf_fm"][b, hp, :, :])
                for sub in range(2):
                    hbase = 64 * sub
                    h = 2 * hp + sub
                    qh = _ap(qcb)[hbase:hbase + 64, :]
                    kh = _ap(kcb)[hbase:hbase + 64, :]
                    nc_ = negc.nxt()
                    P.tt("vector", tmp1, bnd_q[0:1, h:h + 1],
                         bnd_k[0:1, h:h + 1], ALU.mult)
                    P.act(tmp1, tmp1, AF.Ln, bias=1e-30)
                    P.act(tmp1, tmp1, AF.Exp, scale=0.5)
                    P.mm(_ap(ps_n)[:, 0:1], _ap(P.ones_f32)[0:1, :], tmp1)
                    P.act(nc_, _ap(ps_n)[:, 0:1], AF.Copy, scale=-0.75 * 0.125)
                    softmax_TQ(P, at_bufs, kh, qh, v_s, h, nc_,
                               T["ctx0_fm"][b, hp, hbase:hbase + 64, :], 0.125)
                    next(precast)
                    next(precast)
                    next(precast)
                    next(precast)

    P.scope = "p2_d"
    with P.phase("p2d"):
        wo = P.sbuf("p2_wo", [128, DC, D], BF16)
        load_w(P, "p2wo", T["L0_ffn_Wo"], wo, D)
        wg = P.sbuf("p2_wg", [128, DC, 8], F32)
        wgl = P.sbuf("p2_wgl", [128, 8], F32)
        for dc in range(DC):
            P.dma("sync", wgl[:], T["L0_ffn_Wg"][dc * 128:(dc + 1) * 128, :])
            P.cp("vector", wg[:, dc, :], wgl)
        g2r = row_of(P, "p2g2", T["L0_ffn_ln2_g"], D)
        b2r = row_of(P, "p2b2", T["L0_ffn_ln2_b"], D)
        g2 = bcast_row(P, "p2g2", g2r, D, P.ones_f32)
        b2 = bcast_row(P, "p2b2", b2r, D, P.ones_f32)
        ot = P.sbuf("p2_ot", [128, DC, 128], BF16)
        xnt = P.sbuf("p2_xnt", [128, D], F32)
        x2s = P.sbuf("p2_x2s", [128, D], F32)
        mb = P.sbuf("p2_mb", [128, D], BF16)
        m32 = P.sbuf("p2_m32", [128, D], F32)
        xc2 = P.sbuf("p2_xc2", [128, D], F32)
        mu = P.sbuf("p2_mu2", [128, 1], F32)
        rstd = P.sbuf("p2_rstd2", [128, 1], F32)
        lsq = P.sbuf("p2_sq2", [128, D], F32)
        lacc = P.sbuf("p2_acc2", [128, 1], F32)
        pw = Rot([P.psum(f"p2_pw{i}", [128, 512], F32) for i in range(2)])
        ptr = P.psum("p2_ptr2", [128, D], BF16)
        stg = P.sbuf("p2_stg2", [128, D], BF16)
        ptr32 = P.psum("p2_ptr32", [128, D], F32)
        stg32 = P.sbuf("p2_stg32", [128, D], F32)
        ps_r = P.psum("p2_psr", [128, 512], F32)
        E8 = P.sbuf("p2_E8", [128, 8], F32)
        mx = P.sbuf("p2_mx", [128, 1], F32)
        m1 = P.sbuf("p2_m1", [128, 1], F32)
        m2 = P.sbuf("p2_m2", [128, 1], F32)
        iseq = P.sbuf("p2_iseq", [128, 8], F32)
        e8b = P.sbuf("p2_e8b", [128, 8], F32)
        cw = P.sbuf("p2_cw", [128, 8], F32)
        for b in range(BLOC):
            for tt_ in range(S // 128):
                ti = b * (S // 128) + tt_
                sl = slice(ti * 128, (ti + 1) * 128)
                P.dma("sync", ot[:],
                      T["ctx0_fm"][b, :, :, tt_ * 128:(tt_ + 1) * 128]
                      .rearrange("c p t -> p c t"))
                P.dma("sync", xnt[:], T["xn_tm"][sl, :])
                for half in range(2):
                    p_ = pw.nxt()
                    hsl = slice(half * 512, (half + 1) * 512)
                    for ec in range(DC):
                        P.mm(p_, ot[:, ec, :], wo[:, ec, hsl],
                             start=(ec == 0), stop=(ec == DC - 1))
                    P.tt("vector", x2s[:, hsl], p_, _ap(xnt)[:, hsl], ALU.add)
                P.dma("gpsimd", T["x2_tm"][sl, :], x2s)
                ln_stats(P, x2s, mu, rstd, xc2, lsq, lacc)
                P.ts("vector", m32, xc2, rstd, ALU.mult)
                P.tt("vector", m32, m32, g2, ALU.mult)
                P.tt("vector", m32, m32, b2, ALU.add)
                P.cp("scalar", mb, m32)
                tm_tile_to_fm(P, "p2d", mb, eye, T["m_fm"], ti, ptr, stg,
                              also_f32=True, psum32=ptr32, eye32=eye32,
                              src32=m32, stage32=stg32, dst32=T["m_fm32"])
                for dc in range(DC):
                    P.mm(ps_r[:, 0:8], stg32[:, dc * 128:(dc + 1) * 128],
                         wg[:, dc, :], start=(dc == 0), stop=(dc == DC - 1))
                P.red("vector", mx, ps_r[:, 0:8], ALU.max)
                P.ts("vector", mx, mx, -1.0, ALU.mult)
                P.act(E8, ps_r[:, 0:8], AF.Exp, bias=mx)
                P.red("vector", m1, E8, ALU.max)
                P.ts("vector", iseq, E8, m1, ALU.is_equal)
                P.ts("vector", iseq, iseq, -1.0, ALU.mult, 1.0, ALU.add)
                P.tt("vector", e8b, E8, iseq, ALU.mult)
                P.red("vector", m2, e8b, ALU.max)
                P.ts("vector", e8b, E8, m2, ALU.is_ge)
                P.tt("vector", cw, E8, e8b, ALU.mult)
                P.tt("vector", m1, m1, m2, ALU.add)
                P.recip(m1, m1)
                P.ts("vector", cw, cw, m1, ALU.mult)
                P.dma("gpsimd", T["cw_tm"][sl, :], cw)



@phase("p3_moe0",
       reads=["m_fm", "cw_tm", "x2_tm", "w1m_bf", "L0_ffn_b1",
              "w2m_bf", "L0_ffn_b2", "c_ones_f32", "c_eye_f32"],
       writes=["ffn0_tm"])
def p3_moe0(P, T):
    P.ones_f32 = T["c_ones_f32"]
    NE, F_ = 8, 4096
    FC = F_ // 128
    with P.phase("p3"):
        eye32 = T["c_eye_f32"]
        b1row = P.sbuf("p3_b1row", [1, 1024], F32)
        w1 = P.sbuf("p3_w1", [128, DC, F_], BF16)
        w2 = P.sbuf("p3_w2", [128, FC, D], BF16)
        b1 = P.sbuf("p3_b1", [128, FC], F32)

        b2r = P.sbuf("p3_b2r", [1, D], F32)
        b2bc = P.sbuf("p3_b2bc", [128, D], F32)
        ps_b = P.psum("p3_psb", [128, 512], F32)
        mt = Rot([P.sbuf(f"p3_mt{i}", [128, DC, 512], BF16) for i in range(2)])
        h_s = P.sbuf("p3_h", [128, FC, 512], BF16)
        cw_s = Rot([P.sbuf(f"p3_cw{i}", [128, 4, NE], F32) for i in range(2)])
        acc = Rot([P.sbuf(f"p3_acc{i}", [128, D], F32) for i in range(2)])
        ytmp = Rot([P.sbuf(f"p3_yt{i}", [128, 512], F32) for i in range(1)])
        ph = Rot([P.psum(f"p3_ph{i}", [128, 512], F32) for i in range(3)])
        py = Rot([P.psum(f"p3_py{i}", [128, 512], F32) for i in range(4)])

        def load_w1(e):
            for dc in range(DC):
                P.dma("gpsimd" if dc % 2 else "sync", w1[:, dc, :],
                      T["w1m_bf"][e, dc, :, :])

        def load_w2b(e):
            for g in range(4):
                P.dma("gpsimd" if g % 2 else "sync",
                      w2[:, 8 * g:8 * g + 8, :]
                      .rearrange("p c d -> p c d"),
                      T["w2m_bf"][e, 8 * g:8 * g + 8, :, :]
                      .rearrange("c p d -> p c d"))
            for g in range(4):
                P.dma("gpsimd", b1row[:],
                      T["L0_ffn_b1"][e, g * 1024:(g + 1) * 1024]
                      .rearrange("(a d) -> a d", a=1))
                for j in range(8):
                    fc = g * 8 + j
                    P.transp(_ap(ps_b)[:, fc:fc + 1],
                             _ap(b1row)[0:1, j * 128:(j + 1) * 128],
                             _ap(eye32)[0:1, 0:1],
                             register=(fc == FC - 1), war=(fc == 0))
            P.cp("vector", b1[:], _ap(ps_b)[:, 0:FC])
            P.dma("gpsimd", b2r[:], T["L0_ffn_b2"][e, :]
                  .rearrange("(a d) -> a d", a=1))
            for j in range(2):
                P.mm(ps_b[:, 0:512], _ap(P.ones_f32)[0:1, :],
                     _ap(b2r)[0:1, j * 512:(j + 1) * 512])
                P.cp("vector", b2bc[:, j * 512:(j + 1) * 512], ps_b[:, 0:512])

        load_w1(0)
        load_w2b(0)
        for e in range(NE):
            for tmac in range(TLOC // 512):
                t0 = tmac * 512
                m_, c_ = mt.nxt(), cw_s.nxt()
                P.dma("sync", m_[:],
                      T["m_fm"][:, :, t0:t0 + 512].rearrange("c p t -> p c t"))
                P.dma("sync", c_[:],
                      T["cw_tm"][t0:t0 + 512, :]
                      .rearrange("(s p) e -> p s e", p=128))
                for fc in range(FC):
                    p_ = ph.nxt()
                    for dc in range(DC):
                        P.mm(p_, w1[:, dc, fc * 128:(fc + 1) * 128],
                             m_[:, dc, :], start=(dc == 0), stop=(dc == DC - 1))
                    P.act(h_s[:, fc, :], p_, AF.Gelu_apprx_tanh,
                          bias=_ap(b1)[:, fc:fc + 1])
                if tmac == TLOC // 512 - 1 and e + 1 < NE:
                    load_w1(e + 1)
                for st in range(4):
                    ssl = slice(st * 128, (st + 1) * 128)
                    a_ = acc.nxt()
                    src_t = T["x2_tm"] if e == 0 else T["ffn0_tm"]
                    P.dma("sync", a_[:],
                          src_t[t0 + st * 128:t0 + (st + 1) * 128, :])
                    for half in range(2):
                        hsl = slice(half * 512, (half + 1) * 512)
                        p_ = py.nxt()
                        yt = ytmp.nxt()
                        for fc in range(FC):
                            P.mm(p_, h_s[:, fc, ssl], w2[:, fc, hsl],
                                 start=(fc == 0), stop=(fc == FC - 1))
                        P.tt("vector", yt, p_, _ap(b2bc)[:, hsl], ALU.add)
                        P.stt("vector", a_[:, hsl], yt,
                              _ap(c_)[:, st, e:e + 1], a_[:, hsl],
                              ALU.mult, ALU.add)
                    P.dma("gpsimd",
                          T["ffn0_tm"][t0 + st * 128:t0 + (st + 1) * 128, :],
                          a_)
            if e + 1 < NE:
                load_w2b(e + 1)



@phase("p4_comb0",
       reads=["L0_attn_hidden_states", "att0_tm", "ffn0_tm", "c_eye_bf"],
       writes=["x1_tm", "xn1_tm", "xn1_fm"])
def p4_comb0(P, T):
    x = T["L0_attn_hidden_states"].rearrange("b s d -> (b s) d")
    eye_bf = T["c_eye_bf"]
    with P.phase("p4"):
        R2 = lambda nm, sh, dt: Rot([P.sbuf(f"{nm}{i}", sh, dt) for i in range(2)])
        xt_r = R2("p4_xt", [128, D], F32)
        at_r = R2("p4_at", [128, D], F32)
        ft_r = R2("p4_ft", [128, D], F32)
        xc_r = R2("p4_xc", [128, D], F32)
        xnb_r = R2("p4_xnb", [128, D], BF16)
        mu_r = R2("p4_mu", [128, 1], F32)
        rstd_r = R2("p4_rstd", [128, 1], F32)
        lsq = P.sbuf("p4_sq", [128, D], F32)
        lacc_r = R2("p4_acc", [128, 1], F32)
        ptr_r = Rot([P.psum(f"p4_ptr{i}", [128, D], BF16) for i in range(2)])
        stg_r = R2("p4_stg", [128, D], BF16)
        for ti in range(TLOC // 128):
            sl = slice(ti * 128, (ti + 1) * 128)
            xt, at, ft, xc = xt_r.nxt(), at_r.nxt(), ft_r.nxt(), xc_r.nxt()
            xnb, mu = xnb_r.nxt(), mu_r.nxt()
            rstd, lacc = rstd_r.nxt(), lacc_r.nxt()
            ptr, stg = ptr_r.nxt(), stg_r.nxt()
            P.dma("sync", xt[:], x[sl, :])
            P.dma("sync", at[:], T["att0_tm"][sl, :])
            P.dma("sync", ft[:], T["ffn0_tm"][sl, :])
            P.tt("vector", xt, xt, at, ALU.add)
            P.tt("vector", xt, xt, ft, ALU.add)
            P.dma("gpsimd", T["x1_tm"][sl, :], xt)
            ln_stats(P, xt, mu, rstd, xc, lsq, lacc)
            P.ts("vector", xc[:], xc[:], rstd[:], ALU.mult)
            P.dma("gpsimd", T["xn1_tm"][sl, :], xc)
            P.cp("scalar", xnb, xc)
            tm_tile_to_fm(P, "p4", xnb, eye_bf, T["xn1_fm"], ti, ptr, stg)



@phase("p5_att1",
       reads=["xn1_fm", "L1_attn_w_qkv", "L1_attn_b_qkv", "L1_attn_w_out",
              "L1_attn_b_out", "c_eye_bf", "c_ones_f32", "c_ones_bf"],
       writes=["att1_tm"],
       privs=["qkv1_fm", "v1_tm", "ctx1_fm"])
def p5_att1(P, T):
    eye = T["c_eye_bf"]
    P.ones_f32 = T["c_ones_f32"]
    P.ones_bf_col = _ap(T["c_ones_bf"])[:, 0:1]
    P.ones_f32_col64 = _ap(T["c_ones_f32"])[0:64, 0:1]
    P.ones_f32_row64 = _ap(T["c_ones_f32"])[0:1, 0:64]
    NH = 16
    bnd_q = P.sbuf_persist("p5_bndq", [1, 16], F32)
    bnd_k = P.sbuf_persist("p5_bndk", [1, 16], F32)
    bt1 = P.sbuf_persist("p5_bt1", [1, 1], F32)
    P.memset("vector", bnd_q[:], 0.0)
    P.memset("vector", bnd_k[:], 0.0)
    qkmap = {}
    for _h in range(NH):
        for _kind, _pos in ((0, 192 * _h), (1, 192 * _h + 64)):
            qkmap.setdefault(_pos // 128, []).append((_kind, _h, _pos % 128))

    P.scope = "p5_b"
    with P.phase("p5b"):
        wqkvT = P.sbuf("p5_wqkvT", [128, DC, 3 * D], BF16)
        load_wT(P, "p5qkv", T["L1_attn_w_qkv"], wqkvT, eye, n_out=3 * D)
        bq = P.sbuf("p5_bq", [128, 24], F32)
        for oc in range(24):
            P.dma("sync", bq[:, oc:oc + 1],
                  T["L1_attn_b_qkv"][oc * 128:(oc + 1) * 128]
                  .rearrange("(p a) -> p a", a=1))
        xt = P.sbuf("p5_xt", [128, DC, 512], BF16)
        pos = Rot([P.psum(f"p5_po{i}", [128, 512], F32) for i in range(3)])
        obs = Rot([P.sbuf(f"p5_ob{i}", [128, 512], BF16) for i in range(4)])
        ps_bnd = P.psum("p5_psbnd", [128, 512], F32)
        sqb = P.sbuf("p5_sqb", [64, 512], BF16)
        ones_bf = _ap(T["c_ones_bf"])
        for b in range(BLOC):
            for tmac in range(S // 512):
                t0 = b * S + tmac * 512
                P.dma("sync", xt[:],
                      T["xn1_fm"][:, :, t0:t0 + 512]
                      .rearrange("c p t -> p c t"))
                for oc in range(24):
                    po, ob = pos.nxt(), obs.nxt()
                    for dc in range(DC):
                        P.mm(po, wqkvT[:, dc, oc * 128:(oc + 1) * 128],
                             xt[:, dc, :], start=(dc == 0), stop=(dc == DC - 1))
                    P.ts("vector", ob, po, _ap(bq)[:, oc:oc + 1], ALU.add)
                    P.dma("gpsimd",
                          T["qkv1_fm"][b, oc, :, tmac * 512:(tmac + 1) * 512],
                          ob)
                    for kind, hh, base in qkmap.get(oc, []):
                        bnd = bnd_k if kind else bnd_q
                        P.tt("vector", sqb, _ap(ob)[base:base + 64, :],
                             _ap(ob)[base:base + 64, :], ALU.mult)
                        P.mm(_ap(ps_bnd)[0:1, :], ones_bf[0:64, 0:1], sqb)
                        P.red("vector", bt1, _ap(ps_bnd)[0:1, :], ALU.max)
                        P.tt("vector", bnd[0:1, hh:hh + 1],
                             bnd[0:1, hh:hh + 1], bt1, ALU.max)
                for st in range(4):
                    tc = tmac * 4 + st
                    for half in range(2):
                        po, ob = pos.nxt(), obs.nxt()
                        rhsv = _ap(wqkvT)[:, :, :].rearrange(
                            "p c (h x) -> p c h x", h=NH)[
                            :, :, half * 8:(half + 1) * 8, 128:192]
                        for dc in range(DC):
                            P.mm(po, xt[:, dc, st * 128:(st + 1) * 128],
                                 rhsv[:, dc], start=(dc == 0),
                                 stop=(dc == DC - 1))
                        P.cp("vector", ob, po)
                        P.dma("gpsimd",
                              T["v1_tm"][b, tc, :, half * 512:(half + 1) * 512],
                              ob)

    P.scope = "p5_c"
    with P.phase("p5c"):
        v_r = Rot([P.sbuf(f"p5_vs{i}", [128, S // 128, NH * 65], BF16)
                   for i in range(2)])
        qh_r = Rot([P.sbuf(f"p5_qh{i}", [64, S], BF16) for i in range(2)])
        kh_r = Rot([P.sbuf(f"p5_kh{i}", [64, S], BF16) for i in range(2)])
        tmp1 = P.sbuf("p5_tmp1", [1, 1], F32)
        negc = Rot([P.sbuf(f"p5_negc{i}", [128, 1], F32) for i in range(2)])
        ps_n = P.psum("p5_psn", [128, 512], F32)
        ps_sT = Rot([P.psum(f"p5_psT{i}", [128, 1024], F32) for i in range(2)])
        E_r = Rot([P.sbuf(f"p5_E{i}", [128, 1024], BF16) for i in range(2)])
        ps_ctx = Rot([P.psum(f"p5_psctx{i}", [128, 512], F32) for i in range(2)])
        ps_sig = P.psum("p5_pssig", [128, 512], F32)
        sig_r = Rot([P.sbuf(f"p5_sig{i}", [1, 512], F32) for i in range(2)])
        sg_r = Rot([P.sbuf(f"p5_sg{i}", [64, 512], F32) for i in range(2)])
        ctx_r = Rot([P.sbuf(f"p5_ctxb{i}", [64, 512], BF16) for i in range(2)])
        bq2 = P.sbuf("p5_bq2", [128, 24], F32)
        for oc in range(24):
            P.dma("sync", bq2[:, oc:oc + 1],
                  T["L1_attn_b_qkv"][oc * 128:(oc + 1) * 128]
                  .rearrange("(p a) -> p a", a=1))
        at_bufs = (ps_sT, E_r, ps_ctx, ps_sig, sig_r, sg_r, ctx_r)
        for vb in v_r.items:
            P.memset("vector",
                 _ap(vb).rearrange("p c (h w) -> p c h w", w=65)[:, :, :, 64:65],
                 1.0)
        for b in range(BLOC):
            v_s = v_r.nxt()
            for tc in range(S // 128):
                P.dma("gpsimd",
                      _ap(v_s)[:, tc, :].rearrange("p (h w) -> p h w", w=65)
                      [:, :, 0:64],
                      T["v1_tm"][b, tc, :, :].rearrange("p (h w) -> p h w", w=64))
            for h in range(NH):
                fq, fk, fv = 192 * h, 192 * h + 64, 192 * h + 128
                qh_s, kh_s = qh_r.nxt(), kh_r.nxt()
                P.dma("sync", qh_s[:],
                      T["qkv1_fm"][b, fq // 128, fq % 128:fq % 128 + 64, :])
                P.dma("sync", kh_s[:],
                      T["qkv1_fm"][b, fk // 128, fk % 128:fk % 128 + 64, :])
                bv_col = _ap(bq2)[fv % 128:fv % 128 + 64,
                                  fv // 128:fv // 128 + 1]
                nc_ = negc.nxt()
                P.tt("vector", tmp1, bnd_q[0:1, h:h + 1],
                     bnd_k[0:1, h:h + 1], ALU.mult)
                P.act(tmp1, tmp1, AF.Ln, bias=1e-30)
                P.act(tmp1, tmp1, AF.Exp, scale=0.5)
                P.mm(_ap(ps_n)[:, 0:1], _ap(P.ones_f32)[0:1, :], tmp1)
                P.act(nc_, _ap(ps_n)[:, 0:1], AF.Copy, scale=-0.75 * 0.125)
                softmax_TQ(P, at_bufs, _ap(kh_s), _ap(qh_s), v_s, h, nc_,
                           T["ctx1_fm"][b, h // 2,
                                        64 * (h % 2):64 * (h % 2) + 64, :],
                           0.125, bv=bv_col)

    P.scope = "p5_d"
    with P.phase("p5d"):
        woT = P.sbuf("p5_woT", [128, DC, D], BF16)
        load_wT(P, "p5wo", T["L1_attn_w_out"], woT, eye)
        bor = row_of(P, "p5bo", T["L1_attn_b_out"], D)
        bo = bcast_row(P, "p5bo", bor, D, P.ones_f32)
        ots = Rot([P.sbuf(f"p5_ot{i}", [128, DC, 128], BF16) for i in range(2)])
        pws = Rot([P.psum(f"p5_pw{i}", [128, 512], F32) for i in range(4)])
        oss = Rot([P.sbuf(f"p5_os{i}", [128, D], F32) for i in range(2)])
        for b in range(BLOC):
            for tt_ in range(S // 128):
                ot = ots.nxt()
                os_ = oss.nxt()
                P.dma("sync", ot[:],
                      T["ctx1_fm"][b, :, :, tt_ * 128:(tt_ + 1) * 128]
                      .rearrange("c p t -> p c t"))
                for half in range(2):
                    pw = pws.nxt()
                    hsl = slice(half * 512, (half + 1) * 512)
                    for ec in range(DC):
                        P.mm(pw, ot[:, ec, :], woT[:, ec, hsl],
                             start=(ec == 0), stop=(ec == DC - 1))
                    P.tt("vector", os_[:, hsl], pw, _ap(bo)[:, hsl], ALU.add)
                P.dma("gpsimd",
                      T["att1_tm"][b * S + tt_ * 128:b * S + (tt_ + 1) * 128, :],
                      os_)


def rms_tile(P, xt, gbc, out32, ssq, bufs):
    sq, r = bufs
    P.act(sq, xt, AF.Square, accum=ssq)
    P.act(ssq, ssq, AF.Ln, bias=1e-30)
    P.act(r, ssq, AF.Exp, scale=-0.5)
    P.ts("vector", out32, xt, r, ALU.mult)
    P.tt("vector", out32, out32, gbc, ALU.mult)



@phase("p6_sm_a",
       reads=["xn1_tm", "L1_ffn_gamma_x", "L1_ffn_gamma_slot",
              "L1_ffn_slot_embeds", "L1_ffn_w1", "L1_ffn_w2",
              "c_eye_bf", "c_ones_f32", "c_ones_bf"],
       writes=["xnn_fm", "se_fm", "proc_tm"],
       privs=["xnn_tmb", "w1l1_bf", "w2l1_bf"])
def p6_sm_a(P, T):
    eye = T["c_eye_bf"]
    P.ones_f32 = T["c_ones_f32"]
    NES = 4096
    FC = 32

    P.scope = "p6_prep"
    with P.phase("p6prep"):
        wt32 = P.sbuf("p6_wt32", [128, 4096], F32)
        wtb = P.sbuf("p6_wtb", [128, 4096], BF16)
        for dc in range(DC):
            P.dma("sync", wt32[:], T["L1_ffn_w1"][dc * 128:(dc + 1) * 128, :])
            P.cp("vector", wtb, wt32)
            P.dma("gpsimd",
                  T["w1l1_bf"][:, :, dc * 128:(dc + 1) * 128]
                  .rearrange("c p f -> p c f"),
                  wtb[:].rearrange("p (c f) -> p c f", c=32))
        for fc in range(FC):
            P.dma("sync", _ap(wt32)[:, 0:D],
                  T["L1_ffn_w2"][fc * 128:(fc + 1) * 128, :])
            P.cp("vector", _ap(wtb)[:, 0:D], _ap(wt32)[:, 0:D])
            P.dma("gpsimd", T["w2l1_bf"][fc, :, :], _ap(wtb)[:, 0:D])
        gxr = row_of(P, "p6gx", T["L1_ffn_gamma_x"], D)
        gsr = row_of(P, "p6gs", T["L1_ffn_gamma_slot"], D)
        gx = bcast_row(P, "p6gx", gxr, D, P.ones_f32, scale=32.0)
        gs = bcast_row(P, "p6gs", gsr, D, P.ones_f32, scale=32.0)
        R2 = lambda nm, sh, dt: Rot([P.sbuf(f"{nm}{i}", sh, dt) for i in range(2)])
        xt_r = R2("p6_xt", [128, D], F32)
        o32_r = R2("p6_o32", [128, D], F32)
        ob_r = R2("p6_ob", [128, D], BF16)
        sq = P.sbuf("p6_sq", [128, D], F32)
        ssq_r = R2("p6_ssq", [128, 1], F32)
        rr_r = R2("p6_rr", [128, 1], F32)
        ptr_r = Rot([P.psum(f"p6_ptr{i}", [128, D], BF16) for i in range(2)])
        stg_r = R2("p6_stg", [128, D], BF16)
        for ti in range(TLOC // 128):
            b, tt_ = ti // (S // 128), ti % (S // 128)
            xt, o32, ob = xt_r.nxt(), o32_r.nxt(), ob_r.nxt()
            ssq, rr, ptr, stg = ssq_r.nxt(), rr_r.nxt(), ptr_r.nxt(), stg_r.nxt()
            P.dma("sync", xt[:], T["xn1_tm"][ti * 128:(ti + 1) * 128, :])
            rms_tile(P, xt, gx, o32, ssq, (sq, rr))
            P.cp("scalar", ob, o32)
            P.dma("gpsimd", T["xnn_tmb"][b, tt_, :, :], ob)
            tm_tile_to_fm(P, "p6", ob, eye, T["xnn_fm"][b], tt_, ptr, stg)
        sed = T["L1_ffn_slot_embeds"].rearrange("e t d -> (e t) d")
        for si in range(NES // 128):
            xt, o32, ob = xt_r.nxt(), o32_r.nxt(), ob_r.nxt()
            ssq, rr, ptr, stg = ssq_r.nxt(), rr_r.nxt(), ptr_r.nxt(), stg_r.nxt()
            P.dma("sync", xt[:], sed[si * 128:(si + 1) * 128, :])
            rms_tile(P, xt, gs, o32, ssq, (sq, rr))
            P.cp("scalar", ob, o32)
            tm_tile_to_fm(P, "p6se", ob, eye, T["se_fm"], si, ptr, stg)

    P.scope = "p6_a"
    with P.phase("p6a"):
        xnn_fm_s = P.sbuf("p6_xf", [128, DC, S], BF16)
        xnn_tm_s = P.sbuf("p6_xt2", [128, S // 128, D], BF16)
        se_s = P.sbuf("p6_ses", [128, DC, 512], BF16)
        dT_s = P.sbuf("p6_dT", [128, S // 128, 512], BF16)
        disp_r = Rot([P.sbuf(f"p6_disp{i}", [128, S], BF16) for i in range(2)])
        slots_s = P.sbuf("p6_slots", [128, DC, 512], BF16)
        h_s = P.sbuf("p6_h", [128, FC, 512], BF16)
        proc_s = P.sbuf("p6_proc", [128, 4, D], BF16)
        w1t = Rot([P.sbuf(f"p6_w1t{i}", [128, DC, 128], BF16) for i in range(2)])
        w2t = Rot([P.sbuf(f"p6_w2t{i}", [128, 512], BF16) for i in range(2)])
        mx = P.sbuf("p6_mx", [128, 1], F32)
        tmx = P.sbuf("p6_tmx", [128, 1], F32)
        as_ = P.sbuf("p6_as", [128, 4], F32)
        rsum = P.sbuf("p6_rsum", [128, 1], F32)
        pb = [P.psum(f"p6_pb{i}", [128, 512], F32) for i in range(6)]
        ptr_r6 = Rot([P.psum(f"p6_ptrb{i}", [128, 1024], BF16)
                      for i in range(2)])
        slots_ps = Rot([pb[5], pb[0]])
        h_ps = Rot([pb[1], pb[2]])
        for b in range(BLOC):
            P.dma("sync", xnn_fm_s[:],
                  T["xnn_fm"][b].rearrange("c p t -> p c t"))
            P.dma("sync", xnn_tm_s[:],
                  T["xnn_tmb"][b].rearrange("c p d -> p c d"))
            for em in range(8):
                e0 = em * 512
                P.dma("sync", se_s[:],
                      T["se_fm"][:, :, e0:e0 + 512].rearrange("c p t -> p c t"))
                def disp_tail(sc, scl, dsp, banks):
                    def exps():
                        for tq in range(4):
                            P.act(_ap(dsp)[:, tq * 512:(tq + 1) * 512],
                                  banks[tq], AF.Exp, bias=mx,
                                  accum=_ap(as_)[:, tq:tq + 1])
                            yield
                        P.red("vector", rsum, as_, ALU.add)
                        P.recip(rsum, rsum)
                        P.ts("vector", dsp, dsp, rsum, ALU.mult)
                        yield
                        for g in range(2):
                            pt6 = ptr_r6.nxt()
                            for j in range(8):
                                tc = g * 8 + j
                                P.transp(_ap(pt6)[:, j * 128:(j + 1) * 128],
                                         _ap(dsp)[:, tc * 128:(tc + 1) * 128],
                                         eye, register=(j == 7), war=(j == 0))
                            P.cp("vector",
                                 dT_s[:, g * 8:(g + 1) * 8, scl],
                                 _ap(pt6).rearrange("p (c t) -> p c t", c=8))
                            yield
                    return exps()

                pend = None
                for sc in range(4):
                    scl = slice(sc * 128, (sc + 1) * 128)
                    banks = [pb[(sc * 4 + tq) % 5] for tq in range(4)]
                    for tq in range(4):
                        if pend is not None:
                            next(pend, None)
                        for dc in range(DC):
                            P.mm(banks[tq], se_s[:, dc, scl],
                                 xnn_fm_s[:, dc, tq * 512:(tq + 1) * 512],
                                 start=(dc == 0), stop=(dc == DC - 1))
                    if pend is not None:
                        for _ in pend:
                            pass
                    for tq in range(4):
                        if tq == 0:
                            P.red("vector", mx, banks[0], ALU.max)
                        else:
                            P.red("vector", tmx, banks[tq], ALU.max)
                            P.tt("vector", mx, mx, tmx, ALU.max)
                    P.ts("vector", mx, mx, -1.0, ALU.mult)
                    pend = disp_tail(sc, scl, disp_r.nxt(), banks)
                for _ in pend:
                    pass
                for dc in range(DC):
                    p_ = slots_ps.nxt()
                    for tc in range(S // 128):
                        P.mm(p_, xnn_tm_s[:, tc, dc * 128:(dc + 1) * 128],
                             dT_s[:, tc, :], start=(tc == 0),
                             stop=(tc == S // 128 - 1))
                    P.cp("vector", slots_s[:, dc, :], p_)
                for fc in range(FC):
                    w1c = w1t.nxt()
                    P.dma("sync", w1c[:].rearrange("p c f -> p (c f)"),
                          T["w1l1_bf"][fc, :, :])
                    p_ = h_ps.nxt()
                    for dc in range(DC):
                        P.mm(p_, w1c[:, dc, :], slots_s[:, dc, :],
                             start=(dc == 0), stop=(dc == DC - 1))
                    P.act(h_s[:, fc, :], p_, AF.Gelu)
                for half in range(2):
                    hsl = slice(half * 512, (half + 1) * 512)
                    for fc in range(FC):
                        w2c = w2t.nxt()
                        P.dma("sync", w2c[:], T["w2l1_bf"][fc, :, hsl])
                        for st in range(4):
                            P.mm(pb[st], h_s[:, fc, st * 128:(st + 1) * 128],
                                 w2c[:], start=(fc == 0), stop=(fc == FC - 1))
                    for st in range(4):
                        P.cp("vector", proc_s[:, st, hsl], pb[st])
                for st in range(4):
                    P.dma("gpsimd",
                          T["proc_tm"][b, e0 + st * 128:e0 + (st + 1) * 128, :],
                          proc_s[:, st, :])



@phase("p7_sm_b",
       reads=["xnn_fm", "se_fm", "proc_tm", "c_eye_bf", "c_ones_f32"],
       writes=["ffn1_tm"])
def p7_sm_b(P, T):
    eye = T["c_eye_bf"]
    NES = 4096
    with P.phase("p7"):
        se_fm_s = P.sbuf("p7_se", [128, DC, NES], BF16)
        xnn_fm_s = P.sbuf("p7_xf", [128, DC, S], BF16)
        combT_s = P.sbuf("p7_cT", [128, NES // 128, 512], BF16)
        comb_r = Rot([P.sbuf(f"p7_comb{i}", [128, NES], BF16) for i in range(2)])
        lg_r = Rot([P.sbuf(f"p7_lg{i}", [128, NES], F32) for i in range(1)])
        out_s = P.sbuf("p7_out", [128, 4, D], F32)
        proc_t = Rot([P.sbuf(f"p7_pt{i}", [128, 512], BF16) for i in range(2)])
        mx = P.sbuf("p7_mx", [128, 1], F32)
        rsum = P.sbuf("p7_rsum", [128, 1], F32)
        sq8 = P.sbuf("p7_sq8", [128, 8], F32)
        pb = [P.psum(f"p7_pb{i}", [128, 512], F32) for i in range(4)]
        ptr_bf = P.psum("p7_ptrb", [128, 1024], BF16)
        P.dma("sync", se_fm_s[:], T["se_fm"].rearrange("c p t -> p c t"))
        for b in range(BLOC):
            P.dma("sync", xnn_fm_s[:],
                  T["xnn_fm"][b].rearrange("c p t -> p c t"))
            for sm in range(4):
                s0 = sm * 512
                def comb_tail(tq, lg, cmb):
                    def gen():
                        P.red("vector", mx, lg, ALU.max)
                        P.ts("vector", mx, mx, -1.0, ALU.mult)
                        P.act(cmb, lg, AF.Exp, bias=mx, accum=rsum)
                        P.recip(rsum, rsum)
                        P.ts("vector", cmb, cmb, rsum, ALU.mult)
                        yield
                        for g in range(4):
                            for j in range(8):
                                etc = g * 8 + j
                                P.transp(ptr_bf[:, j * 128:(j + 1) * 128],
                                         _ap(cmb)[:, etc * 128:(etc + 1) * 128],
                                         eye, register=(j == 7), war=(j == 0))
                            P.cp("vector",
                                 combT_s[:, g * 8:(g + 1) * 8,
                                         tq * 128:(tq + 1) * 128],
                                 ptr_bf[:].rearrange("p (c t) -> p c t", c=8))
                            yield
                    return gen()

                pend = None
                for tq in range(4):
                    tsl = slice(s0 + tq * 128, s0 + (tq + 1) * 128)
                    lg = lg_r.nxt()
                    for eb in range(8):
                        if pend is not None:
                            next(pend, None)
                        p_ = pb[eb % 4]
                        for dc in range(DC):
                            P.mm(p_, xnn_fm_s[:, dc, tsl],
                                 se_fm_s[:, dc, eb * 512:(eb + 1) * 512],
                                 start=(dc == 0), stop=(dc == DC - 1))
                        P.cp("vector", _ap(lg)[:, eb * 512:(eb + 1) * 512], p_)
                    if pend is not None:
                        for _ in pend:
                            pass
                    pend = comb_tail(tq, lg, comb_r.nxt())
                for _ in pend:
                    pass
                for half in range(2):
                    hsl = slice(half * 512, (half + 1) * 512)
                    for etc in range(NES // 128):
                        pc = proc_t.nxt()
                        P.dma("sync", pc[:],
                              T["proc_tm"][b, etc * 128:(etc + 1) * 128, hsl])
                        for st in range(4):
                            P.mm(pb[st], combT_s[:, etc, st * 128:(st + 1) * 128],
                                 pc[:], start=(etc == 0),
                                 stop=(etc == NES // 128 - 1))
                    for st in range(4):
                        P.cp("vector", out_s[:, st, hsl], pb[st])
                for st in range(4):
                    P.dma("gpsimd",
                          T["ffn1_tm"][b * S + s0 + st * 128:
                                       b * S + s0 + (st + 1) * 128, :],
                          out_s[:, st, :])



@phase("p8_out",
       reads=["x1_tm", "att1_tm", "ffn1_tm"],
       writes=["out"])
def p8_out(P, T):
    o = T["out"].rearrange("b s d -> (b s) d")
    with P.phase("p8"):
        xts = Rot([P.sbuf(f"p8_xt{i}", [128, D], F32) for i in range(2)])
        ats = Rot([P.sbuf(f"p8_at{i}", [128, D], F32) for i in range(2)])
        fts = Rot([P.sbuf(f"p8_ft{i}", [128, D], F32) for i in range(2)])
        for ti in range(TLOC // 128):
            sl = slice(ti * 128, (ti + 1) * 128)
            xt, at, ft = xts.nxt(), ats.nxt(), fts.nxt()
            P.dma("sync", xt[:], T["x1_tm"][sl, :])
            P.dma("sync", at[:], T["att1_tm"][sl, :])
            P.dma("sync", ft[:], T["ffn1_tm"][sl, :])
            P.tt("vector", xt, xt, at, ALU.add)
            P.tt("vector", xt, xt, ft, ALU.add)
            P.dma("gpsimd", o[sl, :], xt)



def make_consts():
    return {
        "c_eye_f32": np.eye(128, dtype=np.float32),
        "c_eye_bf": np.eye(128, dtype=ml_bf16()),
        "c_mask_ut": np.triu(np.ones((128, 128), dtype=np.float32)),
        "c_ones_f32": np.ones((128, 128), dtype=np.float32),
        "c_ones_bf": np.ones((128, 128), dtype=ml_bf16()),
    }


def ml_bf16():
    import ml_dtypes
    return ml_dtypes.bfloat16


def _declare(P, names, kind):
    out = {}
    for n in names:
        sh, dt = spec_of(n)
        out[n] = P.dram(n, sh, dt, kind=kind)
    return out


def run_unfused(inputs, upto=None, only=None):
    consts = make_consts()
    x = np.ascontiguousarray(inputs["L0_attn_hidden_states"])
    pools = [dict() for _ in range(NCORES)]
    phases = PHASES[:upto] if upto else PHASES
    if only:
        phases = [p for p in phases if p[0] in only]
    for (name, fn, reads, writes, privs) in phases:
        P = Prog(name)
        T = {}
        T.update(_declare(P, reads, "ExternalInput"))
        T.update(_declare(P, writes, "ExternalOutput"))
        T.update(_declare(P, privs, "Internal"))
        for n in reads:
            if n in CONST_SPECS:
                sh, dt = CONST_SPECS[n]
                sb = P.sbuf("s_" + n, sh, dt)
                P.dma("sync", sb[:], T[n])
                T[n] = sb
        fn(P, T)
        P.barrier()
        nc = P.finalize()
        in_maps = []
        for c in range(NCORES):
            m = {}
            for n in reads:
                if n == "L0_attn_hidden_states":
                    m[n] = x[BLOC * c:BLOC * (c + 1)]
                elif n in INPUT_SPECS:
                    m[n] = inputs[n]
                elif n in CONST_SPECS:
                    m[n] = consts[n]
                else:
                    m[n] = pools[c][n]
            in_maps.append(m)
        res = run_bass_kernel_spmd(nc, in_maps, core_ids=list(range(NCORES)))
        for c in range(NCORES):
            pools[c].update(res.results[c])
    return pools


def run_fused(inputs):
    consts = make_consts()
    x = np.ascontiguousarray(inputs["L0_attn_hidden_states"])
    P = Prog("mega")
    T = {}
    ext_in, scratch = [], set()
    for (name, fn, reads, writes, privs) in PHASES:
        for n in reads:
            if n in INPUT_SPECS or n in CONST_SPECS:
                if n not in T:
                    T[n] = _declare(P, [n], "ExternalInput")[n]
                    ext_in.append(n)
            else:
                scratch.add(n)
        for n in writes:
            if n != "out":
                scratch.add(n)
        for n in privs:
            scratch.add(n)
    T.update(_declare(P, sorted(scratch), "Internal"))
    T.update(_declare(P, ["out"], "ExternalOutput"))
    for n in list(T):
        if n in CONST_SPECS:
            sh, dt = CONST_SPECS[n]
            sb = P.sbuf("s_" + n, sh, dt)
            P.dma("sync", sb[:], T[n])
            T[n] = sb
    for (name, fn, reads, writes, privs) in PHASES:
        P.scope = name
        fn(P, T)
    P.scope = None
    P.barrier()
    nc = P.finalize()
    in_maps = []
    for c in range(NCORES):
        m = {}
        for n in ext_in:
            if n == "L0_attn_hidden_states":
                m[n] = x[BLOC * c:BLOC * (c + 1)]
            elif n in CONST_SPECS:
                m[n] = consts[n]
            else:
                m[n] = inputs[n]
        in_maps.append(m)
    res = run_bass_kernel_spmd(nc, in_maps, core_ids=list(range(NCORES)))
    return np.concatenate([res.results[c]["out"] for c in range(NCORES)],
                          axis=0)


FUSED = True


def kernel(**inputs):
    if FUSED:
        return run_fused(inputs)
    pools = run_unfused(inputs)
    return np.concatenate([pools[c]["out"] for c in range(NCORES)], axis=0)


# version 15
# speedup vs baseline: 1.3120x; 1.0706x over previous
import contextlib
import numpy as np
import concourse.bass as bass
import concourse.mybir as mybir
from concourse.bass_utils import run_bass_kernel_spmd

F32 = mybir.dt.float32
BF16 = mybir.dt.bfloat16
AF = mybir.ActivationFunctionType
ALU = mybir.AluOpType
AX = mybir.AxisListType

NCORES = 8
BLOC = 2
S = 2048
TLOC = BLOC * S
D = 1024
DC = D // 128

ENGINES = ("tensor", "vector", "scalar", "gpsimd", "sync")
DMA_QUEUES = ("gpsimd", "sync")
NSEM = 4


def _ap_name(x):
    if isinstance(x, str):
        return x
    t = getattr(x, "tensor", None)
    if t is not None:
        return t.name
    n = getattr(x, "name", None)
    if n is not None:
        return n
    raise ValueError(f"cannot extract tensor from {x!r}")



def _ap(x):
    if isinstance(x, (float, int)) or x is None:
        return x
    if getattr(x, "tensor", None) is not None:
        return x
    return x[:]

class Prog:

    def __init__(self, name="prog"):
        self.name = name
        self.nc = bass.Bass("TRN2", target_bir_lowering=False)
        self.es = contextlib.ExitStack()
        self.phase_es = None
        self.sems = {}
        self.cnt = {e: 0 for e in ENGINES}
        self.opbuf = {e: [] for e in ENGINES}
        self.reg = {}
        self.aps = {}
        self._cvecs = {}
        self.scope = None
        for e in ENGINES:
            if e in DMA_QUEUES:
                for j in range(NSEM):
                    self.sems[(e, j)] = self.es.enter_context(
                        self.nc.semaphore(f"sem_{e}{j}"))
            else:
                self.sems[e] = self.es.enter_context(
                    self.nc.semaphore(f"sem_{e}"))
        self.known = {e: {} for e in ENGINES}

    def dram(self, name, shape, dtype, kind):
        ap = self.nc.dram_tensor(name, list(shape), dtype, kind=kind).ap()
        self.aps[name] = ap
        return ap

    def sbuf(self, name, shape, dtype):
        if self.phase_es is not None:
            return self.phase_es.enter_context(
                self.nc.sbuf_tensor(name, list(shape), dtype))
        return self.nc.alloc_sbuf_tensor(name, list(shape), dtype)

    def psum(self, name, shape, dtype=F32):
        if self.phase_es is not None:
            return self.phase_es.enter_context(
                self.nc.psum_tensor(name, list(shape), dtype))
        return self.nc.alloc_psum_tensor(name, list(shape), dtype)

    def sbuf_persist(self, name, shape, dtype):
        return self.nc.alloc_sbuf_tensor(name, list(shape), dtype, side="right")

    def cvec(self, val, dtype=F32, rows=128):
        key = (float(val), dtype, rows)
        if key not in self._cvecs:
            t = self.nc.alloc_sbuf_tensor(f"cv{len(self._cvecs)}", [rows, 1], dtype,
                                          side="right")
            self.memset("vector", t[:], float(val))
            self._cvecs[key] = t[:]
        return self._cvecs[key]

    PHASE_GROUPS = [("p0", "p1proj"), ("p2a", "p2b"), ("p4", "p5b"),
                    ("p7", "p8")]

    def _group_of(self, tag):
        for g in self.PHASE_GROUPS:
            if tag in g:
                return g
        return (tag,)

    @contextlib.contextmanager
    def phase(self, tag):
        g = self._group_of(tag)
        if getattr(self, "_open_group", None) == g:
            yield
            if tag == g[-1]:
                self._open_group = None
                self.barrier()
                self.phase_es.close()
                self.phase_es = self._prev_es
            return
        prev = self.phase_es
        self.phase_es = contextlib.ExitStack()
        if len(g) > 1 and tag == g[0]:
            self._open_group = g
            self._prev_es = prev
            yield
            return
        try:
            yield
        finally:
            self.barrier()
            self.phase_es.close()
            self.phase_es = prev

    def _covers(self, eng, idx):
        if eng in DMA_QUEUES:
            out = []
            for j in range(NSEM):
                cnt = (idx - 1 - j) // NSEM + 1 if idx - 1 >= j else 0
                if cnt > 0:
                    out.append(((eng, j), 16 * cnt))
            return out
        return [(eng, idx)]

    def emit(self, eng, thunk, reads=(), writes=(), register=True, war=True):
        waits = {}

        def need(src_eng, src_idx):
            if src_idx <= 0:
                return
            if src_eng == eng == "tensor":
                return
            for key, v in self._covers(src_eng, src_idx):
                if v <= self.known[eng].get(key, 0):
                    continue
                waits[key] = max(waits.get(key, 0), v)

        rnames = [_ap_name(r) for r in reads]
        wnames = [_ap_name(w) for w in writes]
        for n in rnames:
            e = self.reg.get(n)
            if e:
                for we, wi in e["w"].items():
                    need(we, wi)
        for n in wnames:
            e = self.reg.get(n)
            if e:
                for we, wi in e["w"].items():
                    need(we, wi)
                if war:
                    for re_, ri in e["r"].items():
                        need(re_, ri)

        self.cnt[eng] += 1
        idx = self.cnt[eng]
        if eng in DMA_QUEUES:
            semkey = (eng, (idx - 1) % NSEM)
            pv = 16 * ((idx - 1) // NSEM)
            if pv > 0 and pv > self.known[eng].get(semkey, 0):
                waits[semkey] = max(waits.get(semkey, 0), pv)
            inc = 16
        else:
            semkey, inc = eng, 1
        for s, v in waits.items():
            self.known[eng][s] = v
        self.opbuf[eng].append((sorted(waits.items(), key=str),
                                (semkey, inc), thunk, self.scope))
        for n in rnames:
            e = self.reg.setdefault(n, {"w": {}, "r": {}})
            e["r"][eng] = idx
        for n in wnames:
            e = self.reg.setdefault(n, {"w": {}, "r": {}})
            if register:
                e["r"] = {}
                if eng not in DMA_QUEUES:
                    e["w"] = {k: v for k, v in e["w"].items()
                              if k in DMA_QUEUES or k != eng}
                e["w"][eng] = idx
        return idx

    def barrier(self):
        snap = {e: self.cnt[e] for e in ENGINES}
        targets = []
        for s in ENGINES:
            if snap[s] == 0:
                continue
            if s in DMA_QUEUES:
                for j in range(NSEM):
                    u = (snap[s] - j + NSEM - 1) // NSEM
                    if u > 0:
                        targets.append(((s, j), 16 * u, s))
            else:
                targets.append((s, snap[s], s))
        for e in ENGINES:
            waits = []
            for key, v, src in targets:
                if src == e:
                    continue
                if v > self.known[e].get(key, 0):
                    waits.append((key, v))
                    self.known[e][key] = v
            if waits:
                self.opbuf[e].append((waits, None, None, self.scope))

    def finalize(self):
        with self.nc.Block() as blk:
            for e in ENGINES:
                ops = self.opbuf[e]
                if not ops:
                    continue
                sec = getattr(blk, e)

                def body(eng, ops=ops, e=e):
                    cur = [None]
                    st = contextlib.ExitStack()
                    for waits, inc_spec, thunk, scope in ops:
                        if scope != cur[0]:
                            st.close()
                            st = contextlib.ExitStack()
                            if scope is not None:
                                st.enter_context(self.nc.named_scope(scope))
                            cur[0] = scope
                        for (key, val) in waits:
                            eng.wait_ge(self.sems[key], val)
                        if thunk is not None:
                            ins = thunk(eng)
                            semkey, inc = inc_spec
                            ins.then_inc(self.sems[semkey], inc)
                    st.close()

                sec(body)
        self.es.close()
        return self.nc

    def dma_t(self, q, out, in_, rnames, wnames):
        assert q in DMA_QUEUES
        out, in_ = _ap(out), _ap(in_)
        self.emit(q, lambda e: e.dma_start_transpose(out, in_),
                  reads=rnames, writes=wnames)

    def dma(self, q, out, in_, **kw):
        assert q in DMA_QUEUES
        out, in_ = _ap(out), _ap(in_)
        self.emit(q, lambda e: e.dma_start(out=out, in_=in_, **kw),
                  reads=[in_], writes=[out])

    def mm(self, out, lhsT, rhs, start=True, stop=True, register=None,
           war=None, extra_reads=()):
        out, lhsT, rhs = _ap(out), _ap(lhsT), _ap(rhs)
        if register is None:
            register = stop
        if war is None:
            war = start
        self.emit("tensor",
                  lambda e: e.matmul(out, lhsT, rhs, start=start, stop=stop),
                  reads=[lhsT, rhs] + list(extra_reads), writes=[out],
                  register=register, war=war)

    def transp(self, out, in_, ident, register=True, war=True):
        out, in_, ident = _ap(out), _ap(in_), _ap(ident)
        self.emit("tensor", lambda e: e.transpose(out, in_, ident),
                  reads=[in_, ident], writes=[out], register=register, war=war)

    def cp(self, eng, out, in_):
        out, in_ = _ap(out), _ap(in_)
        if eng == "scalar":
            self.emit(eng, lambda e: e.copy(out, in_),
                      reads=[in_], writes=[out])
        else:
            self.emit(eng, lambda e: e.tensor_copy(out, in_),
                      reads=[in_], writes=[out])

    def tt(self, eng, out, a, b, op):
        out, a, b = _ap(out), _ap(a), _ap(b)
        self.emit(eng, lambda e: e.tensor_tensor(out, a, b, op),
                  reads=[a, b], writes=[out])

    def ts(self, eng, out, a, s1, op, s2=None, op2=None, accum=None):
        out, a, s1, s2, accum = _ap(out), _ap(a), _ap(s1), _ap(s2), _ap(accum)
        extra = [s1] if not isinstance(s1, float) else []
        if s2 is not None and not isinstance(s2, float):
            extra.append(s2)
        kw = {}
        if op2 is not None:
            kw["op1"] = op2
        if accum is not None:
            kw["accum_out"] = accum
        wr = [out] + ([accum] if accum is not None else [])
        self.emit(eng, lambda e: e.tensor_scalar(out, a, s1, s2, op, **kw),
                  reads=[a] + extra, writes=wr)

    def stt(self, eng, out, in0, scalar, in1, op0, op1):
        out, in0, scalar, in1 = _ap(out), _ap(in0), _ap(scalar), _ap(in1)
        extra = [scalar] if not isinstance(scalar, float) else []
        self.emit(eng, lambda e: e.scalar_tensor_tensor(out, in0, scalar, in1, op0, op1),
                  reads=[in0, in1] + extra, writes=[out])

    def act(self, out, in_, func, bias=0.0, scale=1.0, accum=None):
        out, in_, bias, scale, accum = _ap(out), _ap(in_), _ap(bias), _ap(scale), _ap(accum)
        if isinstance(bias, float) and bias != 0.0:
            rows = _ap(in_).shape[0]
            bias = self.cvec(bias, rows=rows if rows in (1, 64, 128) else 128)
        extra = [x for x in (bias, scale) if not isinstance(x, float)]
        kw = {}
        if accum is not None:
            kw["accum_out"] = accum
        wr = [out] + ([accum] if accum is not None else [])
        self.emit("scalar",
                  lambda e: e.activation(out, in_, func, bias=bias, scale=scale, **kw),
                  reads=[in_] + extra, writes=wr)

    def red(self, eng, out, in_, op, absval=False, negate=False):
        out, in_ = _ap(out), _ap(in_)
        self.emit(eng,
                  lambda e: e.tensor_reduce(out, in_, axis=AX.X, op=op,
                                            apply_absolute_value=absval,
                                            negate=negate),
                  reads=[in_], writes=[out])

    def recip(self, out, in_):
        out, in_ = _ap(out), _ap(in_)
        self.emit("vector", lambda e: e.reciprocal(out, in_),
                  reads=[in_], writes=[out])

    def memset(self, eng, ap, v):
        ap = _ap(ap)
        self.emit(eng, lambda e: e.memset(ap, v), reads=[], writes=[ap])


INPUT_SPECS = {
    "L0_attn_hidden_states": ((16, 2048, 1024), F32),
    "L0_attn_Wq": ((1024, 1024), F32), "L0_attn_Wk": ((1024, 1024), F32),
    "L0_attn_Wv": ((1024, 1024), F32), "L0_attn_Wo": ((1024, 1024), F32),
    "L0_attn_norm_w": ((128,), F32),
    "L0_ffn_ln1_g": ((1024,), F32), "L0_ffn_ln1_b": ((1024,), F32),
    "L0_ffn_ln2_g": ((1024,), F32), "L0_ffn_ln2_b": ((1024,), F32),
    "L0_ffn_Wq": ((1024, 1024), F32), "L0_ffn_Wk": ((1024, 1024), F32),
    "L0_ffn_Wv": ((1024, 1024), F32), "L0_ffn_Wo": ((1024, 1024), F32),
    "L0_ffn_Wg": ((1024, 8), F32),
    "L0_ffn_W1": ((8, 1024, 4096), F32), "L0_ffn_b1": ((8, 4096), F32),
    "L0_ffn_W2": ((8, 4096, 1024), F32), "L0_ffn_b2": ((8, 1024), F32),
    "L1_attn_w_qkv": ((3072, 1024), F32), "L1_attn_b_qkv": ((3072,), F32),
    "L1_attn_w_out": ((1024, 1024), F32), "L1_attn_b_out": ((1024,), F32),
    "L1_ffn_gamma_x": ((1024,), F32), "L1_ffn_gamma_slot": ((1024,), F32),
    "L1_ffn_slot_embeds": ((8, 512, 1024), F32),
    "L1_ffn_w1": ((1024, 4096), F32), "L1_ffn_w2": ((4096, 1024), F32),
}
CONST_SPECS = {
    "c_eye_f32": ((128, 128), F32),
    "c_eye_bf": ((128, 128), BF16),
    "c_mask_ut": ((128, 128), F32),
    "c_ones_f32": ((128, 128), F32),
    "c_ones_bf": ((128, 128), BF16),
}
SCRATCH_SPECS = {
    "xn_tm": ((TLOC, D), F32),
    "xn_fm": ((DC, 128, TLOC), BF16),
    "att0_tm": ((TLOC, D), F32),
    "x2_tm": ((TLOC, D), F32),
    "m_fm": ((DC, 128, TLOC), BF16),
    "cw_tm": ((TLOC, 8), F32),
    "ffn0_tm": ((TLOC, D), F32),
    "w1m_bf": ((8, DC, 128, 4096), BF16),
    "w2m_bf": ((8, 32, 128, D), BF16),
    "x1_tm": ((TLOC, D), F32),
    "xn1_tm": ((TLOC, D), F32),
    "xn1_fm": ((DC, 128, TLOC), BF16),
    "att1_tm": ((TLOC, D), F32),
    "xnn_fm": ((BLOC, DC, 128, S), BF16),
    "se_fm": ((DC, 128, 4096), BF16),
    "proc_tm": ((BLOC, 4096, D), BF16),
    "ffn1_tm": ((TLOC, D), F32),
}
OUTPUT_SPECS = {"out": ((BLOC, S, D), F32)}

PRIV_SPECS = {
    "q0_fm": ((BLOC, DC, 128, S), BF16),
    "k0_fm": ((BLOC, DC, 128, S), BF16),
    "v0_fm": ((BLOC, DC, 128, S), BF16),
    "o0_fm": ((BLOC, DC, 128, S), BF16),
    "h0_fm": ((DC, 128, TLOC), BF16),
    "qf_fm": ((BLOC, DC, 128, S), BF16),
    "kf_fm": ((BLOC, DC, 128, S), BF16),
    "vf_tm": ((BLOC, 16, 128, D), BF16),
    "ctx0_fm": ((BLOC, DC, 128, S), BF16),
    "m_fm32": ((DC, 128, TLOC), F32),
    "qkv1_fm": ((BLOC, 24, 128, S), BF16),
    "v1_tm": ((BLOC, 16, 128, D), BF16),
    "ctx1_fm": ((BLOC, DC, 128, S), BF16),
    "xnn_tmb": ((BLOC, 16, 128, D), BF16),
    "w1l1_bf": ((32, 128, D), BF16),
    "w2l1_bf": ((32, 128, D), BF16),
}

PHASES = []


def phase(name, reads, writes, privs=()):
    def deco(fn):
        PHASES.append((name, fn, list(reads), list(writes), list(privs)))
        return fn
    return deco


def spec_of(name):
    for d in (INPUT_SPECS, CONST_SPECS, SCRATCH_SPECS, OUTPUT_SPECS, PRIV_SPECS):
        if name in d:
            sh, dt = d[name]
            if name == "L0_attn_hidden_states":
                sh = (BLOC, S, D)
            return sh, dt
    raise KeyError(name)



def ln_stats(P, xt, mu, rstd, xc, sq, acc, eps=1e-5):
    P.red("vector", mu, xt, ALU.add)
    P.ts("vector", mu, mu, 1.0 / D, ALU.mult)
    P.ts("vector", xc, xt, mu, ALU.subtract)
    P.act(sq, xc, AF.Square, accum=acc)
    P.act(acc, acc, AF.Ln, scale=1.0 / D, bias=eps)
    P.act(rstd, acc, AF.Exp, scale=-0.5)


def tm_tile_to_fm(P, tag, src_bf, eye_bf, dst_hbm_fm, ti, psum_bf, stage_bf,
                  n_chunks=DC, also_f32=None, psum32=None, eye32=None,
                  src32=None, stage32=None, dst32=None):
    for dc in range(n_chunks):
        P.transp(psum_bf[:, dc * 128:(dc + 1) * 128],
                 src_bf[:, dc * 128:(dc + 1) * 128], eye_bf,
                 register=(dc == n_chunks - 1), war=(dc == 0))
    P.cp("vector", stage_bf[:], psum_bf[:])
    P.dma("gpsimd",
          dst_hbm_fm[:, :, ti * 128:(ti + 1) * 128].rearrange("c p t -> p c t"),
          stage_bf[:].rearrange("p (c t) -> p c t", c=n_chunks))
    if also_f32:
        for dc in range(n_chunks):
            P.transp(psum32[:, dc * 128:(dc + 1) * 128],
                     src32[:, dc * 128:(dc + 1) * 128], eye32,
                     register=(dc == n_chunks - 1), war=(dc == 0))
        P.cp("vector", stage32[:], psum32[:])
        P.dma("gpsimd",
              dst32[:, :, ti * 128:(ti + 1) * 128].rearrange("c p t -> p c t"),
              stage32[:].rearrange("p (c t) -> p c t", c=n_chunks))



@phase("p0_ln0",
       reads=["L0_attn_hidden_states", "c_eye_bf"],
       writes=["xn_tm", "xn_fm"])
def p0_ln0(P, T):
    x = T["L0_attn_hidden_states"].rearrange("b s d -> (b s) d")
    eye_bf = T["c_eye_bf"]
    with P.phase("p0"):
        R2 = lambda nm, sh, dt: Rot([P.sbuf(f"{nm}{i}", sh, dt) for i in range(2)])
        xt_r = R2("p0_xt", [128, D], F32)
        xc_r = R2("p0_xc", [128, D], F32)
        xnb_r = R2("p0_xnb", [128, D], BF16)
        mu_r = R2("p0_mu", [128, 1], F32)
        rstd_r = R2("p0_rstd", [128, 1], F32)
        lsq = P.sbuf("p0_sq", [128, D], F32)
        lacc_r = R2("p0_acc", [128, 1], F32)
        ptr_r = Rot([P.psum(f"p0_ptr{i}", [128, D], BF16) for i in range(2)])
        stg_r = R2("p0_stg", [128, D], BF16)
        for ti in range(TLOC // 128):
            sl = slice(ti * 128, (ti + 1) * 128)
            xt, xc, xnb = xt_r.nxt(), xc_r.nxt(), xnb_r.nxt()
            mu, rstd, lacc = mu_r.nxt(), rstd_r.nxt(), lacc_r.nxt()
            ptr, stg = ptr_r.nxt(), stg_r.nxt()
            P.dma("sync", xt[:], x[sl, :])
            ln_stats(P, xt, mu, rstd, xc, lsq, lacc)
            P.ts("vector", xc[:], xc[:], rstd[:], ALU.mult)
            P.dma("gpsimd", T["xn_tm"][sl, :], xc[:])
            P.cp("scalar", xnb[:], xc[:])
            tm_tile_to_fm(P, "p0", xnb, eye_bf, T["xn_fm"], ti, ptr, stg)


class Rot:
    def __init__(self, items):
        self.items = list(items)
        self.i = 0

    def nxt(self):
        r = self.items[self.i % len(self.items)]
        self.i += 1
        return r


def load_wT(P, tag, w_dram, wt_sb, eye_bf, n_out=D, n_in=D, bufs=None):
    nic = n_in // 128
    if bufs is None:
        bufs = wT_bufs(P, tag, n_in)
    f32t, bft, ptr = bufs
    for oc in range(n_out // 128):
        f3, bf, pt = f32t.nxt(), bft.nxt(), ptr.nxt()
        P.dma("sync", f3[:], w_dram[oc * 128:(oc + 1) * 128, :])
        P.cp("vector", bf, f3)
        for dc in range(nic):
            P.transp(_ap(pt)[:, dc * 128:(dc + 1) * 128],
                     _ap(bf)[:, dc * 128:(dc + 1) * 128], eye_bf,
                     register=(dc == nic - 1), war=(dc == 0))
        P.cp("scalar",
             wt_sb[:, :, oc * 128:(oc + 1) * 128],
             _ap(pt).rearrange("p (c t) -> p c t", c=nic))


def wT_bufs(P, tag, n_in=D):
    return (
        Rot([P.sbuf(f"{tag}_wl32{i}", [128, n_in], F32) for i in range(2)]),
        Rot([P.sbuf(f"{tag}_wlbf{i}", [128, n_in], BF16) for i in range(2)]),
        Rot([P.psum(f"{tag}_wlps{i}", [128, n_in], BF16) for i in range(2)]),
    )


def load_w(P, tag, w_dram, w_sb, n_out):
    f32t = P.sbuf(f"{tag}_wl32", [128, n_out], F32)
    for dc in range(DC):
        P.dma("sync", f32t[:], w_dram[dc * 128:(dc + 1) * 128, :])
        P.cp("vector", w_sb[:, dc, :], f32t)


def row_of(P, tag, vec_dram, n):
    r = P.sbuf(f"{tag}_row", [1, n], F32)
    P.dma("sync", r[:], vec_dram.rearrange("(a d) -> a d", a=1))
    return r



@phase("p1_att0",
       reads=["xn_fm", "L0_attn_Wq", "L0_attn_Wk", "L0_attn_Wv", "L0_attn_Wo",
              "L0_attn_norm_w", "c_eye_bf", "c_mask_ut", "c_ones_bf",
              "c_ones_f32"],
       writes=["att0_tm"],
       privs=["q0_fm", "k0_fm", "v0_fm", "o0_fm"])
def p1_att0(P, T):
    eye = T["c_eye_bf"]
    ones_bf_col = _ap(T["c_ones_bf"])[:, 0:1]
    ones_f32_col = _ap(T["c_ones_f32"])[:, 0:1]
    mask = T["c_mask_ut"]

    P.scope = "p1_proj"
    with P.phase("p1proj"):
        wts = {}
        wb = wT_bufs(P, "p1w")
        for wn, key in (("L0_attn_Wq", "q0_fm"), ("L0_attn_Wk", "k0_fm"),
                        ("L0_attn_Wv", "v0_fm")):
            wt = P.sbuf(f"p1_{wn}T", [128, DC, D], BF16)
            load_wT(P, f"p1{wn}", T[wn], wt, eye, bufs=wb)
            wts[key] = wt
        xt = P.sbuf("p1_xt", [128, DC, 512], BF16)
        pos = Rot([P.psum(f"p1_po{i}", [128, 512], F32) for i in range(2)])
        obs = Rot([P.sbuf(f"p1_ob{i}", [128, 512], BF16) for i in range(2)])
        for b in range(BLOC):
            for tmac in range(S // 512):
                t0 = b * S + tmac * 512
                P.dma("sync", xt[:],
                      T["xn_fm"][:, :, t0:t0 + 512].rearrange("c p t -> p c t"))
                for key, wt in wts.items():
                    for ec in range(DC):
                        po, ob = pos.nxt(), obs.nxt()
                        for dc in range(DC):
                            P.mm(po, wt[:, dc, ec * 128:(ec + 1) * 128],
                                 xt[:, dc, :], start=(dc == 0), stop=(dc == DC - 1))
                        P.cp("vector", ob, po)
                        P.dma("gpsimd",
                              T[key][b, ec, :, tmac * 512:(tmac + 1) * 512], ob)

    NCH = S // 128
    P.scope = "p1_chunk"
    with P.phase("p1attn"):
        nw_row = row_of(P, "p1_nw", T["L0_attn_norm_w"], 128)
        qh = P.sbuf("p1_qh", [128, S], BF16)
        kh = P.sbuf("p1_kh", [128, S], BF16)
        vh = P.sbuf("p1_vh", [128, S], BF16)
        kv_s = P.sbuf("p1_kv", [128, NCH, 256], BF16)
        scst_r = Rot([P.sbuf(f"p1_scst{i}", [128, 128], BF16) for i in range(2)])
        state_s = P.sbuf("p1_state", [128, 129], BF16)
        den_r = Rot([P.sbuf(f"p1_den{i}", [1, 128], F32) for i in range(2)])
        d2_s = P.sbuf("p1_d2", [1, 128], F32)
        osq_s = P.sbuf("p1_osq", [128, 128], F32)
        t_s = P.sbuf("p1_t", [1, 128], F32)
        al_s = P.sbuf("p1_al", [1, 128], F32)
        scale_s = P.sbuf("p1_scale", [128, 128], F32)
        on_r = Rot([P.sbuf(f"p1_on{i}", [128, 128], BF16) for i in range(2)])
        ps_sc = Rot([P.psum(f"p1_pssc{i}", [128, 128], F32) for i in range(1)])
        ps_kv = P.psum("p1_pskv", [128, 256], BF16)
        ps_o = Rot([P.psum(f"p1_pso{i}", [128, 128], F32) for i in range(2)])
        ps_ds = Rot([P.psum(f"p1_psds{i}", [128, 512], F32) for i in range(2)])
        ps_state = P.psum("p1_psst", [128, 129], F32)
        ps_scale = P.psum("p1_pssca", [128, 128], F32)

        def epi(b, h, c, csl, po, pd):
            def run():
                den_s = den_r.nxt()
                on_s = on_r.nxt()
                P.act(den_s, _ap(pd)[0:1, 0:128], AF.Abs)
                P.ts("vector", den_s, den_s, 1.0, ALU.max)
                P.act(osq_s, po, AF.Square)
                P.mm(_ap(pd)[0:1, 128:256], ones_f32_col, osq_s)
                P.tt("vector", d2_s, den_s, den_s, ALU.mult)
                P.ts("vector", d2_s, d2_s, 1e-5, ALU.mult)
                P.stt("vector", t_s, _ap(pd)[0:1, 128:256], 1.0 / 128.0, d2_s,
                      ALU.mult, ALU.add)
                P.act(t_s, t_s, AF.Ln)
                P.act(al_s, t_s, AF.Exp, scale=-0.5)
                P.mm(ps_scale, nw_row, al_s)
                P.cp("vector", scale_s, ps_scale)
                P.tt("vector", on_s, po, scale_s, ALU.mult)
                P.dma("gpsimd", T["o0_fm"][b, h, :, csl], on_s)
            return run

        pend = None
        for b in range(BLOC):
            for h in range(8):
                P.dma("sync", qh[:], T["q0_fm"][b, h, :, :])
                P.dma("sync", kh[:], T["k0_fm"][b, h, :, :])
                P.dma("sync", vh[:], T["v0_fm"][b, h, :, :])
                for c in range(NCH):
                    csl = slice(c * 128, (c + 1) * 128)
                    P.transp(ps_kv[:, 0:128], _ap(kh)[:, csl], eye,
                             register=False, war=True)
                    P.transp(ps_kv[:, 128:256], _ap(vh)[:, csl], eye,
                             register=True, war=False)
                    P.emit("vector",
                           lambda e, o=_ap(kv_s)[:, c, :], i=_ap(ps_kv):
                           e.tensor_copy(o, i),
                           reads=[ps_kv], writes=[f"p1kv@{c}"])
                    ktc = _ap(kv_s)[:, c, 0:128]
                    vtc = _ap(kv_s)[:, c, 128:256]
                    psc = ps_sc.nxt()
                    po = ps_o.nxt()
                    pd = ps_ds.nxt()
                    scst = scst_r.nxt()
                    P.mm(psc, kh[:, csl], qh[:, csl])
                    P.emit("vector",
                           lambda e, o=_ap(scst), a=_ap(psc), m=_ap(mask):
                           e.tensor_tensor(o, a, m, ALU.mult),
                           reads=[psc, mask, f"p1kv@{c}"], writes=[scst])
                    if c > 0:
                        P.cp("scalar", state_s, ps_state)
                        P.mm(po, _ap(state_s)[:, 0:128], qh[:, csl],
                             start=True, stop=False)
                        P.mm(po, vtc, scst, start=False, stop=True,
                             extra_reads=[f"p1kv@{c}"])
                        P.mm(_ap(pd)[0:1, 0:128], _ap(state_s)[:, 128:129],
                             qh[:, csl], start=True, stop=False)
                        P.mm(_ap(pd)[0:1, 0:128], ones_bf_col, scst,
                             start=False, stop=True)
                    else:
                        P.mm(po, vtc, scst, start=True, stop=True,
                             extra_reads=[f"p1kv@{c}"])
                        P.mm(_ap(pd)[0:1, 0:128], ones_bf_col, scst,
                             start=True, stop=True)
                    P.mm(ps_state[:, 0:128], ktc, vtc,
                         start=(c == 0), stop=False, register=False,
                         extra_reads=[f"p1kv@{c}"])
                    P.mm(ps_state[:, 128:129], ktc, ones_bf_col,
                         start=False, stop=(c == NCH - 1), register=True,
                         extra_reads=[f"p1kv@{c}"])
                    if pend is not None:
                        pend()
                    pend = epi(b, h, c, csl, po, pd)
        pend()

    P.scope = "p1_wo"
    with P.phase("p1wo"):
        woT = P.sbuf("p1_woT", [128, DC, D], BF16)
        load_wT(P, "p1wo", T["L0_attn_Wo"], woT, eye)
        ots = Rot([P.sbuf(f"p1_ot{i}", [128, DC, 128], BF16) for i in range(2)])
        pws = Rot([P.psum(f"p1_pw{i}", [128, 512], F32) for i in range(4)])
        oss = Rot([P.sbuf(f"p1_os{i}", [128, D], F32) for i in range(2)])
        for b in range(BLOC):
            for tt_ in range(S // 128):
                ts_ = slice(tt_ * 128, (tt_ + 1) * 128)
                ot = ots.nxt()
                os_ = oss.nxt()
                P.dma("sync", ot[:],
                      T["o0_fm"][b, :, :, ts_].rearrange("c p t -> p c t"))
                for half in range(2):
                    pw = pws.nxt()
                    for ec in range(DC):
                        P.mm(pw, ot[:, ec, :],
                             woT[:, ec, half * 512:(half + 1) * 512],
                             start=(ec == 0), stop=(ec == DC - 1))
                    P.cp("vector", os_[:, half * 512:(half + 1) * 512], pw)
                P.dma("gpsimd", T["att0_tm"][b * S + tt_ * 128:
                                             b * S + (tt_ + 1) * 128, :], os_)


def bcast_row(P, tag, row_ap, n, ones_f32, scale=1.0):
    ps = P.psum(f"{tag}_bps", [128, 512], F32)
    out = P.sbuf(f"{tag}_bc", [128, n], F32)
    for j in range(0, n, 512):
        w = min(512, n - j)
        P.mm(ps[:, 0:w], _ap(ones_f32)[0:1, :], _ap(row_ap)[0:1, j:j + w])
        if scale == 1.0:
            P.cp("vector", out[:, j:j + w], ps[:, 0:w])
        else:
            P.ts("vector", out[:, j:j + w], ps[:, 0:w], float(scale), ALU.mult)
    return out


def softmax_TQ(P, bufs, kh, qh, v_s, head, negc_s, out_dst, scale, bv=None):
    (ps_sT, E_r, ps_ctx_r, ps_sig, sig_r, sg_r, ctx_r) = bufs
    NK2 = S // 256
    c0 = 65 * head

    def make_epilogue(pc, qsl):
        sig_s, sg_s, ctxb = sig_r.nxt(), sg_r.nxt(), ctx_r.nxt()

        def epi():
            P.act(sig_s, _ap(pc)[64:65, :], AF.Ln)
            P.act(sig_s, sig_s, AF.Exp, scale=-1.0)
            P.mm(_ap(ps_sig)[0:64, :], P.ones_f32_row64, sig_s)
            P.cp("vector", sg_s, _ap(ps_sig)[0:64, :])
            if bv is None:
                P.tt("vector", ctxb, _ap(pc)[0:64, :], sg_s, ALU.mult)
            else:
                P.tt("vector", sg_s, _ap(pc)[0:64, :], sg_s, ALU.mult)
                P.ts("vector", ctxb, sg_s, bv, ALU.add)
            P.dma("gpsimd", out_dst[:, qsl], ctxb)
        return epi

    pend = None
    for qm in range(S // 512):
        qsl = slice(qm * 512, (qm + 1) * 512)
        pc = ps_ctx_r.nxt()
        prev = None
        for kt2 in range(NK2):
            ka, kb = 2 * kt2, 2 * kt2 + 1
            pt = ps_sT.nxt()
            P.mm(_ap(pt)[:, 0:512], kh[:, ka * 128:(ka + 1) * 128],
                 qh[:, qsl], register=False)
            P.mm(_ap(pt)[:, 512:1024], kh[:, kb * 128:(kb + 1) * 128],
                 qh[:, qsl], war=False)
            E = E_r.nxt()
            P.act(E, pt, AF.Exp, bias=negc_s, scale=scale)
            if kt2 == 1 and pend is not None:
                pend()
                pend = None
            if prev is not None:
                pa, pE = prev
                P.mm(_ap(pc)[0:65, :], v_s[:, pa, c0:c0 + 65],
                     _ap(pE)[:, 0:512],
                     start=(pa == 0), stop=False, register=False)
                P.mm(_ap(pc)[0:65, :], v_s[:, pa + 1, c0:c0 + 65],
                     _ap(pE)[:, 512:1024], start=False,
                     stop=False)
            prev = (ka, E)
        pa, pE = prev
        P.mm(_ap(pc)[0:65, :], v_s[:, pa, c0:c0 + 65], _ap(pE)[:, 0:512],
             start=(pa == 0), stop=False, register=False)
        P.mm(_ap(pc)[0:65, :], v_s[:, pa + 1, c0:c0 + 65],
             _ap(pE)[:, 512:1024], start=False, stop=True)
        pend = make_epilogue(pc, qsl)
    pend()


def head_bound(P, bufs, qh, kh, negc_s, scale_mag):
    (sq_s, ps_n, nmax, tmp1) = bufs
    for which, src in ((0, qh), (1, kh)):
        P.tt("vector", sq_s, src, src, ALU.mult)
        for j in range(S // 512):
            P.mm(ps_n[0:1, :], P.ones_f32_col64, sq_s[:, j * 512:(j + 1) * 512])
            P.red("vector", tmp1, ps_n[0:1, :], ALU.max)
            if j == 0:
                P.cp("vector", nmax[0:1, which:which + 1], tmp1)
            else:
                P.tt("vector", nmax[0:1, which:which + 1],
                     nmax[0:1, which:which + 1], tmp1, ALU.max)
    P.tt("vector", tmp1, nmax[0:1, 0:1], nmax[0:1, 1:2], ALU.mult)
    P.act(tmp1, tmp1, AF.Ln, bias=1e-30)
    P.act(tmp1, tmp1, AF.Exp, scale=0.5)
    P.mm(ps_n[:, 0:1], _ap(P.ones_f32)[0:1, :], tmp1)
    P.act(negc_s, ps_n[:, 0:1], AF.Copy, scale=-0.75 * scale_mag)



@phase("p2_mha0",
       reads=["xn_tm", "xn_fm", "L0_ffn_ln1_g", "L0_ffn_ln1_b",
              "L0_ffn_ln2_g", "L0_ffn_ln2_b", "L0_ffn_Wq", "L0_ffn_Wk",
              "L0_ffn_Wv", "L0_ffn_Wo", "L0_ffn_Wg", "L0_ffn_W1", "L0_ffn_W2",
              "c_eye_bf", "c_eye_f32", "c_ones_f32", "c_ones_bf"],
       writes=["x2_tm", "m_fm", "cw_tm", "w1m_bf", "w2m_bf"],
       privs=["h0_fm", "qf_fm", "kf_fm", "vf_tm", "ctx0_fm", "m_fm32"])
def p2_mha0(P, T):
    eye = T["c_eye_bf"]
    eye32 = T["c_eye_f32"]
    P.ones_f32 = T["c_ones_f32"]
    P.ones_bf_col = _ap(T["c_ones_bf"])[:, 0:1]
    P.ones_f32_col64 = _ap(T["c_ones_f32"])[0:64, 0:1]
    P.ones_f32_row64 = _ap(T["c_ones_f32"])[0:1, 0:64]
    bnd_q = P.sbuf_persist("p2_bndq", [1, 16], F32)
    bnd_k = P.sbuf_persist("p2_bndk", [1, 16], F32)
    bt1 = P.sbuf_persist("p2_bt1", [1, 1], F32)
    P.memset("vector", bnd_q[:], 0.0)
    P.memset("vector", bnd_k[:], 0.0)

    P.scope = "p2_a"
    with P.phase("p2a"):
        g1r = row_of(P, "p2g1", T["L0_ffn_ln1_g"], D)
        b1r = row_of(P, "p2b1", T["L0_ffn_ln1_b"], D)
        g1 = bcast_row(P, "p2g1", g1r, D, P.ones_f32)
        b1 = bcast_row(P, "p2b1", b1r, D, P.ones_f32)
        R2 = lambda nm, sh, dt: Rot([P.sbuf(f"{nm}{i}", sh, dt) for i in range(2)])
        xt_r = R2("p2_xt", [128, D], F32)
        xc_r = R2("p2_xc", [128, D], F32)
        hb_r = R2("p2_hb", [128, D], BF16)
        mu_r = R2("p2_mu", [128, 1], F32)
        rstd_r = R2("p2_rstd", [128, 1], F32)
        lsq = P.sbuf("p2_sq", [128, D], F32)
        lacc_r = R2("p2_acc", [128, 1], F32)
        ptr_r = Rot([P.psum(f"p2_ptr{i}", [128, D], BF16) for i in range(2)])
        stg_r = R2("p2_stg", [128, D], BF16)
        for ti in range(TLOC // 128):
            sl = slice(ti * 128, (ti + 1) * 128)
            xt, xc, hb = xt_r.nxt(), xc_r.nxt(), hb_r.nxt()
            mu, rstd, lacc = mu_r.nxt(), rstd_r.nxt(), lacc_r.nxt()
            ptr, stg = ptr_r.nxt(), stg_r.nxt()
            P.dma("sync", xt[:], T["xn_tm"][sl, :])
            ln_stats(P, xt, mu, rstd, xc, lsq, lacc)
            P.ts("vector", xc[:], xc[:], rstd[:], ALU.mult)
            P.tt("vector", xc, xc, g1, ALU.mult)
            P.tt("vector", xc, xc, b1, ALU.add)
            P.cp("scalar", hb, xc)
            tm_tile_to_fm(P, "p2", hb, eye, T["h0_fm"], ti, ptr, stg)

    P.scope = "p2_b"
    with P.phase("p2b"):
        wq = P.sbuf("p2_wq", [128, DC, D], BF16)
        wk = P.sbuf("p2_wk", [128, DC, D], BF16)
        wv = P.sbuf("p2_wv", [128, DC, D], BF16)
        load_w(P, "p2wq", T["L0_ffn_Wq"], wq, D)
        load_w(P, "p2wk", T["L0_ffn_Wk"], wk, D)
        load_w(P, "p2wv", T["L0_ffn_Wv"], wv, D)
        xt = P.sbuf("p2_xh", [128, DC, 512], BF16)
        pos = Rot([P.psum(f"p2_po{i}", [128, 512], F32) for i in range(2)])
        obs = Rot([P.sbuf(f"p2_ob{i}", [128, 512], BF16) for i in range(4)])
        ps_bnd = Rot([P.psum(f"p2_psbnd{i}", [128, 512], F32) for i in range(2)])
        sqb_r = Rot([P.sbuf(f"p2_sqb{i}", [128, 512], BF16) for i in range(2)])
        ones_bf = _ap(T["c_ones_bf"])
        for b in range(BLOC):
            for tmac in range(S // 512):
                t0 = b * S + tmac * 512
                P.dma("sync", xt[:],
                      T["h0_fm"][:, :, t0:t0 + 512].rearrange("c p t -> p c t"))
                for key, wt, bnd in (("qf_fm", wq, bnd_q), ("kf_fm", wk, bnd_k)):
                    for ec in range(DC):
                        po, ob = pos.nxt(), obs.nxt()
                        for dc in range(DC):
                            P.mm(po, wt[:, dc, ec * 128:(ec + 1) * 128],
                                 xt[:, dc, :], start=(dc == 0),
                                 stop=(dc == DC - 1))
                        P.cp("vector", ob, po)
                        P.dma("gpsimd",
                              T[key][b, ec, :, tmac * 512:(tmac + 1) * 512], ob)
                        sqb = sqb_r.nxt()
                        P.tt("vector", sqb, ob, ob, ALU.mult)
                        for sub in (0, 64):
                            hh = 2 * ec + sub // 64
                            pbn = ps_bnd.nxt()
                            P.mm(_ap(pbn)[0:1, :],
                                 ones_bf[sub:sub + 64, 0:1],
                                 _ap(sqb)[sub:sub + 64, :])
                            P.red("vector", bt1, _ap(pbn)[0:1, :], ALU.max)
                            P.tt("vector", bnd[0:1, hh:hh + 1],
                                 bnd[0:1, hh:hh + 1], bt1, ALU.max)
                for st in range(4):
                    tc = tmac * 4 + st
                    for half in range(2):
                        po, ob = pos.nxt(), obs.nxt()
                        hsl = slice(half * 512, (half + 1) * 512)
                        for dc in range(DC):
                            P.mm(po, xt[:, dc, st * 128:(st + 1) * 128],
                                 wv[:, dc, hsl], start=(dc == 0),
                                 stop=(dc == DC - 1))
                        P.cp("vector", ob, po)
                        P.dma("gpsimd", T["vf_tm"][b, tc, :, hsl], ob)

    def precast_ops(wc32, wcb):
        for e in range(8):
            for dc in range(DC):
                c32, cb = wc32.nxt(), wcb.nxt()
                P.dma("sync", c32[:],
                      T["L0_ffn_W1"][e, dc * 128:(dc + 1) * 128, :])
                P.cp("vector", cb, c32)
                P.dma("gpsimd", T["w1m_bf"][e, dc, :, :], cb)
                yield
        for e in range(8):
            for g in range(8):
                c32, cb = wc32.nxt(), wcb.nxt()
                P.dma("sync",
                      c32[:].rearrange("p (a d) -> p a d", a=4),
                      T["L0_ffn_W2"][e, g * 512:(g + 1) * 512, :]
                      .rearrange("(a p) d -> p a d", p=128))
                P.cp("vector", cb, c32)
                P.dma("gpsimd",
                      T["w2m_bf"][e, 4 * g:4 * g + 4, :, :]
                      .rearrange("c p d -> p c d"),
                      cb[:].rearrange("p (c d) -> p c d", c=4))
                yield
        while True:
            yield

    P.scope = "p2_c"
    with P.phase("p2c"):
        NH = 16
        wc32 = Rot([P.sbuf(f"p2_wc32{i}", [128, 4096], F32) for i in range(2)])
        wcb = Rot([P.sbuf(f"p2_wcb{i}", [128, 4096], BF16) for i in range(2)])
        precast = precast_ops(wc32, wcb)
        v_r = Rot([P.sbuf(f"p2_vs{i}", [128, S // 128, NH * 65], BF16)
                   for i in range(2)])
        qc = Rot([P.sbuf(f"p2_qc{i}", [128, S], BF16) for i in range(2)])
        kc = Rot([P.sbuf(f"p2_kc{i}", [128, S], BF16) for i in range(2)])
        tmp1 = P.sbuf("p2_tmp1", [1, 1], F32)
        negc = Rot([P.sbuf(f"p2_negc{i}", [128, 1], F32) for i in range(2)])
        ps_n = P.psum("p2_psn", [128, 512], F32)
        ps_sT = Rot([P.psum(f"p2_psT{i}", [128, 1024], F32) for i in range(2)])
        E_r = Rot([P.sbuf(f"p2_E{i}", [128, 1024], BF16) for i in range(2)])
        ps_ctx = Rot([P.psum(f"p2_psctx{i}", [128, 512], F32) for i in range(2)])
        ps_sig = P.psum("p2_pssig", [128, 512], F32)
        sig_r = Rot([P.sbuf(f"p2_sig{i}", [1, 512], F32) for i in range(2)])
        sg_r = Rot([P.sbuf(f"p2_sg{i}", [64, 512], F32) for i in range(2)])
        ctx_r = Rot([P.sbuf(f"p2_ctxb{i}", [64, 512], BF16) for i in range(2)])
        at_bufs = (ps_sT, E_r, ps_ctx, ps_sig, sig_r, sg_r, ctx_r)
        for vb in v_r.items:
            P.memset("vector",
                 _ap(vb).rearrange("p c (h w) -> p c h w", w=65)[:, :, :, 64:65],
                 1.0)
        for b in range(BLOC):
            v_s = v_r.nxt()
            for tc in range(S // 128):
                P.dma("gpsimd",
                      _ap(v_s)[:, tc, :].rearrange("p (h w) -> p h w", w=65)
                      [:, :, 0:64],
                      T["vf_tm"][b, tc, :, :].rearrange("p (h w) -> p h w", w=64))
            for hp in range(8):
                qcb, kcb = qc.nxt(), kc.nxt()
                P.dma("sync", qcb[:], T["qf_fm"][b, hp, :, :])
                P.dma("sync", kcb[:], T["kf_fm"][b, hp, :, :])
                for sub in range(2):
                    hbase = 64 * sub
                    h = 2 * hp + sub
                    qh = _ap(qcb)[hbase:hbase + 64, :]
                    kh = _ap(kcb)[hbase:hbase + 64, :]
                    nc_ = negc.nxt()
                    P.tt("vector", tmp1, bnd_q[0:1, h:h + 1],
                         bnd_k[0:1, h:h + 1], ALU.mult)
                    P.act(tmp1, tmp1, AF.Ln, bias=1e-30)
                    P.act(tmp1, tmp1, AF.Exp, scale=0.5)
                    P.mm(_ap(ps_n)[:, 0:1], _ap(P.ones_f32)[0:1, :], tmp1)
                    P.act(nc_, _ap(ps_n)[:, 0:1], AF.Copy, scale=-0.75 * 0.125)
                    softmax_TQ(P, at_bufs, kh, qh, v_s, h, nc_,
                               T["ctx0_fm"][b, hp, hbase:hbase + 64, :], 0.125)
                    next(precast)
                    next(precast)
                    next(precast)
                    next(precast)

    P.scope = "p2_d"
    with P.phase("p2d"):
        wo = P.sbuf("p2_wo", [128, DC, D], BF16)
        load_w(P, "p2wo", T["L0_ffn_Wo"], wo, D)
        wg = P.sbuf("p2_wg", [128, DC, 8], F32)
        wgl = P.sbuf("p2_wgl", [128, 8], F32)
        for dc in range(DC):
            P.dma("sync", wgl[:], T["L0_ffn_Wg"][dc * 128:(dc + 1) * 128, :])
            P.cp("vector", wg[:, dc, :], wgl)
        g2r = row_of(P, "p2g2", T["L0_ffn_ln2_g"], D)
        b2r = row_of(P, "p2b2", T["L0_ffn_ln2_b"], D)
        g2 = bcast_row(P, "p2g2", g2r, D, P.ones_f32)
        b2 = bcast_row(P, "p2b2", b2r, D, P.ones_f32)
        ot = P.sbuf("p2_ot", [128, DC, 128], BF16)
        xnt = P.sbuf("p2_xnt", [128, D], F32)
        x2s = P.sbuf("p2_x2s", [128, D], F32)
        mb = P.sbuf("p2_mb", [128, D], BF16)
        m32 = P.sbuf("p2_m32", [128, D], F32)
        xc2 = P.sbuf("p2_xc2", [128, D], F32)
        mu = P.sbuf("p2_mu2", [128, 1], F32)
        rstd = P.sbuf("p2_rstd2", [128, 1], F32)
        lsq = P.sbuf("p2_sq2", [128, D], F32)
        lacc = P.sbuf("p2_acc2", [128, 1], F32)
        pw = Rot([P.psum(f"p2_pw{i}", [128, 512], F32) for i in range(2)])
        ptr = P.psum("p2_ptr2", [128, D], BF16)
        stg = P.sbuf("p2_stg2", [128, D], BF16)
        ptr32 = P.psum("p2_ptr32", [128, D], F32)
        stg32 = P.sbuf("p2_stg32", [128, D], F32)
        ps_r = P.psum("p2_psr", [128, 512], F32)
        E8 = P.sbuf("p2_E8", [128, 8], F32)
        mx = P.sbuf("p2_mx", [128, 1], F32)
        m1 = P.sbuf("p2_m1", [128, 1], F32)
        m2 = P.sbuf("p2_m2", [128, 1], F32)
        iseq = P.sbuf("p2_iseq", [128, 8], F32)
        e8b = P.sbuf("p2_e8b", [128, 8], F32)
        cw = P.sbuf("p2_cw", [128, 8], F32)
        for b in range(BLOC):
            for tt_ in range(S // 128):
                ti = b * (S // 128) + tt_
                sl = slice(ti * 128, (ti + 1) * 128)
                P.dma("sync", ot[:],
                      T["ctx0_fm"][b, :, :, tt_ * 128:(tt_ + 1) * 128]
                      .rearrange("c p t -> p c t"))
                P.dma("sync", xnt[:], T["xn_tm"][sl, :])
                for half in range(2):
                    p_ = pw.nxt()
                    hsl = slice(half * 512, (half + 1) * 512)
                    for ec in range(DC):
                        P.mm(p_, ot[:, ec, :], wo[:, ec, hsl],
                             start=(ec == 0), stop=(ec == DC - 1))
                    P.tt("vector", x2s[:, hsl], p_, _ap(xnt)[:, hsl], ALU.add)
                P.dma("gpsimd", T["x2_tm"][sl, :], x2s)
                ln_stats(P, x2s, mu, rstd, xc2, lsq, lacc)
                P.ts("vector", m32, xc2, rstd, ALU.mult)
                P.tt("vector", m32, m32, g2, ALU.mult)
                P.tt("vector", m32, m32, b2, ALU.add)
                P.cp("scalar", mb, m32)
                tm_tile_to_fm(P, "p2d", mb, eye, T["m_fm"], ti, ptr, stg,
                              also_f32=True, psum32=ptr32, eye32=eye32,
                              src32=m32, stage32=stg32, dst32=T["m_fm32"])
                for dc in range(DC):
                    P.mm(ps_r[:, 0:8], stg32[:, dc * 128:(dc + 1) * 128],
                         wg[:, dc, :], start=(dc == 0), stop=(dc == DC - 1))
                P.red("vector", mx, ps_r[:, 0:8], ALU.max)
                P.ts("vector", mx, mx, -1.0, ALU.mult)
                P.act(E8, ps_r[:, 0:8], AF.Exp, bias=mx)
                P.red("vector", m1, E8, ALU.max)
                P.ts("vector", iseq, E8, m1, ALU.is_equal)
                P.ts("vector", iseq, iseq, -1.0, ALU.mult, 1.0, ALU.add)
                P.tt("vector", e8b, E8, iseq, ALU.mult)
                P.red("vector", m2, e8b, ALU.max)
                P.ts("vector", e8b, E8, m2, ALU.is_ge)
                P.tt("vector", cw, E8, e8b, ALU.mult)
                P.tt("vector", m1, m1, m2, ALU.add)
                P.recip(m1, m1)
                P.ts("vector", cw, cw, m1, ALU.mult)
                P.dma("gpsimd", T["cw_tm"][sl, :], cw)



@phase("p3_moe0",
       reads=["m_fm", "cw_tm", "x2_tm", "w1m_bf", "L0_ffn_b1",
              "w2m_bf", "L0_ffn_b2", "c_ones_f32", "c_eye_f32"],
       writes=["ffn0_tm"])
def p3_moe0(P, T):
    P.ones_f32 = T["c_ones_f32"]
    NE, F_ = 8, 4096
    FC = F_ // 128
    with P.phase("p3"):
        eye32 = T["c_eye_f32"]
        b1row = P.sbuf("p3_b1row", [1, 1024], F32)
        w1 = P.sbuf("p3_w1", [128, DC, F_], BF16)
        w2 = P.sbuf("p3_w2", [128, FC, D], BF16)
        b1 = P.sbuf("p3_b1", [128, FC], F32)

        b2r = P.sbuf("p3_b2r", [1, D], F32)
        b2bc = P.sbuf("p3_b2bc", [128, D], F32)
        ps_b = P.psum("p3_psb", [128, 512], F32)
        mt = Rot([P.sbuf(f"p3_mt{i}", [128, DC, 512], BF16) for i in range(2)])
        h_s = P.sbuf("p3_h", [128, FC, 512], BF16)
        cw_s = Rot([P.sbuf(f"p3_cw{i}", [128, 4, NE], F32) for i in range(2)])
        acc = Rot([P.sbuf(f"p3_acc{i}", [128, D], F32) for i in range(2)])
        ytmp = Rot([P.sbuf(f"p3_yt{i}", [128, 512], F32) for i in range(1)])
        ph = Rot([P.psum(f"p3_ph{i}", [128, 512], F32) for i in range(3)])
        py = Rot([P.psum(f"p3_py{i}", [128, 512], F32) for i in range(4)])

        def load_w1(e):
            for dc in range(DC):
                P.dma("gpsimd" if dc % 2 else "sync", w1[:, dc, :],
                      T["w1m_bf"][e, dc, :, :])

        def load_w2b(e):
            for g in range(4):
                P.dma("gpsimd" if g % 2 else "sync",
                      w2[:, 8 * g:8 * g + 8, :]
                      .rearrange("p c d -> p c d"),
                      T["w2m_bf"][e, 8 * g:8 * g + 8, :, :]
                      .rearrange("c p d -> p c d"))
            for g in range(4):
                P.dma("gpsimd", b1row[:],
                      T["L0_ffn_b1"][e, g * 1024:(g + 1) * 1024]
                      .rearrange("(a d) -> a d", a=1))
                for j in range(8):
                    fc = g * 8 + j
                    P.transp(_ap(ps_b)[:, fc:fc + 1],
                             _ap(b1row)[0:1, j * 128:(j + 1) * 128],
                             _ap(eye32)[0:1, 0:1],
                             register=(fc == FC - 1), war=(fc == 0))
            P.cp("vector", b1[:], _ap(ps_b)[:, 0:FC])
            P.dma("gpsimd", b2r[:], T["L0_ffn_b2"][e, :]
                  .rearrange("(a d) -> a d", a=1))
            for j in range(2):
                P.mm(ps_b[:, 0:512], _ap(P.ones_f32)[0:1, :],
                     _ap(b2r)[0:1, j * 512:(j + 1) * 512])
                P.cp("vector", b2bc[:, j * 512:(j + 1) * 512], ps_b[:, 0:512])

        load_w1(0)
        load_w2b(0)
        for e in range(NE):
            for tmac in range(TLOC // 512):
                t0 = tmac * 512
                m_, c_ = mt.nxt(), cw_s.nxt()
                P.dma("sync", m_[:],
                      T["m_fm"][:, :, t0:t0 + 512].rearrange("c p t -> p c t"))
                P.dma("sync", c_[:],
                      T["cw_tm"][t0:t0 + 512, :]
                      .rearrange("(s p) e -> p s e", p=128))
                for fc in range(FC):
                    p_ = ph.nxt()
                    for dc in range(DC):
                        P.mm(p_, w1[:, dc, fc * 128:(fc + 1) * 128],
                             m_[:, dc, :], start=(dc == 0), stop=(dc == DC - 1))
                    P.act(h_s[:, fc, :], p_, AF.Gelu_apprx_tanh,
                          bias=_ap(b1)[:, fc:fc + 1])
                if tmac == TLOC // 512 - 1 and e + 1 < NE:
                    load_w1(e + 1)
                for st in range(4):
                    ssl = slice(st * 128, (st + 1) * 128)
                    a_ = acc.nxt()
                    src_t = T["x2_tm"] if e == 0 else T["ffn0_tm"]
                    P.dma("sync", a_[:],
                          src_t[t0 + st * 128:t0 + (st + 1) * 128, :])
                    for half in range(2):
                        hsl = slice(half * 512, (half + 1) * 512)
                        p_ = py.nxt()
                        yt = ytmp.nxt()
                        for fc in range(FC):
                            P.mm(p_, h_s[:, fc, ssl], w2[:, fc, hsl],
                                 start=(fc == 0), stop=(fc == FC - 1))
                        P.tt("vector", yt, p_, _ap(b2bc)[:, hsl], ALU.add)
                        P.stt("vector", a_[:, hsl], yt,
                              _ap(c_)[:, st, e:e + 1], a_[:, hsl],
                              ALU.mult, ALU.add)
                    P.dma("gpsimd",
                          T["ffn0_tm"][t0 + st * 128:t0 + (st + 1) * 128, :],
                          a_)
            if e + 1 < NE:
                load_w2b(e + 1)



@phase("p4_comb0",
       reads=["L0_attn_hidden_states", "att0_tm", "ffn0_tm", "c_eye_bf"],
       writes=["x1_tm", "xn1_tm", "xn1_fm"])
def p4_comb0(P, T):
    x = T["L0_attn_hidden_states"].rearrange("b s d -> (b s) d")
    eye_bf = T["c_eye_bf"]
    with P.phase("p4"):
        R2 = lambda nm, sh, dt: Rot([P.sbuf(f"{nm}{i}", sh, dt) for i in range(2)])
        xt_r = R2("p4_xt", [128, D], F32)
        at_r = R2("p4_at", [128, D], F32)
        ft_r = R2("p4_ft", [128, D], F32)
        xc_r = R2("p4_xc", [128, D], F32)
        xnb_r = R2("p4_xnb", [128, D], BF16)
        mu_r = R2("p4_mu", [128, 1], F32)
        rstd_r = R2("p4_rstd", [128, 1], F32)
        lsq = P.sbuf("p4_sq", [128, D], F32)
        lacc_r = R2("p4_acc", [128, 1], F32)
        ptr_r = Rot([P.psum(f"p4_ptr{i}", [128, D], BF16) for i in range(2)])
        stg_r = R2("p4_stg", [128, D], BF16)
        for ti in range(TLOC // 128):
            sl = slice(ti * 128, (ti + 1) * 128)
            xt, at, ft, xc = xt_r.nxt(), at_r.nxt(), ft_r.nxt(), xc_r.nxt()
            xnb, mu = xnb_r.nxt(), mu_r.nxt()
            rstd, lacc = rstd_r.nxt(), lacc_r.nxt()
            ptr, stg = ptr_r.nxt(), stg_r.nxt()
            P.dma("sync", xt[:], x[sl, :])
            P.dma("sync", at[:], T["att0_tm"][sl, :])
            P.dma("sync", ft[:], T["ffn0_tm"][sl, :])
            P.tt("vector", xt, xt, at, ALU.add)
            P.tt("vector", xt, xt, ft, ALU.add)
            P.dma("gpsimd", T["x1_tm"][sl, :], xt)
            ln_stats(P, xt, mu, rstd, xc, lsq, lacc)
            P.ts("vector", xc[:], xc[:], rstd[:], ALU.mult)
            P.dma("gpsimd", T["xn1_tm"][sl, :], xc)
            P.cp("scalar", xnb, xc)
            tm_tile_to_fm(P, "p4", xnb, eye_bf, T["xn1_fm"], ti, ptr, stg)



@phase("p5_att1",
       reads=["xn1_fm", "L1_attn_w_qkv", "L1_attn_b_qkv", "L1_attn_w_out",
              "L1_attn_b_out", "c_eye_bf", "c_ones_f32", "c_ones_bf"],
       writes=["att1_tm"],
       privs=["qkv1_fm", "v1_tm", "ctx1_fm"])
def p5_att1(P, T):
    eye = T["c_eye_bf"]
    P.ones_f32 = T["c_ones_f32"]
    P.ones_bf_col = _ap(T["c_ones_bf"])[:, 0:1]
    P.ones_f32_col64 = _ap(T["c_ones_f32"])[0:64, 0:1]
    P.ones_f32_row64 = _ap(T["c_ones_f32"])[0:1, 0:64]
    NH = 16
    bnd_q = P.sbuf_persist("p5_bndq", [1, 16], F32)
    bnd_k = P.sbuf_persist("p5_bndk", [1, 16], F32)
    bt1 = P.sbuf_persist("p5_bt1", [1, 1], F32)
    P.memset("vector", bnd_q[:], 0.0)
    P.memset("vector", bnd_k[:], 0.0)
    qkmap = {}
    for _h in range(NH):
        for _kind, _pos in ((0, 192 * _h), (1, 192 * _h + 64)):
            qkmap.setdefault(_pos // 128, []).append((_kind, _h, _pos % 128))

    P.scope = "p5_b"
    with P.phase("p5b"):
        wqkvT = P.sbuf("p5_wqkvT", [128, DC, 3 * D], BF16)
        load_wT(P, "p5qkv", T["L1_attn_w_qkv"], wqkvT, eye, n_out=3 * D)
        bq = P.sbuf("p5_bq", [128, 24], F32)
        for oc in range(24):
            P.dma("sync", bq[:, oc:oc + 1],
                  T["L1_attn_b_qkv"][oc * 128:(oc + 1) * 128]
                  .rearrange("(p a) -> p a", a=1))
        xt = P.sbuf("p5_xt", [128, DC, 512], BF16)
        pos = Rot([P.psum(f"p5_po{i}", [128, 512], F32) for i in range(2)])
        obs = Rot([P.sbuf(f"p5_ob{i}", [128, 512], BF16) for i in range(4)])
        ps_bnd = Rot([P.psum(f"p5_psbnd{i}", [128, 512], F32) for i in range(2)])
        sqb_r = Rot([P.sbuf(f"p5_sqb{i}", [64, 512], BF16) for i in range(2)])
        ones_bf = _ap(T["c_ones_bf"])
        for b in range(BLOC):
            for tmac in range(S // 512):
                t0 = b * S + tmac * 512
                P.dma("sync", xt[:],
                      T["xn1_fm"][:, :, t0:t0 + 512]
                      .rearrange("c p t -> p c t"))
                for oc in range(24):
                    po, ob = pos.nxt(), obs.nxt()
                    for dc in range(DC):
                        P.mm(po, wqkvT[:, dc, oc * 128:(oc + 1) * 128],
                             xt[:, dc, :], start=(dc == 0), stop=(dc == DC - 1))
                    P.ts("vector", ob, po, _ap(bq)[:, oc:oc + 1], ALU.add)
                    P.dma("gpsimd",
                          T["qkv1_fm"][b, oc, :, tmac * 512:(tmac + 1) * 512],
                          ob)
                    for kind, hh, base in qkmap.get(oc, []):
                        bnd = bnd_k if kind else bnd_q
                        sqb = sqb_r.nxt()
                        pbn = ps_bnd.nxt()
                        P.tt("vector", sqb, _ap(ob)[base:base + 64, :],
                             _ap(ob)[base:base + 64, :], ALU.mult)
                        P.mm(_ap(pbn)[0:1, :], ones_bf[0:64, 0:1], sqb)
                        P.red("vector", bt1, _ap(pbn)[0:1, :], ALU.max)
                        P.tt("vector", bnd[0:1, hh:hh + 1],
                             bnd[0:1, hh:hh + 1], bt1, ALU.max)
                for st in range(4):
                    tc = tmac * 4 + st
                    for half in range(2):
                        po, ob = pos.nxt(), obs.nxt()
                        rhsv = _ap(wqkvT)[:, :, :].rearrange(
                            "p c (h x) -> p c h x", h=NH)[
                            :, :, half * 8:(half + 1) * 8, 128:192]
                        for dc in range(DC):
                            P.mm(po, xt[:, dc, st * 128:(st + 1) * 128],
                                 rhsv[:, dc], start=(dc == 0),
                                 stop=(dc == DC - 1))
                        P.cp("vector", ob, po)
                        P.dma("gpsimd",
                              T["v1_tm"][b, tc, :, half * 512:(half + 1) * 512],
                              ob)

    P.scope = "p5_c"
    with P.phase("p5c"):
        v_r = Rot([P.sbuf(f"p5_vs{i}", [128, S // 128, NH * 65], BF16)
                   for i in range(2)])
        qh_r = Rot([P.sbuf(f"p5_qh{i}", [64, S], BF16) for i in range(2)])
        kh_r = Rot([P.sbuf(f"p5_kh{i}", [64, S], BF16) for i in range(2)])
        tmp1 = P.sbuf("p5_tmp1", [1, 1], F32)
        negc = Rot([P.sbuf(f"p5_negc{i}", [128, 1], F32) for i in range(2)])
        ps_n = P.psum("p5_psn", [128, 512], F32)
        ps_sT = Rot([P.psum(f"p5_psT{i}", [128, 1024], F32) for i in range(2)])
        E_r = Rot([P.sbuf(f"p5_E{i}", [128, 1024], BF16) for i in range(2)])
        ps_ctx = Rot([P.psum(f"p5_psctx{i}", [128, 512], F32) for i in range(2)])
        ps_sig = P.psum("p5_pssig", [128, 512], F32)
        sig_r = Rot([P.sbuf(f"p5_sig{i}", [1, 512], F32) for i in range(2)])
        sg_r = Rot([P.sbuf(f"p5_sg{i}", [64, 512], F32) for i in range(2)])
        ctx_r = Rot([P.sbuf(f"p5_ctxb{i}", [64, 512], BF16) for i in range(2)])
        bq2 = P.sbuf("p5_bq2", [128, 24], F32)
        for oc in range(24):
            P.dma("sync", bq2[:, oc:oc + 1],
                  T["L1_attn_b_qkv"][oc * 128:(oc + 1) * 128]
                  .rearrange("(p a) -> p a", a=1))
        at_bufs = (ps_sT, E_r, ps_ctx, ps_sig, sig_r, sg_r, ctx_r)
        for vb in v_r.items:
            P.memset("vector",
                 _ap(vb).rearrange("p c (h w) -> p c h w", w=65)[:, :, :, 64:65],
                 1.0)
        for b in range(BLOC):
            v_s = v_r.nxt()
            for tc in range(S // 128):
                P.dma("gpsimd",
                      _ap(v_s)[:, tc, :].rearrange("p (h w) -> p h w", w=65)
                      [:, :, 0:64],
                      T["v1_tm"][b, tc, :, :].rearrange("p (h w) -> p h w", w=64))
            for h in range(NH):
                fq, fk, fv = 192 * h, 192 * h + 64, 192 * h + 128
                qh_s, kh_s = qh_r.nxt(), kh_r.nxt()
                P.dma("sync", qh_s[:],
                      T["qkv1_fm"][b, fq // 128, fq % 128:fq % 128 + 64, :])
                P.dma("sync", kh_s[:],
                      T["qkv1_fm"][b, fk // 128, fk % 128:fk % 128 + 64, :])
                bv_col = _ap(bq2)[fv % 128:fv % 128 + 64,
                                  fv // 128:fv // 128 + 1]
                nc_ = negc.nxt()
                P.tt("vector", tmp1, bnd_q[0:1, h:h + 1],
                     bnd_k[0:1, h:h + 1], ALU.mult)
                P.act(tmp1, tmp1, AF.Ln, bias=1e-30)
                P.act(tmp1, tmp1, AF.Exp, scale=0.5)
                P.mm(_ap(ps_n)[:, 0:1], _ap(P.ones_f32)[0:1, :], tmp1)
                P.act(nc_, _ap(ps_n)[:, 0:1], AF.Copy, scale=-0.75 * 0.125)
                softmax_TQ(P, at_bufs, _ap(kh_s), _ap(qh_s), v_s, h, nc_,
                           T["ctx1_fm"][b, h // 2,
                                        64 * (h % 2):64 * (h % 2) + 64, :],
                           0.125, bv=bv_col)

    P.scope = "p5_d"
    with P.phase("p5d"):
        woT = P.sbuf("p5_woT", [128, DC, D], BF16)
        load_wT(P, "p5wo", T["L1_attn_w_out"], woT, eye)
        bor = row_of(P, "p5bo", T["L1_attn_b_out"], D)
        bo = bcast_row(P, "p5bo", bor, D, P.ones_f32)
        ots = Rot([P.sbuf(f"p5_ot{i}", [128, DC, 128], BF16) for i in range(2)])
        pws = Rot([P.psum(f"p5_pw{i}", [128, 512], F32) for i in range(4)])
        oss = Rot([P.sbuf(f"p5_os{i}", [128, D], F32) for i in range(2)])
        for b in range(BLOC):
            for tt_ in range(S // 128):
                ot = ots.nxt()
                os_ = oss.nxt()
                P.dma("sync", ot[:],
                      T["ctx1_fm"][b, :, :, tt_ * 128:(tt_ + 1) * 128]
                      .rearrange("c p t -> p c t"))
                for half in range(2):
                    pw = pws.nxt()
                    hsl = slice(half * 512, (half + 1) * 512)
                    for ec in range(DC):
                        P.mm(pw, ot[:, ec, :], woT[:, ec, hsl],
                             start=(ec == 0), stop=(ec == DC - 1))
                    P.tt("vector", os_[:, hsl], pw, _ap(bo)[:, hsl], ALU.add)
                P.dma("gpsimd",
                      T["att1_tm"][b * S + tt_ * 128:b * S + (tt_ + 1) * 128, :],
                      os_)


def rms_tile(P, xt, gbc, out32, ssq, bufs):
    sq, r = bufs
    P.act(sq, xt, AF.Square, accum=ssq)
    P.act(ssq, ssq, AF.Ln, bias=1e-30)
    P.act(r, ssq, AF.Exp, scale=-0.5)
    P.ts("vector", out32, xt, r, ALU.mult)
    P.tt("vector", out32, out32, gbc, ALU.mult)



@phase("p6_sm_a",
       reads=["xn1_tm", "L1_ffn_gamma_x", "L1_ffn_gamma_slot",
              "L1_ffn_slot_embeds", "L1_ffn_w1", "L1_ffn_w2",
              "c_eye_bf", "c_ones_f32", "c_ones_bf"],
       writes=["xnn_fm", "se_fm", "proc_tm"],
       privs=["xnn_tmb", "w1l1_bf", "w2l1_bf"])
def p6_sm_a(P, T):
    eye = T["c_eye_bf"]
    P.ones_f32 = T["c_ones_f32"]
    NES = 4096
    FC = 32

    P.scope = "p6_prep"
    with P.phase("p6prep"):
        wt32 = P.sbuf("p6_wt32", [128, 4096], F32)
        wtb = P.sbuf("p6_wtb", [128, 4096], BF16)
        for dc in range(DC):
            P.dma("sync", wt32[:], T["L1_ffn_w1"][dc * 128:(dc + 1) * 128, :])
            P.cp("vector", wtb, wt32)
            P.dma("gpsimd",
                  T["w1l1_bf"][:, :, dc * 128:(dc + 1) * 128]
                  .rearrange("c p f -> p c f"),
                  wtb[:].rearrange("p (c f) -> p c f", c=32))
        for fc in range(FC):
            P.dma("sync", _ap(wt32)[:, 0:D],
                  T["L1_ffn_w2"][fc * 128:(fc + 1) * 128, :])
            P.cp("vector", _ap(wtb)[:, 0:D], _ap(wt32)[:, 0:D])
            P.dma("gpsimd", T["w2l1_bf"][fc, :, :], _ap(wtb)[:, 0:D])
        gxr = row_of(P, "p6gx", T["L1_ffn_gamma_x"], D)
        gsr = row_of(P, "p6gs", T["L1_ffn_gamma_slot"], D)
        gx = bcast_row(P, "p6gx", gxr, D, P.ones_f32, scale=32.0)
        gs = bcast_row(P, "p6gs", gsr, D, P.ones_f32, scale=32.0)
        R2 = lambda nm, sh, dt: Rot([P.sbuf(f"{nm}{i}", sh, dt) for i in range(2)])
        xt_r = R2("p6_xt", [128, D], F32)
        o32_r = R2("p6_o32", [128, D], F32)
        ob_r = R2("p6_ob", [128, D], BF16)
        sq = P.sbuf("p6_sq", [128, D], F32)
        ssq_r = R2("p6_ssq", [128, 1], F32)
        rr_r = R2("p6_rr", [128, 1], F32)
        ptr_r = Rot([P.psum(f"p6_ptr{i}", [128, D], BF16) for i in range(2)])
        stg_r = R2("p6_stg", [128, D], BF16)
        for ti in range(TLOC // 128):
            b, tt_ = ti // (S // 128), ti % (S // 128)
            xt, o32, ob = xt_r.nxt(), o32_r.nxt(), ob_r.nxt()
            ssq, rr, ptr, stg = ssq_r.nxt(), rr_r.nxt(), ptr_r.nxt(), stg_r.nxt()
            P.dma("sync", xt[:], T["xn1_tm"][ti * 128:(ti + 1) * 128, :])
            rms_tile(P, xt, gx, o32, ssq, (sq, rr))
            P.cp("scalar", ob, o32)
            P.dma("gpsimd", T["xnn_tmb"][b, tt_, :, :], ob)
            tm_tile_to_fm(P, "p6", ob, eye, T["xnn_fm"][b], tt_, ptr, stg)
        sed = T["L1_ffn_slot_embeds"].rearrange("e t d -> (e t) d")
        for si in range(NES // 128):
            xt, o32, ob = xt_r.nxt(), o32_r.nxt(), ob_r.nxt()
            ssq, rr, ptr, stg = ssq_r.nxt(), rr_r.nxt(), ptr_r.nxt(), stg_r.nxt()
            P.dma("sync", xt[:], sed[si * 128:(si + 1) * 128, :])
            rms_tile(P, xt, gs, o32, ssq, (sq, rr))
            P.cp("scalar", ob, o32)
            tm_tile_to_fm(P, "p6se", ob, eye, T["se_fm"], si, ptr, stg)

    P.scope = "p6_a"
    with P.phase("p6a"):
        xnn_fm_s = P.sbuf("p6_xf", [128, DC, S], BF16)
        xnn_tm_s = P.sbuf("p6_xt2", [128, S // 128, D], BF16)
        se_s = P.sbuf("p6_ses", [128, DC, 512], BF16)
        dT_s = P.sbuf("p6_dT", [128, S // 128, 512], BF16)
        disp_r = Rot([P.sbuf(f"p6_disp{i}", [128, S], BF16) for i in range(2)])
        slots_s = P.sbuf("p6_slots", [128, DC, 512], BF16)
        h_s = P.sbuf("p6_h", [128, FC, 512], BF16)
        proc_s = P.sbuf("p6_proc", [128, 4, D], BF16)
        w1t = Rot([P.sbuf(f"p6_w1t{i}", [128, 2, DC, 128], BF16)
                   for i in range(2)])
        w2t = Rot([P.sbuf(f"p6_w2t{i}", [128, 4, 512], BF16) for i in range(2)])
        mx = P.sbuf("p6_mx", [128, 1], F32)
        tmx = P.sbuf("p6_tmx", [128, 1], F32)
        as_ = P.sbuf("p6_as", [128, 4], F32)
        rsum = P.sbuf("p6_rsum", [128, 1], F32)
        pb = [P.psum(f"p6_pb{i}", [128, 512], F32) for i in range(6)]
        ptr_r6 = Rot([P.psum(f"p6_ptrb{i}", [128, 1024], BF16)
                      for i in range(2)])
        slots_ps = Rot([pb[5], pb[0]])
        h_ps = Rot([pb[1], pb[2]])
        for b in range(BLOC):
            P.dma("sync", xnn_fm_s[:],
                  T["xnn_fm"][b].rearrange("c p t -> p c t"))
            P.dma("sync", xnn_tm_s[:],
                  T["xnn_tmb"][b].rearrange("c p d -> p c d"))
            for em in range(8):
                e0 = em * 512
                P.dma("sync", se_s[:],
                      T["se_fm"][:, :, e0:e0 + 512].rearrange("c p t -> p c t"))
                def disp_tail(sc, scl, dsp, banks):
                    def exps():
                        for tq in range(4):
                            P.act(_ap(dsp)[:, tq * 512:(tq + 1) * 512],
                                  banks[tq], AF.Exp, bias=mx,
                                  accum=_ap(as_)[:, tq:tq + 1])
                            yield
                        P.red("vector", rsum, as_, ALU.add)
                        P.recip(rsum, rsum)
                        P.ts("vector", dsp, dsp, rsum, ALU.mult)
                        yield
                        for g in range(2):
                            pt6 = ptr_r6.nxt()
                            for j in range(8):
                                tc = g * 8 + j
                                P.transp(_ap(pt6)[:, j * 128:(j + 1) * 128],
                                         _ap(dsp)[:, tc * 128:(tc + 1) * 128],
                                         eye, register=(j == 7), war=(j == 0))
                            P.cp("vector",
                                 dT_s[:, g * 8:(g + 1) * 8, scl],
                                 _ap(pt6).rearrange("p (c t) -> p c t", c=8))
                            yield
                    return exps()

                pend = None
                for sc in range(4):
                    scl = slice(sc * 128, (sc + 1) * 128)
                    banks = [pb[(sc * 4 + tq) % 5] for tq in range(4)]
                    for tq in range(4):
                        if pend is not None:
                            next(pend, None)
                        for dc in range(DC):
                            P.mm(banks[tq], se_s[:, dc, scl],
                                 xnn_fm_s[:, dc, tq * 512:(tq + 1) * 512],
                                 start=(dc == 0), stop=(dc == DC - 1))
                    if pend is not None:
                        for _ in pend:
                            pass
                    for tq in range(4):
                        if tq == 0:
                            P.red("vector", mx, banks[0], ALU.max)
                        else:
                            P.red("vector", tmx, banks[tq], ALU.max)
                            P.tt("vector", mx, mx, tmx, ALU.max)
                    P.ts("vector", mx, mx, -1.0, ALU.mult)
                    pend = disp_tail(sc, scl, disp_r.nxt(), banks)
                for _ in pend:
                    pass
                for dc in range(DC):
                    p_ = slots_ps.nxt()
                    for tc in range(S // 128):
                        P.mm(p_, xnn_tm_s[:, tc, dc * 128:(dc + 1) * 128],
                             dT_s[:, tc, :], start=(tc == 0),
                             stop=(tc == S // 128 - 1))
                    P.cp("vector", slots_s[:, dc, :], p_)
                for fc2 in range(FC // 2):
                    w1c = w1t.nxt()
                    P.dma("sync" if fc2 % 2 else "gpsimd",
                          w1c[:].rearrange("p a c f -> p a (c f)"),
                          T["w1l1_bf"][2 * fc2:2 * fc2 + 2, :, :]
                          .rearrange("a p f -> p a f"))
                    for j in range(2):
                        fc = 2 * fc2 + j
                        p_ = h_ps.nxt()
                        for dc in range(DC):
                            P.mm(p_, w1c[:, j, dc, :], slots_s[:, dc, :],
                                 start=(dc == 0), stop=(dc == DC - 1))
                        P.act(h_s[:, fc, :], p_, AF.Gelu)
                for half in range(2):
                    hsl = slice(half * 512, (half + 1) * 512)
                    for fc4 in range(FC // 4):
                        w2c = w2t.nxt()
                        P.dma("sync" if fc4 % 2 else "gpsimd",
                              w2c[:],
                              T["w2l1_bf"][4 * fc4:4 * fc4 + 4, :, hsl]
                              .rearrange("a p f -> p a f"))
                        for j in range(4):
                            fc = 4 * fc4 + j
                            for st in range(4):
                                P.mm(pb[st],
                                     h_s[:, fc, st * 128:(st + 1) * 128],
                                     w2c[:, j, :], start=(fc == 0),
                                     stop=(fc == FC - 1))
                    for st in range(4):
                        P.cp("vector", proc_s[:, st, hsl], pb[st])
                for st in range(4):
                    P.dma("gpsimd",
                          T["proc_tm"][b, e0 + st * 128:e0 + (st + 1) * 128, :],
                          proc_s[:, st, :])



@phase("p7_sm_b",
       reads=["xnn_fm", "se_fm", "proc_tm", "c_eye_bf", "c_ones_f32"],
       writes=["ffn1_tm"])
def p7_sm_b(P, T):
    eye = T["c_eye_bf"]
    NES = 4096
    with P.phase("p7"):
        se_fm_s = P.sbuf("p7_se", [128, DC, NES], BF16)
        xnn_fm_s = P.sbuf("p7_xf", [128, DC, S], BF16)
        combT_s = P.sbuf("p7_cT", [128, NES // 128, 512], BF16)
        comb_r = Rot([P.sbuf(f"p7_comb{i}", [128, NES], BF16) for i in range(2)])
        lg_r = Rot([P.sbuf(f"p7_lg{i}", [128, NES], F32) for i in range(1)])
        out_s = P.sbuf("p7_out", [128, 4, D], F32)
        proc_t = Rot([P.sbuf(f"p7_pt{i}", [128, 512], BF16) for i in range(2)])
        mx = P.sbuf("p7_mx", [128, 1], F32)
        rsum = P.sbuf("p7_rsum", [128, 1], F32)
        sq8 = P.sbuf("p7_sq8", [128, 8], F32)
        pb = [P.psum(f"p7_pb{i}", [128, 512], F32) for i in range(4)]
        ptr_bf = P.psum("p7_ptrb", [128, 1024], BF16)
        P.dma("sync", se_fm_s[:], T["se_fm"].rearrange("c p t -> p c t"))
        for b in range(BLOC):
            P.dma("sync", xnn_fm_s[:],
                  T["xnn_fm"][b].rearrange("c p t -> p c t"))
            for sm in range(4):
                s0 = sm * 512
                def comb_tail(tq, lg, cmb):
                    def gen():
                        P.red("vector", mx, lg, ALU.max)
                        P.ts("vector", mx, mx, -1.0, ALU.mult)
                        P.act(cmb, lg, AF.Exp, bias=mx, accum=rsum)
                        P.recip(rsum, rsum)
                        P.ts("vector", cmb, cmb, rsum, ALU.mult)
                        yield
                        for g in range(4):
                            for j in range(8):
                                etc = g * 8 + j
                                P.transp(ptr_bf[:, j * 128:(j + 1) * 128],
                                         _ap(cmb)[:, etc * 128:(etc + 1) * 128],
                                         eye, register=(j == 7), war=(j == 0))
                            P.cp("vector",
                                 combT_s[:, g * 8:(g + 1) * 8,
                                         tq * 128:(tq + 1) * 128],
                                 ptr_bf[:].rearrange("p (c t) -> p c t", c=8))
                            yield
                    return gen()

                pend = None
                for tq in range(4):
                    tsl = slice(s0 + tq * 128, s0 + (tq + 1) * 128)
                    lg = lg_r.nxt()
                    for eb in range(8):
                        if pend is not None:
                            next(pend, None)
                        p_ = pb[eb % 4]
                        for dc in range(DC):
                            P.mm(p_, xnn_fm_s[:, dc, tsl],
                                 se_fm_s[:, dc, eb * 512:(eb + 1) * 512],
                                 start=(dc == 0), stop=(dc == DC - 1))
                        P.cp("vector", _ap(lg)[:, eb * 512:(eb + 1) * 512], p_)
                    if pend is not None:
                        for _ in pend:
                            pass
                    pend = comb_tail(tq, lg, comb_r.nxt())
                for _ in pend:
                    pass
                for half in range(2):
                    hsl = slice(half * 512, (half + 1) * 512)
                    for etc in range(NES // 128):
                        pc = proc_t.nxt()
                        P.dma("sync" if etc % 2 else "gpsimd", pc[:],
                              T["proc_tm"][b, etc * 128:(etc + 1) * 128, hsl])
                        for st in range(4):
                            P.mm(pb[st], combT_s[:, etc, st * 128:(st + 1) * 128],
                                 pc[:], start=(etc == 0),
                                 stop=(etc == NES // 128 - 1))
                    for st in range(4):
                        P.cp("vector", out_s[:, st, hsl], pb[st])
                for st in range(4):
                    P.dma("gpsimd",
                          T["ffn1_tm"][b * S + s0 + st * 128:
                                       b * S + s0 + (st + 1) * 128, :],
                          out_s[:, st, :])



@phase("p8_out",
       reads=["x1_tm", "att1_tm", "ffn1_tm"],
       writes=["out"])
def p8_out(P, T):
    o = T["out"].rearrange("b s d -> (b s) d")
    with P.phase("p8"):
        xts = Rot([P.sbuf(f"p8_xt{i}", [128, D], F32) for i in range(2)])
        ats = Rot([P.sbuf(f"p8_at{i}", [128, D], F32) for i in range(2)])
        fts = Rot([P.sbuf(f"p8_ft{i}", [128, D], F32) for i in range(2)])
        for ti in range(TLOC // 128):
            sl = slice(ti * 128, (ti + 1) * 128)
            xt, at, ft = xts.nxt(), ats.nxt(), fts.nxt()
            P.dma("sync", xt[:], T["x1_tm"][sl, :])
            P.dma("sync", at[:], T["att1_tm"][sl, :])
            P.dma("sync", ft[:], T["ffn1_tm"][sl, :])
            P.tt("vector", xt, xt, at, ALU.add)
            P.tt("vector", xt, xt, ft, ALU.add)
            P.dma("gpsimd", o[sl, :], xt)



def make_consts():
    return {
        "c_eye_f32": np.eye(128, dtype=np.float32),
        "c_eye_bf": np.eye(128, dtype=ml_bf16()),
        "c_mask_ut": np.triu(np.ones((128, 128), dtype=np.float32)),
        "c_ones_f32": np.ones((128, 128), dtype=np.float32),
        "c_ones_bf": np.ones((128, 128), dtype=ml_bf16()),
    }


def ml_bf16():
    import ml_dtypes
    return ml_dtypes.bfloat16


def _declare(P, names, kind):
    out = {}
    for n in names:
        sh, dt = spec_of(n)
        out[n] = P.dram(n, sh, dt, kind=kind)
    return out


def run_unfused(inputs, upto=None, only=None):
    consts = make_consts()
    x = np.ascontiguousarray(inputs["L0_attn_hidden_states"])
    pools = [dict() for _ in range(NCORES)]
    phases = PHASES[:upto] if upto else PHASES
    if only:
        phases = [p for p in phases if p[0] in only]
    for (name, fn, reads, writes, privs) in phases:
        P = Prog(name)
        T = {}
        T.update(_declare(P, reads, "ExternalInput"))
        T.update(_declare(P, writes, "ExternalOutput"))
        T.update(_declare(P, privs, "Internal"))
        for n in reads:
            if n in CONST_SPECS:
                sh, dt = CONST_SPECS[n]
                sb = P.sbuf("s_" + n, sh, dt)
                P.dma("sync", sb[:], T[n])
                T[n] = sb
        fn(P, T)
        P.barrier()
        nc = P.finalize()
        in_maps = []
        for c in range(NCORES):
            m = {}
            for n in reads:
                if n == "L0_attn_hidden_states":
                    m[n] = x[BLOC * c:BLOC * (c + 1)]
                elif n in INPUT_SPECS:
                    m[n] = inputs[n]
                elif n in CONST_SPECS:
                    m[n] = consts[n]
                else:
                    m[n] = pools[c][n]
            in_maps.append(m)
        res = run_bass_kernel_spmd(nc, in_maps, core_ids=list(range(NCORES)))
        for c in range(NCORES):
            pools[c].update(res.results[c])
    return pools


def run_fused(inputs):
    consts = make_consts()
    x = np.ascontiguousarray(inputs["L0_attn_hidden_states"])
    P = Prog("mega")
    T = {}
    ext_in, scratch = [], set()
    for (name, fn, reads, writes, privs) in PHASES:
        for n in reads:
            if n in INPUT_SPECS or n in CONST_SPECS:
                if n not in T:
                    T[n] = _declare(P, [n], "ExternalInput")[n]
                    ext_in.append(n)
            else:
                scratch.add(n)
        for n in writes:
            if n != "out":
                scratch.add(n)
        for n in privs:
            scratch.add(n)
    T.update(_declare(P, sorted(scratch), "Internal"))
    T.update(_declare(P, ["out"], "ExternalOutput"))
    for n in list(T):
        if n in CONST_SPECS:
            sh, dt = CONST_SPECS[n]
            sb = P.sbuf("s_" + n, sh, dt)
            P.dma("sync", sb[:], T[n])
            T[n] = sb
    for (name, fn, reads, writes, privs) in PHASES:
        P.scope = name
        fn(P, T)
    P.scope = None
    P.barrier()
    nc = P.finalize()
    in_maps = []
    for c in range(NCORES):
        m = {}
        for n in ext_in:
            if n == "L0_attn_hidden_states":
                m[n] = x[BLOC * c:BLOC * (c + 1)]
            elif n in CONST_SPECS:
                m[n] = consts[n]
            else:
                m[n] = inputs[n]
        in_maps.append(m)
    res = run_bass_kernel_spmd(nc, in_maps, core_ids=list(range(NCORES)))
    return np.concatenate([res.results[c]["out"] for c in range(NCORES)],
                          axis=0)


FUSED = True


def kernel(**inputs):
    if FUSED:
        return run_fused(inputs)
    pools = run_unfused(inputs)
    return np.concatenate([pools[c]["out"] for c in range(NCORES)], axis=0)
